# version 05
# baseline (speedup 1.0000x reference)
import numpy as np
import concourse.bass as bass
import concourse.mybir as mybir
from concourse.bass_utils import run_bass_kernel_spmd
from contextlib import ExitStack

F32 = mybir.dt.float32
BF16 = mybir.dt.bfloat16
AF = mybir.ActivationFunctionType
ALU = mybir.AluOpType
AX = mybir.AxisListType

T = 2048
D = 1024
E = 2048
NST = 16
DTR = 64
NEXP = 8
HMOE = 2048
NH = 16
HD = 64
NB = 512
H1 = 4096
DT = D // 128
ET = E // 128

LN_EPS = 1e-5
EPS_ATTN = 1e-6
EPS_LN_ATTN = 1e-12
FFN_EPS = 1e-6
TAU = 10.0

ENGS = ("tensor", "vector", "scalar", "gpsimd", "sync")


class _Node:
    __slots__ = ("eng", "fn", "inc", "deps", "need", "val", "dma", "sid")

    def __init__(self, eng, fn, inc, dma):
        self.eng = eng
        self.fn = fn
        self.inc = inc
        self.deps = []
        self.need = False
        self.val = 0
        self.dma = dma
        self.sid = ("d", eng) if dma else ("c", eng)


class P:

    def __init__(self, nc):
        self.nc = nc
        self.nodes = []
        self.res = {}
        self.cvals = None
        self.cmap = {}
        self.frozen_consts = False
        self.last_dma = {}

    def emit(self, eng_name, fn, rd=(), wr=(), inc=1, dma=False):
        node = _Node(eng_name, fn, inc, dma)
        dep = {}

        def add_dep(pn):
            if pn is node:
                return
            if pn.eng == eng_name == "tensor" and not (pn.dma or dma):
                return
            dep[id(pn)] = pn
        for k in rd:
            e = self.res.get(k)
            if e and e[0] is not None:
                add_dep(e[0])
        for k in wr:
            e = self.res.get(k)
            if e:
                if e[0] is not None:
                    add_dep(e[0])
                for rn in e[1].values():
                    add_dep(rn)
        if dma:
            prev = self.last_dma.get(eng_name)
            if prev is not None:
                add_dep(prev)
            self.last_dma[eng_name] = node
            node.need = True
        node.deps = list(dep.values())
        for pn in node.deps:
            pn.need = True
        for k in rd:
            e = self.res.setdefault(k, [None, {}])
            e[1][eng_name] = node
        for k in wr:
            self.res[k] = [node, {}]
        self.nodes.append(node)
        return node

    def barrier(self):
        self.nodes.append("BARRIER")

    def finalize(self, sems):
        SIDS = [("c", e) for e in ENGS] + [("d", e) for e in ENGS]
        counts = {s: 0 for s in SIDS}
        last = {s: None for s in SIDS}
        for nd in self.nodes:
            if nd == "BARRIER":
                for s in SIDS:
                    if last[s] is not None:
                        last[s].need = True
                continue
            last[nd.sid] = nd
        for s in SIDS:
            if last[s] is not None:
                last[s].need = True
        for nd in self.nodes:
            if nd == "BARRIER":
                continue
            if nd.need:
                counts[nd.sid] += nd.inc
                nd.val = counts[nd.sid]
        lastv = {s: 0 for s in SIDS}
        waited = {e: {s: 0 for s in SIDS} for e in ENGS}
        for nd in self.nodes:
            if nd == "BARRIER":
                snap = dict(lastv)
                for e in ENGS:
                    eng = getattr(self.nc, e)
                    for s in SIDS:
                        if s != ("c", e) and snap[s] > waited[e][s]:
                            eng.wait_ge(sems[s], snap[s])
                            waited[e][s] = snap[s]
                continue
            eng = getattr(self.nc, nd.eng)
            need_wait = {}
            for pn in nd.deps:
                if pn.val > waited[nd.eng].get(pn.sid, 0):
                    need_wait[pn.sid] = max(need_wait.get(pn.sid, 0), pn.val)
            for s, v in need_wait.items():
                eng.wait_ge(sems[s], v)
                waited[nd.eng][s] = v
            inst = nd.fn(eng)
            if nd.need:
                inst.then_inc(sems[nd.sid], nd.inc)
                lastv[nd.sid] = nd.val
        for s in SIDS:
            if s != ("c", "sync") and lastv[s] > waited["sync"][s]:
                self.nc.sync.wait_ge(sems[s], lastv[s])

    def dma(self, out, in_, rd=(), wr=(), q="sync"):
        self.emit(q, lambda e: e.dma_start(out=out, in_=in_), rd=rd, wr=wr,
                  inc=16, dma=True)

    def mm(self, out, lhsT, rhs, rd=(), wr=(), start=True, stop=True):
        self.emit("tensor", lambda e: e.matmul(out, lhsT, rhs, start=start, stop=stop),
                  rd=rd, wr=wr)

    def trans(self, out, in_, ident, rd=(), wr=()):
        self.emit("tensor", lambda e: e.transpose(out, in_, ident), rd=rd, wr=wr)

    def act(self, out, in_, func, rd=(), wr=(), bias=0.0, scale=1.0, accum_out=None):
        self.emit("scalar", lambda e: e.activation(out, in_, func, bias=bias,
                                                   scale=scale, accum_out=accum_out),
                  rd=rd, wr=wr)

    def cp(self, out, in_, rd=(), wr=(), eng="vector"):
        if eng == "scalar":
            self.emit(eng, lambda e: e.copy(out, in_), rd=rd, wr=wr)
        else:
            self.emit(eng, lambda e: e.tensor_copy(out, in_), rd=rd, wr=wr)

    def tt(self, out, a, b, op, rd=(), wr=(), eng="vector"):
        self.emit(eng, lambda e: e.tensor_tensor(out, a, b, op), rd=rd, wr=wr)

    def ts(self, out, in0, s1, op0, s2=None, op1=ALU.bypass, rd=(), wr=(), eng="vector"):
        if s2 is None:
            self.emit(eng, lambda e: e.tensor_single_scalar(out, in0, s1, op0),
                      rd=rd, wr=wr)
        else:
            self.emit(eng, lambda e: e.tensor_scalar(out, in0, s1, s2, op0=op0, op1=op1),
                      rd=rd, wr=wr)

    def stt(self, out, in0, scalar, in1, op0, op1, rd=(), wr=(), eng="vector"):
        self.emit(eng, lambda e: e.scalar_tensor_tensor(out, in0, scalar, in1,
                                                        op0=op0, op1=op1), rd=rd, wr=wr)

    def scan(self, out, d0, d1, init, rd=(), wr=(), eng="vector"):
        self.emit(eng, lambda e: e.tensor_tensor_scan(out, d0, d1, init,
                                                      op0=ALU.mult, op1=ALU.add),
                  rd=rd, wr=wr)

    def red(self, out, in_, op, rd=(), wr=(), eng="vector"):
        self.emit(eng, lambda e: e.tensor_reduce(out, in_, axis=AX.X, op=op),
                  rd=rd, wr=wr)

    def max8(self, out, in_, rd=(), wr=()):
        self.emit("vector", lambda e: e.max(out, in_), rd=rd, wr=wr)

    def recip(self, out, in_, rd=(), wr=()):
        self.emit("vector", lambda e: e.reciprocal(out, in_), rd=rd, wr=wr)

    def memset(self, ap, v, rd=(), wr=(), eng="vector"):
        self.emit(eng, lambda e: e.memset(ap, v), rd=rd, wr=wr)

    def iota(self, ap, pattern, rd=(), wr=(), base=0, mult=0):
        self.emit("gpsimd", lambda e: e.iota(ap, pattern=pattern, base=base,
                                             channel_multiplier=mult,
                                             allow_small_or_imprecise_dtypes=True),
                  rd=rd, wr=wr)

    def ccol(self, v):
        v = float(v)
        if v not in self.cmap:
            assert not self.frozen_consts, f"const {v} after freeze"
            i = len(self.cmap)
            self.cmap[v] = i
            self.memset(self.cvals[:, i:i + 1], v)
        return self.cvals[:, self.cmap[v]:self.cmap[v] + 1]


def make_consts(p, st):
    nc = p.nc
    c = {}
    c["ones_bf"] = st.enter_context(nc.sbuf_tensor("ones_bf", [1, 128], BF16))
    c["ident_f"] = st.enter_context(nc.sbuf_tensor("ident_f", [128, 128], F32))
    c["ident_bf"] = st.enter_context(nc.sbuf_tensor("ident_bf", [128, 128], BF16))
    c["onescol_bf"] = st.enter_context(nc.sbuf_tensor("onescol_bf", [128, 1], BF16))
    p.cvals = st.enter_context(nc.sbuf_tensor("cvals", [128, 16], F32))
    p.memset(c["ones_bf"][:, :], 1.0)
    p.memset(c["onescol_bf"][:, :], 1.0)
    p.iota(c["ident_f"][:, :], pattern=[[1, 128]], base=0, mult=-1, wr=[("iota",)])
    p.ts(c["ident_f"][:, :], c["ident_f"][:, :], 0.0, ALU.is_equal, rd=[("iota",)], wr=[("iota",)])
    p.cp(c["ident_bf"][:, :], c["ident_f"][:, :], rd=[("iota",)])
    for v in (LN_EPS, EPS_LN_ATTN, EPS_ATTN, -0.5, 0.5):
        p.ccol(v)
    p.frozen_consts = True
    return c



PRECAST = [
    ("m_win", D, 2 * E, "B_win"),
    ("m_wxp", E, 128, "B_wxp"),
    ("m_wdt", DTR, E, "B_wdt"),
    ("m_wout", E, D, "B_wout"),
    ("g_wT", D, NEXP, "B_gwT"),
    ("g_w1", NEXP * D, HMOE, "B_w1"),
    ("g_w2", NEXP * HMOE, D, "B_w2"),
    ("a_wq", D, D, "B_wq"),
    ("a_wk", D, D, "B_wk"),
    ("a_wv", D, D, "B_wv"),
    ("a_wd", D, D, "B_wd"),
    ("f_UT", D, NB, "B_UT"),
    ("f_rWT", D, NB, "B_rWT"),
    ("f_rV", NB, D, "B_rV"),
    ("f_w1T", D, H1, "B_w1T"),
    ("f_w2T", H1, D, "B_w2T"),
]

_PC_ENGS = ("vector", "scalar", "gpsimd")


def emit_precast(p, st, dram, scratch, sf, sb):
    i = 0
    for key, R, C, bname in PRECAST:
        src = dram[key]
        if len(src.shape) == 3:
            src = src.rearrange("a b c -> (a b) c")
        dst = scratch[bname]
        if len(dst.shape) == 3:
            dst = dst.rearrange("a b c -> (a b) c")
        for r0 in range(0, R, 128):
            rr = min(128, R - r0)
            for c0 in range(0, C, 1024):
                cc = min(1024, C - c0)
                g = i % 2
                eng = _PC_ENGS[i % 3]
                p.dma(sf[g][0:rr, 0:cc], src[r0:r0 + rr, c0:c0 + cc],
                      wr=[("pcf", g)], q="sync")
                p.cp(sb[g][0:rr, 0:cc], sf[g][0:rr, 0:cc],
                     rd=[("pcf", g)], wr=[("pcb", g)], eng=eng)
                p.dma(dst[r0:r0 + rr, c0:c0 + cc], sb[g][0:rr, 0:cc],
                      rd=[("pcb", g)], wr=[("B", bname)], q="scalar")
                i += 1


def emit_ln(p, x_ap, out_ap, stats, scr, eps, rd_x, wr_out, tag, sckey=None):
    Df = x_ap.shape[-1]
    ks = ("st",) + tag
    kc = sckey if sckey is not None else ("sc",) + tag
    mu = stats[:, 0:1]
    sq = stats[:, 1:2]
    var = stats[:, 2:3]
    rstd = stats[:, 3:4]
    p.red(mu, x_ap, ALU.add, rd=list(rd_x), wr=[ks])
    p.ts(mu, mu, 1.0 / Df, ALU.mult, rd=[ks], wr=[ks])
    p.act(scr, x_ap, AF.Square, accum_out=sq, rd=list(rd_x), wr=[kc, ks])
    p.ts(var, sq, 1.0 / Df, ALU.mult, rd=[ks], wr=[ks])
    p.tt(stats[:, 4:5], mu, mu, ALU.mult, rd=[ks], wr=[ks])
    p.tt(var, var, stats[:, 4:5], ALU.subtract, rd=[ks], wr=[ks])
    p.act(stats[:, 5:6], var, AF.Ln, bias=p.ccol(eps), rd=[ks], wr=[ks])
    p.act(rstd, stats[:, 5:6], AF.Exp, scale=p.ccol(-0.5), rd=[ks], wr=[ks])
    p.ts(out_ap, x_ap, mu, ALU.subtract, rstd, ALU.mult,
         rd=list(rd_x) + [ks], wr=list(wr_out))


def emit_tr(p, dst_ap, src_ap, ps_tr, ident_bf, rd_src, wr_dst, pkey):
    p.trans(ps_tr, src_ap, ident_bf[:, :], rd=list(rd_src), wr=[pkey])
    p.cp(dst_ap, ps_tr, rd=[pkey], wr=list(wr_dst))


def emit_tr8(p, dst3d, src_ap, ps_trB, ident_bf, rd_src, wr_dst, pkey, nk=8):
    for kd in range(nk):
        p.trans(ps_trB[:, kd * 128:(kd + 1) * 128], src_ap[:, kd * 128:(kd + 1) * 128],
                ident_bf[:, :], rd=list(rd_src), wr=[pkey])
    p.cp(dst3d, ps_trB[:, 0:nk * 128].rearrange("p (k f) -> p k f", f=128),
         rd=[pkey], wr=list(wr_dst), eng="scalar")



def emit_mamba(p, st, c, X, dram, scratch, TC=256):
    nc = p.nc
    NCH = T // TC
    TSUB = TC // 128

    xnT = [st.enter_context(nc.sbuf_tensor(f"mb_xnT{i}", [128, DT * TC], BF16)) for i in range(2)]
    xi = st.enter_context(nc.sbuf_tensor("mb_xi", [128, ET * (TC + 3)], F32))
    zs = st.enter_context(nc.sbuf_tensor("mb_zs", [128, ET * TC], F32))
    delta = st.enter_context(nc.sbuf_tensor("mb_delta", [128, ET * TC], F32))
    wdu = st.enter_context(nc.sbuf_tensor("mb_w", [128, ET * TC], BF16))
    yac = st.enter_context(nc.sbuf_tensor("mb_y", [128, ET * TC], F32))
    ygbf = st.enter_context(nc.sbuf_tensor("mb_yg", [128, ET * TC], BF16))
    hstate = st.enter_context(nc.sbuf_tensor("mb_h", [128, ET * NST], F32))
    stash = st.enter_context(nc.sbuf_tensor("mb_stash", [128, ET * 3], F32))
    stats = [st.enter_context(nc.sbuf_tensor(f"mb_stats{i}", [128, 8], F32)) for i in range(2)]
    scrA = st.enter_context(nc.sbuf_tensor("mb_scrA", [128, 1024], F32))
    xnbf_t = [st.enter_context(nc.sbuf_tensor(f"mb_xnbf{i}", [128, 1024], BF16)) for i in range(2)]
    cbuf0 = st.enter_context(nc.sbuf_tensor("mb_cb0", [128, TC], F32))
    cbuf1 = st.enter_context(nc.sbuf_tensor("mb_cb1", [128, TC], F32))
    ubf = [st.enter_context(nc.sbuf_tensor(f"mb_ubf{i}", [128, TC], BF16)) for i in range(2)]
    dtlr_bf = st.enter_context(nc.sbuf_tensor("mb_dtlr", [64, TC], BF16))
    Bt_bf = st.enter_context(nc.sbuf_tensor("mb_Bt", [16, TC], BF16))
    Ct_bf = st.enter_context(nc.sbuf_tensor("mb_Ct", [16, TC], BF16))
    Bb = [st.enter_context(nc.sbuf_tensor(f"mb_Bb{i}", [128, TC], BF16)) for i in range(2)]
    Cb = [st.enter_context(nc.sbuf_tensor(f"mb_Cb{i}", [128, TC], BF16)) for i in range(2)]
    dAb = [st.enter_context(nc.sbuf_tensor(f"mb_dA{i}", [128, TC], F32)) for i in range(2)]
    dBub = [st.enter_context(nc.sbuf_tensor(f"mb_dBu{i}", [128, TC], F32)) for i in range(2)]
    hb = [st.enter_context(nc.sbuf_tensor(f"mb_hb{i}", [128, TC], F32)) for i in range(2)]
    sp0 = st.enter_context(nc.sbuf_tensor("mb_sp0", [128, TC], F32))
    Asb = st.enter_context(nc.sbuf_tensor("mb_A", [128, ET * NST], F32))
    cwsb = st.enter_context(nc.sbuf_tensor("mb_cw", [128, ET * 4], F32))
    cbsb = st.enter_context(nc.sbuf_tensor("mb_cbv", [128, ET], F32))
    bdtsb = st.enter_context(nc.sbuf_tensor("mb_bdt", [128, ET], F32))
    dsksb = st.enter_context(nc.sbuf_tensor("mb_dsk", [128, ET], F32))
    wxp = st.enter_context(nc.sbuf_tensor("mb_wxp", [128, ET * 128], BF16))
    wdt = st.enter_context(nc.sbuf_tensor("mb_wdt", [64, E], BF16))
    sel = st.enter_context(nc.sbuf_tensor("mb_sel", [16, NST * 128], BF16))
    win_bf = [st.enter_context(nc.sbuf_tensor(f"mb_winb{i}", [128, DT * 128], BF16)) for i in range(2)]
    wo_et = win_bf
    ps_trB = st.enter_context(nc.psum_tensor("mb_ptr", [128, 1024], BF16))
    ps_m0 = st.enter_context(nc.psum_tensor("mb_pm0", [128, 256], F32))
    ps_m1 = st.enter_context(nc.psum_tensor("mb_pm1", [128, 256], F32))
    ps_xd = st.enter_context(nc.psum_tensor("mb_pxd", [128, 512], F32))
    ps_o = st.enter_context(nc.psum_tensor("mb_po", [128, 2048], F32))
    assert TC == 256
    ps_mm = [ps_m0[:, :], ps_m1[:, :]]
    ps_d = ps_xd[:, TC:2 * TC]
    ps_B = ps_o[:, 0:TC]
    ps_C = ps_o[:, TC:2 * TC]

    p.dma(Asb[:, :], dram["m_A"], wr=[("mA",)])
    p.dma(cwsb[:, :], dram["m_cw"], wr=[("mcw",)])
    p.dma(cbsb[:, :], dram["m_cb"], wr=[("mcb",)])
    p.dma(bdtsb[:, :], dram["m_bdt"], wr=[("mbdt",)])
    p.dma(dsksb[:, :], dram["m_dsk"], wr=[("mdsk",)])
    p.dma(wxp[:, :].rearrange("p (et cc) -> p et cc", cc=128),
          scratch["B_wxp"].rearrange("(et p) cc -> p et cc", p=128),
          rd=[("B", "B_wxp")], wr=[("mwxp",)], q="scalar")
    p.dma(wdt[:, :], scratch["B_wdt"], rd=[("B", "B_wdt")], wr=[("mwdt",)], q="scalar")
    p.act(Asb[:, :], Asb[:, :], AF.Exp, rd=[("mA",)], wr=[("mA",)])
    p.ts(Asb[:, :], Asb[:, :], -1.0, ALU.mult, rd=[("mA",)], wr=[("mA",)])
    for half in range(2):
        p.iota(scrA[0:16, 0:1024], pattern=[[1, 8], [0, 128]], base=-half * 8, mult=-1,
               wr=[("scA",)])
        p.ts(scrA[0:16, 0:1024], scrA[0:16, 0:1024], 0.0, ALU.is_equal,
             rd=[("scA",)], wr=[("scA",)])
        p.cp(sel[:, half * 1024:(half + 1) * 1024], scrA[0:16, 0:1024],
             rd=[("scA",)], wr=[("msel",)])
    p.memset(hstate[:, :], 0.0, wr=[("hst", dd, n) for dd in range(ET) for n in range(NST)])

    xi_v = lambda et: xi[:, et * (TC + 3):(et + 1) * (TC + 3)]

    for ch in range(NCH):
        cp_ = ch % 2
        for tsub in range(TSUB):
            tt = ch * TSUB + tsub
            xt = X[:, tt * D:(tt + 1) * D]
            emit_ln(p, xt, xnbf_t[tsub][:, :], stats[tsub], scrA[:, 0:1024],
                    LN_EPS, rd_x=[("X", tt)], wr_out=[("xnbf", tsub)], tag=("mln", tsub),
                    sckey=("scA",))
            dst3d = xnT[cp_][:, :].rearrange("p (k t) -> p k t", t=TC)[:, :, tsub * 128:tsub * 128 + 128]
            emit_tr8(p, dst3d, xnbf_t[tsub][:, :], ps_trB, c["ident_bf"],
                     rd_src=[("xnbf", tsub)],
                     wr_dst=[("xnT", cp_, kd) for kd in range(DT)], pkey=("pstr",))
        for ec in range(32):
            wp = ec % 2
            p.dma(win_bf[wp][:, :].rearrange("p (dc e) -> p dc e", e=128),
                  scratch["B_win"][:, ec * 128:(ec + 1) * 128]
                  .rearrange("(dc p) e -> p dc e", p=128),
                  rd=[("B", "B_win")], wr=[("winb", wp)], q="sync")
            for dc in range(DT):
                p.mm(ps_mm[wp], win_bf[wp][:, dc * 128:(dc + 1) * 128],
                     xnT[cp_][:, dc * TC:(dc + 1) * TC],
                     rd=[("winb", wp), ("xnT", cp_, dc)], wr=[("psmm", wp)],
                     start=(dc == 0), stop=(dc == DT - 1))
            if ec < 16:
                p.act(xi_v(ec)[:, 3:3 + TC], ps_mm[wp], AF.Identity,
                      rd=[("psmm", wp)], wr=[("xi", ec)])
            else:
                p.act(zs[:, (ec - 16) * TC:(ec - 15) * TC], ps_mm[wp], AF.Silu,
                      rd=[("psmm", wp)], wr=[("zs", ec - 16)])
        for et in range(ET):
            xv = xi_v(et)
            up = et % 2
            if ch == 0:
                p.memset(xv[:, 0:3], 0.0, wr=[("xi", et)])
            else:
                p.cp(xv[:, 0:3], stash[:, et * 3:(et + 1) * 3],
                     rd=[("stash", et)], wr=[("xi", et)])
            cw = lambda k: cwsb[:, et * 4 + k: et * 4 + k + 1]
            p.ts(cbuf0[:, :], xv[:, 0:TC], cw(0), ALU.mult, rd=[("xi", et), ("mcw",)], wr=[("cb",)])
            p.stt(cbuf1[:, :], xv[:, 1:1 + TC], cw(1), cbuf0[:, :], ALU.mult, ALU.add,
                  rd=[("xi", et), ("cb",)], wr=[("cb",)])
            p.stt(cbuf0[:, :], xv[:, 2:2 + TC], cw(2), cbuf1[:, :], ALU.mult, ALU.add,
                  rd=[("xi", et), ("cb",)], wr=[("cb",)])
            p.stt(cbuf1[:, :], xv[:, 3:3 + TC], cw(3), cbuf0[:, :], ALU.mult, ALU.add,
                  rd=[("xi", et), ("cb",)], wr=[("cb",)])
            p.cp(stash[:, et * 3:(et + 1) * 3], xv[:, TC:TC + 3],
                 rd=[("xi", et)], wr=[("stash", et)])
            p.act(xv[:, 3:3 + TC], cbuf1[:, :], AF.Silu, bias=cbsb[:, et:et + 1],
                  rd=[("cb",), ("stash", et), ("mcb",)], wr=[("xi", et)])
            p.cp(ubf[up][:, :], xv[:, 3:3 + TC], rd=[("xi", et)], wr=[("ubf", up)], eng="gpsimd")
            p.ts(yac[:, et * TC:(et + 1) * TC], xv[:, 3:3 + TC], dsksb[:, et:et + 1],
                 ALU.mult, rd=[("xi", et), ("mdsk",)], wr=[("yac", et)])
            p.mm(ps_xd[:, 0:TC], wxp[:, et * 128:(et + 1) * 128], ubf[up][:, :],
                 rd=[("ubf", up), ("mwxp",)], wr=[("psxd",)],
                 start=(et == 0), stop=(et == ET - 1))
        p.act(Bt_bf[:, :], ps_xd[0:16, 0:TC], AF.Identity, rd=[("psxd",)], wr=[("Bt",)])
        p.act(Ct_bf[:, :], ps_xd[32:48, 0:TC], AF.Identity, rd=[("psxd",)], wr=[("Ct",)])
        p.act(dtlr_bf[:, :], ps_xd[64:128, 0:TC], AF.Identity, rd=[("psxd",)], wr=[("dtlr",)])
        for dt in range(ET):
            dv = delta[:, dt * TC:(dt + 1) * TC]
            p.mm(ps_d, wdt[:, dt * 128:(dt + 1) * 128], dtlr_bf[:, :],
                 rd=[("dtlr",), ("mwdt",)], wr=[("psxd",)])
            p.act(sp0[:, :], ps_d, AF.Identity, bias=bdtsb[:, dt:dt + 1],
                  rd=[("psxd",), ("mbdt",)], wr=[("sp0",)])
            p.act(cbuf1[:, :], sp0[:, :], AF.Abs, rd=[("sp0",), ("cb",)], wr=[("cb",)])
            p.act(cbuf0[:, :], cbuf1[:, :], AF.Exp, scale=-1.0, rd=[("cb",)], wr=[("cb",)])
            p.act(cbuf1[:, :], cbuf0[:, :], AF.Ln, bias=1.0, rd=[("cb",)], wr=[("cb",)])
            p.stt(dv, sp0[:, :], 0.0, cbuf1[:, :], ALU.max, ALU.add,
                  rd=[("sp0",), ("cb",)], wr=[("delta", dt)])
            p.tt(wdu[:, dt * TC:(dt + 1) * TC], dv, xi_v(dt)[:, 3:3 + TC], ALU.mult,
                 rd=[("delta", dt), ("xi", dt)], wr=[("w", dt)])
        for n in range(NST):
            np_ = n % 2
            p.mm(ps_B, sel[:, n * 128:(n + 1) * 128], Bt_bf[:, :],
                 rd=[("Bt",), ("msel",)], wr=[("pso",)])
            p.act(Bb[np_][:, :], ps_B, AF.Identity, rd=[("pso",)], wr=[("Bb", np_)])
            p.mm(ps_C, sel[:, n * 128:(n + 1) * 128], Ct_bf[:, :],
                 rd=[("Ct",), ("msel",)], wr=[("pso",)])
            p.act(Cb[np_][:, :], ps_C, AF.Identity, rd=[("pso",)], wr=[("Cb", np_)])
            for dt in range(ET):
                it = n * ET + dt
                a3 = it % 2
                i2 = it % 2
                p.act(dAb[a3][:, :], delta[:, dt * TC:(dt + 1) * TC], AF.Exp,
                      scale=Asb[:, dt * NST + n: dt * NST + n + 1],
                      rd=[("delta", dt), ("mA",)], wr=[("dA", a3)])
                p.tt(dBub[i2][:, :], wdu[:, dt * TC:(dt + 1) * TC], Bb[np_][:, :], ALU.mult,
                     rd=[("w", dt), ("Bb", np_)], wr=[("dBu", i2)])
                p.scan(hb[i2][:, :], dAb[a3][:, :], dBub[i2][:, :],
                       hstate[:, dt * NST + n: dt * NST + n + 1],
                       rd=[("dA", a3), ("dBu", i2), ("hst", dt, n)], wr=[("hb", i2)])
                p.cp(hstate[:, dt * NST + n: dt * NST + n + 1], hb[i2][:, TC - 1:TC],
                     rd=[("hb", i2)], wr=[("hst", dt, n)], eng="gpsimd")
                yv = yac[:, dt * TC:(dt + 1) * TC]
                yt = dBub[1 - i2]
                p.tt(yt[:, :], hb[i2][:, :], Cb[np_][:, :], ALU.mult,
                     rd=[("hb", i2), ("Cb", np_)], wr=[("dBu", 1 - i2)])
                p.tt(yv, yv, yt[:, :], ALU.add,
                     rd=[("yac", dt), ("dBu", 1 - i2)], wr=[("yac", dt)], eng="gpsimd")
        for et in range(ET):
            yv = yac[:, et * TC:(et + 1) * TC]
            p.tt(ygbf[:, et * TC:(et + 1) * TC], yv, zs[:, et * TC:(et + 1) * TC],
                 ALU.mult, rd=[("yac", et), ("zs", et)], wr=[("yg", et)])
        for et in range(ET):
            wp = et % 2
            p.dma(wo_et[wp][:, :], scratch["B_wout"][et * 128:(et + 1) * 128, :],
                  rd=[("B", "B_wout")], wr=[("winb", wp)], q="scalar")
            for tsub in range(TSUB):
                for oh in range(2):
                    p.mm(ps_o[:, (tsub * 2 + oh) * 512:(tsub * 2 + oh + 1) * 512],
                         ygbf[:, et * TC + tsub * 128: et * TC + tsub * 128 + 128],
                         wo_et[wp][:, oh * 512:(oh + 1) * 512],
                         rd=[("yg", et), ("winb", wp)], wr=[("pso",)],
                         start=(et == 0), stop=(et == ET - 1))
        for tsub in range(TSUB):
            tt = ch * TSUB + tsub
            for oh in range(2):
                xs = X[:, tt * D + oh * 512: tt * D + (oh + 1) * 512]
                p.tt(xs, xs, ps_o[:, (tsub * 2 + oh) * 512:(tsub * 2 + oh + 1) * 512],
                     ALU.add, rd=[("X", tt), ("pso",)], wr=[("X", tt)])



def emit_moe(p, st, c, X, dram, scratch):
    nc = p.nc
    TTI = T // 128
    HT = HMOE // 128
    xnT = st.enter_context(nc.sbuf_tensor("mo_xnT", [128, DT * T], BF16))
    w1bf = st.enter_context(nc.sbuf_tensor("mo_w1", [128, DT * HMOE], BF16))
    w2bf = st.enter_context(nc.sbuf_tensor("mo_w2", [128, HT * D], BF16))
    hbf = st.enter_context(nc.sbuf_tensor("mo_h", [128, HT * 256], BF16))
    gwts = st.enter_context(nc.sbuf_tensor("mo_gw", [128, TTI * NEXP], F32))
    sg = st.enter_context(nc.sbuf_tensor("mo_sg", [128, NEXP], F32))
    m8 = st.enter_context(nc.sbuf_tensor("mo_m8", [128, 8], F32))
    maskb = st.enter_context(nc.sbuf_tensor("mo_mask", [128, NEXP], F32))
    stats = [st.enter_context(nc.sbuf_tensor(f"mo_stats{i}", [128, 8], F32)) for i in range(2)]
    scrA = st.enter_context(nc.sbuf_tensor("mo_scrA", [128, 1024], F32))
    xnbf_t = [st.enter_context(nc.sbuf_tensor(f"mo_xnbf{i}", [128, 1024], BF16)) for i in range(2)]
    gwT_bf = st.enter_context(nc.sbuf_tensor("mo_gwT", [128, DT * NEXP], BF16))
    gb_bf = st.enter_context(nc.sbuf_tensor("mo_gb", [1, NEXP], BF16))
    b1sb = st.enter_context(nc.sbuf_tensor("mo_b1", [128, NEXP * HT], F32))
    b2bf = [st.enter_context(nc.sbuf_tensor("mo_b2", [1, D], BF16))] * 2
    ps_trB = st.enter_context(nc.psum_tensor("mo_ptr", [128, 1024], BF16))
    ps_a = st.enter_context(nc.psum_tensor("mo_pa", [128, 512], F32))
    ps_h = [st.enter_context(nc.psum_tensor(f"mo_ph{i}", [128, 512], F32)) for i in range(2)]
    ps_o = [st.enter_context(nc.psum_tensor(f"mo_po{i}", [128, 512], F32)) for i in range(2)]
    ps_g = ps_a[:, 0:NEXP]

    p.dma(gwT_bf[:, :].rearrange("p (dc e) -> p dc e", e=NEXP),
          scratch["B_gwT"].rearrange("(dc p) e -> p dc e", p=128),
          rd=[("B", "B_gwT")], wr=[("gwT",)])
    p.dma(scrA[0:1, 0:NEXP], dram["g_b"], wr=[("sA",)])
    p.cp(gb_bf[:, :], scrA[0:1, 0:NEXP], rd=[("sA",)], wr=[("gb",)])
    p.dma(b1sb[:, :].rearrange("p (e h) -> p e h", h=HT), dram["g_b1"], wr=[("b1",)])
    for tt in range(TTI):
        tp = tt % 2
        xt = X[:, tt * D:(tt + 1) * D]
        emit_ln(p, xt, xnbf_t[tp][:, :], stats[tp], scrA[:, 0:1024],
                LN_EPS, rd_x=[("X", tt)], wr_out=[("xnbf", tp)], tag=("gln", tp),
                sckey=("sA",))
        dst3d = xnT[:, :].rearrange("p (k t) -> p k t", t=T)[:, :, tt * 128:tt * 128 + 128]
        emit_tr8(p, dst3d, xnbf_t[tp][:, :], ps_trB, c["ident_bf"],
                 rd_src=[("xnbf", tp)],
                 wr_dst=[("xnT", tt, kd) for kd in range(DT)], pkey=("pstr",))
        for kd in range(DT):
            p.mm(ps_g, xnT[:, kd * T + tt * 128: kd * T + tt * 128 + 128],
                 gwT_bf[:, kd * NEXP:(kd + 1) * NEXP],
                 rd=[("xnT", tt, kd), ("gwT",)], wr=[("psg",)],
                 start=(kd == 0), stop=False)
        p.mm(ps_g, c["ones_bf"][0:1, 0:128], gb_bf[0:1, :], rd=[("gb",)], wr=[("psg",)],
             start=False, stop=True)
        p.act(sg[:, :], ps_g, AF.Identity, rd=[("psg",)], wr=[("sg",)])
        p.red(stats[0][:, 6:7], sg[:, :], ALU.max, rd=[("sg",)], wr=[("gst",)])
        p.ts(sg[:, :], sg[:, :], stats[0][:, 6:7], ALU.subtract, rd=[("sg",), ("gst",)], wr=[("sg",)])
        p.act(sg[:, :], sg[:, :], AF.Exp, rd=[("sg",)], wr=[("sg",)])
        p.red(stats[0][:, 6:7], sg[:, :], ALU.add, rd=[("sg",)], wr=[("gst",)])
        p.recip(stats[0][:, 7:8], stats[0][:, 6:7], rd=[("gst",)], wr=[("gst",)])
        p.ts(sg[:, :], sg[:, :], stats[0][:, 7:8], ALU.mult, rd=[("sg",), ("gst",)], wr=[("sg",)])
        p.max8(m8[:, :], sg[:, :], rd=[("sg",)], wr=[("m8",)])
        p.ts(maskb[:, :], sg[:, :], m8[:, 1:2], ALU.is_ge, rd=[("sg",), ("m8",)], wr=[("mk",)])
        p.tt(gwts[:, tt * NEXP:(tt + 1) * NEXP], sg[:, :], maskb[:, :], ALU.mult,
             rd=[("sg",), ("mk",)], wr=[("gwts", tt)])
    for e in range(NEXP):
        ep = e % 2
        for dc in range(DT):
            p.dma(w1bf[:, dc * HMOE:(dc + 1) * HMOE],
                  scratch["B_w1"][e, dc * 128:(dc + 1) * 128, :],
                  rd=[("B", "B_w1")], wr=[("w1", dc)], q="sync")
        for ht in range(HT):
            p.dma(w2bf[:, ht * D:(ht + 1) * D],
                  scratch["B_w2"][e, ht * 128:(ht + 1) * 128, :],
                  rd=[("B", "B_w2")], wr=[("w2", ht)], q="scalar")
        p.dma(scrA[0:1, 0:1024], dram["g_b2"][e:e + 1, :], wr=[("sA",)])
        p.cp(b2bf[ep][:, :], scrA[0:1, 0:1024], rd=[("sA",)], wr=[("b2", ep)])
        for tc in range(T // 256):
            for ht in range(HT):
                hp = ht % 2
                for dc in range(DT):
                    p.mm(ps_h[hp][:, :].rearrange("p (a b) -> p a b", a=2)[:, 0, 0:256]
                         if False else ps_h[hp][:, 0:256],
                         w1bf[:, dc * HMOE + ht * 128: dc * HMOE + ht * 128 + 128],
                         xnT[:, dc * T + tc * 256: dc * T + tc * 256 + 256],
                         rd=[("w1", dc)] + [("xnT", tc * 2 + q_, dc) for q_ in range(2)],
                         wr=[("psh", hp)],
                         start=(dc == 0), stop=(dc == DT - 1))
                p.act(hbf[:, ht * 256:(ht + 1) * 256], ps_h[hp][:, 0:256], AF.Relu,
                      bias=b1sb[:, e * HT + ht: e * HT + ht + 1],
                      rd=[("psh", hp), ("b1",)], wr=[("hbf", ht)])
            for tsub in range(2):
                tt = tc * 2 + tsub
                gcol = gwts[:, tt * NEXP + e: tt * NEXP + e + 1]
                for oh in range(2):
                    for ht in range(HT):
                        p.mm(ps_o[oh][:, :], hbf[:, ht * 256 + tsub * 128: ht * 256 + tsub * 128 + 128],
                             w2bf[:, ht * D + oh * 512: ht * D + oh * 512 + 512],
                             rd=[("hbf", ht), ("w2", ht)], wr=[("pso", oh)],
                             start=(ht == 0), stop=False)
                    p.mm(ps_o[oh][:, :], c["ones_bf"][0:1, 0:128], b2bf[ep][0:1, oh * 512:(oh + 1) * 512],
                         rd=[("b2", ep)], wr=[("pso", oh)], start=False, stop=True)
                    xs = X[:, tt * D + oh * 512: tt * D + (oh + 1) * 512]
                    p.stt(xs, ps_o[oh][:, :], gcol, xs, ALU.mult, ALU.add,
                          rd=[("pso", oh), ("gwts", tt), ("X", tt)], wr=[("X", tt)])



ATTN_PHASES = "ABqkvwstzcuTd"


def emit_attn(p, st, c, X, dram, scratch):
    nc = p.nc
    TTI = T // 128
    qT = st.enter_context(nc.sbuf_tensor("at_qT", [128, DT * T], BF16))
    wqkv = st.enter_context(nc.sbuf_tensor("at_wqkv", [128, 3 * DT * D], BF16))
    wd_bf = wqkv[:, 0:DT * D]
    bias_bf = st.enter_context(nc.sbuf_tensor("at_bias", [1, 4 * D], BF16))
    gb_bc = st.enter_context(nc.sbuf_tensor("at_gb", [128, 2 * D], F32))
    scr3 = st.enter_context(nc.sbuf_tensor("at_scr3", [128, 3072], F32))
    stats = [st.enter_context(nc.sbuf_tensor(f"at_stats{i}", [128, 8], F32)) for i in range(2)]
    xnbf_t = st.enter_context(nc.sbuf_tensor("at_xnbf", [128, 1024], BF16))
    xnT_t = st.enter_context(nc.sbuf_tensor("at_xnT", [128, DT * 128], BF16))
    qkbf = st.enter_context(nc.sbuf_tensor("at_qkbf", [128, 1024], BF16))
    vbf = st.enter_context(nc.sbuf_tensor("at_vbf", [128, 1024], BF16))
    kvbf = st.enter_context(nc.sbuf_tensor("at_kvbf", [128, NH * HD], BF16))
    kvacc = st.enter_context(nc.sbuf_tensor("at_kvacc", [64, NH * HD], F32))
    ksacc = st.enter_context(nc.sbuf_tensor("at_ksacc", [128, DT], F32))
    ksz = st.enter_context(nc.sbuf_tensor("at_ksz", [128, NH], BF16))
    ksf = st.enter_context(nc.sbuf_tensor("at_ksf", [128, DT], F32))
    zrec = st.enter_context(nc.sbuf_tensor("at_z", [128, NH], F32))
    czbf = st.enter_context(nc.sbuf_tensor("at_cz", [128, 1024], BF16))
    ctxT_t = st.enter_context(nc.sbuf_tensor("at_ctxT", [128, DT * 128], BF16))
    hs_fp = st.enter_context(nc.sbuf_tensor("at_hs", [128, 1024], F32))
    ps_trB = st.enter_context(nc.psum_tensor("at_ptr", [128, 1024], BF16))
    ps_w = [st.enter_context(nc.psum_tensor(f"at_pw{i}", [128, 512], F32)) for i in range(2)]
    ps_kv = st.enter_context(nc.psum_tensor("at_pkv", [64, NH * HD], F32))
    ps_small = st.enter_context(nc.psum_tensor("at_psm", [128, 512], F32))
    ps_ksc = ps_small[:, 0:DT]
    ps_z = ps_small[:, 16:16 + NH]
    ps_ctx = st.enter_context(nc.psum_tensor("at_pctx", [128, 1024], F32))
    p.memset(kvacc[:, :], 0.0, wr=[("kvacc",)])
    p.memset(ksacc[:, :], 0.0, wr=[("ksacc",)])
    efp = scr3[:, 0:1024]
    et0 = scr3[:, 1024:1536]
    et1 = scr3[:, 1536:2048]
    scrB = scr3[:, 1024:2048]
    hsn = scr3[:, 2048:3072]

    for j, wname in enumerate(("B_wq", "B_wk", "B_wv")):
        p.dma(wqkv[:, j * DT * D:(j + 1) * DT * D].rearrange("p (dc dd) -> p dc dd", dd=D),
              scratch[wname].rearrange("(dc p) dd -> p dc dd", p=128),
              rd=[("B", wname)], wr=[("wqkv", j)], q="sync")
    for j, bname in enumerate(("a_bq", "a_bk", "a_bv", "a_bd")):
        p.dma(scrB[0:1, 0:D], dram[bname], rd=[("scd",)], wr=[("scd",)])
        p.cp(bias_bf[:, j * D:(j + 1) * D], scrB[0:1, 0:D], rd=[("scd",)], wr=[("abias", j)])
    for j, gname in enumerate(("a_gamma", "a_beta")):
        p.dma(scrB[0:1, 0:D], dram[gname], rd=[("scd",)], wr=[("scd",)])
        p.cp(qkbf[0:1, 0:D], scrB[0:1, 0:D], rd=[("scd",)], wr=[("qkbf",)])
        for oh in range(2):
            p.mm(ps_w[oh][:, :], c["ones_bf"][0:1, 0:128], qkbf[0:1, oh * 512:(oh + 1) * 512],
                 rd=[("qkbf",)], wr=[("psw", oh)])
            p.act(gb_bc[:, j * D + oh * 512: j * D + (oh + 1) * 512], ps_w[oh][:, :],
                  AF.Identity, rd=[("psw", oh)], wr=[("gbbc",)])

    def proj(j, oh, pw):
        for kd in range(DT):
            p.mm(ps_w[pw][:, :], xnT_t[:, kd * 128:(kd + 1) * 128],
                 wqkv[:, (j * DT + kd) * D + oh * 512:(j * DT + kd) * D + (oh + 1) * 512],
                 rd=[("xnTt", kd), ("wqkv", j)], wr=[("psw", pw)],
                 start=(kd == 0), stop=False)
        p.mm(ps_w[pw][:, :], c["ones_bf"][0:1, 0:128],
             bias_bf[0:1, j * D + oh * 512: j * D + (oh + 1) * 512],
             rd=[("abias", j)], wr=[("psw", pw)], start=False, stop=True)

    def elu1(dst_half, pw, wrk):
        p.ts(et0, ps_w[pw][:, :], 0.0, ALU.min, rd=[("psw", pw), ("scd",)], wr=[("scd",)])
        p.act(et1, et0, AF.Exp, rd=[("scd",)], wr=[("scd",)])
        p.stt(dst_half, ps_w[pw][:, :], 0.0, et1, ALU.max, ALU.add,
              rd=[("psw", pw), ("scd",)], wr=[wrk])

    for tt in (range(TTI) if "A" in ATTN_PHASES else []):
        tp = tt % 2
        xt = X[:, tt * D:(tt + 1) * D]
        emit_ln(p, xt, xnbf_t[:, :], stats[tp], scrB,
                LN_EPS, rd_x=[("X", tt)], wr_out=[("xnbf",)], tag=("aln", tp),
                sckey=("scd",))
        emit_tr8(p, xnT_t[:, :].rearrange("p (k f) -> p k f", f=128),
                 xnbf_t[:, :], ps_trB, c["ident_bf"], rd_src=[("xnbf",)],
                 wr_dst=[("xnTt", kd) for kd in range(DT)], pkey=("pstr",))
        if "2" in ATTN_PHASES:
            emit_tr8(p, xnT_t[:, :].rearrange("p (k f) -> p k f", f=128),
                     xnbf_t[:, :], ps_trB, c["ident_bf"], rd_src=[("xnbf",)],
                     wr_dst=[("xnTt", kd) for kd in range(DT)], pkey=("pstr",))
        if "q" in ATTN_PHASES:
            for oh in range(2):
                proj(0, oh, oh)
                elu1(efp[:, oh * 512:(oh + 1) * 512], oh, ("efp", oh))
            p.cp(qkbf[:, :], efp, rd=[("efp", 0), ("efp", 1)], wr=[("qkbf",)])
            qdst = qT[:, :].rearrange("p (k t) -> p k t", t=T)[:, :, tt * 128:tt * 128 + 128]
            emit_tr8(p, qdst, qkbf[:, :], ps_trB, c["ident_bf"],
                     rd_src=[("qkbf",)],
                     wr_dst=[("qT", tt, kd) for kd in range(DT)], pkey=("pstr",))
        if "k" in ATTN_PHASES:
            for oh in range(2):
                proj(1, oh, oh)
                elu1(efp[:, oh * 512:(oh + 1) * 512], oh, ("efp", oh))
            p.cp(qkbf[:, :], efp, rd=[("efp", 0), ("efp", 1)], wr=[("qkbf",)])
        if "v" in ATTN_PHASES:
            for oh in range(2):
                proj(2, oh, oh)
                p.act(vbf[:, oh * 512:(oh + 1) * 512], ps_w[oh][:, :], AF.Identity,
                      rd=[("psw", oh)], wr=[("vbf",)])
        if "w" in ATTN_PHASES:
            for h in range(NH):
                p.mm(ps_kv[:, h * HD:(h + 1) * HD], qkbf[:, h * HD:(h + 1) * HD],
                     vbf[:, h * HD:(h + 1) * HD],
                     rd=[("qkbf",), ("vbf",)], wr=[("pskv",)])
            p.tt(kvacc[:, :], kvacc[:, :], ps_kv[:, :], ALU.add,
                 rd=[("kvacc",), ("pskv",)], wr=[("kvacc",)])
        if "s" in ATTN_PHASES:
            for ic in range(DT):
                p.mm(ps_small[:, ic:ic + 1], qkbf[:, ic * 128:(ic + 1) * 128],
                     c["onescol_bf"][:, :],
                     rd=[("qkbf",)], wr=[("psks",)])
            p.tt(ksacc[:, :], ksacc[:, :], ps_small[:, 0:DT], ALU.add,
                 rd=[("ksacc",), ("psks",)], wr=[("ksacc",)])
    kvt = vbf[0:64, :]
    p.act(kvt, kvacc[:, :], AF.Identity, rd=[("kvacc",)], wr=[("vbf",), ("kvt",)])
    p.memset(kvbf[:, :], 0.0, wr=[("kv0",)])
    p.cp(kvbf[0:64, :].rearrange("p (h m) -> p h m", m=2 * HD)[:, :, 0:HD],
         kvt.rearrange("p (h m) -> p h m", m=2 * HD)[:, :, 0:HD],
         rd=[("kvt",), ("kv0",)], wr=[("kv0",)])
    p.dma(kvbf[64:128, :].rearrange("p (h m) -> p h m", m=2 * HD)[:, :, HD:2 * HD],
          kvt.rearrange("p (h m) -> p h m", m=2 * HD)[:, :, HD:2 * HD],
          rd=[("kvt",), ("kv0",)], wr=[("kv0",)])
    p.act(ksf[:, :], ksacc[:, :], AF.Identity, bias=p.ccol(EPS_ATTN),
          rd=[("ksacc",)], wr=[("ksf",)])
    p.memset(ksz[:, :], 0.0, wr=[("ksbf",)])
    p.cp(ksz[0:64, :].rearrange("p (a b) -> p a b", b=2)[:, :, 0:1],
         ksf[0:64, :].rearrange("p (a o) -> p a o", o=1),
         rd=[("ksf",), ("ksbf",)], wr=[("ksbf",)])
    p.cp(ksz[64:128, :].rearrange("p (a b) -> p a b", b=2)[:, :, 1:2],
         ksf[64:128, :].rearrange("p (a o) -> p a o", o=1),
         rd=[("ksf",), ("ksbf",)], wr=[("ksbf",)])
    p.dma(wd_bf.rearrange("p (dc dd) -> p dc dd", dd=D),
          scratch["B_wd"].rearrange("(dc p) dd -> p dc dd", p=128),
          rd=[("B", "B_wd"), ("wqkv", 0)], wr=[("wqkv", 0), ("wd",)], q="sync")
    for tt in (range(TTI) if "B" in ATTN_PHASES and "A" in ATTN_PHASES else []):
        tp = tt % 2
        if "z" in ATTN_PHASES:
            for h in range(NH):
                ic = h // 2
                p.mm(ps_small[:, 16 + h:17 + h],
                     qT[:, ic * T + tt * 128: ic * T + tt * 128 + 128],
                     ksz[:, h:h + 1],
                     rd=[("qT", tt, ic), ("ksbf",)], wr=[("psks",)])
            p.recip(zrec[:, :], ps_z, rd=[("psks",)], wr=[("zrec",)])
        if "c" in ATTN_PHASES:
            for h in range(NH):
                ic = h // 2
                p.mm(ps_ctx[:, h * HD:(h + 1) * HD],
                     qT[:, ic * T + tt * 128: ic * T + tt * 128 + 128],
                     kvbf[:, h * HD:(h + 1) * HD],
                     rd=[("qT", tt, ic), ("kv0",)], wr=[("psctx",)])
            for h in (range(NH) if "u" in ATTN_PHASES else []):
                p.ts(czbf[:, h * HD:(h + 1) * HD], ps_ctx[:, h * HD:(h + 1) * HD],
                     zrec[:, h:h + 1], ALU.mult,
                     rd=[("psctx",), ("zrec",)], wr=[("czbf",)])
        if "T" in ATTN_PHASES:
            emit_tr8(p, ctxT_t[:, :].rearrange("p (k f) -> p k f", f=128),
                     czbf[:, :], ps_trB, c["ident_bf"], rd_src=[("czbf",)],
                     wr_dst=[("ctxT", kd) for kd in range(DT)], pkey=("pstr",))
        xt = X[:, tt * D:(tt + 1) * D]
        if "d" in ATTN_PHASES:
            emit_ln(p, xt, efp, stats[tp], scrB,
                    LN_EPS, rd_x=[("X", tt)], wr_out=[("efp", 0), ("efp", 1)], tag=("aln2", tp),
                    sckey=("scd",))
            for oh in range(2):
                for kd in range(DT):
                    p.mm(ps_w[1][:, :], ctxT_t[:, kd * 128:(kd + 1) * 128],
                         wd_bf[:, kd * D + oh * 512: kd * D + (oh + 1) * 512],
                         rd=[("ctxT", kd), ("wd",)], wr=[("psw", 1)],
                         start=(kd == 0), stop=False)
                p.mm(ps_w[1][:, :], c["ones_bf"][0:1, 0:128],
                     bias_bf[0:1, 3 * D + oh * 512: 3 * D + (oh + 1) * 512],
                     rd=[("abias", 3)], wr=[("psw", 1)], start=False, stop=True)
                p.tt(hs_fp[:, oh * 512:(oh + 1) * 512], ps_w[1][:, :],
                     efp[:, oh * 512:(oh + 1) * 512], ALU.add,
                     rd=[("psw", 1), ("efp", oh)], wr=[("hs",)])
            emit_ln(p, hs_fp[:, :], hsn, stats[tp], scrB,
                    EPS_LN_ATTN, rd_x=[("hs",)], wr_out=[("hsn",)], tag=("aln3", tp),
                    sckey=("scd",))
            p.tt(hsn, hsn, gb_bc[:, 0:D], ALU.mult,
                 rd=[("hsn",), ("gbbc",)], wr=[("hsn",)])
            p.tt(hsn, hsn, gb_bc[:, D:2 * D], ALU.add,
                 rd=[("hsn",), ("gbbc",)], wr=[("hsn",)])
            p.tt(xt, xt, hsn, ALU.add,
                 rd=[("X", tt), ("hsn",)], wr=[("X", tt)])



def emit_ffn1(p, st, c, X, dram, scratch):
    nc = p.nc
    TTI = T // 128
    JT = NB // 128
    HT1 = H1 // 128
    xnT = st.enter_context(nc.sbuf_tensor("ff_xnT", [128, DT * T], BF16))
    UT_bf = st.enter_context(nc.sbuf_tensor("ff_UT", [128, DT * NB], BF16))
    rWT_bf = st.enter_context(nc.sbuf_tensor("ff_rWT", [128, DT * NB], BF16))
    rV_bf = st.enter_context(nc.sbuf_tensor("ff_rV", [128, JT * D], BF16))
    rb_bf = st.enter_context(nc.sbuf_tensor("ff_rb", [1, NB], BF16))
    lngb = st.enter_context(nc.sbuf_tensor("ff_lngb", [128, 2 * D], F32))
    gamb = st.enter_context(nc.sbuf_tensor("ff_gamb", [128, D], F32))
    scrA = st.enter_context(nc.sbuf_tensor("ff_scrA", [128, 2048], F32))
    scrB = st.enter_context(nc.sbuf_tensor("ff_scrB", [128, 2048], F32))
    stats = [st.enter_context(nc.sbuf_tensor(f"ff_stats{i}", [128, 8], F32)) for i in range(2)]
    invUV = st.enter_context(nc.sbuf_tensor("ff_iuv", [128, 16], F32))
    ral = st.enter_context(nc.sbuf_tensor("ff_ral", [128, NB], F32))
    alpha = st.enter_context(nc.sbuf_tensor("ff_alpha", [128, NB], F32))
    aw = st.enter_context(nc.sbuf_tensor("ff_aw", [128, NB], F32))
    m8 = st.enter_context(nc.sbuf_tensor("ff_m8", [128, 8], F32))
    xr = st.enter_context(nc.sbuf_tensor("ff_xr", [128, D], F32))
    xrbf = st.enter_context(nc.sbuf_tensor("ff_xrbf", [128, D], BF16))
    xrT_t = st.enter_context(nc.sbuf_tensor("ff_xrT", [128, DT * 128], BF16))
    czbf = st.enter_context(nc.sbuf_tensor("ff_cz", [128, NB], BF16))
    cT_t = st.enter_context(nc.sbuf_tensor("ff_cT", [128, JT * 128], BF16))
    h1t = [st.enter_context(nc.sbuf_tensor(f"ff_h1{i}", [128, 256], BF16)) for i in range(2)]
    w1t = [st.enter_context(nc.sbuf_tensor(f"ff_w1t{i}", [128, DT * 128], BF16)) for i in range(2)]
    w2t = [st.enter_context(nc.sbuf_tensor(f"ff_w2t{i}", [128, D], BF16)) for i in range(2)]
    ps_trB = st.enter_context(nc.psum_tensor("ff_ptr", [128, 1024], BF16))
    ps_s = st.enter_context(nc.psum_tensor("ff_ps", [128, 2048], F32))
    ps_h0 = st.enter_context(nc.psum_tensor("ff_ph0", [128, 256], F32))
    ps_h1 = st.enter_context(nc.psum_tensor("ff_ph1", [128, 256], F32))
    ps_h = [ps_h0[:, :], ps_h1[:, :]]
    ps_wk = st.enter_context(nc.psum_tensor("ff_pwk", [128, 512], F32))

    p.dma(UT_bf[:, :].rearrange("p (dc nb) -> p dc nb", nb=NB),
          scratch["B_UT"].rearrange("(dc p) nb -> p dc nb", p=128),
          rd=[("B", "B_UT")], wr=[("UT",)], q="sync")
    p.dma(rWT_bf[:, :].rearrange("p (dc nb) -> p dc nb", nb=NB),
          scratch["B_rWT"].rearrange("(dc p) nb -> p dc nb", p=128),
          rd=[("B", "B_rWT")], wr=[("rWT",)], q="sync")
    p.dma(rV_bf[:, :].rearrange("p (jt dd) -> p jt dd", dd=D),
          scratch["B_rV"].rearrange("(jt p) dd -> p jt dd", p=128),
          rd=[("B", "B_rV")], wr=[("rV",)], q="sync")
    p.dma(scrB[0:1, 0:NB], dram["f_rb"], wr=[("sB",)])
    p.cp(rb_bf[:, :], scrB[0:1, 0:NB], rd=[("sB",)], wr=[("rb",)])
    for j, gname in enumerate(("f_lng", "f_lnb", "f_gam")):
        p.dma(scrB[0:1, 0:D], dram[gname], wr=[("sB",)])
        p.cp(xrbf[0:1, 0:D], scrB[0:1, 0:D], rd=[("sB",)], wr=[("xrbf",)])
        for oh in range(2):
            p.mm(ps_wk[:, :], c["ones_bf"][0:1, 0:128], xrbf[0:1, oh * 512:(oh + 1) * 512],
                 rd=[("xrbf",)], wr=[("pswk",)])
            dst = lngb[:, j * D + oh * 512: j * D + (oh + 1) * 512] if j < 2 else \
                  gamb[:, oh * 512:(oh + 1) * 512]
            p.act(dst, ps_wk[:, :], AF.Identity, rd=[("pswk",)], wr=[("lngb",)])
    for jt in range(JT):
        p.dma(scrB[:, 0:D], dram["f_U"][jt * 128:(jt + 1) * 128, :], wr=[("sB",)])
        p.act(scrB[:, 1024:2048], scrB[:, 0:D], AF.Square, accum_out=invUV[:, jt:jt + 1],
              rd=[("sB",)], wr=[("sB2",), ("iuv",)])
        p.dma(scrB[:, 0:D], dram["f_V"][jt * 128:(jt + 1) * 128, :], wr=[("sB",)])
        p.act(scrB[:, 1024:2048], scrB[:, 0:D], AF.Square, accum_out=invUV[:, 4 + jt:5 + jt],
              rd=[("sB",)], wr=[("sB2",), ("iuv",)])
    p.act(invUV[:, 8:16], invUV[:, 0:8], AF.Ln, rd=[("iuv",)], wr=[("iuv",)])
    p.act(invUV[:, 0:8], invUV[:, 8:16], AF.Exp, scale=p.ccol(0.5), rd=[("iuv",)], wr=[("iuv",)])
    p.ts(invUV[:, 0:8], invUV[:, 0:8], FFN_EPS, ALU.max, rd=[("iuv",)], wr=[("iuv",)])
    p.recip(invUV[:, 0:8], invUV[:, 0:8], rd=[("iuv",)], wr=[("iuv",)])
    p.tt(invUV[:, 8:12], invUV[:, 0:4], invUV[:, 4:8], ALU.mult, rd=[("iuv",)], wr=[("iuv",)])
    for tt in range(TTI):
        tp = tt % 2
        xt = X[:, tt * D:(tt + 1) * D]
        emit_ln(p, xt, xrbf[:, :], stats[tp], scrA[:, 0:1024],
                LN_EPS, rd_x=[("X", tt)], wr_out=[("xrbf",)], tag=("fln", tp))
        dst3d = xnT[:, :].rearrange("p (k t) -> p k t", t=T)[:, :, tt * 128:tt * 128 + 128]
        emit_tr8(p, dst3d, xrbf[:, :], ps_trB, c["ident_bf"],
                 rd_src=[("xrbf",)],
                 wr_dst=[("xnT", tt, kd) for kd in range(DT)], pkey=("pstr",))
    for tt in range(TTI):
        tp = tt % 2
        xt = X[:, tt * D:(tt + 1) * D]
        emit_ln(p, xt, scrA[:, 0:1024], stats[tp], scrA[:, 1024:2048],
                LN_EPS, rd_x=[("X", tt)], wr_out=[("xn",)], tag=("fln2", tp))
        emit_ln(p, scrA[:, 0:1024], xr[:, :], stats[tp], scrA[:, 1024:2048],
                LN_EPS, rd_x=[("xn",)], wr_out=[("xr",)], tag=("fln3", tp))
        p.tt(xr[:, :], xr[:, :], lngb[:, 0:D], ALU.mult, rd=[("xr",), ("lngb",)], wr=[("xr",)])
        p.tt(xr[:, :], xr[:, :], lngb[:, D:2 * D], ALU.add, rd=[("xr",), ("lngb",)], wr=[("xr",)])
        p.cp(xrbf[:, :], xr[:, :], rd=[("xr",)], wr=[("xrbf",)])
        emit_tr8(p, xrT_t[:, :].rearrange("p (k f) -> p k f", f=128),
                 xrbf[:, :], ps_trB, c["ident_bf"], rd_src=[("xrbf",)],
                 wr_dst=[("xrT", kd) for kd in range(DT)], pkey=("pstr",))
        for kd in range(DT):
            p.mm(ps_wk[:, :], xrT_t[:, kd * 128:(kd + 1) * 128],
                 rWT_bf[:, kd * NB:(kd + 1) * NB],
                 rd=[("xrT", kd), ("rWT",)], wr=[("pswk",)],
                 start=(kd == 0), stop=False)
        p.mm(ps_wk[:, :], c["ones_bf"][0:1, 0:128], rb_bf[0:1, :],
             rd=[("rb",)], wr=[("pswk",)], start=False, stop=True)
        p.ts(ral[:, :], ps_wk[:, :], TAU, ALU.min, -TAU, ALU.max,
             rd=[("pswk",)], wr=[("ral",)])
        p.act(alpha[:, :], ral[:, :], AF.Abs, rd=[("ral",)], wr=[("alpha",)])
        p.act(aw[:, :], alpha[:, :], AF.Exp, scale=-1.0, rd=[("alpha",)], wr=[("aw",)])
        p.act(alpha[:, :], aw[:, :], AF.Ln, bias=1.0, rd=[("aw",)], wr=[("alpha",)])
        p.stt(alpha[:, :], ral[:, :], 0.0, alpha[:, :], ALU.max, ALU.add,
              rd=[("ral",), ("alpha",)], wr=[("alpha",)])
        p.max8(m8[:, :], alpha[:, :], rd=[("alpha",)], wr=[("m8",)])
        p.ts(aw[:, :], alpha[:, :], m8[:, 7:8], ALU.is_ge, rd=[("alpha",), ("m8",)], wr=[("aw",)])
        p.tt(aw[:, :], aw[:, :], alpha[:, :], ALU.mult, rd=[("aw",), ("alpha",)], wr=[("aw",)])
        p.red(stats[tp][:, 6:7], aw[:, :], ALU.add, rd=[("aw",)], wr=[("fst", tp)])
        p.act(stats[tp][:, 7:8], stats[tp][:, 6:7], AF.Tanh, rd=[("fst", tp)], wr=[("fst", tp)])
        p.ts(stats[tp][:, 6:7], stats[tp][:, 6:7], FFN_EPS, ALU.add, rd=[("fst", tp)], wr=[("fst", tp)])
        p.recip(stats[tp][:, 6:7], stats[tp][:, 6:7], rd=[("fst", tp)], wr=[("fst", tp)])
        p.tt(stats[tp][:, 6:7], stats[tp][:, 6:7], stats[tp][:, 7:8], ALU.mult,
             rd=[("fst", tp)], wr=[("fst", tp)])
        p.ts(aw[:, :], aw[:, :], stats[tp][:, 6:7], ALU.mult, rd=[("aw",), ("fst", tp)], wr=[("aw",)])
        for kd in range(DT):
            p.mm(ps_wk[:, :], xnT[:, kd * T + tt * 128: kd * T + tt * 128 + 128],
                 UT_bf[:, kd * NB:(kd + 1) * NB],
                 rd=[("xnT", tt, kd), ("UT",)], wr=[("pswk",)],
                 start=(kd == 0), stop=(kd == DT - 1))
        p.tt(ral[:, :], ps_wk[:, :], aw[:, :], ALU.mult, rd=[("pswk",), ("aw",)], wr=[("ral",)])
        p.cp(czbf[:, :], ral[:, :], rd=[("ral",)], wr=[("czbf",)])
        for jt in range(JT):
            p.trans(ps_trB[:, jt * 128:(jt + 1) * 128], czbf[:, jt * 128:(jt + 1) * 128],
                    c["ident_bf"][:, :], rd=[("czbf",)], wr=[("pstr",)])
        for jt in range(JT):
            p.ts(cT_t[:, jt * 128:(jt + 1) * 128], ps_trB[:, jt * 128:(jt + 1) * 128],
                 invUV[:, 8 + jt:9 + jt], ALU.mult,
                 rd=[("pstr",), ("iuv",)], wr=[("cT", jt)])
        for oh in range(2):
            for jt in range(JT):
                p.mm(ps_wk[:, :], cT_t[:, jt * 128:(jt + 1) * 128],
                     rV_bf[:, jt * D + oh * 512: jt * D + (oh + 1) * 512],
                     rd=[("cT", jt), ("rV",)], wr=[("pswk",)],
                     start=(jt == 0), stop=(jt == JT - 1))
            p.tt(scrA[:, 1024:1536], ps_wk[:, :], gamb[:, oh * 512:(oh + 1) * 512], ALU.mult,
                 rd=[("pswk",), ("lngb",)], wr=[("dyn",)])
            xs = X[:, tt * D + oh * 512: tt * D + (oh + 1) * 512]
            p.tt(xs, xs, scrA[:, 1024:1536], ALU.add,
                 rd=[("X", tt), ("dyn",)], wr=[("X", tt)])
    for tc in range(T // 256):
        for ht in range(HT1):
            hp = ht % 2
            p.dma(w1t[hp][:, :].rearrange("p (dc h) -> p dc h", h=128),
                  scratch["B_w1T"][:, ht * 128:(ht + 1) * 128]
                  .rearrange("(dc p) h -> p dc h", p=128),
                  rd=[("B", "B_w1T")], wr=[("w1t", hp)], q="sync")
            for dc in range(DT):
                p.mm(ps_h[hp], w1t[hp][:, dc * 128:(dc + 1) * 128],
                     xnT[:, dc * T + tc * 256: dc * T + (tc + 1) * 256],
                     rd=[("w1t", hp), ("xnT", tc * 2, dc), ("xnT", tc * 2 + 1, dc)],
                     wr=[("psh", hp)],
                     start=(dc == 0), stop=(dc == DT - 1))
            p.act(h1t[hp][:, :], ps_h[hp], AF.Gelu, rd=[("psh", hp)], wr=[("h1t", hp)])
            p.dma(w2t[hp][:, :], scratch["B_w2T"][ht * 128:(ht + 1) * 128, :],
                  rd=[("B", "B_w2T")], wr=[("w2t", hp)], q="scalar")
            for tsub in range(2):
                for oh in range(2):
                    p.mm(ps_s[:, (tsub * 2 + oh) * 512:(tsub * 2 + oh + 1) * 512],
                         h1t[hp][:, tsub * 128:(tsub + 1) * 128],
                         w2t[hp][:, oh * 512:(oh + 1) * 512],
                         rd=[("h1t", hp), ("w2t", hp)], wr=[("pss",)],
                         start=(ht == 0), stop=(ht == HT1 - 1))
        for tsub in range(2):
            tt = tc * 2 + tsub
            for oh in range(2):
                xs = X[:, tt * D + oh * 512: tt * D + (oh + 1) * 512]
                p.tt(xs, xs, ps_s[:, (tsub * 2 + oh) * 512:(tsub * 2 + oh + 1) * 512],
                     ALU.add, rd=[("X", tt), ("pss",)], wr=[("X", tt)])



def _mamba_host(inp):
    f = np.ascontiguousarray
    cw = inp["L0_attn_conv_w"].reshape(E, 4)
    d = {
        "m_A": f(inp["L0_attn_A_log"].reshape(ET, 128, NST).transpose(1, 0, 2).reshape(128, ET * NST)),
        "m_cw": f(cw.reshape(ET, 128, 4).transpose(1, 0, 2).reshape(128, ET * 4)),
        "m_cb": f(inp["L0_attn_conv_b"].reshape(ET, 128).T),
        "m_bdt": f(inp["L0_attn_b_dt"].reshape(ET, 128).T),
        "m_dsk": f(inp["L0_attn_D_skip"].reshape(ET, 128).T),
        "m_wdt": f(inp["L0_attn_W_dt"]),
        "m_win": f(inp["L0_attn_W_in"]),
        "m_wout": f(inp["L0_attn_W_out"]),
    }
    wxp = np.zeros((E, 128), np.float32)
    wx = inp["L0_attn_W_xproj"]
    wxp[:, 0:16] = wx[:, DTR:DTR + NST]
    wxp[:, 32:48] = wx[:, DTR + NST:DTR + 2 * NST]
    wxp[:, 64:128] = wx[:, 0:DTR]
    d["m_wxp"] = f(wxp)
    return d


def _moe_host(inp):
    f = np.ascontiguousarray
    HT = HMOE // 128
    return {
        "g_wT": f(inp["L0_ffn_gate_w"].T),
        "g_b": f(inp["L0_ffn_gate_b"].reshape(1, NEXP)),
        "g_w1": f(inp["L0_ffn_w1"]),
        "g_b1": f(inp["L0_ffn_b1"].reshape(NEXP, HT, 128).transpose(2, 0, 1).reshape(128, NEXP * HT)),
        "g_w2": f(inp["L0_ffn_w2"]),
        "g_b2": f(inp["L0_ffn_b2"]),
    }


def _attn_host(inp):
    f = np.ascontiguousarray
    return {
        "a_wq": f(inp["L1_attn_Wq"].T),
        "a_wk": f(inp["L1_attn_Wk"].T),
        "a_wv": f(inp["L1_attn_Wv"].T),
        "a_wd": f(inp["L1_attn_Wd"].T),
        "a_bq": f(inp["L1_attn_bq"].reshape(1, D)),
        "a_bk": f(inp["L1_attn_bk"].reshape(1, D)),
        "a_bv": f(inp["L1_attn_bv"].reshape(1, D)),
        "a_bd": f(inp["L1_attn_bd"].reshape(1, D)),
        "a_gamma": f(inp["L1_attn_gamma"].reshape(1, D)),
        "a_beta": f(inp["L1_attn_beta"].reshape(1, D)),
    }


def _ffn1_host(inp):
    f = np.ascontiguousarray
    return {
        "f_UT": f(inp["L1_ffn_raw_U"].T),
        "f_rWT": f(inp["L1_ffn_router_W"].T),
        "f_rV": f(inp["L1_ffn_raw_V"]),
        "f_rb": f(inp["L1_ffn_router_b"].reshape(1, NB)),
        "f_lng": f(inp["L1_ffn_ln_g"].reshape(1, D)),
        "f_lnb": f(inp["L1_ffn_ln_b"].reshape(1, D)),
        "f_gam": f(inp["L1_ffn_gamma"].reshape(1, D)),
        "f_U": f(inp["L1_ffn_raw_U"]),
        "f_V": f(inp["L1_ffn_raw_V"]),
        "f_w1T": f(inp["L1_ffn_W1"].T),
        "f_w2T": f(inp["L1_ffn_W2"].T),
    }


_DRAM_SHAPES = {
    "m_A": (128, ET * NST), "m_cw": (128, ET * 4), "m_cb": (128, ET),
    "m_bdt": (128, ET), "m_dsk": (128, ET), "m_wdt": (DTR, E),
    "m_win": (D, 2 * E), "m_wout": (E, D), "m_wxp": (E, 128),
    "g_wT": (D, NEXP), "g_b": (1, NEXP), "g_w1": (NEXP, D, HMOE),
    "g_b1": (128, NEXP * (HMOE // 128)), "g_w2": (NEXP, HMOE, D), "g_b2": (NEXP, D),
    "a_wq": (D, D), "a_wk": (D, D), "a_wv": (D, D), "a_wd": (D, D),
    "a_bq": (1, D), "a_bk": (1, D), "a_bv": (1, D), "a_bd": (1, D),
    "a_gamma": (1, D), "a_beta": (1, D),
    "f_UT": (D, NB), "f_rWT": (D, NB), "f_rV": (NB, D), "f_rb": (1, NB),
    "f_lng": (1, D), "f_lnb": (1, D), "f_gam": (1, D),
    "f_U": (NB, D), "f_V": (NB, D), "f_w1T": (D, H1), "f_w2T": (H1, D),
}

_SCRATCH_SHAPES = {
    "B_win": (D, 2 * E), "B_wxp": (E, 128), "B_wdt": (DTR, E), "B_wout": (E, D),
    "B_gwT": (D, NEXP), "B_w1": (NEXP, D, HMOE), "B_w2": (NEXP, HMOE, D),
    "B_wq": (D, D), "B_wk": (D, D), "B_wv": (D, D), "B_wd": (D, D),
    "B_UT": (D, NB), "B_rWT": (D, NB), "B_rV": (NB, D),
    "B_w1T": (D, H1), "B_w2T": (H1, D),
}


def _build(emits, TC=256):
    nc = bass.Bass("TRN2", target_bir_lowering=False, debug=False)
    x_in = nc.dram_tensor("x_res_in", [T, D], F32, kind="ExternalInput").ap()
    x_out = nc.dram_tensor("x_res_out", [T, D], F32, kind="ExternalOutput").ap()
    dram = {k: nc.dram_tensor(k, list(v), F32, kind="ExternalInput").ap()
            for k, v in _DRAM_SHAPES.items()}
    scratch = {k: nc.dram_tensor(k, list(v), BF16, kind="Internal").ap()
               for k, v in _SCRATCH_SHAPES.items()}
    with ExitStack() as st:
        X = st.enter_context(nc.sbuf_tensor("X", [128, (T // 128) * D], F32))
        sems = {}
        for e in ENGS:
            sems[("c", e)] = st.enter_context(nc.semaphore(f"semc_{e}"))
            sems[("d", e)] = st.enter_context(nc.semaphore(f"semd_{e}"))
        blk = st.enter_context(nc.Block())
        p = P(nc)
        c = make_consts(p, st)
        p.barrier()
        pc_f = [st.enter_context(nc.sbuf_tensor(f"pc_f{i}", [128, 1024], F32)) for i in range(2)]
        pc_b = [st.enter_context(nc.sbuf_tensor(f"pc_b{i}", [128, 1024], BF16)) for i in range(2)]
        p.dma(X[:, :].rearrange("p (tt d) -> p tt d", d=D),
              x_in.rearrange("(tt p) d -> p tt d", p=128),
              wr=[("X", tt) for tt in range(T // 128)], q="gpsimd")
        emit_precast(p, st, dram, scratch, pc_f, pc_b)
        for em in emits:
            with ExitStack() as sti:
                if em is emit_mamba:
                    em(p, sti, c, X, dram, scratch, TC=TC)
                else:
                    em(p, sti, c, X, dram, scratch)
            p.barrier()
        p.dma(x_out.rearrange("(tt p) d -> p tt d", p=128),
              X[:, :].rearrange("p (tt d) -> p tt d", d=D),
              rd=[("X", tt) for tt in range(T // 128)], wr=[("OUT",)], q="gpsimd")
        p.finalize(sems)
    return nc


def build_mamba_program(TC=256):
    return _build([emit_mamba], TC=TC)


def build_moe_program():
    return _build([emit_moe])


def build_attn_program():
    return _build([emit_attn])


def build_ffn1_program():
    return _build([emit_ffn1])


def build_fused_program(TC=256):
    return _build([emit_mamba, emit_moe, emit_attn, emit_ffn1], TC=TC)


_PROGS = {}


def _all_host(inp):
    h = {}
    h.update(_mamba_host(inp))
    h.update(_moe_host(inp))
    h.update(_attn_host(inp))
    h.update(_ffn1_host(inp))
    return h


def kernel(**inputs):
    inp = {k: np.asarray(v) for k, v in inputs.items()}
    x = np.ascontiguousarray(inp["L0_attn_x"], dtype=np.float32)
    if "fused" not in _PROGS:
        _PROGS["fused"] = build_fused_program(TC=256)
    host = _all_host(inp)
    in_maps = [dict(host, x_res_in=np.ascontiguousarray(x[ci])) for ci in range(8)]
    res = run_bass_kernel_spmd(_PROGS["fused"], in_maps, core_ids=list(range(8)))
    return np.stack([np.asarray(r["x_res_out"]) for r in res.results]).astype(np.float32)


# version 06
# speedup vs baseline: 1.1596x; 1.1596x over previous
import numpy as np
import concourse.bass as bass
import concourse.mybir as mybir
from concourse.bass_utils import run_bass_kernel_spmd
from contextlib import ExitStack

F32 = mybir.dt.float32
BF16 = mybir.dt.bfloat16
AF = mybir.ActivationFunctionType
ALU = mybir.AluOpType
AX = mybir.AxisListType

T = 2048
D = 1024
E = 2048
NST = 16
DTR = 64
NEXP = 8
HMOE = 2048
NH = 16
HD = 64
NB = 512
H1 = 4096
DT = D // 128
ET = E // 128

LN_EPS = 1e-5
EPS_ATTN = 1e-6
EPS_LN_ATTN = 1e-12
FFN_EPS = 1e-6
TAU = 10.0

ENGS = ("tensor", "vector", "scalar", "gpsimd", "sync")


class _Node:
    __slots__ = ("eng", "fn", "inc", "deps", "need", "val", "dma", "sid")

    def __init__(self, eng, fn, inc, dma):
        self.eng = eng
        self.fn = fn
        self.inc = inc
        self.deps = []
        self.need = False
        self.val = 0
        self.dma = dma
        self.sid = ("d", eng) if dma else ("c", eng)


class P:

    def __init__(self, nc):
        self.nc = nc
        self.nodes = []
        self.res = {}
        self.cvals = None
        self.cmap = {}
        self.frozen_consts = False
        self.last_dma = {}

    def emit(self, eng_name, fn, rd=(), wr=(), inc=1, dma=False):
        node = _Node(eng_name, fn, inc, dma)
        dep = {}

        def add_dep(pn):
            if pn is node:
                return
            if pn.eng == eng_name == "tensor" and not (pn.dma or dma):
                return
            dep[id(pn)] = pn
        for k in rd:
            e = self.res.get(k)
            if e and e[0] is not None:
                add_dep(e[0])
        for k in wr:
            e = self.res.get(k)
            if e:
                if e[0] is not None:
                    add_dep(e[0])
                for rn in e[1].values():
                    add_dep(rn)
        if dma:
            prev = self.last_dma.get(eng_name)
            if prev is not None:
                add_dep(prev)
            self.last_dma[eng_name] = node
            node.need = True
        node.deps = list(dep.values())
        for pn in node.deps:
            pn.need = True
        for k in rd:
            e = self.res.setdefault(k, [None, {}])
            e[1][eng_name] = node
        for k in wr:
            self.res[k] = [node, {}]
        self.nodes.append(node)
        return node

    def barrier(self):
        self.nodes.append("BARRIER")

    def finalize(self, sems):
        SIDS = [("c", e) for e in ENGS] + [("d", e) for e in ENGS]
        counts = {s: 0 for s in SIDS}
        last = {s: None for s in SIDS}
        for nd in self.nodes:
            if nd == "BARRIER":
                for s in SIDS:
                    if last[s] is not None:
                        last[s].need = True
                continue
            last[nd.sid] = nd
        for s in SIDS:
            if last[s] is not None:
                last[s].need = True
        for nd in self.nodes:
            if nd == "BARRIER":
                continue
            if nd.need:
                counts[nd.sid] += nd.inc
                nd.val = counts[nd.sid]
        lastv = {s: 0 for s in SIDS}
        waited = {e: {s: 0 for s in SIDS} for e in ENGS}
        for nd in self.nodes:
            if nd == "BARRIER":
                snap = dict(lastv)
                for e in ENGS:
                    eng = getattr(self.nc, e)
                    for s in SIDS:
                        if s != ("c", e) and snap[s] > waited[e][s]:
                            eng.wait_ge(sems[s], snap[s])
                            waited[e][s] = snap[s]
                continue
            eng = getattr(self.nc, nd.eng)
            need_wait = {}
            for pn in nd.deps:
                if pn.val > waited[nd.eng].get(pn.sid, 0):
                    need_wait[pn.sid] = max(need_wait.get(pn.sid, 0), pn.val)
            for s, v in need_wait.items():
                eng.wait_ge(sems[s], v)
                waited[nd.eng][s] = v
            inst = nd.fn(eng)
            if nd.need:
                inst.then_inc(sems[nd.sid], nd.inc)
                lastv[nd.sid] = nd.val
        for s in SIDS:
            if s != ("c", "sync") and lastv[s] > waited["sync"][s]:
                self.nc.sync.wait_ge(sems[s], lastv[s])

    def dma(self, out, in_, rd=(), wr=(), q="sync"):
        self.emit(q, lambda e: e.dma_start(out=out, in_=in_), rd=rd, wr=wr,
                  inc=16, dma=True)

    def mm(self, out, lhsT, rhs, rd=(), wr=(), start=True, stop=True):
        self.emit("tensor", lambda e: e.matmul(out, lhsT, rhs, start=start, stop=stop),
                  rd=rd, wr=wr)

    def trans(self, out, in_, ident, rd=(), wr=()):
        self.emit("tensor", lambda e: e.transpose(out, in_, ident), rd=rd, wr=wr)

    def act(self, out, in_, func, rd=(), wr=(), bias=0.0, scale=1.0, accum_out=None):
        self.emit("scalar", lambda e: e.activation(out, in_, func, bias=bias,
                                                   scale=scale, accum_out=accum_out),
                  rd=rd, wr=wr)

    def cp(self, out, in_, rd=(), wr=(), eng="vector"):
        if eng == "scalar":
            self.emit(eng, lambda e: e.copy(out, in_), rd=rd, wr=wr)
        else:
            self.emit(eng, lambda e: e.tensor_copy(out, in_), rd=rd, wr=wr)

    def tt(self, out, a, b, op, rd=(), wr=(), eng="vector"):
        self.emit(eng, lambda e: e.tensor_tensor(out, a, b, op), rd=rd, wr=wr)

    def ts(self, out, in0, s1, op0, s2=None, op1=ALU.bypass, rd=(), wr=(), eng="vector"):
        if s2 is None:
            self.emit(eng, lambda e: e.tensor_single_scalar(out, in0, s1, op0),
                      rd=rd, wr=wr)
        else:
            self.emit(eng, lambda e: e.tensor_scalar(out, in0, s1, s2, op0=op0, op1=op1),
                      rd=rd, wr=wr)

    def stt(self, out, in0, scalar, in1, op0, op1, rd=(), wr=(), eng="vector"):
        self.emit(eng, lambda e: e.scalar_tensor_tensor(out, in0, scalar, in1,
                                                        op0=op0, op1=op1), rd=rd, wr=wr)

    def scan(self, out, d0, d1, init, rd=(), wr=(), eng="vector"):
        self.emit(eng, lambda e: e.tensor_tensor_scan(out, d0, d1, init,
                                                      op0=ALU.mult, op1=ALU.add),
                  rd=rd, wr=wr)

    def red(self, out, in_, op, rd=(), wr=(), eng="vector"):
        self.emit(eng, lambda e: e.tensor_reduce(out, in_, axis=AX.X, op=op),
                  rd=rd, wr=wr)

    def max8(self, out, in_, rd=(), wr=()):
        self.emit("vector", lambda e: e.max(out, in_), rd=rd, wr=wr)

    def recip(self, out, in_, rd=(), wr=()):
        self.emit("vector", lambda e: e.reciprocal(out, in_), rd=rd, wr=wr)

    def memset(self, ap, v, rd=(), wr=(), eng="vector"):
        self.emit(eng, lambda e: e.memset(ap, v), rd=rd, wr=wr)

    def iota(self, ap, pattern, rd=(), wr=(), base=0, mult=0):
        self.emit("gpsimd", lambda e: e.iota(ap, pattern=pattern, base=base,
                                             channel_multiplier=mult,
                                             allow_small_or_imprecise_dtypes=True),
                  rd=rd, wr=wr)

    def ccol(self, v):
        v = float(v)
        if v not in self.cmap:
            assert not self.frozen_consts, f"const {v} after freeze"
            i = len(self.cmap)
            self.cmap[v] = i
            self.memset(self.cvals[:, i:i + 1], v)
        return self.cvals[:, self.cmap[v]:self.cmap[v] + 1]


def make_consts(p, st):
    nc = p.nc
    c = {}
    c["ones_bf"] = st.enter_context(nc.sbuf_tensor("ones_bf", [1, 128], BF16))
    c["ident_f"] = st.enter_context(nc.sbuf_tensor("ident_f", [128, 128], F32))
    c["ident_bf"] = st.enter_context(nc.sbuf_tensor("ident_bf", [128, 128], BF16))
    c["onescol_bf"] = st.enter_context(nc.sbuf_tensor("onescol_bf", [128, 1], BF16))
    p.cvals = st.enter_context(nc.sbuf_tensor("cvals", [128, 16], F32))
    p.memset(c["ones_bf"][:, :], 1.0)
    p.memset(c["onescol_bf"][:, :], 1.0)
    p.iota(c["ident_f"][:, :], pattern=[[1, 128]], base=0, mult=-1, wr=[("iota",)])
    p.ts(c["ident_f"][:, :], c["ident_f"][:, :], 0.0, ALU.is_equal, rd=[("iota",)], wr=[("iota",)])
    p.cp(c["ident_bf"][:, :], c["ident_f"][:, :], rd=[("iota",)])
    for v in (LN_EPS, EPS_LN_ATTN, EPS_ATTN, -0.5, 0.5):
        p.ccol(v)
    p.frozen_consts = True
    return c



PRECAST = [
    ("m_win", D, 2 * E, "B_win"),
    ("m_wxp", E, 128, "B_wxp"),
    ("m_wdt", DTR, E, "B_wdt"),
    ("m_wout", E, D, "B_wout"),
    ("g_wT", D, NEXP, "B_gwT"),
    ("a_wq", D, D, "B_wq"),
    ("a_wk", D, D, "B_wk"),
    ("a_wv", D, D, "B_wv"),
    ("a_wd", D, D, "B_wd"),
    ("f_UT", D, NB, "B_UT"),
    ("f_rWT", D, NB, "B_rWT"),
    ("f_rV", NB, D, "B_rV"),
    ("f_w1T", D, H1, "B_w1T"),
    ("f_w2T", H1, D, "B_w2T"),
]

_PC_ENGS = ("gpsimd",)


def emit_precast(p, st, dram, scratch, sf, sb):
    i = 0
    for key, R, C, bname in PRECAST:
        src = dram[key]
        if len(src.shape) == 3:
            src = src.rearrange("a b c -> (a b) c")
        dst = scratch[bname]
        if len(dst.shape) == 3:
            dst = dst.rearrange("a b c -> (a b) c")
        for r0 in range(0, R, 128):
            rr = min(128, R - r0)
            for c0 in range(0, C, 1024):
                cc = min(1024, C - c0)
                g = i % 2
                p.dma(sf[g][0:rr, 0:cc], src[r0:r0 + rr, c0:c0 + cc],
                      wr=[("pcf", g)], q="sync" if g == 0 else "gpsimd")
                p.cp(sb[g][0:rr, 0:cc], sf[g][0:rr, 0:cc],
                     rd=[("pcf", g)], wr=[("pcb", g)], eng="gpsimd")
                p.dma(dst[r0:r0 + rr, c0:c0 + cc], sb[g][0:rr, 0:cc],
                      rd=[("pcb", g)], wr=[("B", bname)], q="scalar")
                i += 1


def emit_ln(p, x_ap, out_ap, stats, scr, eps, rd_x, wr_out, tag, sckey=None):
    Df = x_ap.shape[-1]
    ks = ("st",) + tag
    kc = sckey if sckey is not None else ("sc",) + tag
    mu = stats[:, 0:1]
    sq = stats[:, 1:2]
    var = stats[:, 2:3]
    rstd = stats[:, 3:4]
    p.red(mu, x_ap, ALU.add, rd=list(rd_x), wr=[ks])
    p.ts(mu, mu, 1.0 / Df, ALU.mult, rd=[ks], wr=[ks])
    p.act(scr, x_ap, AF.Square, accum_out=sq, rd=list(rd_x), wr=[kc, ks])
    p.ts(var, sq, 1.0 / Df, ALU.mult, rd=[ks], wr=[ks])
    p.tt(stats[:, 4:5], mu, mu, ALU.mult, rd=[ks], wr=[ks])
    p.tt(var, var, stats[:, 4:5], ALU.subtract, rd=[ks], wr=[ks])
    p.act(stats[:, 5:6], var, AF.Ln, bias=p.ccol(eps), rd=[ks], wr=[ks])
    p.act(rstd, stats[:, 5:6], AF.Exp, scale=p.ccol(-0.5), rd=[ks], wr=[ks])
    p.ts(out_ap, x_ap, mu, ALU.subtract, rstd, ALU.mult,
         rd=list(rd_x) + [ks], wr=list(wr_out))


def emit_tr(p, dst_ap, src_ap, ps_tr, ident_bf, rd_src, wr_dst, pkey):
    p.trans(ps_tr, src_ap, ident_bf[:, :], rd=list(rd_src), wr=[pkey])
    p.cp(dst_ap, ps_tr, rd=[pkey], wr=list(wr_dst))


def emit_tr8(p, dst3d, src_ap, ps_trB, ident_bf, rd_src, wr_dst, pkey, nk=8):
    for kd in range(nk):
        p.trans(ps_trB[:, kd * 128:(kd + 1) * 128], src_ap[:, kd * 128:(kd + 1) * 128],
                ident_bf[:, :], rd=list(rd_src), wr=[pkey])
    p.cp(dst3d, ps_trB[:, 0:nk * 128].rearrange("p (k f) -> p k f", f=128),
         rd=[pkey], wr=list(wr_dst), eng="scalar")



def emit_mamba(p, st, c, X, dram, scratch, TC=256):
    nc = p.nc
    NCH = T // TC
    TSUB = TC // 128

    xnT = [st.enter_context(nc.sbuf_tensor(f"mb_xnT{i}", [128, DT * TC], BF16)) for i in range(2)]
    xi = st.enter_context(nc.sbuf_tensor("mb_xi", [128, ET * (TC + 3)], F32))
    zs = st.enter_context(nc.sbuf_tensor("mb_zs", [128, ET * TC], BF16))
    ytp = [st.enter_context(nc.sbuf_tensor(f"mb_ytp{i}", [128, TC], F32)) for i in range(2)]
    delta = st.enter_context(nc.sbuf_tensor("mb_delta", [128, ET * TC], F32))
    wdu = st.enter_context(nc.sbuf_tensor("mb_w", [128, ET * TC], BF16))
    yac = st.enter_context(nc.sbuf_tensor("mb_y", [128, ET * TC], F32))
    ygbf = st.enter_context(nc.sbuf_tensor("mb_yg", [128, ET * TC], BF16))
    hstate = st.enter_context(nc.sbuf_tensor("mb_h", [128, ET * NST], F32))
    stash = st.enter_context(nc.sbuf_tensor("mb_stash", [128, ET * 3], F32))
    stats = [st.enter_context(nc.sbuf_tensor(f"mb_stats{i}", [128, 8], F32)) for i in range(2)]
    scrA = st.enter_context(nc.sbuf_tensor("mb_scrA", [128, 1024], F32))
    xnbf_t = [st.enter_context(nc.sbuf_tensor(f"mb_xnbf{i}", [128, 1024], BF16)) for i in range(2)]
    cbuf0 = st.enter_context(nc.sbuf_tensor("mb_cb0", [128, TC], F32))
    cbuf1 = st.enter_context(nc.sbuf_tensor("mb_cb1", [128, TC], F32))
    ubf = [st.enter_context(nc.sbuf_tensor(f"mb_ubf{i}", [128, TC], BF16)) for i in range(2)]
    dtlr_bf = st.enter_context(nc.sbuf_tensor("mb_dtlr", [64, TC], BF16))
    Bt_bf = st.enter_context(nc.sbuf_tensor("mb_Bt", [16, TC], BF16))
    Ct_bf = st.enter_context(nc.sbuf_tensor("mb_Ct", [16, TC], BF16))
    Bb = [st.enter_context(nc.sbuf_tensor(f"mb_Bb{i}", [128, TC], BF16)) for i in range(2)]
    Cb = [st.enter_context(nc.sbuf_tensor(f"mb_Cb{i}", [128, TC], BF16)) for i in range(2)]
    dAb = [st.enter_context(nc.sbuf_tensor(f"mb_dA{i}", [128, TC], F32)) for i in range(2)]
    dBub = [st.enter_context(nc.sbuf_tensor(f"mb_dBu{i}", [128, TC], F32)) for i in range(2)]
    hb = [st.enter_context(nc.sbuf_tensor(f"mb_hb{i}", [128, TC], F32)) for i in range(2)]
    sp0 = st.enter_context(nc.sbuf_tensor("mb_sp0", [128, TC], F32))
    Asb = st.enter_context(nc.sbuf_tensor("mb_A", [128, ET * NST], F32))
    cwsb = st.enter_context(nc.sbuf_tensor("mb_cw", [128, ET * 4], F32))
    cbsb = st.enter_context(nc.sbuf_tensor("mb_cbv", [128, ET], F32))
    bdtsb = st.enter_context(nc.sbuf_tensor("mb_bdt", [128, ET], F32))
    dsksb = st.enter_context(nc.sbuf_tensor("mb_dsk", [128, ET], F32))
    wxp = st.enter_context(nc.sbuf_tensor("mb_wxp", [128, ET * 128], BF16))
    wdt = st.enter_context(nc.sbuf_tensor("mb_wdt", [64, E], BF16))
    sel = st.enter_context(nc.sbuf_tensor("mb_sel", [16, NST * 128], BF16))
    win_bf = [st.enter_context(nc.sbuf_tensor(f"mb_winb{i}", [128, DT * 128], BF16)) for i in range(2)]
    wo_et = win_bf
    ps_trB = st.enter_context(nc.psum_tensor("mb_ptr", [128, 1024], BF16))
    ps_m0 = st.enter_context(nc.psum_tensor("mb_pm0", [128, 256], F32))
    ps_m1 = st.enter_context(nc.psum_tensor("mb_pm1", [128, 256], F32))
    ps_xd = st.enter_context(nc.psum_tensor("mb_pxd", [128, 512], F32))
    ps_o = st.enter_context(nc.psum_tensor("mb_po", [128, 2048], F32))
    assert TC == 256
    ps_mm = [ps_m0[:, :], ps_m1[:, :]]
    ps_d = ps_xd[:, TC:2 * TC]
    ps_B = ps_o[:, 0:TC]
    ps_C = ps_o[:, TC:2 * TC]

    p.dma(Asb[:, :], dram["m_A"], wr=[("mA",)])
    p.dma(cwsb[:, :], dram["m_cw"], wr=[("mcw",)])
    p.dma(cbsb[:, :], dram["m_cb"], wr=[("mcb",)])
    p.dma(bdtsb[:, :], dram["m_bdt"], wr=[("mbdt",)])
    p.dma(dsksb[:, :], dram["m_dsk"], wr=[("mdsk",)])
    p.dma(wxp[:, :].rearrange("p (et cc) -> p et cc", cc=128),
          scratch["B_wxp"].rearrange("(et p) cc -> p et cc", p=128),
          rd=[("B", "B_wxp")], wr=[("mwxp",)], q="scalar")
    p.dma(wdt[:, :], scratch["B_wdt"], rd=[("B", "B_wdt")], wr=[("mwdt",)], q="scalar")
    p.act(Asb[:, :], Asb[:, :], AF.Exp, rd=[("mA",)], wr=[("mA",)])
    p.ts(Asb[:, :], Asb[:, :], -1.0, ALU.mult, rd=[("mA",)], wr=[("mA",)])
    for half in range(2):
        p.iota(scrA[0:16, 0:1024], pattern=[[1, 8], [0, 128]], base=-half * 8, mult=-1,
               wr=[("scA",)])
        p.ts(scrA[0:16, 0:1024], scrA[0:16, 0:1024], 0.0, ALU.is_equal,
             rd=[("scA",)], wr=[("scA",)])
        p.cp(sel[:, half * 1024:(half + 1) * 1024], scrA[0:16, 0:1024],
             rd=[("scA",)], wr=[("msel",)])
    p.memset(hstate[:, :], 0.0, wr=[("hst", dd, n) for dd in range(ET) for n in range(NST)])

    xi_v = lambda et: xi[:, et * (TC + 3):(et + 1) * (TC + 3)]

    for ch in range(NCH):
        cp_ = ch % 2
        for tsub in range(TSUB):
            tt = ch * TSUB + tsub
            xt = X[:, tt * D:(tt + 1) * D]
            emit_ln(p, xt, xnbf_t[tsub][:, :], stats[tsub], scrA[:, 0:1024],
                    LN_EPS, rd_x=[("X", tt)], wr_out=[("xnbf", tsub)], tag=("mln", tsub),
                    sckey=("scA",))
            dst3d = xnT[cp_][:, :].rearrange("p (k t) -> p k t", t=TC)[:, :, tsub * 128:tsub * 128 + 128]
            emit_tr8(p, dst3d, xnbf_t[tsub][:, :], ps_trB, c["ident_bf"],
                     rd_src=[("xnbf", tsub)],
                     wr_dst=[("xnT", cp_, kd) for kd in range(DT)], pkey=("pstr",))
        for ec in range(32):
            wp = ec % 2
            p.dma(win_bf[wp][:, :].rearrange("p (dc e) -> p dc e", e=128),
                  scratch["B_win"][:, ec * 128:(ec + 1) * 128]
                  .rearrange("(dc p) e -> p dc e", p=128),
                  rd=[("B", "B_win")], wr=[("winb", wp)], q="sync")
            for dc in range(DT):
                p.mm(ps_mm[wp], win_bf[wp][:, dc * 128:(dc + 1) * 128],
                     xnT[cp_][:, dc * TC:(dc + 1) * TC],
                     rd=[("winb", wp), ("xnT", cp_, dc)], wr=[("psmm", wp)],
                     start=(dc == 0), stop=(dc == DT - 1))
            if ec < 16:
                p.act(xi_v(ec)[:, 3:3 + TC], ps_mm[wp], AF.Identity,
                      rd=[("psmm", wp)], wr=[("xi", ec)])
            else:
                p.act(zs[:, (ec - 16) * TC:(ec - 15) * TC], ps_mm[wp], AF.Silu,
                      rd=[("psmm", wp)], wr=[("zs", ec - 16)])
        for et in range(ET):
            xv = xi_v(et)
            up = et % 2
            if ch == 0:
                p.memset(xv[:, 0:3], 0.0, wr=[("xi", et)])
            else:
                p.cp(xv[:, 0:3], stash[:, et * 3:(et + 1) * 3],
                     rd=[("stash", et)], wr=[("xi", et)])
            cw = lambda k: cwsb[:, et * 4 + k: et * 4 + k + 1]
            p.ts(cbuf0[:, :], xv[:, 0:TC], cw(0), ALU.mult, rd=[("xi", et), ("mcw",)], wr=[("cb",)])
            p.stt(cbuf1[:, :], xv[:, 1:1 + TC], cw(1), cbuf0[:, :], ALU.mult, ALU.add,
                  rd=[("xi", et), ("cb",)], wr=[("cb",)])
            p.stt(cbuf0[:, :], xv[:, 2:2 + TC], cw(2), cbuf1[:, :], ALU.mult, ALU.add,
                  rd=[("xi", et), ("cb",)], wr=[("cb",)])
            p.stt(cbuf1[:, :], xv[:, 3:3 + TC], cw(3), cbuf0[:, :], ALU.mult, ALU.add,
                  rd=[("xi", et), ("cb",)], wr=[("cb",)])
            p.cp(stash[:, et * 3:(et + 1) * 3], xv[:, TC:TC + 3],
                 rd=[("xi", et)], wr=[("stash", et)])
            p.act(xv[:, 3:3 + TC], cbuf1[:, :], AF.Silu, bias=cbsb[:, et:et + 1],
                  rd=[("cb",), ("stash", et), ("mcb",)], wr=[("xi", et)])
            p.cp(ubf[up][:, :], xv[:, 3:3 + TC], rd=[("xi", et)], wr=[("ubf", up)], eng="gpsimd")
            p.ts(yac[:, et * TC:(et + 1) * TC], xv[:, 3:3 + TC], dsksb[:, et:et + 1],
                 ALU.mult, rd=[("xi", et), ("mdsk",)], wr=[("yac", et)])
            p.mm(ps_xd[:, 0:TC], wxp[:, et * 128:(et + 1) * 128], ubf[up][:, :],
                 rd=[("ubf", up), ("mwxp",)], wr=[("psxd",)],
                 start=(et == 0), stop=(et == ET - 1))
        p.act(Bt_bf[:, :], ps_xd[0:16, 0:TC], AF.Identity, rd=[("psxd",)], wr=[("Bt",)])
        p.act(Ct_bf[:, :], ps_xd[32:48, 0:TC], AF.Identity, rd=[("psxd",)], wr=[("Ct",)])
        p.act(dtlr_bf[:, :], ps_xd[64:128, 0:TC], AF.Identity, rd=[("psxd",)], wr=[("dtlr",)])
        for dt in range(ET):
            dv = delta[:, dt * TC:(dt + 1) * TC]
            p.mm(ps_d, wdt[:, dt * 128:(dt + 1) * 128], dtlr_bf[:, :],
                 rd=[("dtlr",), ("mwdt",)], wr=[("psxd",)])
            p.act(sp0[:, :], ps_d, AF.Identity, bias=bdtsb[:, dt:dt + 1],
                  rd=[("psxd",), ("mbdt",)], wr=[("sp0",)])
            p.act(cbuf1[:, :], sp0[:, :], AF.Abs, rd=[("sp0",), ("cb",)], wr=[("cb",)])
            p.act(cbuf0[:, :], cbuf1[:, :], AF.Exp, scale=-1.0, rd=[("cb",)], wr=[("cb",)])
            p.act(cbuf1[:, :], cbuf0[:, :], AF.Ln, bias=1.0, rd=[("cb",)], wr=[("cb",)])
            p.stt(dv, sp0[:, :], 0.0, cbuf1[:, :], ALU.max, ALU.add,
                  rd=[("sp0",), ("cb",)], wr=[("delta", dt)])
            p.tt(wdu[:, dt * TC:(dt + 1) * TC], dv, xi_v(dt)[:, 3:3 + TC], ALU.mult,
                 rd=[("delta", dt), ("xi", dt)], wr=[("w", dt)])
        for n in range(NST):
            np_ = n % 2
            p.mm(ps_B, sel[:, n * 128:(n + 1) * 128], Bt_bf[:, :],
                 rd=[("Bt",), ("msel",)], wr=[("pso",)])
            p.act(Bb[np_][:, :], ps_B, AF.Identity, rd=[("pso",)], wr=[("Bb", np_)])
            p.mm(ps_C, sel[:, n * 128:(n + 1) * 128], Ct_bf[:, :],
                 rd=[("Ct",), ("msel",)], wr=[("pso",)])
            p.act(Cb[np_][:, :], ps_C, AF.Identity, rd=[("pso",)], wr=[("Cb", np_)])
            for dt in range(ET):
                it = n * ET + dt
                a3 = it % 2
                i2 = it % 2
                p.act(dAb[a3][:, :], delta[:, dt * TC:(dt + 1) * TC], AF.Exp,
                      scale=Asb[:, dt * NST + n: dt * NST + n + 1],
                      rd=[("delta", dt), ("mA",)], wr=[("dA", a3)])
                p.tt(dBub[i2][:, :], wdu[:, dt * TC:(dt + 1) * TC], Bb[np_][:, :], ALU.mult,
                     rd=[("w", dt), ("Bb", np_)], wr=[("dBu", i2)])
                p.scan(hb[i2][:, :], dAb[a3][:, :], dBub[i2][:, :],
                       hstate[:, dt * NST + n: dt * NST + n + 1],
                       rd=[("dA", a3), ("dBu", i2), ("hst", dt, n)], wr=[("hb", i2)])
                p.cp(hstate[:, dt * NST + n: dt * NST + n + 1], hb[i2][:, TC - 1:TC],
                     rd=[("hb", i2)], wr=[("hst", dt, n)], eng="gpsimd")
                yv = yac[:, dt * TC:(dt + 1) * TC]
                yt = ytp[i2]
                p.tt(yt[:, :], hb[i2][:, :], Cb[np_][:, :], ALU.mult,
                     rd=[("hb", i2), ("Cb", np_)], wr=[("ytp", i2)])
                p.tt(yv, yv, yt[:, :], ALU.add,
                     rd=[("yac", dt), ("ytp", i2)], wr=[("yac", dt)], eng="gpsimd")
        for et in range(ET):
            yv = yac[:, et * TC:(et + 1) * TC]
            p.tt(ygbf[:, et * TC:(et + 1) * TC], yv, zs[:, et * TC:(et + 1) * TC],
                 ALU.mult, rd=[("yac", et), ("zs", et)], wr=[("yg", et)])
        for et in range(ET):
            wp = et % 2
            p.dma(wo_et[wp][:, :], scratch["B_wout"][et * 128:(et + 1) * 128, :],
                  rd=[("B", "B_wout")], wr=[("winb", wp)], q="scalar")
            for tsub in range(TSUB):
                for oh in range(2):
                    p.mm(ps_o[:, (tsub * 2 + oh) * 512:(tsub * 2 + oh + 1) * 512],
                         ygbf[:, et * TC + tsub * 128: et * TC + tsub * 128 + 128],
                         wo_et[wp][:, oh * 512:(oh + 1) * 512],
                         rd=[("yg", et), ("winb", wp)], wr=[("pso",)],
                         start=(et == 0), stop=(et == ET - 1))
        for tsub in range(TSUB):
            tt = ch * TSUB + tsub
            for oh in range(2):
                xs = X[:, tt * D + oh * 512: tt * D + (oh + 1) * 512]
                p.tt(xs, xs, ps_o[:, (tsub * 2 + oh) * 512:(tsub * 2 + oh + 1) * 512],
                     ALU.add, rd=[("X", tt), ("pso",)], wr=[("X", tt)])



def emit_moe(p, st, c, X, dram, scratch):
    nc = p.nc
    TTI = T // 128
    HT = HMOE // 128
    xnT = st.enter_context(nc.sbuf_tensor("mo_xnT", [128, DT * T], BF16))
    w1bf = st.enter_context(nc.sbuf_tensor("mo_w1", [128, DT * HMOE], BF16))
    w2bf = st.enter_context(nc.sbuf_tensor("mo_w2", [128, HT * D], BF16))
    hbf = st.enter_context(nc.sbuf_tensor("mo_h", [128, HT * 256], BF16))
    gwts = st.enter_context(nc.sbuf_tensor("mo_gw", [128, TTI * NEXP], F32))
    sg_ = [st.enter_context(nc.sbuf_tensor(f"mo_sg{i}", [128, NEXP], F32)) for i in range(2)]
    m8_ = [st.enter_context(nc.sbuf_tensor(f"mo_m8{i}", [128, 8], F32)) for i in range(2)]
    mk_ = [st.enter_context(nc.sbuf_tensor(f"mo_mask{i}", [128, NEXP], F32)) for i in range(2)]
    stats = [st.enter_context(nc.sbuf_tensor(f"mo_stats{i}", [128, 8], F32)) for i in range(2)]
    scrA = st.enter_context(nc.sbuf_tensor("mo_scrA", [128, 2048], F32))
    scrB = st.enter_context(nc.sbuf_tensor("mo_scrB", [128, 2048], F32))
    rowst = st.enter_context(nc.sbuf_tensor("mo_rowst", [1, 512], F32))
    _xb = st.enter_context(nc.sbuf_tensor("mo_xnbf", [128, 1024], BF16))
    xnbf_t = [_xb, _xb]
    gwT_bf = st.enter_context(nc.sbuf_tensor("mo_gwT", [128, DT * NEXP], BF16))
    gb_bf = st.enter_context(nc.sbuf_tensor("mo_gb", [1, NEXP], BF16))
    b1sb = st.enter_context(nc.sbuf_tensor("mo_b1", [128, NEXP * HT], F32))
    b2bf = [st.enter_context(nc.sbuf_tensor("mo_b2", [1, D], BF16))] * 2
    ps_trB = st.enter_context(nc.psum_tensor("mo_ptr", [128, 1024], BF16))
    ps_a = st.enter_context(nc.psum_tensor("mo_pa", [128, 512], F32))
    ps_h = [st.enter_context(nc.psum_tensor(f"mo_ph{i}", [128, 512], F32)) for i in range(2)]
    ps_o = [st.enter_context(nc.psum_tensor(f"mo_po{i}", [128, 512], F32)) for i in range(2)]
    ps_g = ps_a[:, 0:NEXP]

    p.dma(gwT_bf[:, :].rearrange("p (dc e) -> p dc e", e=NEXP),
          scratch["B_gwT"].rearrange("(dc p) e -> p dc e", p=128),
          rd=[("B", "B_gwT")], wr=[("gwT",)], q="scalar")
    p.dma(rowst[0:1, 0:NEXP], dram["g_b"], wr=[("sAb",)])
    p.cp(gb_bf[:, :], rowst[0:1, 0:NEXP], rd=[("sAb",)], wr=[("gb",)])
    p.dma(b1sb[:, :].rearrange("p (e h) -> p e h", h=HT), dram["g_b1"], wr=[("b1",)])
    for tt in range(TTI):
        tp = tt % 2
        xt = X[:, tt * D:(tt + 1) * D]
        emit_ln(p, xt, xnbf_t[tp][:, :], stats[tp], scrA[:, 0:1024],
                LN_EPS, rd_x=[("X", tt)], wr_out=[("xnbf", tp)], tag=("gln", tp),
                sckey=("sA",))
        dst3d = xnT[:, :].rearrange("p (k t) -> p k t", t=T)[:, :, tt * 128:tt * 128 + 128]
        emit_tr8(p, dst3d, xnbf_t[tp][:, :], ps_trB, c["ident_bf"],
                 rd_src=[("xnbf", tp)],
                 wr_dst=[("xnT", tt, kd) for kd in range(DT)], pkey=("pstr",))
        for kd in range(DT):
            p.mm(ps_g, xnT[:, kd * T + tt * 128: kd * T + tt * 128 + 128],
                 gwT_bf[:, kd * NEXP:(kd + 1) * NEXP],
                 rd=[("xnT", tt, kd), ("gwT",)], wr=[("psg",)],
                 start=(kd == 0), stop=False)
        p.mm(ps_g, c["ones_bf"][0:1, 0:128], gb_bf[0:1, :], rd=[("gb",)], wr=[("psg",)],
             start=False, stop=True)
        sg = sg_[tp]
        m8 = m8_[tp]
        maskb = mk_[tp]
        p.act(sg[:, :], ps_g, AF.Identity, rd=[("psg",)], wr=[("sg", tp)])
        p.red(stats[tp][:, 6:7], sg[:, :], ALU.max, rd=[("sg", tp)], wr=[("gst", tp)])
        p.ts(sg[:, :], sg[:, :], stats[tp][:, 6:7], ALU.subtract, rd=[("sg", tp), ("gst", tp)], wr=[("sg", tp)])
        p.act(sg[:, :], sg[:, :], AF.Exp, rd=[("sg", tp)], wr=[("sg", tp)])
        p.red(stats[tp][:, 6:7], sg[:, :], ALU.add, rd=[("sg", tp)], wr=[("gst", tp)])
        p.recip(stats[tp][:, 7:8], stats[tp][:, 6:7], rd=[("gst", tp)], wr=[("gst", tp)])
        p.ts(sg[:, :], sg[:, :], stats[tp][:, 7:8], ALU.mult, rd=[("sg", tp), ("gst", tp)], wr=[("sg", tp)])
        p.max8(m8[:, :], sg[:, :], rd=[("sg", tp)], wr=[("m8", tp)])
        p.ts(maskb[:, :], sg[:, :], m8[:, 1:2], ALU.is_ge, rd=[("sg", tp), ("m8", tp)], wr=[("mk", tp)])
        p.tt(gwts[:, tt * NEXP:(tt + 1) * NEXP], sg[:, :], maskb[:, :], ALU.mult,
             rd=[("sg", tp), ("mk", tp)], wr=[("gwts", tt)])
    for e in range(NEXP):
        ep = e % 2
        for dc in range(DT):
            sc, key, qq = (scrA, ("sw1", 0), "sync") if dc % 2 == 0 else (scrB, ("sw1", 1), "gpsimd")
            p.dma(sc[:, :], dram["g_w1"][e, dc * 128:(dc + 1) * 128, :],
                  rd=[key, ("sA",)], wr=[key, ("sA",)] if dc % 2 == 0 else [key], q=qq)
            p.cp(w1bf[:, dc * HMOE:(dc + 1) * HMOE], sc[:, :],
                 rd=[key], wr=[("w1", dc)], eng="vector" if dc % 2 == 0 else "gpsimd")
        for ht in range(HT):
            sc, key = (scrA[:, 0:D], ("sw1", 0)) if ht % 2 == 0 else (scrB[:, 0:D], ("sw1", 1))
            p.dma(sc, dram["g_w2"][e, ht * 128:(ht + 1) * 128, :],
                  rd=[key], wr=[key], q="scalar")
            p.cp(w2bf[:, ht * D:(ht + 1) * D], sc,
                 rd=[key], wr=[("w2", ht)], eng="vector" if ht % 2 == 0 else "gpsimd")
        for bh in range(2):
            p.dma(rowst[0:1, :], dram["g_b2"][e:e + 1, bh * 512:(bh + 1) * 512],
                  rd=[("sAb",)], wr=[("sAb",)])
            p.cp(b2bf[ep][:, bh * 512:(bh + 1) * 512], rowst[0:1, :],
                 rd=[("sAb",)], wr=[("b2", ep)])
        for tc in range(T // 256):
            for ht in range(HT):
                hp = ht % 2
                for dc in range(DT):
                    p.mm(ps_h[hp][:, :].rearrange("p (a b) -> p a b", a=2)[:, 0, 0:256]
                         if False else ps_h[hp][:, 0:256],
                         w1bf[:, dc * HMOE + ht * 128: dc * HMOE + ht * 128 + 128],
                         xnT[:, dc * T + tc * 256: dc * T + tc * 256 + 256],
                         rd=[("w1", dc)] + [("xnT", tc * 2 + q_, dc) for q_ in range(2)],
                         wr=[("psh", hp)],
                         start=(dc == 0), stop=(dc == DT - 1))
                p.act(hbf[:, ht * 256:(ht + 1) * 256], ps_h[hp][:, 0:256], AF.Relu,
                      bias=b1sb[:, e * HT + ht: e * HT + ht + 1],
                      rd=[("psh", hp), ("b1",)], wr=[("hbf", ht)])
            for tsub in range(2):
                tt = tc * 2 + tsub
                gcol = gwts[:, tt * NEXP + e: tt * NEXP + e + 1]
                for oh in range(2):
                    for ht in range(HT):
                        p.mm(ps_o[oh][:, :], hbf[:, ht * 256 + tsub * 128: ht * 256 + tsub * 128 + 128],
                             w2bf[:, ht * D + oh * 512: ht * D + oh * 512 + 512],
                             rd=[("hbf", ht), ("w2", ht)], wr=[("pso", oh)],
                             start=(ht == 0), stop=False)
                    p.mm(ps_o[oh][:, :], c["ones_bf"][0:1, 0:128], b2bf[ep][0:1, oh * 512:(oh + 1) * 512],
                         rd=[("b2", ep)], wr=[("pso", oh)], start=False, stop=True)
                    xs = X[:, tt * D + oh * 512: tt * D + (oh + 1) * 512]
                    p.stt(xs, ps_o[oh][:, :], gcol, xs, ALU.mult, ALU.add,
                          rd=[("pso", oh), ("gwts", tt), ("X", tt)], wr=[("X", tt)])



ATTN_PHASES = "ABqkvwstzcuTd"


def emit_attn(p, st, c, X, dram, scratch):
    nc = p.nc
    TTI = T // 128
    qT = st.enter_context(nc.sbuf_tensor("at_qT", [128, DT * T], BF16))
    wqkv = st.enter_context(nc.sbuf_tensor("at_wqkv", [128, 3 * DT * D], BF16))
    wd_bf = wqkv[:, 0:DT * D]
    bias_bf = st.enter_context(nc.sbuf_tensor("at_bias", [1, 4 * D], BF16))
    gb_bc = st.enter_context(nc.sbuf_tensor("at_gb", [128, 2 * D], F32))
    scr3 = st.enter_context(nc.sbuf_tensor("at_scr3", [128, 3072], F32))
    stats = [st.enter_context(nc.sbuf_tensor(f"at_stats{i}", [128, 8], F32)) for i in range(2)]
    xnbf_t = st.enter_context(nc.sbuf_tensor("at_xnbf", [128, 1024], BF16))
    xnT_t = st.enter_context(nc.sbuf_tensor("at_xnT", [128, DT * 128], BF16))
    qkbf = st.enter_context(nc.sbuf_tensor("at_qkbf", [128, 1024], BF16))
    vbf = st.enter_context(nc.sbuf_tensor("at_vbf", [128, 1024], BF16))
    kvbf = st.enter_context(nc.sbuf_tensor("at_kvbf", [128, NH * HD], BF16))
    kvacc = st.enter_context(nc.sbuf_tensor("at_kvacc", [64, NH * HD], F32))
    ksacc = st.enter_context(nc.sbuf_tensor("at_ksacc", [128, DT], F32))
    ksz = st.enter_context(nc.sbuf_tensor("at_ksz", [128, NH], BF16))
    ksf = st.enter_context(nc.sbuf_tensor("at_ksf", [128, DT], F32))
    zrec = st.enter_context(nc.sbuf_tensor("at_z", [128, NH], F32))
    czbf = st.enter_context(nc.sbuf_tensor("at_cz", [128, 1024], BF16))
    ctxT_t = st.enter_context(nc.sbuf_tensor("at_ctxT", [128, DT * 128], BF16))
    hs_fp = st.enter_context(nc.sbuf_tensor("at_hs", [128, 1024], F32))
    ps_trB = st.enter_context(nc.psum_tensor("at_ptr", [128, 1024], BF16))
    ps_w = [st.enter_context(nc.psum_tensor(f"at_pw{i}", [128, 512], F32)) for i in range(2)]
    ps_kv = st.enter_context(nc.psum_tensor("at_pkv", [64, NH * HD], F32))
    ps_small = st.enter_context(nc.psum_tensor("at_psm", [128, 512], F32))
    ps_ksc = ps_small[:, 0:DT]
    ps_z = ps_small[:, 16:16 + NH]
    ps_ctx = st.enter_context(nc.psum_tensor("at_pctx", [128, 1024], F32))
    p.memset(kvacc[:, :], 0.0, wr=[("kvacc",)])
    p.memset(ksacc[:, :], 0.0, wr=[("ksacc",)])
    efp = scr3[:, 0:1024]
    et0 = scr3[:, 1024:1536]
    et1 = scr3[:, 1536:2048]
    scrB = scr3[:, 1024:2048]
    hsn = scr3[:, 2048:3072]

    for j, wname in enumerate(("B_wq", "B_wk", "B_wv")):
        p.dma(wqkv[:, j * DT * D:(j + 1) * DT * D].rearrange("p (dc dd) -> p dc dd", dd=D),
              scratch[wname].rearrange("(dc p) dd -> p dc dd", p=128),
              rd=[("B", wname)], wr=[("wqkv", j)], q="sync")
    for j, bname in enumerate(("a_bq", "a_bk", "a_bv", "a_bd")):
        p.dma(scrB[0:1, 0:D], dram[bname], rd=[("scd",)], wr=[("scd",)])
        p.cp(bias_bf[:, j * D:(j + 1) * D], scrB[0:1, 0:D], rd=[("scd",)], wr=[("abias", j)])
    for j, gname in enumerate(("a_gamma", "a_beta")):
        p.dma(scrB[0:1, 0:D], dram[gname], rd=[("scd",)], wr=[("scd",)])
        p.cp(qkbf[0:1, 0:D], scrB[0:1, 0:D], rd=[("scd",)], wr=[("qkbf",)])
        for oh in range(2):
            p.mm(ps_w[oh][:, :], c["ones_bf"][0:1, 0:128], qkbf[0:1, oh * 512:(oh + 1) * 512],
                 rd=[("qkbf",)], wr=[("psw", oh)])
            p.act(gb_bc[:, j * D + oh * 512: j * D + (oh + 1) * 512], ps_w[oh][:, :],
                  AF.Identity, rd=[("psw", oh)], wr=[("gbbc",)])

    def proj(j, oh, pw):
        for kd in range(DT):
            p.mm(ps_w[pw][:, :], xnT_t[:, kd * 128:(kd + 1) * 128],
                 wqkv[:, (j * DT + kd) * D + oh * 512:(j * DT + kd) * D + (oh + 1) * 512],
                 rd=[("xnTt", kd), ("wqkv", j)], wr=[("psw", pw)],
                 start=(kd == 0), stop=False)
        p.mm(ps_w[pw][:, :], c["ones_bf"][0:1, 0:128],
             bias_bf[0:1, j * D + oh * 512: j * D + (oh + 1) * 512],
             rd=[("abias", j)], wr=[("psw", pw)], start=False, stop=True)

    def elu1(dst_half, pw, wrk):
        p.ts(et0, ps_w[pw][:, :], 0.0, ALU.min, rd=[("psw", pw), ("scd",)], wr=[("scd",)])
        p.act(et1, et0, AF.Exp, rd=[("scd",)], wr=[("scd",)])
        p.stt(dst_half, ps_w[pw][:, :], 0.0, et1, ALU.max, ALU.add,
              rd=[("psw", pw), ("scd",)], wr=[wrk])

    for tt in (range(TTI) if "A" in ATTN_PHASES else []):
        tp = tt % 2
        xt = X[:, tt * D:(tt + 1) * D]
        emit_ln(p, xt, xnbf_t[:, :], stats[tp], scrB,
                LN_EPS, rd_x=[("X", tt)], wr_out=[("xnbf",)], tag=("aln", tp),
                sckey=("scd",))
        emit_tr8(p, xnT_t[:, :].rearrange("p (k f) -> p k f", f=128),
                 xnbf_t[:, :], ps_trB, c["ident_bf"], rd_src=[("xnbf",)],
                 wr_dst=[("xnTt", kd) for kd in range(DT)], pkey=("pstr",))
        if "2" in ATTN_PHASES:
            emit_tr8(p, xnT_t[:, :].rearrange("p (k f) -> p k f", f=128),
                     xnbf_t[:, :], ps_trB, c["ident_bf"], rd_src=[("xnbf",)],
                     wr_dst=[("xnTt", kd) for kd in range(DT)], pkey=("pstr",))
        if "q" in ATTN_PHASES:
            for oh in range(2):
                proj(0, oh, oh)
                elu1(efp[:, oh * 512:(oh + 1) * 512], oh, ("efp", oh))
            p.cp(qkbf[:, :], efp, rd=[("efp", 0), ("efp", 1)], wr=[("qkbf",)])
            qdst = qT[:, :].rearrange("p (k t) -> p k t", t=T)[:, :, tt * 128:tt * 128 + 128]
            emit_tr8(p, qdst, qkbf[:, :], ps_trB, c["ident_bf"],
                     rd_src=[("qkbf",)],
                     wr_dst=[("qT", tt, kd) for kd in range(DT)], pkey=("pstr",))
        if "k" in ATTN_PHASES:
            for oh in range(2):
                proj(1, oh, oh)
                elu1(efp[:, oh * 512:(oh + 1) * 512], oh, ("efp", oh))
            p.cp(qkbf[:, :], efp, rd=[("efp", 0), ("efp", 1)], wr=[("qkbf",)])
        if "v" in ATTN_PHASES:
            for oh in range(2):
                proj(2, oh, oh)
                p.act(vbf[:, oh * 512:(oh + 1) * 512], ps_w[oh][:, :], AF.Identity,
                      rd=[("psw", oh)], wr=[("vbf",)])
        if "w" in ATTN_PHASES:
            for h in range(NH):
                p.mm(ps_kv[:, h * HD:(h + 1) * HD], qkbf[:, h * HD:(h + 1) * HD],
                     vbf[:, h * HD:(h + 1) * HD],
                     rd=[("qkbf",), ("vbf",)], wr=[("pskv",)])
            p.tt(kvacc[:, :], kvacc[:, :], ps_kv[:, :], ALU.add,
                 rd=[("kvacc",), ("pskv",)], wr=[("kvacc",)])
        if "s" in ATTN_PHASES:
            for ic in range(DT):
                p.mm(ps_small[:, ic:ic + 1], qkbf[:, ic * 128:(ic + 1) * 128],
                     c["onescol_bf"][:, :],
                     rd=[("qkbf",)], wr=[("psks",)])
            p.tt(ksacc[:, :], ksacc[:, :], ps_small[:, 0:DT], ALU.add,
                 rd=[("ksacc",), ("psks",)], wr=[("ksacc",)])
    kvt = vbf[0:64, :]
    p.act(kvt, kvacc[:, :], AF.Identity, rd=[("kvacc",)], wr=[("vbf",), ("kvt",)])
    p.memset(kvbf[:, :], 0.0, wr=[("kv0",)])
    p.cp(kvbf[0:64, :].rearrange("p (h m) -> p h m", m=2 * HD)[:, :, 0:HD],
         kvt.rearrange("p (h m) -> p h m", m=2 * HD)[:, :, 0:HD],
         rd=[("kvt",), ("kv0",)], wr=[("kv0",)])
    p.dma(kvbf[64:128, :].rearrange("p (h m) -> p h m", m=2 * HD)[:, :, HD:2 * HD],
          kvt.rearrange("p (h m) -> p h m", m=2 * HD)[:, :, HD:2 * HD],
          rd=[("kvt",), ("kv0",)], wr=[("kv0",)])
    p.act(ksf[:, :], ksacc[:, :], AF.Identity, bias=p.ccol(EPS_ATTN),
          rd=[("ksacc",)], wr=[("ksf",)])
    p.memset(ksz[:, :], 0.0, wr=[("ksbf",)])
    p.cp(ksz[0:64, :].rearrange("p (a b) -> p a b", b=2)[:, :, 0:1],
         ksf[0:64, :].rearrange("p (a o) -> p a o", o=1),
         rd=[("ksf",), ("ksbf",)], wr=[("ksbf",)])
    p.cp(ksz[64:128, :].rearrange("p (a b) -> p a b", b=2)[:, :, 1:2],
         ksf[64:128, :].rearrange("p (a o) -> p a o", o=1),
         rd=[("ksf",), ("ksbf",)], wr=[("ksbf",)])
    p.dma(wd_bf.rearrange("p (dc dd) -> p dc dd", dd=D),
          scratch["B_wd"].rearrange("(dc p) dd -> p dc dd", p=128),
          rd=[("B", "B_wd"), ("wqkv", 0)], wr=[("wqkv", 0), ("wd",)], q="sync")
    for tt in (range(TTI) if "B" in ATTN_PHASES and "A" in ATTN_PHASES else []):
        tp = tt % 2
        if "z" in ATTN_PHASES:
            for h in range(NH):
                ic = h // 2
                p.mm(ps_small[:, 16 + h:17 + h],
                     qT[:, ic * T + tt * 128: ic * T + tt * 128 + 128],
                     ksz[:, h:h + 1],
                     rd=[("qT", tt, ic), ("ksbf",)], wr=[("psks",)])
            p.recip(zrec[:, :], ps_z, rd=[("psks",)], wr=[("zrec",)])
        if "c" in ATTN_PHASES:
            for h in range(NH):
                ic = h // 2
                p.mm(ps_ctx[:, h * HD:(h + 1) * HD],
                     qT[:, ic * T + tt * 128: ic * T + tt * 128 + 128],
                     kvbf[:, h * HD:(h + 1) * HD],
                     rd=[("qT", tt, ic), ("kv0",)], wr=[("psctx",)])
            for h in (range(NH) if "u" in ATTN_PHASES else []):
                p.ts(czbf[:, h * HD:(h + 1) * HD], ps_ctx[:, h * HD:(h + 1) * HD],
                     zrec[:, h:h + 1], ALU.mult,
                     rd=[("psctx",), ("zrec",)], wr=[("czbf",)])
        if "T" in ATTN_PHASES:
            emit_tr8(p, ctxT_t[:, :].rearrange("p (k f) -> p k f", f=128),
                     czbf[:, :], ps_trB, c["ident_bf"], rd_src=[("czbf",)],
                     wr_dst=[("ctxT", kd) for kd in range(DT)], pkey=("pstr",))
        xt = X[:, tt * D:(tt + 1) * D]
        if "d" in ATTN_PHASES:
            emit_ln(p, xt, efp, stats[tp], scrB,
                    LN_EPS, rd_x=[("X", tt)], wr_out=[("efp", 0), ("efp", 1)], tag=("aln2", tp),
                    sckey=("scd",))
            for oh in range(2):
                for kd in range(DT):
                    p.mm(ps_w[1][:, :], ctxT_t[:, kd * 128:(kd + 1) * 128],
                         wd_bf[:, kd * D + oh * 512: kd * D + (oh + 1) * 512],
                         rd=[("ctxT", kd), ("wd",)], wr=[("psw", 1)],
                         start=(kd == 0), stop=False)
                p.mm(ps_w[1][:, :], c["ones_bf"][0:1, 0:128],
                     bias_bf[0:1, 3 * D + oh * 512: 3 * D + (oh + 1) * 512],
                     rd=[("abias", 3)], wr=[("psw", 1)], start=False, stop=True)
                p.tt(hs_fp[:, oh * 512:(oh + 1) * 512], ps_w[1][:, :],
                     efp[:, oh * 512:(oh + 1) * 512], ALU.add,
                     rd=[("psw", 1), ("efp", oh)], wr=[("hs",)])
            emit_ln(p, hs_fp[:, :], hsn, stats[tp], scrB,
                    EPS_LN_ATTN, rd_x=[("hs",)], wr_out=[("hsn",)], tag=("aln3", tp),
                    sckey=("scd",))
            p.tt(hsn, hsn, gb_bc[:, 0:D], ALU.mult,
                 rd=[("hsn",), ("gbbc",)], wr=[("hsn",)])
            p.tt(hsn, hsn, gb_bc[:, D:2 * D], ALU.add,
                 rd=[("hsn",), ("gbbc",)], wr=[("hsn",)])
            p.tt(xt, xt, hsn, ALU.add,
                 rd=[("X", tt), ("hsn",)], wr=[("X", tt)])



def emit_ffn1(p, st, c, X, dram, scratch):
    nc = p.nc
    TTI = T // 128
    JT = NB // 128
    HT1 = H1 // 128
    xnT = st.enter_context(nc.sbuf_tensor("ff_xnT", [128, DT * T], BF16))
    UT_bf = st.enter_context(nc.sbuf_tensor("ff_UT", [128, DT * NB], BF16))
    rWT_bf = st.enter_context(nc.sbuf_tensor("ff_rWT", [128, DT * NB], BF16))
    rV_bf = st.enter_context(nc.sbuf_tensor("ff_rV", [128, JT * D], BF16))
    rb_bf = st.enter_context(nc.sbuf_tensor("ff_rb", [1, NB], BF16))
    lngb = st.enter_context(nc.sbuf_tensor("ff_lngb", [128, 2 * D], F32))
    gamb = st.enter_context(nc.sbuf_tensor("ff_gamb", [128, D], F32))
    scrA = st.enter_context(nc.sbuf_tensor("ff_scrA", [128, 2048], F32))
    scrB = st.enter_context(nc.sbuf_tensor("ff_scrB", [128, 2048], F32))
    stats = [st.enter_context(nc.sbuf_tensor(f"ff_stats{i}", [128, 8], F32)) for i in range(2)]
    invUV = st.enter_context(nc.sbuf_tensor("ff_iuv", [128, 16], F32))
    ral = st.enter_context(nc.sbuf_tensor("ff_ral", [128, NB], F32))
    alpha = st.enter_context(nc.sbuf_tensor("ff_alpha", [128, NB], F32))
    aw = st.enter_context(nc.sbuf_tensor("ff_aw", [128, NB], F32))
    m8 = st.enter_context(nc.sbuf_tensor("ff_m8", [128, 8], F32))
    xr = st.enter_context(nc.sbuf_tensor("ff_xr", [128, D], F32))
    xrbf = st.enter_context(nc.sbuf_tensor("ff_xrbf", [128, D], BF16))
    xrT_t = st.enter_context(nc.sbuf_tensor("ff_xrT", [128, DT * 128], BF16))
    czbf = st.enter_context(nc.sbuf_tensor("ff_cz", [128, NB], BF16))
    cT_t = st.enter_context(nc.sbuf_tensor("ff_cT", [128, JT * 128], BF16))
    h1t = [st.enter_context(nc.sbuf_tensor(f"ff_h1{i}", [128, 256], BF16)) for i in range(2)]
    w1t = [st.enter_context(nc.sbuf_tensor(f"ff_w1t{i}", [128, DT * 128], BF16)) for i in range(2)]
    w2t = [st.enter_context(nc.sbuf_tensor(f"ff_w2t{i}", [128, D], BF16)) for i in range(2)]
    ps_trB = st.enter_context(nc.psum_tensor("ff_ptr", [128, 1024], BF16))
    ps_s = st.enter_context(nc.psum_tensor("ff_ps", [128, 2048], F32))
    ps_h0 = st.enter_context(nc.psum_tensor("ff_ph0", [128, 256], F32))
    ps_h1 = st.enter_context(nc.psum_tensor("ff_ph1", [128, 256], F32))
    ps_h = [ps_h0[:, :], ps_h1[:, :]]
    ps_wk = st.enter_context(nc.psum_tensor("ff_pwk", [128, 512], F32))

    p.dma(UT_bf[:, :].rearrange("p (dc nb) -> p dc nb", nb=NB),
          scratch["B_UT"].rearrange("(dc p) nb -> p dc nb", p=128),
          rd=[("B", "B_UT")], wr=[("UT",)], q="sync")
    p.dma(rWT_bf[:, :].rearrange("p (dc nb) -> p dc nb", nb=NB),
          scratch["B_rWT"].rearrange("(dc p) nb -> p dc nb", p=128),
          rd=[("B", "B_rWT")], wr=[("rWT",)], q="sync")
    p.dma(rV_bf[:, :].rearrange("p (jt dd) -> p jt dd", dd=D),
          scratch["B_rV"].rearrange("(jt p) dd -> p jt dd", p=128),
          rd=[("B", "B_rV")], wr=[("rV",)], q="sync")
    p.dma(scrB[0:1, 0:NB], dram["f_rb"], wr=[("sB",)])
    p.cp(rb_bf[:, :], scrB[0:1, 0:NB], rd=[("sB",)], wr=[("rb",)])
    for j, gname in enumerate(("f_lng", "f_lnb", "f_gam")):
        p.dma(scrB[0:1, 0:D], dram[gname], wr=[("sB",)])
        p.cp(xrbf[0:1, 0:D], scrB[0:1, 0:D], rd=[("sB",)], wr=[("xrbf",)])
        for oh in range(2):
            p.mm(ps_wk[:, :], c["ones_bf"][0:1, 0:128], xrbf[0:1, oh * 512:(oh + 1) * 512],
                 rd=[("xrbf",)], wr=[("pswk",)])
            dst = lngb[:, j * D + oh * 512: j * D + (oh + 1) * 512] if j < 2 else \
                  gamb[:, oh * 512:(oh + 1) * 512]
            p.act(dst, ps_wk[:, :], AF.Identity, rd=[("pswk",)], wr=[("lngb",)])
    for jt in range(JT):
        p.dma(scrB[:, 0:D], dram["f_U"][jt * 128:(jt + 1) * 128, :], wr=[("sB",)])
        p.act(scrB[:, 1024:2048], scrB[:, 0:D], AF.Square, accum_out=invUV[:, jt:jt + 1],
              rd=[("sB",)], wr=[("sB2",), ("iuv",)])
        p.dma(scrB[:, 0:D], dram["f_V"][jt * 128:(jt + 1) * 128, :], wr=[("sB",)])
        p.act(scrB[:, 1024:2048], scrB[:, 0:D], AF.Square, accum_out=invUV[:, 4 + jt:5 + jt],
              rd=[("sB",)], wr=[("sB2",), ("iuv",)])
    p.act(invUV[:, 8:16], invUV[:, 0:8], AF.Ln, rd=[("iuv",)], wr=[("iuv",)])
    p.act(invUV[:, 0:8], invUV[:, 8:16], AF.Exp, scale=p.ccol(0.5), rd=[("iuv",)], wr=[("iuv",)])
    p.ts(invUV[:, 0:8], invUV[:, 0:8], FFN_EPS, ALU.max, rd=[("iuv",)], wr=[("iuv",)])
    p.recip(invUV[:, 0:8], invUV[:, 0:8], rd=[("iuv",)], wr=[("iuv",)])
    p.tt(invUV[:, 8:12], invUV[:, 0:4], invUV[:, 4:8], ALU.mult, rd=[("iuv",)], wr=[("iuv",)])
    for tt in range(TTI):
        tp = tt % 2
        xt = X[:, tt * D:(tt + 1) * D]
        emit_ln(p, xt, xrbf[:, :], stats[tp], scrA[:, 0:1024],
                LN_EPS, rd_x=[("X", tt)], wr_out=[("xrbf",)], tag=("fln", tp))
        dst3d = xnT[:, :].rearrange("p (k t) -> p k t", t=T)[:, :, tt * 128:tt * 128 + 128]
        emit_tr8(p, dst3d, xrbf[:, :], ps_trB, c["ident_bf"],
                 rd_src=[("xrbf",)],
                 wr_dst=[("xnT", tt, kd) for kd in range(DT)], pkey=("pstr",))
    for tt in range(TTI):
        tp = tt % 2
        xt = X[:, tt * D:(tt + 1) * D]
        emit_ln(p, xt, scrA[:, 0:1024], stats[tp], scrA[:, 1024:2048],
                LN_EPS, rd_x=[("X", tt)], wr_out=[("xn",)], tag=("fln2", tp))
        emit_ln(p, scrA[:, 0:1024], xr[:, :], stats[tp], scrA[:, 1024:2048],
                LN_EPS, rd_x=[("xn",)], wr_out=[("xr",)], tag=("fln3", tp))
        p.tt(xr[:, :], xr[:, :], lngb[:, 0:D], ALU.mult, rd=[("xr",), ("lngb",)], wr=[("xr",)])
        p.tt(xr[:, :], xr[:, :], lngb[:, D:2 * D], ALU.add, rd=[("xr",), ("lngb",)], wr=[("xr",)])
        p.cp(xrbf[:, :], xr[:, :], rd=[("xr",)], wr=[("xrbf",)])
        emit_tr8(p, xrT_t[:, :].rearrange("p (k f) -> p k f", f=128),
                 xrbf[:, :], ps_trB, c["ident_bf"], rd_src=[("xrbf",)],
                 wr_dst=[("xrT", kd) for kd in range(DT)], pkey=("pstr",))
        for kd in range(DT):
            p.mm(ps_wk[:, :], xrT_t[:, kd * 128:(kd + 1) * 128],
                 rWT_bf[:, kd * NB:(kd + 1) * NB],
                 rd=[("xrT", kd), ("rWT",)], wr=[("pswk",)],
                 start=(kd == 0), stop=False)
        p.mm(ps_wk[:, :], c["ones_bf"][0:1, 0:128], rb_bf[0:1, :],
             rd=[("rb",)], wr=[("pswk",)], start=False, stop=True)
        p.ts(ral[:, :], ps_wk[:, :], TAU, ALU.min, -TAU, ALU.max,
             rd=[("pswk",)], wr=[("ral",)])
        p.act(alpha[:, :], ral[:, :], AF.Abs, rd=[("ral",)], wr=[("alpha",)])
        p.act(aw[:, :], alpha[:, :], AF.Exp, scale=-1.0, rd=[("alpha",)], wr=[("aw",)])
        p.act(alpha[:, :], aw[:, :], AF.Ln, bias=1.0, rd=[("aw",)], wr=[("alpha",)])
        p.stt(alpha[:, :], ral[:, :], 0.0, alpha[:, :], ALU.max, ALU.add,
              rd=[("ral",), ("alpha",)], wr=[("alpha",)])
        p.max8(m8[:, :], alpha[:, :], rd=[("alpha",)], wr=[("m8",)])
        p.ts(aw[:, :], alpha[:, :], m8[:, 7:8], ALU.is_ge, rd=[("alpha",), ("m8",)], wr=[("aw",)])
        p.tt(aw[:, :], aw[:, :], alpha[:, :], ALU.mult, rd=[("aw",), ("alpha",)], wr=[("aw",)])
        p.red(stats[tp][:, 6:7], aw[:, :], ALU.add, rd=[("aw",)], wr=[("fst", tp)])
        p.act(stats[tp][:, 7:8], stats[tp][:, 6:7], AF.Tanh, rd=[("fst", tp)], wr=[("fst", tp)])
        p.ts(stats[tp][:, 6:7], stats[tp][:, 6:7], FFN_EPS, ALU.add, rd=[("fst", tp)], wr=[("fst", tp)])
        p.recip(stats[tp][:, 6:7], stats[tp][:, 6:7], rd=[("fst", tp)], wr=[("fst", tp)])
        p.tt(stats[tp][:, 6:7], stats[tp][:, 6:7], stats[tp][:, 7:8], ALU.mult,
             rd=[("fst", tp)], wr=[("fst", tp)])
        p.ts(aw[:, :], aw[:, :], stats[tp][:, 6:7], ALU.mult, rd=[("aw",), ("fst", tp)], wr=[("aw",)])
        for kd in range(DT):
            p.mm(ps_wk[:, :], xnT[:, kd * T + tt * 128: kd * T + tt * 128 + 128],
                 UT_bf[:, kd * NB:(kd + 1) * NB],
                 rd=[("xnT", tt, kd), ("UT",)], wr=[("pswk",)],
                 start=(kd == 0), stop=(kd == DT - 1))
        p.tt(ral[:, :], ps_wk[:, :], aw[:, :], ALU.mult, rd=[("pswk",), ("aw",)], wr=[("ral",)])
        p.cp(czbf[:, :], ral[:, :], rd=[("ral",)], wr=[("czbf",)])
        for jt in range(JT):
            p.trans(ps_trB[:, jt * 128:(jt + 1) * 128], czbf[:, jt * 128:(jt + 1) * 128],
                    c["ident_bf"][:, :], rd=[("czbf",)], wr=[("pstr",)])
        for jt in range(JT):
            p.ts(cT_t[:, jt * 128:(jt + 1) * 128], ps_trB[:, jt * 128:(jt + 1) * 128],
                 invUV[:, 8 + jt:9 + jt], ALU.mult,
                 rd=[("pstr",), ("iuv",)], wr=[("cT", jt)])
        for oh in range(2):
            for jt in range(JT):
                p.mm(ps_wk[:, :], cT_t[:, jt * 128:(jt + 1) * 128],
                     rV_bf[:, jt * D + oh * 512: jt * D + (oh + 1) * 512],
                     rd=[("cT", jt), ("rV",)], wr=[("pswk",)],
                     start=(jt == 0), stop=(jt == JT - 1))
            p.tt(scrA[:, 1024:1536], ps_wk[:, :], gamb[:, oh * 512:(oh + 1) * 512], ALU.mult,
                 rd=[("pswk",), ("lngb",)], wr=[("dyn",)])
            xs = X[:, tt * D + oh * 512: tt * D + (oh + 1) * 512]
            p.tt(xs, xs, scrA[:, 1024:1536], ALU.add,
                 rd=[("X", tt), ("dyn",)], wr=[("X", tt)])
    for tc in range(T // 256):
        for ht in range(HT1):
            hp = ht % 2
            p.dma(w1t[hp][:, :].rearrange("p (dc h) -> p dc h", h=128),
                  scratch["B_w1T"][:, ht * 128:(ht + 1) * 128]
                  .rearrange("(dc p) h -> p dc h", p=128),
                  rd=[("B", "B_w1T")], wr=[("w1t", hp)], q="sync" if hp == 0 else "gpsimd")
            for dc in range(DT):
                p.mm(ps_h[hp], w1t[hp][:, dc * 128:(dc + 1) * 128],
                     xnT[:, dc * T + tc * 256: dc * T + (tc + 1) * 256],
                     rd=[("w1t", hp), ("xnT", tc * 2, dc), ("xnT", tc * 2 + 1, dc)],
                     wr=[("psh", hp)],
                     start=(dc == 0), stop=(dc == DT - 1))
            p.act(h1t[hp][:, :], ps_h[hp], AF.Gelu, rd=[("psh", hp)], wr=[("h1t", hp)])
            p.dma(w2t[hp][:, :], scratch["B_w2T"][ht * 128:(ht + 1) * 128, :],
                  rd=[("B", "B_w2T")], wr=[("w2t", hp)], q="scalar")
            for tsub in range(2):
                for oh in range(2):
                    p.mm(ps_s[:, (tsub * 2 + oh) * 512:(tsub * 2 + oh + 1) * 512],
                         h1t[hp][:, tsub * 128:(tsub + 1) * 128],
                         w2t[hp][:, oh * 512:(oh + 1) * 512],
                         rd=[("h1t", hp), ("w2t", hp)], wr=[("pss",)],
                         start=(ht == 0), stop=(ht == HT1 - 1))
        for tsub in range(2):
            tt = tc * 2 + tsub
            for oh in range(2):
                xs = X[:, tt * D + oh * 512: tt * D + (oh + 1) * 512]
                p.tt(xs, xs, ps_s[:, (tsub * 2 + oh) * 512:(tsub * 2 + oh + 1) * 512],
                     ALU.add, rd=[("X", tt), ("pss",)], wr=[("X", tt)])



def _mamba_host(inp):
    f = np.ascontiguousarray
    cw = inp["L0_attn_conv_w"].reshape(E, 4)
    d = {
        "m_A": f(inp["L0_attn_A_log"].reshape(ET, 128, NST).transpose(1, 0, 2).reshape(128, ET * NST)),
        "m_cw": f(cw.reshape(ET, 128, 4).transpose(1, 0, 2).reshape(128, ET * 4)),
        "m_cb": f(inp["L0_attn_conv_b"].reshape(ET, 128).T),
        "m_bdt": f(inp["L0_attn_b_dt"].reshape(ET, 128).T),
        "m_dsk": f(inp["L0_attn_D_skip"].reshape(ET, 128).T),
        "m_wdt": f(inp["L0_attn_W_dt"]),
        "m_win": f(inp["L0_attn_W_in"]),
        "m_wout": f(inp["L0_attn_W_out"]),
    }
    wxp = np.zeros((E, 128), np.float32)
    wx = inp["L0_attn_W_xproj"]
    wxp[:, 0:16] = wx[:, DTR:DTR + NST]
    wxp[:, 32:48] = wx[:, DTR + NST:DTR + 2 * NST]
    wxp[:, 64:128] = wx[:, 0:DTR]
    d["m_wxp"] = f(wxp)
    return d


def _moe_host(inp):
    f = np.ascontiguousarray
    HT = HMOE // 128
    return {
        "g_wT": f(inp["L0_ffn_gate_w"].T),
        "g_b": f(inp["L0_ffn_gate_b"].reshape(1, NEXP)),
        "g_w1": f(inp["L0_ffn_w1"]),
        "g_b1": f(inp["L0_ffn_b1"].reshape(NEXP, HT, 128).transpose(2, 0, 1).reshape(128, NEXP * HT)),
        "g_w2": f(inp["L0_ffn_w2"]),
        "g_b2": f(inp["L0_ffn_b2"]),
    }


def _attn_host(inp):
    f = np.ascontiguousarray
    return {
        "a_wq": f(inp["L1_attn_Wq"].T),
        "a_wk": f(inp["L1_attn_Wk"].T),
        "a_wv": f(inp["L1_attn_Wv"].T),
        "a_wd": f(inp["L1_attn_Wd"].T),
        "a_bq": f(inp["L1_attn_bq"].reshape(1, D)),
        "a_bk": f(inp["L1_attn_bk"].reshape(1, D)),
        "a_bv": f(inp["L1_attn_bv"].reshape(1, D)),
        "a_bd": f(inp["L1_attn_bd"].reshape(1, D)),
        "a_gamma": f(inp["L1_attn_gamma"].reshape(1, D)),
        "a_beta": f(inp["L1_attn_beta"].reshape(1, D)),
    }


def _ffn1_host(inp):
    f = np.ascontiguousarray
    return {
        "f_UT": f(inp["L1_ffn_raw_U"].T),
        "f_rWT": f(inp["L1_ffn_router_W"].T),
        "f_rV": f(inp["L1_ffn_raw_V"]),
        "f_rb": f(inp["L1_ffn_router_b"].reshape(1, NB)),
        "f_lng": f(inp["L1_ffn_ln_g"].reshape(1, D)),
        "f_lnb": f(inp["L1_ffn_ln_b"].reshape(1, D)),
        "f_gam": f(inp["L1_ffn_gamma"].reshape(1, D)),
        "f_U": f(inp["L1_ffn_raw_U"]),
        "f_V": f(inp["L1_ffn_raw_V"]),
        "f_w1T": f(inp["L1_ffn_W1"].T),
        "f_w2T": f(inp["L1_ffn_W2"].T),
    }


_DRAM_SHAPES = {
    "m_A": (128, ET * NST), "m_cw": (128, ET * 4), "m_cb": (128, ET),
    "m_bdt": (128, ET), "m_dsk": (128, ET), "m_wdt": (DTR, E),
    "m_win": (D, 2 * E), "m_wout": (E, D), "m_wxp": (E, 128),
    "g_wT": (D, NEXP), "g_b": (1, NEXP), "g_w1": (NEXP, D, HMOE),
    "g_b1": (128, NEXP * (HMOE // 128)), "g_w2": (NEXP, HMOE, D), "g_b2": (NEXP, D),
    "a_wq": (D, D), "a_wk": (D, D), "a_wv": (D, D), "a_wd": (D, D),
    "a_bq": (1, D), "a_bk": (1, D), "a_bv": (1, D), "a_bd": (1, D),
    "a_gamma": (1, D), "a_beta": (1, D),
    "f_UT": (D, NB), "f_rWT": (D, NB), "f_rV": (NB, D), "f_rb": (1, NB),
    "f_lng": (1, D), "f_lnb": (1, D), "f_gam": (1, D),
    "f_U": (NB, D), "f_V": (NB, D), "f_w1T": (D, H1), "f_w2T": (H1, D),
}

_SCRATCH_SHAPES = {
    "B_win": (D, 2 * E), "B_wxp": (E, 128), "B_wdt": (DTR, E), "B_wout": (E, D),
    "B_gwT": (D, NEXP),
    "B_wq": (D, D), "B_wk": (D, D), "B_wv": (D, D), "B_wd": (D, D),
    "B_UT": (D, NB), "B_rWT": (D, NB), "B_rV": (NB, D),
    "B_w1T": (D, H1), "B_w2T": (H1, D),
}


def _build(emits, TC=256):
    nc = bass.Bass("TRN2", target_bir_lowering=False, debug=False)
    x_in = nc.dram_tensor("x_res_in", [T, D], F32, kind="ExternalInput").ap()
    x_out = nc.dram_tensor("x_res_out", [T, D], F32, kind="ExternalOutput").ap()
    dram = {k: nc.dram_tensor(k, list(v), F32, kind="ExternalInput").ap()
            for k, v in _DRAM_SHAPES.items()}
    scratch = {k: nc.dram_tensor(k, list(v), BF16, kind="Internal").ap()
               for k, v in _SCRATCH_SHAPES.items()}
    with ExitStack() as st:
        X = st.enter_context(nc.sbuf_tensor("X", [128, (T // 128) * D], F32))
        sems = {}
        for e in ENGS:
            sems[("c", e)] = st.enter_context(nc.semaphore(f"semc_{e}"))
            sems[("d", e)] = st.enter_context(nc.semaphore(f"semd_{e}"))
        blk = st.enter_context(nc.Block())
        p = P(nc)
        c = make_consts(p, st)
        p.barrier()
        pc_f = [st.enter_context(nc.sbuf_tensor(f"pc_f{i}", [128, 1024], F32)) for i in range(2)]
        pc_b = [st.enter_context(nc.sbuf_tensor(f"pc_b{i}", [128, 1024], BF16)) for i in range(2)]
        p.dma(X[:, :].rearrange("p (tt d) -> p tt d", d=D),
              x_in.rearrange("(tt p) d -> p tt d", p=128),
              wr=[("X", tt) for tt in range(T // 128)], q="gpsimd")
        emit_precast(p, st, dram, scratch, pc_f, pc_b)
        for em in emits:
            with ExitStack() as sti:
                if em is emit_mamba:
                    em(p, sti, c, X, dram, scratch, TC=TC)
                else:
                    em(p, sti, c, X, dram, scratch)
            p.barrier()
        p.dma(x_out.rearrange("(tt p) d -> p tt d", p=128),
              X[:, :].rearrange("p (tt d) -> p tt d", d=D),
              rd=[("X", tt) for tt in range(T // 128)], wr=[("OUT",)], q="gpsimd")
        p.finalize(sems)
    return nc


def build_mamba_program(TC=256):
    return _build([emit_mamba], TC=TC)


def build_moe_program():
    return _build([emit_moe])


def build_attn_program():
    return _build([emit_attn])


def build_ffn1_program():
    return _build([emit_ffn1])


def build_fused_program(TC=256):
    return _build([emit_mamba, emit_moe, emit_attn, emit_ffn1], TC=TC)


_PROGS = {}


def _all_host(inp):
    h = {}
    h.update(_mamba_host(inp))
    h.update(_moe_host(inp))
    h.update(_attn_host(inp))
    h.update(_ffn1_host(inp))
    return h


def kernel(**inputs):
    inp = {k: np.asarray(v) for k, v in inputs.items()}
    x = np.ascontiguousarray(inp["L0_attn_x"], dtype=np.float32)
    if "fused" not in _PROGS:
        _PROGS["fused"] = build_fused_program(TC=256)
    host = _all_host(inp)
    in_maps = [dict(host, x_res_in=np.ascontiguousarray(x[ci])) for ci in range(8)]
    res = run_bass_kernel_spmd(_PROGS["fused"], in_maps, core_ids=list(range(8)))
    return np.stack([np.asarray(r["x_res_out"]) for r in res.results]).astype(np.float32)


# version 07
# speedup vs baseline: 1.3080x; 1.1280x over previous
import numpy as np
import concourse.bass as bass
import concourse.mybir as mybir
from concourse.bass_utils import run_bass_kernel_spmd
from contextlib import ExitStack

F32 = mybir.dt.float32
BF16 = mybir.dt.bfloat16
AF = mybir.ActivationFunctionType
ALU = mybir.AluOpType
AX = mybir.AxisListType

T = 2048
D = 1024
E = 2048
NST = 16
DTR = 64
NEXP = 8
HMOE = 2048
NH = 16
HD = 64
NB = 512
H1 = 4096
DT = D // 128
ET = E // 128

LN_EPS = 1e-5
EPS_ATTN = 1e-6
EPS_LN_ATTN = 1e-12
FFN_EPS = 1e-6
TAU = 10.0

ENGS = ("tensor", "vector", "scalar", "gpsimd", "sync")


class _Node:
    __slots__ = ("eng", "fn", "inc", "deps", "need", "val", "dma", "sid")

    def __init__(self, eng, fn, inc, dma):
        self.eng = eng
        self.fn = fn
        self.inc = inc
        self.deps = []
        self.need = False
        self.val = 0
        self.dma = dma
        self.sid = ("d", eng) if dma else ("c", eng)


class P:

    def __init__(self, nc):
        self.nc = nc
        self.nodes = []
        self.res = {}
        self.cvals = None
        self.cmap = {}
        self.frozen_consts = False
        self.last_dma = {}

    def emit(self, eng_name, fn, rd=(), wr=(), inc=1, dma=False):
        node = _Node(eng_name, fn, inc, dma)
        dep = {}

        def add_dep(pn):
            if pn is node:
                return
            if pn.eng == eng_name == "tensor" and not (pn.dma or dma):
                return
            dep[id(pn)] = pn
        for k in rd:
            e = self.res.get(k)
            if e and e[0] is not None:
                add_dep(e[0])
        for k in wr:
            e = self.res.get(k)
            if e:
                if e[0] is not None:
                    add_dep(e[0])
                for rn in e[1].values():
                    add_dep(rn)
        if dma:
            prev = self.last_dma.get(eng_name)
            if prev is not None:
                add_dep(prev)
            self.last_dma[eng_name] = node
            node.need = True
        node.deps = list(dep.values())
        for pn in node.deps:
            pn.need = True
        for k in rd:
            e = self.res.setdefault(k, [None, {}])
            e[1][eng_name] = node
        for k in wr:
            self.res[k] = [node, {}]
        self.nodes.append(node)
        return node

    def barrier(self):
        self.nodes.append("BARRIER")

    def finalize(self, sems):
        SIDS = [("c", e) for e in ENGS] + [("d", e) for e in ENGS]
        counts = {s: 0 for s in SIDS}
        last = {s: None for s in SIDS}
        for nd in self.nodes:
            if nd == "BARRIER":
                for s in SIDS:
                    if last[s] is not None:
                        last[s].need = True
                continue
            last[nd.sid] = nd
        for s in SIDS:
            if last[s] is not None:
                last[s].need = True
        for nd in self.nodes:
            if nd == "BARRIER":
                continue
            if nd.need:
                counts[nd.sid] += nd.inc
                nd.val = counts[nd.sid]
        lastv = {s: 0 for s in SIDS}
        waited = {e: {s: 0 for s in SIDS} for e in ENGS}
        for nd in self.nodes:
            if nd == "BARRIER":
                snap = dict(lastv)
                for e in ENGS:
                    eng = getattr(self.nc, e)
                    for s in SIDS:
                        if s != ("c", e) and snap[s] > waited[e][s]:
                            eng.wait_ge(sems[s], snap[s])
                            waited[e][s] = snap[s]
                continue
            eng = getattr(self.nc, nd.eng)
            need_wait = {}
            for pn in nd.deps:
                if pn.val > waited[nd.eng].get(pn.sid, 0):
                    need_wait[pn.sid] = max(need_wait.get(pn.sid, 0), pn.val)
            for s, v in need_wait.items():
                eng.wait_ge(sems[s], v)
                waited[nd.eng][s] = v
            inst = nd.fn(eng)
            if nd.need:
                inst.then_inc(sems[nd.sid], nd.inc)
                lastv[nd.sid] = nd.val
        for s in SIDS:
            if s != ("c", "sync") and lastv[s] > waited["sync"][s]:
                self.nc.sync.wait_ge(sems[s], lastv[s])

    def dma(self, out, in_, rd=(), wr=(), q="sync"):
        self.emit(q, lambda e: e.dma_start(out=out, in_=in_), rd=rd, wr=wr,
                  inc=16, dma=True)

    def mm(self, out, lhsT, rhs, rd=(), wr=(), start=True, stop=True):
        self.emit("tensor", lambda e: e.matmul(out, lhsT, rhs, start=start, stop=stop),
                  rd=rd, wr=wr)

    def trans(self, out, in_, ident, rd=(), wr=()):
        self.emit("tensor", lambda e: e.transpose(out, in_, ident), rd=rd, wr=wr)

    def act(self, out, in_, func, rd=(), wr=(), bias=0.0, scale=1.0, accum_out=None):
        self.emit("scalar", lambda e: e.activation(out, in_, func, bias=bias,
                                                   scale=scale, accum_out=accum_out),
                  rd=rd, wr=wr)

    def cp(self, out, in_, rd=(), wr=(), eng="vector"):
        if eng == "scalar":
            self.emit(eng, lambda e: e.copy(out, in_), rd=rd, wr=wr)
        else:
            self.emit(eng, lambda e: e.tensor_copy(out, in_), rd=rd, wr=wr)

    def tt(self, out, a, b, op, rd=(), wr=(), eng="vector"):
        self.emit(eng, lambda e: e.tensor_tensor(out, a, b, op), rd=rd, wr=wr)

    def ts(self, out, in0, s1, op0, s2=None, op1=ALU.bypass, rd=(), wr=(), eng="vector"):
        if s2 is None:
            self.emit(eng, lambda e: e.tensor_single_scalar(out, in0, s1, op0),
                      rd=rd, wr=wr)
        else:
            self.emit(eng, lambda e: e.tensor_scalar(out, in0, s1, s2, op0=op0, op1=op1),
                      rd=rd, wr=wr)

    def stt(self, out, in0, scalar, in1, op0, op1, rd=(), wr=(), eng="vector"):
        self.emit(eng, lambda e: e.scalar_tensor_tensor(out, in0, scalar, in1,
                                                        op0=op0, op1=op1), rd=rd, wr=wr)

    def scan(self, out, d0, d1, init, rd=(), wr=(), eng="vector"):
        self.emit(eng, lambda e: e.tensor_tensor_scan(out, d0, d1, init,
                                                      op0=ALU.mult, op1=ALU.add),
                  rd=rd, wr=wr)

    def red(self, out, in_, op, rd=(), wr=(), eng="vector"):
        self.emit(eng, lambda e: e.tensor_reduce(out, in_, axis=AX.X, op=op),
                  rd=rd, wr=wr)

    def max8(self, out, in_, rd=(), wr=()):
        self.emit("vector", lambda e: e.max(out, in_), rd=rd, wr=wr)

    def recip(self, out, in_, rd=(), wr=()):
        self.emit("vector", lambda e: e.reciprocal(out, in_), rd=rd, wr=wr)

    def memset(self, ap, v, rd=(), wr=(), eng="vector"):
        self.emit(eng, lambda e: e.memset(ap, v), rd=rd, wr=wr)

    def iota(self, ap, pattern, rd=(), wr=(), base=0, mult=0):
        self.emit("gpsimd", lambda e: e.iota(ap, pattern=pattern, base=base,
                                             channel_multiplier=mult,
                                             allow_small_or_imprecise_dtypes=True),
                  rd=rd, wr=wr)

    def ccol(self, v):
        v = float(v)
        if v not in self.cmap:
            assert not self.frozen_consts, f"const {v} after freeze"
            i = len(self.cmap)
            self.cmap[v] = i
            self.memset(self.cvals[:, i:i + 1], v)
        return self.cvals[:, self.cmap[v]:self.cmap[v] + 1]


def make_consts(p, st):
    nc = p.nc
    c = {}
    c["ones_bf"] = st.enter_context(nc.sbuf_tensor("ones_bf", [1, 128], BF16))
    c["ident_f"] = st.enter_context(nc.sbuf_tensor("ident_f", [128, 128], F32))
    c["ident_bf"] = st.enter_context(nc.sbuf_tensor("ident_bf", [128, 128], BF16))
    c["onescol_bf"] = st.enter_context(nc.sbuf_tensor("onescol_bf", [128, 1], BF16))
    p.cvals = st.enter_context(nc.sbuf_tensor("cvals", [128, 16], F32))
    p.memset(c["ones_bf"][:, :], 1.0)
    p.memset(c["onescol_bf"][:, :], 1.0)
    p.iota(c["ident_f"][:, :], pattern=[[1, 128]], base=0, mult=-1, wr=[("iota",)])
    p.ts(c["ident_f"][:, :], c["ident_f"][:, :], 0.0, ALU.is_equal, rd=[("iota",)], wr=[("iota",)])
    p.cp(c["ident_bf"][:, :], c["ident_f"][:, :], rd=[("iota",)])
    for v in (LN_EPS, EPS_LN_ATTN, EPS_ATTN, -0.5, 0.5):
        p.ccol(v)
    p.frozen_consts = True
    return c



PRECAST = [
    ("m_win", D, 2 * E, "B_win"),
    ("m_wxp", E, 128, "B_wxp"),
    ("m_wdt", DTR, E, "B_wdt"),
    ("m_wout", E, D, "B_wout"),
    ("g_wT", D, NEXP, "B_gwT"),
    ("a_wq", D, D, "B_wq"),
    ("a_wk", D, D, "B_wk"),
    ("a_wv", D, D, "B_wv"),
    ("a_wd", D, D, "B_wd"),
    ("f_UT", D, NB, "B_UT"),
    ("f_rWT", D, NB, "B_rWT"),
    ("f_rV", NB, D, "B_rV"),
    ("f_w1T", D, H1, "B_w1T"),
    ("f_w2T", H1, D, "B_w2T"),
]

_PC_ENGS = ("gpsimd",)


def emit_precast(p, st, dram, scratch, sf, sb):
    i = 0
    for key, R, C, bname in PRECAST:
        src = dram[key]
        if len(src.shape) == 3:
            src = src.rearrange("a b c -> (a b) c")
        dst = scratch[bname]
        if len(dst.shape) == 3:
            dst = dst.rearrange("a b c -> (a b) c")
        for r0 in range(0, R, 128):
            rr = min(128, R - r0)
            for c0 in range(0, C, 1024):
                cc = min(1024, C - c0)
                g = i % 2
                p.dma(sf[g][0:rr, 0:cc], src[r0:r0 + rr, c0:c0 + cc],
                      wr=[("pcf", g)], q="sync" if g == 0 else "gpsimd")
                p.cp(sb[g][0:rr, 0:cc], sf[g][0:rr, 0:cc],
                     rd=[("pcf", g)], wr=[("pcb", g)], eng="gpsimd")
                p.dma(dst[r0:r0 + rr, c0:c0 + cc], sb[g][0:rr, 0:cc],
                      rd=[("pcb", g)], wr=[("B", bname)], q="scalar")
                i += 1


def emit_ln(p, x_ap, out_ap, stats, scr, eps, rd_x, wr_out, tag, sckey=None):
    Df = x_ap.shape[-1]
    ks = ("st",) + tag
    kc = sckey if sckey is not None else ("sc",) + tag
    mu = stats[:, 0:1]
    sq = stats[:, 1:2]
    var = stats[:, 2:3]
    rstd = stats[:, 3:4]
    p.red(mu, x_ap, ALU.add, rd=list(rd_x), wr=[ks])
    p.ts(mu, mu, 1.0 / Df, ALU.mult, rd=[ks], wr=[ks])
    p.act(scr, x_ap, AF.Square, accum_out=sq, rd=list(rd_x), wr=[kc, ks])
    p.ts(var, sq, 1.0 / Df, ALU.mult, rd=[ks], wr=[ks])
    p.tt(stats[:, 4:5], mu, mu, ALU.mult, rd=[ks], wr=[ks])
    p.tt(var, var, stats[:, 4:5], ALU.subtract, rd=[ks], wr=[ks])
    p.act(stats[:, 5:6], var, AF.Ln, bias=p.ccol(eps), rd=[ks], wr=[ks])
    p.act(rstd, stats[:, 5:6], AF.Exp, scale=p.ccol(-0.5), rd=[ks], wr=[ks])
    p.ts(out_ap, x_ap, mu, ALU.subtract, rstd, ALU.mult,
         rd=list(rd_x) + [ks], wr=list(wr_out))


def emit_tr(p, dst_ap, src_ap, ps_tr, ident_bf, rd_src, wr_dst, pkey):
    p.trans(ps_tr, src_ap, ident_bf[:, :], rd=list(rd_src), wr=[pkey])
    p.cp(dst_ap, ps_tr, rd=[pkey], wr=list(wr_dst))


def emit_tr8(p, dst3d, src_ap, ps_trB, ident_bf, rd_src, wr_dst, pkey, nk=8):
    for kd in range(nk):
        p.trans(ps_trB[:, kd * 128:(kd + 1) * 128], src_ap[:, kd * 128:(kd + 1) * 128],
                ident_bf[:, :], rd=list(rd_src), wr=[pkey])
    p.cp(dst3d, ps_trB[:, 0:nk * 128].rearrange("p (k f) -> p k f", f=128),
         rd=[pkey], wr=list(wr_dst), eng="scalar")



def emit_mamba(p, st, c, X, dram, scratch, TC=256):
    nc = p.nc
    NCH = T // TC
    TSUB = TC // 128

    _xnT = st.enter_context(nc.sbuf_tensor("mb_xnT", [128, DT * TC], BF16))
    xnT = [_xnT, _xnT]
    xi = st.enter_context(nc.sbuf_tensor("mb_xi", [128, ET * (TC + 3)], F32))
    zs = st.enter_context(nc.sbuf_tensor("mb_zs", [128, ET * TC], BF16))
    delta = st.enter_context(nc.sbuf_tensor("mb_delta", [128, ET * TC], BF16))
    wdu = st.enter_context(nc.sbuf_tensor("mb_w", [128, ET * TC], BF16))
    yac = st.enter_context(nc.sbuf_tensor("mb_y", [128, ET * TC], F32))
    hall = st.enter_context(nc.sbuf_tensor("mb_hall", [128, ET * TC], F32))
    dBuall = st.enter_context(nc.sbuf_tensor("mb_dbu", [128, ET * TC], BF16))
    ygbf = hall[:, 0:ET * TC // 2].bitcast(BF16)
    hstate = st.enter_context(nc.sbuf_tensor("mb_h", [128, ET * NST], F32))
    stash = st.enter_context(nc.sbuf_tensor("mb_stash", [128, ET * 3], F32))
    stats = [st.enter_context(nc.sbuf_tensor(f"mb_stats{i}", [128, 8], F32)) for i in range(2)]
    scrA = st.enter_context(nc.sbuf_tensor("mb_scrA", [128, 1024], F32))
    xnbf_t = [st.enter_context(nc.sbuf_tensor(f"mb_xnbf{i}", [128, 1024], BF16)) for i in range(2)]
    cbuf0 = st.enter_context(nc.sbuf_tensor("mb_cb0", [128, TC], F32))
    cbuf1 = st.enter_context(nc.sbuf_tensor("mb_cb1", [128, TC], F32))
    ubf = [st.enter_context(nc.sbuf_tensor(f"mb_ubf{i}", [128, TC], BF16)) for i in range(2)]
    dtlr_bf = st.enter_context(nc.sbuf_tensor("mb_dtlr", [64, TC], BF16))
    Bt_bf = st.enter_context(nc.sbuf_tensor("mb_Bt", [16, TC], BF16))
    Ct_bf = st.enter_context(nc.sbuf_tensor("mb_Ct", [16, TC], BF16))
    Bb = [st.enter_context(nc.sbuf_tensor(f"mb_Bb{i}", [128, TC], BF16)) for i in range(2)]
    Cb = [st.enter_context(nc.sbuf_tensor(f"mb_Cb{i}", [128, TC], BF16)) for i in range(2)]
    dAb = [st.enter_context(nc.sbuf_tensor(f"mb_dA{i}", [128, TC], F32)) for i in range(2)]
    sp0 = st.enter_context(nc.sbuf_tensor("mb_sp0", [128, TC], F32))
    Asb = st.enter_context(nc.sbuf_tensor("mb_A", [128, ET * NST], F32))
    cwsb = st.enter_context(nc.sbuf_tensor("mb_cw", [128, ET * 4], F32))
    cbsb = st.enter_context(nc.sbuf_tensor("mb_cbv", [128, ET], F32))
    bdtsb = st.enter_context(nc.sbuf_tensor("mb_bdt", [128, ET], F32))
    dsksb = st.enter_context(nc.sbuf_tensor("mb_dsk", [128, ET], F32))
    wxp = st.enter_context(nc.sbuf_tensor("mb_wxp", [128, ET * 128], BF16))
    wdt = st.enter_context(nc.sbuf_tensor("mb_wdt", [64, E], BF16))
    sel = st.enter_context(nc.sbuf_tensor("mb_sel", [16, NST * 128], BF16))
    win_bf = [st.enter_context(nc.sbuf_tensor(f"mb_winb{i}", [128, DT * 128], BF16)) for i in range(2)]
    wo_et = win_bf
    ps_trB = st.enter_context(nc.psum_tensor("mb_ptr", [128, 1024], BF16))
    ps_m0 = st.enter_context(nc.psum_tensor("mb_pm0", [128, 256], F32))
    ps_m1 = st.enter_context(nc.psum_tensor("mb_pm1", [128, 256], F32))
    ps_xd = st.enter_context(nc.psum_tensor("mb_pxd", [128, 512], F32))
    ps_o = st.enter_context(nc.psum_tensor("mb_po", [128, 2048], F32))
    assert TC == 256
    ps_mm = [ps_m0[:, :], ps_m1[:, :]]
    ps_d = ps_xd[:, TC:2 * TC]
    ps_B = ps_o[:, 0:TC]
    ps_C = ps_o[:, TC:2 * TC]

    p.dma(Asb[:, :], dram["m_A"], wr=[("mA",)])
    p.dma(cwsb[:, :], dram["m_cw"], wr=[("mcw",)])
    p.dma(cbsb[:, :], dram["m_cb"], wr=[("mcb",)])
    p.dma(bdtsb[:, :], dram["m_bdt"], wr=[("mbdt",)])
    p.dma(dsksb[:, :], dram["m_dsk"], wr=[("mdsk",)])
    p.dma(wxp[:, :].rearrange("p (et cc) -> p et cc", cc=128),
          scratch["B_wxp"].rearrange("(et p) cc -> p et cc", p=128),
          rd=[("B", "B_wxp")], wr=[("mwxp",)], q="scalar")
    p.dma(wdt[:, :], scratch["B_wdt"], rd=[("B", "B_wdt")], wr=[("mwdt",)], q="scalar")
    p.act(Asb[:, :], Asb[:, :], AF.Exp, rd=[("mA",)], wr=[("mA",)])
    p.ts(Asb[:, :], Asb[:, :], -1.0, ALU.mult, rd=[("mA",)], wr=[("mA",)])
    for half in range(2):
        p.iota(scrA[0:16, 0:1024], pattern=[[1, 8], [0, 128]], base=-half * 8, mult=-1,
               wr=[("scA",)])
        p.ts(scrA[0:16, 0:1024], scrA[0:16, 0:1024], 0.0, ALU.is_equal,
             rd=[("scA",)], wr=[("scA",)])
        p.cp(sel[:, half * 1024:(half + 1) * 1024], scrA[0:16, 0:1024],
             rd=[("scA",)], wr=[("msel",)])
    p.memset(hstate[:, :], 0.0, wr=[("hst", n) for n in range(NST)])

    xi_v = lambda et: xi[:, et * (TC + 3):(et + 1) * (TC + 3)]

    for ch in range(NCH):
        cp_ = ch % 2
        for tsub in range(TSUB):
            tt = ch * TSUB + tsub
            xt = X[:, tt * D:(tt + 1) * D]
            emit_ln(p, xt, xnbf_t[tsub][:, :], stats[tsub], scrA[:, 0:1024],
                    LN_EPS, rd_x=[("X", tt)], wr_out=[("xnbf", tsub)], tag=("mln", tsub),
                    sckey=("scA",))
            dst3d = xnT[cp_][:, :].rearrange("p (k t) -> p k t", t=TC)[:, :, tsub * 128:tsub * 128 + 128]
            emit_tr8(p, dst3d, xnbf_t[tsub][:, :], ps_trB, c["ident_bf"],
                     rd_src=[("xnbf", tsub)],
                     wr_dst=[("xnT", cp_, kd) for kd in range(DT)], pkey=("pstr",))
        for ec in range(32):
            wp = ec % 2
            p.dma(win_bf[wp][:, :].rearrange("p (dc e) -> p dc e", e=128),
                  scratch["B_win"][:, ec * 128:(ec + 1) * 128]
                  .rearrange("(dc p) e -> p dc e", p=128),
                  rd=[("B", "B_win")], wr=[("winb", wp)], q="sync")
            for dc in range(DT):
                p.mm(ps_mm[wp], win_bf[wp][:, dc * 128:(dc + 1) * 128],
                     xnT[cp_][:, dc * TC:(dc + 1) * TC],
                     rd=[("winb", wp), ("xnT", cp_, dc)], wr=[("psmm", wp)],
                     start=(dc == 0), stop=(dc == DT - 1))
            if ec < 16:
                p.act(xi_v(ec)[:, 3:3 + TC], ps_mm[wp], AF.Identity,
                      rd=[("psmm", wp)], wr=[("xi", ec)])
            else:
                p.act(zs[:, (ec - 16) * TC:(ec - 15) * TC], ps_mm[wp], AF.Silu,
                      rd=[("psmm", wp)], wr=[("zs", ec - 16)])
        for et in range(ET):
            xv = xi_v(et)
            up = et % 2
            if ch == 0:
                p.memset(xv[:, 0:3], 0.0, wr=[("xi", et)])
            else:
                p.cp(xv[:, 0:3], stash[:, et * 3:(et + 1) * 3],
                     rd=[("stash", et)], wr=[("xi", et)])
            cw = lambda k: cwsb[:, et * 4 + k: et * 4 + k + 1]
            p.ts(cbuf0[:, :], xv[:, 0:TC], cw(0), ALU.mult, rd=[("xi", et), ("mcw",)], wr=[("cb",)])
            p.stt(cbuf1[:, :], xv[:, 1:1 + TC], cw(1), cbuf0[:, :], ALU.mult, ALU.add,
                  rd=[("xi", et), ("cb",)], wr=[("cb",)])
            p.stt(cbuf0[:, :], xv[:, 2:2 + TC], cw(2), cbuf1[:, :], ALU.mult, ALU.add,
                  rd=[("xi", et), ("cb",)], wr=[("cb",)])
            p.stt(cbuf1[:, :], xv[:, 3:3 + TC], cw(3), cbuf0[:, :], ALU.mult, ALU.add,
                  rd=[("xi", et), ("cb",)], wr=[("cb",)])
            p.cp(stash[:, et * 3:(et + 1) * 3], xv[:, TC:TC + 3],
                 rd=[("xi", et)], wr=[("stash", et)])
            p.act(xv[:, 3:3 + TC], cbuf1[:, :], AF.Silu, bias=cbsb[:, et:et + 1],
                  rd=[("cb",), ("stash", et), ("mcb",)], wr=[("xi", et)])
            p.cp(ubf[up][:, :], xv[:, 3:3 + TC], rd=[("xi", et)], wr=[("ubf", up)], eng="gpsimd")
            p.ts(yac[:, et * TC:(et + 1) * TC], xv[:, 3:3 + TC], dsksb[:, et:et + 1],
                 ALU.mult, rd=[("xi", et), ("mdsk",)], wr=[("yac", et)])
            p.mm(ps_xd[:, 0:TC], wxp[:, et * 128:(et + 1) * 128], ubf[up][:, :],
                 rd=[("ubf", up), ("mwxp",)], wr=[("psxd",)],
                 start=(et == 0), stop=(et == ET - 1))
        p.act(Bt_bf[:, :], ps_xd[0:16, 0:TC], AF.Identity, rd=[("psxd",)], wr=[("Bt",)])
        p.act(Ct_bf[:, :], ps_xd[32:48, 0:TC], AF.Identity, rd=[("psxd",)], wr=[("Ct",)])
        p.act(dtlr_bf[:, :], ps_xd[64:128, 0:TC], AF.Identity, rd=[("psxd",)], wr=[("dtlr",)])
        for dt in range(ET):
            dv = delta[:, dt * TC:(dt + 1) * TC]
            p.mm(ps_d, wdt[:, dt * 128:(dt + 1) * 128], dtlr_bf[:, :],
                 rd=[("dtlr",), ("mwdt",)], wr=[("psxd",)])
            p.act(sp0[:, :], ps_d, AF.Identity, bias=bdtsb[:, dt:dt + 1],
                  rd=[("psxd",), ("mbdt",)], wr=[("sp0",)])
            p.act(cbuf1[:, :], sp0[:, :], AF.Abs, rd=[("sp0",), ("cb",)], wr=[("cb",)])
            p.act(cbuf0[:, :], cbuf1[:, :], AF.Exp, scale=-1.0, rd=[("cb",)], wr=[("cb",)])
            p.act(cbuf1[:, :], cbuf0[:, :], AF.Ln, bias=1.0, rd=[("cb",)], wr=[("cb",)])
            p.stt(dv, sp0[:, :], 0.0, cbuf1[:, :], ALU.max, ALU.add,
                  rd=[("sp0",), ("cb",)], wr=[("delta", dt)])
            p.tt(wdu[:, dt * TC:(dt + 1) * TC], dv, xi_v(dt)[:, 3:3 + TC], ALU.mult,
                 rd=[("delta", dt), ("xi", dt)], wr=[("w", dt)])
        hkeys = [("hall", d_) for d_ in range(ET)]
        for n in range(NST):
            np_ = n % 2
            p.mm(ps_B, sel[:, n * 128:(n + 1) * 128], Bt_bf[:, :],
                 rd=[("Bt",), ("msel",)], wr=[("pso",)])
            p.act(Bb[np_][:, :], ps_B, AF.Identity, rd=[("pso",)], wr=[("Bb", np_)])
            p.mm(ps_C, sel[:, n * 128:(n + 1) * 128], Ct_bf[:, :],
                 rd=[("Ct",), ("msel",)], wr=[("pso",)])
            p.act(Cb[np_][:, :], ps_C, AF.Identity, rd=[("pso",)], wr=[("Cb", np_)])
            p.tt(dBuall[:, :].rearrange("p (d t) -> p d t", t=TC),
                 wdu[:, :].rearrange("p (d t) -> p d t", t=TC),
                 Bb[np_][:, :].rearrange("p (o t) -> p o t", o=1).broadcast_to((128, ET, TC)),
                 ALU.mult,
                 rd=[("w", d_) for d_ in range(ET)] + [("Bb", np_)], wr=[("dbu",)])
            for dt in range(ET):
                a3 = dt % 2
                p.act(dAb[a3][:, :], delta[:, dt * TC:(dt + 1) * TC], AF.Exp,
                      scale=Asb[:, dt * NST + n: dt * NST + n + 1],
                      rd=[("delta", dt), ("mA",)], wr=[("dA", a3)])
                p.scan(hall[:, dt * TC:(dt + 1) * TC], dAb[a3][:, :],
                       dBuall[:, dt * TC:(dt + 1) * TC],
                       hstate[:, dt * NST + n: dt * NST + n + 1],
                       rd=[("dA", a3), ("dbu",), ("hst", n)], wr=[("hall", dt)])
            p.cp(hstate[:, :].rearrange("p (d n2) -> p d n2", n2=NST)[:, :, n:n + 1],
                 hall[:, :].rearrange("p (d t) -> p d t", t=TC)[:, :, TC - 1:TC],
                 rd=hkeys, wr=[("hst", n)], eng="gpsimd")
            p.tt(hall[:, :].rearrange("p (d t) -> p d t", t=TC),
                 hall[:, :].rearrange("p (d t) -> p d t", t=TC),
                 Cb[np_][:, :].rearrange("p (o t) -> p o t", o=1).broadcast_to((128, ET, TC)),
                 ALU.mult,
                 rd=hkeys + [("Cb", np_), ("hst", n)], wr=hkeys)
            p.tt(yac[:, :], yac[:, :], hall[:, :], ALU.add,
                 rd=hkeys + [("yac", d_) for d_ in range(ET)],
                 wr=[("yac", d_) for d_ in range(ET)], eng="gpsimd")
        for et in range(ET):
            yv = yac[:, et * TC:(et + 1) * TC]
            p.tt(ygbf[:, et * TC:(et + 1) * TC], yv, zs[:, et * TC:(et + 1) * TC],
                 ALU.mult, rd=[("yac", et), ("zs", et)],
                 wr=[("yg", et), ("hall", et // 2)])
        for et in range(ET):
            wp = et % 2
            p.dma(wo_et[wp][:, :], scratch["B_wout"][et * 128:(et + 1) * 128, :],
                  rd=[("B", "B_wout")], wr=[("winb", wp)], q="scalar")
            for tsub in range(TSUB):
                for oh in range(2):
                    p.mm(ps_o[:, (tsub * 2 + oh) * 512:(tsub * 2 + oh + 1) * 512],
                         ygbf[:, et * TC + tsub * 128: et * TC + tsub * 128 + 128],
                         wo_et[wp][:, oh * 512:(oh + 1) * 512],
                         rd=[("yg", et), ("hall", et // 2), ("winb", wp)], wr=[("pso",)],
                         start=(et == 0), stop=(et == ET - 1))
        for tsub in range(TSUB):
            tt = ch * TSUB + tsub
            for oh in range(2):
                xs = X[:, tt * D + oh * 512: tt * D + (oh + 1) * 512]
                p.tt(xs, xs, ps_o[:, (tsub * 2 + oh) * 512:(tsub * 2 + oh + 1) * 512],
                     ALU.add, rd=[("X", tt), ("pso",)], wr=[("X", tt)])



def emit_moe(p, st, c, X, dram, scratch):
    nc = p.nc
    TTI = T // 128
    HT = HMOE // 128
    xnT = st.enter_context(nc.sbuf_tensor("mo_xnT", [128, DT * T], BF16))
    w1bf = st.enter_context(nc.sbuf_tensor("mo_w1", [128, DT * HMOE], BF16))
    w2bf = st.enter_context(nc.sbuf_tensor("mo_w2", [128, HT * D], BF16))
    hbf = st.enter_context(nc.sbuf_tensor("mo_h", [128, HT * 256], BF16))
    gwts = st.enter_context(nc.sbuf_tensor("mo_gw", [128, TTI * NEXP], F32))
    sg_ = [st.enter_context(nc.sbuf_tensor(f"mo_sg{i}", [128, NEXP], F32)) for i in range(2)]
    m8_ = [st.enter_context(nc.sbuf_tensor(f"mo_m8{i}", [128, 8], F32)) for i in range(2)]
    mk_ = [st.enter_context(nc.sbuf_tensor(f"mo_mask{i}", [128, NEXP], F32)) for i in range(2)]
    stats = [st.enter_context(nc.sbuf_tensor(f"mo_stats{i}", [128, 8], F32)) for i in range(2)]
    scrA = st.enter_context(nc.sbuf_tensor("mo_scrA", [128, 2048], F32))
    scrB = st.enter_context(nc.sbuf_tensor("mo_scrB", [128, 2048], F32))
    rowst = st.enter_context(nc.sbuf_tensor("mo_rowst", [1, 512], F32))
    _xb = st.enter_context(nc.sbuf_tensor("mo_xnbf", [128, 1024], BF16))
    xnbf_t = [_xb, _xb]
    gwT_bf = st.enter_context(nc.sbuf_tensor("mo_gwT", [128, DT * NEXP], BF16))
    gb_bf = st.enter_context(nc.sbuf_tensor("mo_gb", [1, NEXP], BF16))
    b1sb = st.enter_context(nc.sbuf_tensor("mo_b1", [128, NEXP * HT], F32))
    b2bf = [st.enter_context(nc.sbuf_tensor("mo_b2", [1, D], BF16))] * 2
    ps_trB = st.enter_context(nc.psum_tensor("mo_ptr", [128, 1024], BF16))
    ps_a = st.enter_context(nc.psum_tensor("mo_pa", [128, 512], F32))
    ps_h = [st.enter_context(nc.psum_tensor(f"mo_ph{i}", [128, 512], F32)) for i in range(2)]
    ps_o = [st.enter_context(nc.psum_tensor(f"mo_po{i}", [128, 512], F32)) for i in range(2)]
    ps_g = ps_a[:, 0:NEXP]

    p.dma(gwT_bf[:, :].rearrange("p (dc e) -> p dc e", e=NEXP),
          scratch["B_gwT"].rearrange("(dc p) e -> p dc e", p=128),
          rd=[("B", "B_gwT")], wr=[("gwT",)], q="scalar")
    p.dma(rowst[0:1, 0:NEXP], dram["g_b"], wr=[("sAb",)])
    p.cp(gb_bf[:, :], rowst[0:1, 0:NEXP], rd=[("sAb",)], wr=[("gb",)])
    p.dma(b1sb[:, :].rearrange("p (e h) -> p e h", h=HT), dram["g_b1"], wr=[("b1",)])
    for tt in range(TTI):
        tp = tt % 2
        xt = X[:, tt * D:(tt + 1) * D]
        emit_ln(p, xt, xnbf_t[tp][:, :], stats[tp], scrA[:, 0:1024],
                LN_EPS, rd_x=[("X", tt)], wr_out=[("xnbf", tp)], tag=("gln", tp),
                sckey=("sA",))
        dst3d = xnT[:, :].rearrange("p (k t) -> p k t", t=T)[:, :, tt * 128:tt * 128 + 128]
        emit_tr8(p, dst3d, xnbf_t[tp][:, :], ps_trB, c["ident_bf"],
                 rd_src=[("xnbf", tp)],
                 wr_dst=[("xnT", tt, kd) for kd in range(DT)], pkey=("pstr",))
        for kd in range(DT):
            p.mm(ps_g, xnT[:, kd * T + tt * 128: kd * T + tt * 128 + 128],
                 gwT_bf[:, kd * NEXP:(kd + 1) * NEXP],
                 rd=[("xnT", tt, kd), ("gwT",)], wr=[("psg",)],
                 start=(kd == 0), stop=False)
        p.mm(ps_g, c["ones_bf"][0:1, 0:128], gb_bf[0:1, :], rd=[("gb",)], wr=[("psg",)],
             start=False, stop=True)
        sg = sg_[tp]
        m8 = m8_[tp]
        maskb = mk_[tp]
        p.act(sg[:, :], ps_g, AF.Identity, rd=[("psg",)], wr=[("sg", tp)])
        p.red(stats[tp][:, 6:7], sg[:, :], ALU.max, rd=[("sg", tp)], wr=[("gst", tp)])
        p.ts(sg[:, :], sg[:, :], stats[tp][:, 6:7], ALU.subtract, rd=[("sg", tp), ("gst", tp)], wr=[("sg", tp)])
        p.act(sg[:, :], sg[:, :], AF.Exp, rd=[("sg", tp)], wr=[("sg", tp)])
        p.red(stats[tp][:, 6:7], sg[:, :], ALU.add, rd=[("sg", tp)], wr=[("gst", tp)])
        p.recip(stats[tp][:, 7:8], stats[tp][:, 6:7], rd=[("gst", tp)], wr=[("gst", tp)])
        p.ts(sg[:, :], sg[:, :], stats[tp][:, 7:8], ALU.mult, rd=[("sg", tp), ("gst", tp)], wr=[("sg", tp)])
        p.max8(m8[:, :], sg[:, :], rd=[("sg", tp)], wr=[("m8", tp)])
        p.ts(maskb[:, :], sg[:, :], m8[:, 1:2], ALU.is_ge, rd=[("sg", tp), ("m8", tp)], wr=[("mk", tp)])
        p.tt(gwts[:, tt * NEXP:(tt + 1) * NEXP], sg[:, :], maskb[:, :], ALU.mult,
             rd=[("sg", tp), ("mk", tp)], wr=[("gwts", tt)])
    for e in range(NEXP):
        ep = e % 2
        for dc in range(DT):
            sc, key, qq = (scrA, ("sw1", 0), "sync") if dc % 2 == 0 else (scrB, ("sw1", 1), "gpsimd")
            p.dma(sc[:, :], dram["g_w1"][e, dc * 128:(dc + 1) * 128, :],
                  rd=[key, ("sA",)], wr=[key, ("sA",)] if dc % 2 == 0 else [key], q=qq)
            p.cp(w1bf[:, dc * HMOE:(dc + 1) * HMOE], sc[:, :],
                 rd=[key], wr=[("w1", dc)], eng="vector" if dc % 2 == 0 else "gpsimd")
        for ht in range(HT):
            sc, key = (scrA[:, 0:D], ("sw1", 0)) if ht % 2 == 0 else (scrB[:, 0:D], ("sw1", 1))
            p.dma(sc, dram["g_w2"][e, ht * 128:(ht + 1) * 128, :],
                  rd=[key], wr=[key], q="scalar")
            p.cp(w2bf[:, ht * D:(ht + 1) * D], sc,
                 rd=[key], wr=[("w2", ht)], eng="vector" if ht % 2 == 0 else "gpsimd")
        for bh in range(2):
            p.dma(rowst[0:1, :], dram["g_b2"][e:e + 1, bh * 512:(bh + 1) * 512],
                  rd=[("sAb",)], wr=[("sAb",)])
            p.cp(b2bf[ep][:, bh * 512:(bh + 1) * 512], rowst[0:1, :],
                 rd=[("sAb",)], wr=[("b2", ep)])
        for tc in range(T // 256):
            for ht in range(HT):
                hp = ht % 2
                for dc in range(DT):
                    p.mm(ps_h[hp][:, :].rearrange("p (a b) -> p a b", a=2)[:, 0, 0:256]
                         if False else ps_h[hp][:, 0:256],
                         w1bf[:, dc * HMOE + ht * 128: dc * HMOE + ht * 128 + 128],
                         xnT[:, dc * T + tc * 256: dc * T + tc * 256 + 256],
                         rd=[("w1", dc)] + [("xnT", tc * 2 + q_, dc) for q_ in range(2)],
                         wr=[("psh", hp)],
                         start=(dc == 0), stop=(dc == DT - 1))
                p.act(hbf[:, ht * 256:(ht + 1) * 256], ps_h[hp][:, 0:256], AF.Relu,
                      bias=b1sb[:, e * HT + ht: e * HT + ht + 1],
                      rd=[("psh", hp), ("b1",)], wr=[("hbf", ht)])
            for tsub in range(2):
                tt = tc * 2 + tsub
                gcol = gwts[:, tt * NEXP + e: tt * NEXP + e + 1]
                for oh in range(2):
                    for ht in range(HT):
                        p.mm(ps_o[oh][:, :], hbf[:, ht * 256 + tsub * 128: ht * 256 + tsub * 128 + 128],
                             w2bf[:, ht * D + oh * 512: ht * D + oh * 512 + 512],
                             rd=[("hbf", ht), ("w2", ht)], wr=[("pso", oh)],
                             start=(ht == 0), stop=False)
                    p.mm(ps_o[oh][:, :], c["ones_bf"][0:1, 0:128], b2bf[ep][0:1, oh * 512:(oh + 1) * 512],
                         rd=[("b2", ep)], wr=[("pso", oh)], start=False, stop=True)
                    xs = X[:, tt * D + oh * 512: tt * D + (oh + 1) * 512]
                    p.stt(xs, ps_o[oh][:, :], gcol, xs, ALU.mult, ALU.add,
                          rd=[("pso", oh), ("gwts", tt), ("X", tt)], wr=[("X", tt)])



ATTN_PHASES = "ABqkvwstzcuTd"


def emit_attn(p, st, c, X, dram, scratch):
    nc = p.nc
    TTI = T // 128
    qT = st.enter_context(nc.sbuf_tensor("at_qT", [128, DT * T], BF16))
    wqkv = st.enter_context(nc.sbuf_tensor("at_wqkv", [128, 3 * DT * D], BF16))
    wd_bf = wqkv[:, 0:DT * D]
    bias_bf = st.enter_context(nc.sbuf_tensor("at_bias", [1, 4 * D], BF16))
    gb_bc = st.enter_context(nc.sbuf_tensor("at_gb", [128, 2 * D], F32))
    scr3 = st.enter_context(nc.sbuf_tensor("at_scr3", [128, 3072], F32))
    stats = [st.enter_context(nc.sbuf_tensor(f"at_stats{i}", [128, 8], F32)) for i in range(2)]
    xnbf_t = st.enter_context(nc.sbuf_tensor("at_xnbf", [128, 1024], BF16))
    xnT_t = st.enter_context(nc.sbuf_tensor("at_xnT", [128, DT * 128], BF16))
    qkbf = st.enter_context(nc.sbuf_tensor("at_qkbf", [128, 1024], BF16))
    vbf = st.enter_context(nc.sbuf_tensor("at_vbf", [128, 1024], BF16))
    kvbf = st.enter_context(nc.sbuf_tensor("at_kvbf", [128, NH * HD], BF16))
    kvacc = st.enter_context(nc.sbuf_tensor("at_kvacc", [64, NH * HD], F32))
    ksacc = st.enter_context(nc.sbuf_tensor("at_ksacc", [128, DT], F32))
    ksz = st.enter_context(nc.sbuf_tensor("at_ksz", [128, NH], BF16))
    ksf = st.enter_context(nc.sbuf_tensor("at_ksf", [128, DT], F32))
    zrec = st.enter_context(nc.sbuf_tensor("at_z", [128, NH], F32))
    czbf = st.enter_context(nc.sbuf_tensor("at_cz", [128, 1024], BF16))
    ctxT_t = st.enter_context(nc.sbuf_tensor("at_ctxT", [128, DT * 128], BF16))
    hs_fp = st.enter_context(nc.sbuf_tensor("at_hs", [128, 1024], F32))
    ps_trB = st.enter_context(nc.psum_tensor("at_ptr", [128, 1024], BF16))
    ps_w = [st.enter_context(nc.psum_tensor(f"at_pw{i}", [128, 512], F32)) for i in range(2)]
    ps_kv = st.enter_context(nc.psum_tensor("at_pkv", [64, NH * HD], F32))
    ps_small = st.enter_context(nc.psum_tensor("at_psm", [128, 512], F32))
    ps_ksc = ps_small[:, 0:DT]
    ps_z = ps_small[:, 16:16 + NH]
    ps_ctx = st.enter_context(nc.psum_tensor("at_pctx", [128, 1024], F32))
    p.memset(kvacc[:, :], 0.0, wr=[("kvacc",)])
    p.memset(ksacc[:, :], 0.0, wr=[("ksacc",)])
    efp = scr3[:, 0:1024]
    et0 = scr3[:, 1024:1536]
    et1 = scr3[:, 1536:2048]
    scrB = scr3[:, 1024:2048]
    hsn = scr3[:, 2048:3072]

    for j, wname in enumerate(("B_wq", "B_wk", "B_wv")):
        p.dma(wqkv[:, j * DT * D:(j + 1) * DT * D].rearrange("p (dc dd) -> p dc dd", dd=D),
              scratch[wname].rearrange("(dc p) dd -> p dc dd", p=128),
              rd=[("B", wname)], wr=[("wqkv", j)], q="sync")
    for j, bname in enumerate(("a_bq", "a_bk", "a_bv", "a_bd")):
        p.dma(scrB[0:1, 0:D], dram[bname], rd=[("scd",)], wr=[("scd",)])
        p.cp(bias_bf[:, j * D:(j + 1) * D], scrB[0:1, 0:D], rd=[("scd",)], wr=[("abias", j)])
    for j, gname in enumerate(("a_gamma", "a_beta")):
        p.dma(scrB[0:1, 0:D], dram[gname], rd=[("scd",)], wr=[("scd",)])
        p.cp(qkbf[0:1, 0:D], scrB[0:1, 0:D], rd=[("scd",)], wr=[("qkbf",)])
        for oh in range(2):
            p.mm(ps_w[oh][:, :], c["ones_bf"][0:1, 0:128], qkbf[0:1, oh * 512:(oh + 1) * 512],
                 rd=[("qkbf",)], wr=[("psw", oh)])
            p.act(gb_bc[:, j * D + oh * 512: j * D + (oh + 1) * 512], ps_w[oh][:, :],
                  AF.Identity, rd=[("psw", oh)], wr=[("gbbc",)])

    def proj(j, oh, pw):
        for kd in range(DT):
            p.mm(ps_w[pw][:, :], xnT_t[:, kd * 128:(kd + 1) * 128],
                 wqkv[:, (j * DT + kd) * D + oh * 512:(j * DT + kd) * D + (oh + 1) * 512],
                 rd=[("xnTt", kd), ("wqkv", j)], wr=[("psw", pw)],
                 start=(kd == 0), stop=False)
        p.mm(ps_w[pw][:, :], c["ones_bf"][0:1, 0:128],
             bias_bf[0:1, j * D + oh * 512: j * D + (oh + 1) * 512],
             rd=[("abias", j)], wr=[("psw", pw)], start=False, stop=True)

    def elu1(dst_half, pw, wrk):
        p.ts(et0, ps_w[pw][:, :], 0.0, ALU.min, rd=[("psw", pw), ("scd",)], wr=[("scd",)])
        p.act(et1, et0, AF.Exp, rd=[("scd",)], wr=[("scd",)])
        p.stt(dst_half, ps_w[pw][:, :], 0.0, et1, ALU.max, ALU.add,
              rd=[("psw", pw), ("scd",)], wr=[wrk])

    for tt in (range(TTI) if "A" in ATTN_PHASES else []):
        tp = tt % 2
        xt = X[:, tt * D:(tt + 1) * D]
        emit_ln(p, xt, xnbf_t[:, :], stats[tp], scrB,
                LN_EPS, rd_x=[("X", tt)], wr_out=[("xnbf",)], tag=("aln", tp),
                sckey=("scd",))
        emit_tr8(p, xnT_t[:, :].rearrange("p (k f) -> p k f", f=128),
                 xnbf_t[:, :], ps_trB, c["ident_bf"], rd_src=[("xnbf",)],
                 wr_dst=[("xnTt", kd) for kd in range(DT)], pkey=("pstr",))
        if "2" in ATTN_PHASES:
            emit_tr8(p, xnT_t[:, :].rearrange("p (k f) -> p k f", f=128),
                     xnbf_t[:, :], ps_trB, c["ident_bf"], rd_src=[("xnbf",)],
                     wr_dst=[("xnTt", kd) for kd in range(DT)], pkey=("pstr",))
        if "q" in ATTN_PHASES:
            for oh in range(2):
                proj(0, oh, oh)
                elu1(efp[:, oh * 512:(oh + 1) * 512], oh, ("efp", oh))
            p.cp(qkbf[:, :], efp, rd=[("efp", 0), ("efp", 1)], wr=[("qkbf",)])
            qdst = qT[:, :].rearrange("p (k t) -> p k t", t=T)[:, :, tt * 128:tt * 128 + 128]
            emit_tr8(p, qdst, qkbf[:, :], ps_trB, c["ident_bf"],
                     rd_src=[("qkbf",)],
                     wr_dst=[("qT", tt, kd) for kd in range(DT)], pkey=("pstr",))
        if "k" in ATTN_PHASES:
            for oh in range(2):
                proj(1, oh, oh)
                elu1(efp[:, oh * 512:(oh + 1) * 512], oh, ("efp", oh))
            p.cp(qkbf[:, :], efp, rd=[("efp", 0), ("efp", 1)], wr=[("qkbf",)])
        if "v" in ATTN_PHASES:
            for oh in range(2):
                proj(2, oh, oh)
                p.act(vbf[:, oh * 512:(oh + 1) * 512], ps_w[oh][:, :], AF.Identity,
                      rd=[("psw", oh)], wr=[("vbf",)])
        if "w" in ATTN_PHASES:
            for h in range(NH):
                p.mm(ps_kv[:, h * HD:(h + 1) * HD], qkbf[:, h * HD:(h + 1) * HD],
                     vbf[:, h * HD:(h + 1) * HD],
                     rd=[("qkbf",), ("vbf",)], wr=[("pskv",)])
            p.tt(kvacc[:, :], kvacc[:, :], ps_kv[:, :], ALU.add,
                 rd=[("kvacc",), ("pskv",)], wr=[("kvacc",)])
        if "s" in ATTN_PHASES:
            for ic in range(DT):
                p.mm(ps_small[:, ic:ic + 1], qkbf[:, ic * 128:(ic + 1) * 128],
                     c["onescol_bf"][:, :],
                     rd=[("qkbf",)], wr=[("psks",)])
            p.tt(ksacc[:, :], ksacc[:, :], ps_small[:, 0:DT], ALU.add,
                 rd=[("ksacc",), ("psks",)], wr=[("ksacc",)])
    kvt = vbf[0:64, :]
    p.act(kvt, kvacc[:, :], AF.Identity, rd=[("kvacc",)], wr=[("vbf",), ("kvt",)])
    p.memset(kvbf[:, :], 0.0, wr=[("kv0",)])
    p.cp(kvbf[0:64, :].rearrange("p (h m) -> p h m", m=2 * HD)[:, :, 0:HD],
         kvt.rearrange("p (h m) -> p h m", m=2 * HD)[:, :, 0:HD],
         rd=[("kvt",), ("kv0",)], wr=[("kv0",)])
    p.dma(kvbf[64:128, :].rearrange("p (h m) -> p h m", m=2 * HD)[:, :, HD:2 * HD],
          kvt.rearrange("p (h m) -> p h m", m=2 * HD)[:, :, HD:2 * HD],
          rd=[("kvt",), ("kv0",)], wr=[("kv0",)])
    p.act(ksf[:, :], ksacc[:, :], AF.Identity, bias=p.ccol(EPS_ATTN),
          rd=[("ksacc",)], wr=[("ksf",)])
    p.memset(ksz[:, :], 0.0, wr=[("ksbf",)])
    p.cp(ksz[0:64, :].rearrange("p (a b) -> p a b", b=2)[:, :, 0:1],
         ksf[0:64, :].rearrange("p (a o) -> p a o", o=1),
         rd=[("ksf",), ("ksbf",)], wr=[("ksbf",)])
    p.cp(ksz[64:128, :].rearrange("p (a b) -> p a b", b=2)[:, :, 1:2],
         ksf[64:128, :].rearrange("p (a o) -> p a o", o=1),
         rd=[("ksf",), ("ksbf",)], wr=[("ksbf",)])
    p.dma(wd_bf.rearrange("p (dc dd) -> p dc dd", dd=D),
          scratch["B_wd"].rearrange("(dc p) dd -> p dc dd", p=128),
          rd=[("B", "B_wd"), ("wqkv", 0)], wr=[("wqkv", 0), ("wd",)], q="sync")
    for tt in (range(TTI) if "B" in ATTN_PHASES and "A" in ATTN_PHASES else []):
        tp = tt % 2
        if "z" in ATTN_PHASES:
            for h in range(NH):
                ic = h // 2
                p.mm(ps_small[:, 16 + h:17 + h],
                     qT[:, ic * T + tt * 128: ic * T + tt * 128 + 128],
                     ksz[:, h:h + 1],
                     rd=[("qT", tt, ic), ("ksbf",)], wr=[("psks",)])
            p.recip(zrec[:, :], ps_z, rd=[("psks",)], wr=[("zrec",)])
        if "c" in ATTN_PHASES:
            for h in range(NH):
                ic = h // 2
                p.mm(ps_ctx[:, h * HD:(h + 1) * HD],
                     qT[:, ic * T + tt * 128: ic * T + tt * 128 + 128],
                     kvbf[:, h * HD:(h + 1) * HD],
                     rd=[("qT", tt, ic), ("kv0",)], wr=[("psctx",)])
            for h in (range(NH) if "u" in ATTN_PHASES else []):
                p.ts(czbf[:, h * HD:(h + 1) * HD], ps_ctx[:, h * HD:(h + 1) * HD],
                     zrec[:, h:h + 1], ALU.mult,
                     rd=[("psctx",), ("zrec",)], wr=[("czbf",)])
        if "T" in ATTN_PHASES:
            emit_tr8(p, ctxT_t[:, :].rearrange("p (k f) -> p k f", f=128),
                     czbf[:, :], ps_trB, c["ident_bf"], rd_src=[("czbf",)],
                     wr_dst=[("ctxT", kd) for kd in range(DT)], pkey=("pstr",))
        xt = X[:, tt * D:(tt + 1) * D]
        if "d" in ATTN_PHASES:
            emit_ln(p, xt, efp, stats[tp], scrB,
                    LN_EPS, rd_x=[("X", tt)], wr_out=[("efp", 0), ("efp", 1)], tag=("aln2", tp),
                    sckey=("scd",))
            for oh in range(2):
                for kd in range(DT):
                    p.mm(ps_w[1][:, :], ctxT_t[:, kd * 128:(kd + 1) * 128],
                         wd_bf[:, kd * D + oh * 512: kd * D + (oh + 1) * 512],
                         rd=[("ctxT", kd), ("wd",)], wr=[("psw", 1)],
                         start=(kd == 0), stop=False)
                p.mm(ps_w[1][:, :], c["ones_bf"][0:1, 0:128],
                     bias_bf[0:1, 3 * D + oh * 512: 3 * D + (oh + 1) * 512],
                     rd=[("abias", 3)], wr=[("psw", 1)], start=False, stop=True)
                p.tt(hs_fp[:, oh * 512:(oh + 1) * 512], ps_w[1][:, :],
                     efp[:, oh * 512:(oh + 1) * 512], ALU.add,
                     rd=[("psw", 1), ("efp", oh)], wr=[("hs",)])
            emit_ln(p, hs_fp[:, :], hsn, stats[tp], scrB,
                    EPS_LN_ATTN, rd_x=[("hs",)], wr_out=[("hsn",)], tag=("aln3", tp),
                    sckey=("scd",))
            p.tt(hsn, hsn, gb_bc[:, 0:D], ALU.mult,
                 rd=[("hsn",), ("gbbc",)], wr=[("hsn",)])
            p.tt(hsn, hsn, gb_bc[:, D:2 * D], ALU.add,
                 rd=[("hsn",), ("gbbc",)], wr=[("hsn",)])
            p.tt(xt, xt, hsn, ALU.add,
                 rd=[("X", tt), ("hsn",)], wr=[("X", tt)])



def emit_ffn1(p, st, c, X, dram, scratch):
    nc = p.nc
    TTI = T // 128
    JT = NB // 128
    HT1 = H1 // 128
    xnT = st.enter_context(nc.sbuf_tensor("ff_xnT", [128, DT * T], BF16))
    UT_bf = st.enter_context(nc.sbuf_tensor("ff_UT", [128, DT * NB], BF16))
    rWT_bf = st.enter_context(nc.sbuf_tensor("ff_rWT", [128, DT * NB], BF16))
    rV_bf = st.enter_context(nc.sbuf_tensor("ff_rV", [128, JT * D], BF16))
    rb_bf = st.enter_context(nc.sbuf_tensor("ff_rb", [1, NB], BF16))
    lngb = st.enter_context(nc.sbuf_tensor("ff_lngb", [128, 2 * D], F32))
    gamb = st.enter_context(nc.sbuf_tensor("ff_gamb", [128, D], F32))
    scrA = st.enter_context(nc.sbuf_tensor("ff_scrA", [128, 2048], F32))
    scrB = st.enter_context(nc.sbuf_tensor("ff_scrB", [128, 2048], F32))
    stats = [st.enter_context(nc.sbuf_tensor(f"ff_stats{i}", [128, 8], F32)) for i in range(2)]
    invUV = st.enter_context(nc.sbuf_tensor("ff_iuv", [128, 16], F32))
    ral = st.enter_context(nc.sbuf_tensor("ff_ral", [128, NB], F32))
    alpha = st.enter_context(nc.sbuf_tensor("ff_alpha", [128, NB], F32))
    aw = st.enter_context(nc.sbuf_tensor("ff_aw", [128, NB], F32))
    m8 = st.enter_context(nc.sbuf_tensor("ff_m8", [128, 8], F32))
    xr = st.enter_context(nc.sbuf_tensor("ff_xr", [128, D], F32))
    xrbf = st.enter_context(nc.sbuf_tensor("ff_xrbf", [128, D], BF16))
    xrT_t = st.enter_context(nc.sbuf_tensor("ff_xrT", [128, DT * 128], BF16))
    czbf = st.enter_context(nc.sbuf_tensor("ff_cz", [128, NB], BF16))
    cT_t = st.enter_context(nc.sbuf_tensor("ff_cT", [128, JT * 128], BF16))
    h1t = [st.enter_context(nc.sbuf_tensor(f"ff_h1{i}", [128, 256], BF16)) for i in range(2)]
    w1t = [st.enter_context(nc.sbuf_tensor(f"ff_w1t{i}", [128, 2 * DT * 128], BF16)) for i in range(2)]
    w2t = [st.enter_context(nc.sbuf_tensor(f"ff_w2t{i}", [128, 2 * D], BF16)) for i in range(2)]
    ps_trB = st.enter_context(nc.psum_tensor("ff_ptr", [128, 1024], BF16))
    ps_s = st.enter_context(nc.psum_tensor("ff_ps", [128, 2048], F32))
    ps_h0 = st.enter_context(nc.psum_tensor("ff_ph0", [128, 256], F32))
    ps_h1 = st.enter_context(nc.psum_tensor("ff_ph1", [128, 256], F32))
    ps_h = [ps_h0[:, :], ps_h1[:, :]]
    ps_wk = st.enter_context(nc.psum_tensor("ff_pwk", [128, 512], F32))

    p.dma(UT_bf[:, :].rearrange("p (dc nb) -> p dc nb", nb=NB),
          scratch["B_UT"].rearrange("(dc p) nb -> p dc nb", p=128),
          rd=[("B", "B_UT")], wr=[("UT",)], q="sync")
    p.dma(rWT_bf[:, :].rearrange("p (dc nb) -> p dc nb", nb=NB),
          scratch["B_rWT"].rearrange("(dc p) nb -> p dc nb", p=128),
          rd=[("B", "B_rWT")], wr=[("rWT",)], q="sync")
    p.dma(rV_bf[:, :].rearrange("p (jt dd) -> p jt dd", dd=D),
          scratch["B_rV"].rearrange("(jt p) dd -> p jt dd", p=128),
          rd=[("B", "B_rV")], wr=[("rV",)], q="sync")
    p.dma(scrB[0:1, 0:NB], dram["f_rb"], wr=[("sB",)])
    p.cp(rb_bf[:, :], scrB[0:1, 0:NB], rd=[("sB",)], wr=[("rb",)])
    for j, gname in enumerate(("f_lng", "f_lnb", "f_gam")):
        p.dma(scrB[0:1, 0:D], dram[gname], wr=[("sB",)])
        p.cp(xrbf[0:1, 0:D], scrB[0:1, 0:D], rd=[("sB",)], wr=[("xrbf",)])
        for oh in range(2):
            p.mm(ps_wk[:, :], c["ones_bf"][0:1, 0:128], xrbf[0:1, oh * 512:(oh + 1) * 512],
                 rd=[("xrbf",)], wr=[("pswk",)])
            dst = lngb[:, j * D + oh * 512: j * D + (oh + 1) * 512] if j < 2 else \
                  gamb[:, oh * 512:(oh + 1) * 512]
            p.act(dst, ps_wk[:, :], AF.Identity, rd=[("pswk",)], wr=[("lngb",)])
    for jt in range(JT):
        p.dma(scrB[:, 0:D], dram["f_U"][jt * 128:(jt + 1) * 128, :], wr=[("sB",)])
        p.act(scrB[:, 1024:2048], scrB[:, 0:D], AF.Square, accum_out=invUV[:, jt:jt + 1],
              rd=[("sB",)], wr=[("sB2",), ("iuv",)])
        p.dma(scrB[:, 0:D], dram["f_V"][jt * 128:(jt + 1) * 128, :], wr=[("sB",)])
        p.act(scrB[:, 1024:2048], scrB[:, 0:D], AF.Square, accum_out=invUV[:, 4 + jt:5 + jt],
              rd=[("sB",)], wr=[("sB2",), ("iuv",)])
    p.act(invUV[:, 8:16], invUV[:, 0:8], AF.Ln, rd=[("iuv",)], wr=[("iuv",)])
    p.act(invUV[:, 0:8], invUV[:, 8:16], AF.Exp, scale=p.ccol(0.5), rd=[("iuv",)], wr=[("iuv",)])
    p.ts(invUV[:, 0:8], invUV[:, 0:8], FFN_EPS, ALU.max, rd=[("iuv",)], wr=[("iuv",)])
    p.recip(invUV[:, 0:8], invUV[:, 0:8], rd=[("iuv",)], wr=[("iuv",)])
    p.tt(invUV[:, 8:12], invUV[:, 0:4], invUV[:, 4:8], ALU.mult, rd=[("iuv",)], wr=[("iuv",)])
    for tt in range(TTI):
        tp = tt % 2
        xt = X[:, tt * D:(tt + 1) * D]
        emit_ln(p, xt, xrbf[:, :], stats[tp], scrA[:, 0:1024],
                LN_EPS, rd_x=[("X", tt)], wr_out=[("xrbf",)], tag=("fln", tp))
        dst3d = xnT[:, :].rearrange("p (k t) -> p k t", t=T)[:, :, tt * 128:tt * 128 + 128]
        emit_tr8(p, dst3d, xrbf[:, :], ps_trB, c["ident_bf"],
                 rd_src=[("xrbf",)],
                 wr_dst=[("xnT", tt, kd) for kd in range(DT)], pkey=("pstr",))
    for tt in range(TTI):
        tp = tt % 2
        xt = X[:, tt * D:(tt + 1) * D]
        emit_ln(p, xt, scrA[:, 0:1024], stats[tp], scrA[:, 1024:2048],
                LN_EPS, rd_x=[("X", tt)], wr_out=[("xn",)], tag=("fln2", tp))
        emit_ln(p, scrA[:, 0:1024], xr[:, :], stats[tp], scrA[:, 1024:2048],
                LN_EPS, rd_x=[("xn",)], wr_out=[("xr",)], tag=("fln3", tp))
        p.tt(xr[:, :], xr[:, :], lngb[:, 0:D], ALU.mult, rd=[("xr",), ("lngb",)], wr=[("xr",)])
        p.tt(xr[:, :], xr[:, :], lngb[:, D:2 * D], ALU.add, rd=[("xr",), ("lngb",)], wr=[("xr",)])
        p.cp(xrbf[:, :], xr[:, :], rd=[("xr",)], wr=[("xrbf",)])
        emit_tr8(p, xrT_t[:, :].rearrange("p (k f) -> p k f", f=128),
                 xrbf[:, :], ps_trB, c["ident_bf"], rd_src=[("xrbf",)],
                 wr_dst=[("xrT", kd) for kd in range(DT)], pkey=("pstr",))
        for kd in range(DT):
            p.mm(ps_wk[:, :], xrT_t[:, kd * 128:(kd + 1) * 128],
                 rWT_bf[:, kd * NB:(kd + 1) * NB],
                 rd=[("xrT", kd), ("rWT",)], wr=[("pswk",)],
                 start=(kd == 0), stop=False)
        p.mm(ps_wk[:, :], c["ones_bf"][0:1, 0:128], rb_bf[0:1, :],
             rd=[("rb",)], wr=[("pswk",)], start=False, stop=True)
        p.ts(ral[:, :], ps_wk[:, :], TAU, ALU.min, -TAU, ALU.max,
             rd=[("pswk",)], wr=[("ral",)])
        p.act(alpha[:, :], ral[:, :], AF.Abs, rd=[("ral",)], wr=[("alpha",)])
        p.act(aw[:, :], alpha[:, :], AF.Exp, scale=-1.0, rd=[("alpha",)], wr=[("aw",)])
        p.act(alpha[:, :], aw[:, :], AF.Ln, bias=1.0, rd=[("aw",)], wr=[("alpha",)])
        p.stt(alpha[:, :], ral[:, :], 0.0, alpha[:, :], ALU.max, ALU.add,
              rd=[("ral",), ("alpha",)], wr=[("alpha",)])
        p.max8(m8[:, :], alpha[:, :], rd=[("alpha",)], wr=[("m8",)])
        p.ts(aw[:, :], alpha[:, :], m8[:, 7:8], ALU.is_ge, rd=[("alpha",), ("m8",)], wr=[("aw",)])
        p.tt(aw[:, :], aw[:, :], alpha[:, :], ALU.mult, rd=[("aw",), ("alpha",)], wr=[("aw",)])
        p.red(stats[tp][:, 6:7], aw[:, :], ALU.add, rd=[("aw",)], wr=[("fst", tp)])
        p.act(stats[tp][:, 7:8], stats[tp][:, 6:7], AF.Tanh, rd=[("fst", tp)], wr=[("fst", tp)])
        p.ts(stats[tp][:, 6:7], stats[tp][:, 6:7], FFN_EPS, ALU.add, rd=[("fst", tp)], wr=[("fst", tp)])
        p.recip(stats[tp][:, 6:7], stats[tp][:, 6:7], rd=[("fst", tp)], wr=[("fst", tp)])
        p.tt(stats[tp][:, 6:7], stats[tp][:, 6:7], stats[tp][:, 7:8], ALU.mult,
             rd=[("fst", tp)], wr=[("fst", tp)])
        p.ts(aw[:, :], aw[:, :], stats[tp][:, 6:7], ALU.mult, rd=[("aw",), ("fst", tp)], wr=[("aw",)])
        for kd in range(DT):
            p.mm(ps_wk[:, :], xnT[:, kd * T + tt * 128: kd * T + tt * 128 + 128],
                 UT_bf[:, kd * NB:(kd + 1) * NB],
                 rd=[("xnT", tt, kd), ("UT",)], wr=[("pswk",)],
                 start=(kd == 0), stop=(kd == DT - 1))
        p.tt(ral[:, :], ps_wk[:, :], aw[:, :], ALU.mult, rd=[("pswk",), ("aw",)], wr=[("ral",)])
        p.cp(czbf[:, :], ral[:, :], rd=[("ral",)], wr=[("czbf",)])
        for jt in range(JT):
            p.trans(ps_trB[:, jt * 128:(jt + 1) * 128], czbf[:, jt * 128:(jt + 1) * 128],
                    c["ident_bf"][:, :], rd=[("czbf",)], wr=[("pstr",)])
        for jt in range(JT):
            p.ts(cT_t[:, jt * 128:(jt + 1) * 128], ps_trB[:, jt * 128:(jt + 1) * 128],
                 invUV[:, 8 + jt:9 + jt], ALU.mult,
                 rd=[("pstr",), ("iuv",)], wr=[("cT", jt)])
        for oh in range(2):
            for jt in range(JT):
                p.mm(ps_wk[:, :], cT_t[:, jt * 128:(jt + 1) * 128],
                     rV_bf[:, jt * D + oh * 512: jt * D + (oh + 1) * 512],
                     rd=[("cT", jt), ("rV",)], wr=[("pswk",)],
                     start=(jt == 0), stop=(jt == JT - 1))
            p.tt(scrA[:, 1024:1536], ps_wk[:, :], gamb[:, oh * 512:(oh + 1) * 512], ALU.mult,
                 rd=[("pswk",), ("lngb",)], wr=[("dyn",)])
            xs = X[:, tt * D + oh * 512: tt * D + (oh + 1) * 512]
            p.tt(xs, xs, scrA[:, 1024:1536], ALU.add,
                 rd=[("X", tt), ("dyn",)], wr=[("X", tt)])
    for tc in range(T // 256):
        for ht in range(HT1):
            hp = (ht // 2) % 2
            sub = ht % 2
            if sub == 0:
                p.dma(w1t[hp][:, :].rearrange("p (dc h) -> p dc h", h=256),
                      scratch["B_w1T"][:, ht * 128:(ht + 2) * 128]
                      .rearrange("(dc p) h -> p dc h", p=128),
                      rd=[("B", "B_w1T")], wr=[("w1t", hp)], q="sync" if hp == 0 else "gpsimd")
                p.dma(w2t[hp][:, :].rearrange("p (o d) -> p o d", d=D),
                      scratch["B_w2T"][ht * 128:(ht + 2) * 128, :]
                      .rearrange("(o p) d -> p o d", p=128),
                      rd=[("B", "B_w2T")], wr=[("w2t", hp)], q="scalar")
            hq = hp * 2 + sub if False else hp
            for dc in range(DT):
                p.mm(ps_h[ht % 2], w1t[hp][:, (dc * 2 + sub) * 128:(dc * 2 + sub + 1) * 128],
                     xnT[:, dc * T + tc * 256: dc * T + (tc + 1) * 256],
                     rd=[("w1t", hp), ("xnT", tc * 2, dc), ("xnT", tc * 2 + 1, dc)],
                     wr=[("psh", ht % 2)],
                     start=(dc == 0), stop=(dc == DT - 1))
            p.act(h1t[ht % 2][:, :], ps_h[ht % 2], AF.Gelu, rd=[("psh", ht % 2)], wr=[("h1t", ht % 2)])
            for tsub in range(2):
                for oh in range(2):
                    p.mm(ps_s[:, (tsub * 2 + oh) * 512:(tsub * 2 + oh + 1) * 512],
                         h1t[ht % 2][:, tsub * 128:(tsub + 1) * 128],
                         w2t[hp][:, sub * D + oh * 512: sub * D + (oh + 1) * 512],
                         rd=[("h1t", ht % 2), ("w2t", hp)], wr=[("pss",)],
                         start=(ht == 0), stop=(ht == HT1 - 1))
        for tsub in range(2):
            tt = tc * 2 + tsub
            for oh in range(2):
                xs = X[:, tt * D + oh * 512: tt * D + (oh + 1) * 512]
                p.tt(xs, xs, ps_s[:, (tsub * 2 + oh) * 512:(tsub * 2 + oh + 1) * 512],
                     ALU.add, rd=[("X", tt), ("pss",)], wr=[("X", tt)])



def _mamba_host(inp):
    f = np.ascontiguousarray
    cw = inp["L0_attn_conv_w"].reshape(E, 4)
    d = {
        "m_A": f(inp["L0_attn_A_log"].reshape(ET, 128, NST).transpose(1, 0, 2).reshape(128, ET * NST)),
        "m_cw": f(cw.reshape(ET, 128, 4).transpose(1, 0, 2).reshape(128, ET * 4)),
        "m_cb": f(inp["L0_attn_conv_b"].reshape(ET, 128).T),
        "m_bdt": f(inp["L0_attn_b_dt"].reshape(ET, 128).T),
        "m_dsk": f(inp["L0_attn_D_skip"].reshape(ET, 128).T),
        "m_wdt": f(inp["L0_attn_W_dt"]),
        "m_win": f(inp["L0_attn_W_in"]),
        "m_wout": f(inp["L0_attn_W_out"]),
    }
    wxp = np.zeros((E, 128), np.float32)
    wx = inp["L0_attn_W_xproj"]
    wxp[:, 0:16] = wx[:, DTR:DTR + NST]
    wxp[:, 32:48] = wx[:, DTR + NST:DTR + 2 * NST]
    wxp[:, 64:128] = wx[:, 0:DTR]
    d["m_wxp"] = f(wxp)
    return d


def _moe_host(inp):
    f = np.ascontiguousarray
    HT = HMOE // 128
    return {
        "g_wT": f(inp["L0_ffn_gate_w"].T),
        "g_b": f(inp["L0_ffn_gate_b"].reshape(1, NEXP)),
        "g_w1": f(inp["L0_ffn_w1"]),
        "g_b1": f(inp["L0_ffn_b1"].reshape(NEXP, HT, 128).transpose(2, 0, 1).reshape(128, NEXP * HT)),
        "g_w2": f(inp["L0_ffn_w2"]),
        "g_b2": f(inp["L0_ffn_b2"]),
    }


def _attn_host(inp):
    f = np.ascontiguousarray
    return {
        "a_wq": f(inp["L1_attn_Wq"].T),
        "a_wk": f(inp["L1_attn_Wk"].T),
        "a_wv": f(inp["L1_attn_Wv"].T),
        "a_wd": f(inp["L1_attn_Wd"].T),
        "a_bq": f(inp["L1_attn_bq"].reshape(1, D)),
        "a_bk": f(inp["L1_attn_bk"].reshape(1, D)),
        "a_bv": f(inp["L1_attn_bv"].reshape(1, D)),
        "a_bd": f(inp["L1_attn_bd"].reshape(1, D)),
        "a_gamma": f(inp["L1_attn_gamma"].reshape(1, D)),
        "a_beta": f(inp["L1_attn_beta"].reshape(1, D)),
    }


def _ffn1_host(inp):
    f = np.ascontiguousarray
    return {
        "f_UT": f(inp["L1_ffn_raw_U"].T),
        "f_rWT": f(inp["L1_ffn_router_W"].T),
        "f_rV": f(inp["L1_ffn_raw_V"]),
        "f_rb": f(inp["L1_ffn_router_b"].reshape(1, NB)),
        "f_lng": f(inp["L1_ffn_ln_g"].reshape(1, D)),
        "f_lnb": f(inp["L1_ffn_ln_b"].reshape(1, D)),
        "f_gam": f(inp["L1_ffn_gamma"].reshape(1, D)),
        "f_U": f(inp["L1_ffn_raw_U"]),
        "f_V": f(inp["L1_ffn_raw_V"]),
        "f_w1T": f(inp["L1_ffn_W1"].T),
        "f_w2T": f(inp["L1_ffn_W2"].T),
    }


_DRAM_SHAPES = {
    "m_A": (128, ET * NST), "m_cw": (128, ET * 4), "m_cb": (128, ET),
    "m_bdt": (128, ET), "m_dsk": (128, ET), "m_wdt": (DTR, E),
    "m_win": (D, 2 * E), "m_wout": (E, D), "m_wxp": (E, 128),
    "g_wT": (D, NEXP), "g_b": (1, NEXP), "g_w1": (NEXP, D, HMOE),
    "g_b1": (128, NEXP * (HMOE // 128)), "g_w2": (NEXP, HMOE, D), "g_b2": (NEXP, D),
    "a_wq": (D, D), "a_wk": (D, D), "a_wv": (D, D), "a_wd": (D, D),
    "a_bq": (1, D), "a_bk": (1, D), "a_bv": (1, D), "a_bd": (1, D),
    "a_gamma": (1, D), "a_beta": (1, D),
    "f_UT": (D, NB), "f_rWT": (D, NB), "f_rV": (NB, D), "f_rb": (1, NB),
    "f_lng": (1, D), "f_lnb": (1, D), "f_gam": (1, D),
    "f_U": (NB, D), "f_V": (NB, D), "f_w1T": (D, H1), "f_w2T": (H1, D),
}

_SCRATCH_SHAPES = {
    "B_win": (D, 2 * E), "B_wxp": (E, 128), "B_wdt": (DTR, E), "B_wout": (E, D),
    "B_gwT": (D, NEXP),
    "B_wq": (D, D), "B_wk": (D, D), "B_wv": (D, D), "B_wd": (D, D),
    "B_UT": (D, NB), "B_rWT": (D, NB), "B_rV": (NB, D),
    "B_w1T": (D, H1), "B_w2T": (H1, D),
}


def _build(emits, TC=256):
    nc = bass.Bass("TRN2", target_bir_lowering=False, debug=False)
    x_in = nc.dram_tensor("x_res_in", [T, D], F32, kind="ExternalInput").ap()
    x_out = nc.dram_tensor("x_res_out", [T, D], F32, kind="ExternalOutput").ap()
    dram = {k: nc.dram_tensor(k, list(v), F32, kind="ExternalInput").ap()
            for k, v in _DRAM_SHAPES.items()}
    scratch = {k: nc.dram_tensor(k, list(v), BF16, kind="Internal").ap()
               for k, v in _SCRATCH_SHAPES.items()}
    with ExitStack() as st:
        X = st.enter_context(nc.sbuf_tensor("X", [128, (T // 128) * D], F32))
        sems = {}
        for e in ENGS:
            sems[("c", e)] = st.enter_context(nc.semaphore(f"semc_{e}"))
            sems[("d", e)] = st.enter_context(nc.semaphore(f"semd_{e}"))
        blk = st.enter_context(nc.Block())
        p = P(nc)
        c = make_consts(p, st)
        p.barrier()
        pc_f = [st.enter_context(nc.sbuf_tensor(f"pc_f{i}", [128, 1024], F32)) for i in range(2)]
        pc_b = [st.enter_context(nc.sbuf_tensor(f"pc_b{i}", [128, 1024], BF16)) for i in range(2)]
        p.dma(X[:, :].rearrange("p (tt d) -> p tt d", d=D),
              x_in.rearrange("(tt p) d -> p tt d", p=128),
              wr=[("X", tt) for tt in range(T // 128)], q="gpsimd")
        emit_precast(p, st, dram, scratch, pc_f, pc_b)
        for em in emits:
            with ExitStack() as sti:
                if em is emit_mamba:
                    em(p, sti, c, X, dram, scratch, TC=TC)
                else:
                    em(p, sti, c, X, dram, scratch)
            p.barrier()
        p.dma(x_out.rearrange("(tt p) d -> p tt d", p=128),
              X[:, :].rearrange("p (tt d) -> p tt d", d=D),
              rd=[("X", tt) for tt in range(T // 128)], wr=[("OUT",)], q="gpsimd")
        p.finalize(sems)
    return nc


def build_mamba_program(TC=256):
    return _build([emit_mamba], TC=TC)


def build_moe_program():
    return _build([emit_moe])


def build_attn_program():
    return _build([emit_attn])


def build_ffn1_program():
    return _build([emit_ffn1])


def build_fused_program(TC=256):
    return _build([emit_mamba, emit_moe, emit_attn, emit_ffn1], TC=TC)


_PROGS = {}


def _all_host(inp):
    h = {}
    h.update(_mamba_host(inp))
    h.update(_moe_host(inp))
    h.update(_attn_host(inp))
    h.update(_ffn1_host(inp))
    return h


def kernel(**inputs):
    inp = {k: np.asarray(v) for k, v in inputs.items()}
    x = np.ascontiguousarray(inp["L0_attn_x"], dtype=np.float32)
    if "fused" not in _PROGS:
        _PROGS["fused"] = build_fused_program(TC=256)
    host = _all_host(inp)
    in_maps = [dict(host, x_res_in=np.ascontiguousarray(x[ci])) for ci in range(8)]
    res = run_bass_kernel_spmd(_PROGS["fused"], in_maps, core_ids=list(range(8)))
    return np.stack([np.asarray(r["x_res_out"]) for r in res.results]).astype(np.float32)


# version 09
# speedup vs baseline: 1.3196x; 1.0089x over previous
import numpy as np
import concourse.bass as bass
import concourse.mybir as mybir
from concourse.bass_utils import run_bass_kernel_spmd
from contextlib import ExitStack

F32 = mybir.dt.float32
BF16 = mybir.dt.bfloat16
AF = mybir.ActivationFunctionType
ALU = mybir.AluOpType
AX = mybir.AxisListType

T = 2048
D = 1024
E = 2048
NST = 16
DTR = 64
NEXP = 8
HMOE = 2048
NH = 16
HD = 64
NB = 512
H1 = 4096
DT = D // 128
ET = E // 128

LN_EPS = 1e-5
EPS_ATTN = 1e-6
EPS_LN_ATTN = 1e-12
FFN_EPS = 1e-6
TAU = 10.0

ENGS = ("tensor", "vector", "scalar", "gpsimd", "sync")


class _Node:
    __slots__ = ("eng", "fn", "inc", "deps", "need", "val", "dma", "sid")

    def __init__(self, eng, fn, inc, dma):
        self.eng = eng
        self.fn = fn
        self.inc = inc
        self.deps = []
        self.need = False
        self.val = 0
        self.dma = dma
        self.sid = ("d", eng) if dma else ("c", eng)


class P:

    def __init__(self, nc):
        self.nc = nc
        self.nodes = []
        self.res = {}
        self.cvals = None
        self.cmap = {}
        self.frozen_consts = False
        self.last_dma = {}

    def emit(self, eng_name, fn, rd=(), wr=(), inc=1, dma=False):
        node = _Node(eng_name, fn, inc, dma)
        dep = {}

        def add_dep(pn):
            if pn is node:
                return
            if pn.eng == eng_name == "tensor" and not (pn.dma or dma):
                return
            dep[id(pn)] = pn
        for k in rd:
            e = self.res.get(k)
            if e and e[0] is not None:
                add_dep(e[0])
        for k in wr:
            e = self.res.get(k)
            if e:
                if e[0] is not None:
                    add_dep(e[0])
                for rn in e[1].values():
                    add_dep(rn)
        if dma:
            prev = self.last_dma.get(eng_name)
            if prev is not None:
                add_dep(prev)
            self.last_dma[eng_name] = node
            node.need = True
        node.deps = list(dep.values())
        for pn in node.deps:
            pn.need = True
        for k in rd:
            e = self.res.setdefault(k, [None, {}])
            e[1][eng_name] = node
        for k in wr:
            self.res[k] = [node, {}]
        self.nodes.append(node)
        return node

    def barrier(self):
        self.nodes.append("BARRIER")

    def finalize(self, sems):
        SIDS = [("c", e) for e in ENGS] + [("d", e) for e in ENGS]
        counts = {s: 0 for s in SIDS}
        last = {s: None for s in SIDS}
        for nd in self.nodes:
            if nd == "BARRIER":
                for s in SIDS:
                    if last[s] is not None:
                        last[s].need = True
                continue
            last[nd.sid] = nd
        for s in SIDS:
            if last[s] is not None:
                last[s].need = True
        for nd in self.nodes:
            if nd == "BARRIER":
                continue
            if nd.need:
                counts[nd.sid] += nd.inc
                nd.val = counts[nd.sid]
        lastv = {s: 0 for s in SIDS}
        waited = {e: {s: 0 for s in SIDS} for e in ENGS}
        for nd in self.nodes:
            if nd == "BARRIER":
                snap = dict(lastv)
                for e in ENGS:
                    eng = getattr(self.nc, e)
                    for s in SIDS:
                        if s != ("c", e) and snap[s] > waited[e][s]:
                            eng.wait_ge(sems[s], snap[s])
                            waited[e][s] = snap[s]
                continue
            eng = getattr(self.nc, nd.eng)
            need_wait = {}
            for pn in nd.deps:
                if pn.val > waited[nd.eng].get(pn.sid, 0):
                    need_wait[pn.sid] = max(need_wait.get(pn.sid, 0), pn.val)
            for s, v in need_wait.items():
                eng.wait_ge(sems[s], v)
                waited[nd.eng][s] = v
            inst = nd.fn(eng)
            if nd.need:
                inst.then_inc(sems[nd.sid], nd.inc)
                lastv[nd.sid] = nd.val
        for s in SIDS:
            if s != ("c", "sync") and lastv[s] > waited["sync"][s]:
                self.nc.sync.wait_ge(sems[s], lastv[s])

    def dma(self, out, in_, rd=(), wr=(), q="sync"):
        self.emit(q, lambda e: e.dma_start(out=out, in_=in_), rd=rd, wr=wr,
                  inc=16, dma=True)

    def mm(self, out, lhsT, rhs, rd=(), wr=(), start=True, stop=True):
        self.emit("tensor", lambda e: e.matmul(out, lhsT, rhs, start=start, stop=stop),
                  rd=rd, wr=wr)

    def trans(self, out, in_, ident, rd=(), wr=()):
        self.emit("tensor", lambda e: e.transpose(out, in_, ident), rd=rd, wr=wr)

    def act(self, out, in_, func, rd=(), wr=(), bias=0.0, scale=1.0, accum_out=None):
        self.emit("scalar", lambda e: e.activation(out, in_, func, bias=bias,
                                                   scale=scale, accum_out=accum_out),
                  rd=rd, wr=wr)

    def cp(self, out, in_, rd=(), wr=(), eng="vector"):
        if eng == "scalar":
            self.emit(eng, lambda e: e.copy(out, in_), rd=rd, wr=wr)
        else:
            self.emit(eng, lambda e: e.tensor_copy(out, in_), rd=rd, wr=wr)

    def tt(self, out, a, b, op, rd=(), wr=(), eng="vector"):
        self.emit(eng, lambda e: e.tensor_tensor(out, a, b, op), rd=rd, wr=wr)

    def ts(self, out, in0, s1, op0, s2=None, op1=ALU.bypass, rd=(), wr=(), eng="vector"):
        if s2 is None:
            self.emit(eng, lambda e: e.tensor_single_scalar(out, in0, s1, op0),
                      rd=rd, wr=wr)
        else:
            self.emit(eng, lambda e: e.tensor_scalar(out, in0, s1, s2, op0=op0, op1=op1),
                      rd=rd, wr=wr)

    def stt(self, out, in0, scalar, in1, op0, op1, rd=(), wr=(), eng="vector"):
        self.emit(eng, lambda e: e.scalar_tensor_tensor(out, in0, scalar, in1,
                                                        op0=op0, op1=op1), rd=rd, wr=wr)

    def scan(self, out, d0, d1, init, rd=(), wr=(), eng="vector"):
        self.emit(eng, lambda e: e.tensor_tensor_scan(out, d0, d1, init,
                                                      op0=ALU.mult, op1=ALU.add),
                  rd=rd, wr=wr)

    def red(self, out, in_, op, rd=(), wr=(), eng="vector"):
        self.emit(eng, lambda e: e.tensor_reduce(out, in_, axis=AX.X, op=op),
                  rd=rd, wr=wr)

    def max8(self, out, in_, rd=(), wr=()):
        self.emit("vector", lambda e: e.max(out, in_), rd=rd, wr=wr)

    def recip(self, out, in_, rd=(), wr=()):
        self.emit("vector", lambda e: e.reciprocal(out, in_), rd=rd, wr=wr)

    def memset(self, ap, v, rd=(), wr=(), eng="vector"):
        self.emit(eng, lambda e: e.memset(ap, v), rd=rd, wr=wr)

    def iota(self, ap, pattern, rd=(), wr=(), base=0, mult=0):
        self.emit("gpsimd", lambda e: e.iota(ap, pattern=pattern, base=base,
                                             channel_multiplier=mult,
                                             allow_small_or_imprecise_dtypes=True),
                  rd=rd, wr=wr)

    def ccol(self, v):
        v = float(v)
        if v not in self.cmap:
            assert not self.frozen_consts, f"const {v} after freeze"
            i = len(self.cmap)
            self.cmap[v] = i
            self.memset(self.cvals[:, i:i + 1], v)
        return self.cvals[:, self.cmap[v]:self.cmap[v] + 1]


def make_consts(p, st):
    nc = p.nc
    c = {}
    c["ones_bf"] = st.enter_context(nc.sbuf_tensor("ones_bf", [1, 128], BF16))
    c["ident_f"] = st.enter_context(nc.sbuf_tensor("ident_f", [128, 128], F32))
    c["ident_bf"] = st.enter_context(nc.sbuf_tensor("ident_bf", [128, 128], BF16))
    c["onescol_bf"] = st.enter_context(nc.sbuf_tensor("onescol_bf", [128, 1], BF16))
    p.cvals = st.enter_context(nc.sbuf_tensor("cvals", [128, 16], F32))
    p.memset(c["ones_bf"][:, :], 1.0)
    p.memset(c["onescol_bf"][:, :], 1.0)
    p.iota(c["ident_f"][:, :], pattern=[[1, 128]], base=0, mult=-1, wr=[("iota",)])
    p.ts(c["ident_f"][:, :], c["ident_f"][:, :], 0.0, ALU.is_equal, rd=[("iota",)], wr=[("iota",)])
    p.cp(c["ident_bf"][:, :], c["ident_f"][:, :], rd=[("iota",)])
    for v in (LN_EPS, EPS_LN_ATTN, EPS_ATTN, -0.5, 0.5):
        p.ccol(v)
    p.frozen_consts = True
    return c



PRECAST = [
    ("m_win", D, 2 * E, "B_win"),
    ("m_wxp", E, 128, "B_wxp"),
    ("m_wdt", DTR, E, "B_wdt"),
    ("m_wout", E, D, "B_wout"),
    ("g_wT", D, NEXP, "B_gwT"),
    ("a_wq", D, D, "B_wq"),
    ("a_wk", D, D, "B_wk"),
    ("a_wv", D, D, "B_wv"),
    ("a_wd", D, D, "B_wd"),
    ("f_UT", D, NB, "B_UT"),
    ("f_rWT", D, NB, "B_rWT"),
    ("f_rV", NB, D, "B_rV"),
    ("f_w1T", D, H1, "B_w1T"),
    ("f_w2T", H1, D, "B_w2T"),
]

_PC_ENGS = ("gpsimd",)


def emit_precast(p, st, dram, scratch, sf, sb):
    i = 0
    for key, R, C, bname in PRECAST:
        src = dram[key]
        if len(src.shape) == 3:
            src = src.rearrange("a b c -> (a b) c")
        dst = scratch[bname]
        if len(dst.shape) == 3:
            dst = dst.rearrange("a b c -> (a b) c")
        for r0 in range(0, R, 128):
            rr = min(128, R - r0)
            for c0 in range(0, C, 1024):
                cc = min(1024, C - c0)
                g = i % 2
                p.dma(sf[g][0:rr, 0:cc], src[r0:r0 + rr, c0:c0 + cc],
                      wr=[("pcf", g)], q="sync" if g == 0 else "gpsimd")
                p.cp(sb[g][0:rr, 0:cc], sf[g][0:rr, 0:cc],
                     rd=[("pcf", g)], wr=[("pcb", g)], eng="gpsimd")
                p.dma(dst[r0:r0 + rr, c0:c0 + cc], sb[g][0:rr, 0:cc],
                      rd=[("pcb", g)], wr=[("B", bname)], q="scalar")
                i += 1


def emit_ln(p, x_ap, out_ap, stats, scr, eps, rd_x, wr_out, tag, sckey=None):
    Df = x_ap.shape[-1]
    ks = ("st",) + tag
    kc = sckey if sckey is not None else ("sc",) + tag
    mu = stats[:, 0:1]
    sq = stats[:, 1:2]
    var = stats[:, 2:3]
    rstd = stats[:, 3:4]
    p.red(mu, x_ap, ALU.add, rd=list(rd_x), wr=[ks])
    p.ts(mu, mu, 1.0 / Df, ALU.mult, rd=[ks], wr=[ks])
    p.act(scr, x_ap, AF.Square, accum_out=sq, rd=list(rd_x), wr=[kc, ks])
    p.ts(var, sq, 1.0 / Df, ALU.mult, rd=[ks], wr=[ks])
    p.tt(stats[:, 4:5], mu, mu, ALU.mult, rd=[ks], wr=[ks])
    p.tt(var, var, stats[:, 4:5], ALU.subtract, rd=[ks], wr=[ks])
    p.act(stats[:, 5:6], var, AF.Ln, bias=p.ccol(eps), rd=[ks], wr=[ks])
    p.act(rstd, stats[:, 5:6], AF.Exp, scale=p.ccol(-0.5), rd=[ks], wr=[ks])
    p.ts(out_ap, x_ap, mu, ALU.subtract, rstd, ALU.mult,
         rd=list(rd_x) + [ks], wr=list(wr_out))


def emit_tr(p, dst_ap, src_ap, ps_tr, ident_bf, rd_src, wr_dst, pkey):
    p.trans(ps_tr, src_ap, ident_bf[:, :], rd=list(rd_src), wr=[pkey])
    p.cp(dst_ap, ps_tr, rd=[pkey], wr=list(wr_dst))


def emit_tr8(p, dst3d, src_ap, ps_trB, ident_bf, rd_src, wr_dst, pkey, nk=8):
    for kd in range(nk):
        p.trans(ps_trB[:, kd * 128:(kd + 1) * 128], src_ap[:, kd * 128:(kd + 1) * 128],
                ident_bf[:, :], rd=list(rd_src), wr=[pkey])
    p.cp(dst3d, ps_trB[:, 0:nk * 128].rearrange("p (k f) -> p k f", f=128),
         rd=[pkey], wr=list(wr_dst), eng="scalar")



def emit_mamba(p, st, c, X, dram, scratch, TC=256):
    nc = p.nc
    NCH = T // TC
    TSUB = TC // 128

    _xnT = st.enter_context(nc.sbuf_tensor("mb_xnT", [128, DT * TC], BF16))
    xnT = [_xnT, _xnT]
    xi = st.enter_context(nc.sbuf_tensor("mb_xi", [128, ET * (TC + 3)], F32))
    zs = st.enter_context(nc.sbuf_tensor("mb_zs", [128, ET * TC], BF16))
    delta = st.enter_context(nc.sbuf_tensor("mb_delta", [128, ET * TC], BF16))
    wdu = st.enter_context(nc.sbuf_tensor("mb_w", [128, ET * TC], BF16))
    yac = st.enter_context(nc.sbuf_tensor("mb_y", [128, ET * TC], F32))
    hall = st.enter_context(nc.sbuf_tensor("mb_hall", [128, ET * TC], F32))
    dBuall = st.enter_context(nc.sbuf_tensor("mb_dbu", [128, ET * TC], BF16))
    ygbf = hall[:, 0:ET * TC // 2].bitcast(BF16)
    hstate = st.enter_context(nc.sbuf_tensor("mb_h", [128, ET * NST], F32))
    stash = st.enter_context(nc.sbuf_tensor("mb_stash", [128, ET * 3], F32))
    stats = [st.enter_context(nc.sbuf_tensor(f"mb_stats{i}", [128, 8], F32)) for i in range(2)]
    scrA = st.enter_context(nc.sbuf_tensor("mb_scrA", [128, 1024], F32))
    xnbf_t = [st.enter_context(nc.sbuf_tensor(f"mb_xnbf{i}", [128, 1024], BF16)) for i in range(2)]
    cbuf0 = st.enter_context(nc.sbuf_tensor("mb_cb0", [128, 2 * TC], F32))
    cbuf1 = st.enter_context(nc.sbuf_tensor("mb_cb1", [128, 2 * TC], F32))
    ubf = [st.enter_context(nc.sbuf_tensor(f"mb_ubf{i}", [128, TC], BF16)) for i in range(2)]
    dtlr_bf = st.enter_context(nc.sbuf_tensor("mb_dtlr", [64, TC], BF16))
    Bt_bf = st.enter_context(nc.sbuf_tensor("mb_Bt", [16, TC], BF16))
    Ct_bf = st.enter_context(nc.sbuf_tensor("mb_Ct", [16, TC], BF16))
    Bb = [st.enter_context(nc.sbuf_tensor(f"mb_Bb{i}", [128, TC], BF16)) for i in range(2)]
    Cb = [st.enter_context(nc.sbuf_tensor(f"mb_Cb{i}", [128, TC], BF16)) for i in range(2)]
    dAb = [st.enter_context(nc.sbuf_tensor(f"mb_dA{i}", [128, TC], F32)) for i in range(2)]
    sp0 = st.enter_context(nc.sbuf_tensor("mb_sp0", [128, TC], F32))
    Asb = st.enter_context(nc.sbuf_tensor("mb_A", [128, ET * NST], F32))
    cwsb = st.enter_context(nc.sbuf_tensor("mb_cw", [128, ET * 4], F32))
    cbsb = st.enter_context(nc.sbuf_tensor("mb_cbv", [128, ET], F32))
    bdtsb = st.enter_context(nc.sbuf_tensor("mb_bdt", [128, ET], F32))
    dsksb = st.enter_context(nc.sbuf_tensor("mb_dsk", [128, ET], F32))
    wxp = st.enter_context(nc.sbuf_tensor("mb_wxp", [128, ET * 128], BF16))
    wdt = st.enter_context(nc.sbuf_tensor("mb_wdt", [64, E], BF16))
    sel = st.enter_context(nc.sbuf_tensor("mb_sel", [16, NST * 128], BF16))
    win_bf = [st.enter_context(nc.sbuf_tensor(f"mb_winb{i}", [128, DT * 128], BF16)) for i in range(2)]
    wo_et = win_bf
    ps_trB = st.enter_context(nc.psum_tensor("mb_ptr", [128, 1024], BF16))
    ps_m0 = st.enter_context(nc.psum_tensor("mb_pm0", [128, 256], F32))
    ps_m1 = st.enter_context(nc.psum_tensor("mb_pm1", [128, 256], F32))
    ps_xd = st.enter_context(nc.psum_tensor("mb_pxd", [128, 512], F32))
    ps_o = st.enter_context(nc.psum_tensor("mb_po", [128, 2048], F32))
    assert TC == 256
    ps_mm = [ps_m0[:, :], ps_m1[:, :]]
    ps_d = ps_xd[:, TC:2 * TC]
    ps_B = ps_o[:, 0:TC]
    ps_C = ps_o[:, TC:2 * TC]

    p.dma(Asb[:, :], dram["m_A"], wr=[("mA",)])
    p.dma(cwsb[:, :], dram["m_cw"], wr=[("mcw",)])
    p.dma(cbsb[:, :], dram["m_cb"], wr=[("mcb",)])
    p.dma(bdtsb[:, :], dram["m_bdt"], wr=[("mbdt",)])
    p.dma(dsksb[:, :], dram["m_dsk"], wr=[("mdsk",)])
    p.dma(wxp[:, :].rearrange("p (et cc) -> p et cc", cc=128),
          scratch["B_wxp"].rearrange("(et p) cc -> p et cc", p=128),
          rd=[("B", "B_wxp")], wr=[("mwxp",)], q="scalar")
    p.dma(wdt[:, :], scratch["B_wdt"], rd=[("B", "B_wdt")], wr=[("mwdt",)], q="scalar")
    p.act(Asb[:, :], Asb[:, :], AF.Exp, rd=[("mA",)], wr=[("mA",)])
    p.ts(Asb[:, :], Asb[:, :], -1.0, ALU.mult, rd=[("mA",)], wr=[("mA",)])
    for half in range(2):
        p.iota(scrA[0:16, 0:1024], pattern=[[1, 8], [0, 128]], base=-half * 8, mult=-1,
               wr=[("scA",)])
        p.ts(scrA[0:16, 0:1024], scrA[0:16, 0:1024], 0.0, ALU.is_equal,
             rd=[("scA",)], wr=[("scA",)])
        p.cp(sel[:, half * 1024:(half + 1) * 1024], scrA[0:16, 0:1024],
             rd=[("scA",)], wr=[("msel",)])
    p.memset(hstate[:, :], 0.0, wr=[("hst", n) for n in range(NST)])

    xi_v = lambda et: xi[:, et * (TC + 3):(et + 1) * (TC + 3)]

    for ch in range(NCH):
        cp_ = ch % 2
        for tsub in range(TSUB):
            tt = ch * TSUB + tsub
            xt = X[:, tt * D:(tt + 1) * D]
            emit_ln(p, xt, xnbf_t[tsub][:, :], stats[tsub], scrA[:, 0:1024],
                    LN_EPS, rd_x=[("X", tt)], wr_out=[("xnbf", tsub)], tag=("mln", tsub),
                    sckey=("scA",))
            dst3d = xnT[cp_][:, :].rearrange("p (k t) -> p k t", t=TC)[:, :, tsub * 128:tsub * 128 + 128]
            emit_tr8(p, dst3d, xnbf_t[tsub][:, :], ps_trB, c["ident_bf"],
                     rd_src=[("xnbf", tsub)],
                     wr_dst=[("xnT", cp_, kd) for kd in range(DT)], pkey=("pstr",))
        for ec in range(32):
            wp = ec % 2
            p.dma(win_bf[wp][:, :].rearrange("p (dc e) -> p dc e", e=128),
                  scratch["B_win"][:, ec * 128:(ec + 1) * 128]
                  .rearrange("(dc p) e -> p dc e", p=128),
                  rd=[("B", "B_win")], wr=[("winb", wp)], q="sync")
            for dc in range(DT):
                p.mm(ps_mm[wp], win_bf[wp][:, dc * 128:(dc + 1) * 128],
                     xnT[cp_][:, dc * TC:(dc + 1) * TC],
                     rd=[("winb", wp), ("xnT", cp_, dc)], wr=[("psmm", wp)],
                     start=(dc == 0), stop=(dc == DT - 1))
            if ec < 16:
                p.act(xi_v(ec)[:, 3:3 + TC], ps_mm[wp], AF.Identity,
                      rd=[("psmm", wp)], wr=[("xi", ec)])
            else:
                p.act(zs[:, (ec - 16) * TC:(ec - 15) * TC], ps_mm[wp], AF.Silu,
                      rd=[("psmm", wp)], wr=[("zs", ec - 16)])
        for et in range(ET):
            xv = xi_v(et)
            up = et % 2
            if ch == 0:
                p.memset(xv[:, 0:3], 0.0, wr=[("xi", et)])
            else:
                p.cp(xv[:, 0:3], stash[:, et * 3:(et + 1) * 3],
                     rd=[("stash", et)], wr=[("xi", et)])
            cw = lambda k: cwsb[:, et * 4 + k: et * 4 + k + 1]
            ce = "vector"
            ck = ("cb", et % 2)
            cb0 = cbuf0[:, (et % 2) * TC:(et % 2 + 1) * TC]
            cb1 = cbuf1[:, (et % 2) * TC:(et % 2 + 1) * TC]
            p.ts(cb0, xv[:, 0:TC], cw(0), ALU.mult, rd=[("xi", et), ("mcw",)], wr=[ck], eng=ce)
            p.stt(cb1, xv[:, 1:1 + TC], cw(1), cb0, ALU.mult, ALU.add,
                  rd=[("xi", et), ck], wr=[ck], eng=ce)
            p.stt(cb0, xv[:, 2:2 + TC], cw(2), cb1, ALU.mult, ALU.add,
                  rd=[("xi", et), ck], wr=[ck], eng=ce)
            p.stt(cb1, xv[:, 3:3 + TC], cw(3), cb0, ALU.mult, ALU.add,
                  rd=[("xi", et), ck], wr=[ck], eng=ce)
            p.cp(stash[:, et * 3:(et + 1) * 3], xv[:, TC:TC + 3],
                 rd=[("xi", et)], wr=[("stash", et)], eng=ce)
            p.act(xv[:, 3:3 + TC], cb1, AF.Silu, bias=cbsb[:, et:et + 1],
                  rd=[ck, ("stash", et), ("mcb",)], wr=[("xi", et)])
            p.cp(ubf[up][:, :], xv[:, 3:3 + TC], rd=[("xi", et)], wr=[("ubf", up)], eng="gpsimd")
            p.act(yac[:, et * TC:(et + 1) * TC], xv[:, 3:3 + TC], AF.Copy,
                  scale=dsksb[:, et:et + 1], rd=[("xi", et), ("mdsk",)], wr=[("yac", et)])
            p.mm(ps_xd[:, 0:TC], wxp[:, et * 128:(et + 1) * 128], ubf[up][:, :],
                 rd=[("ubf", up), ("mwxp",)], wr=[("psxd",)],
                 start=(et == 0), stop=(et == ET - 1))
        p.act(Bt_bf[:, :], ps_xd[0:16, 0:TC], AF.Identity, rd=[("psxd",)], wr=[("Bt",)])
        p.act(Ct_bf[:, :], ps_xd[32:48, 0:TC], AF.Identity, rd=[("psxd",)], wr=[("Ct",)])
        p.act(dtlr_bf[:, :], ps_xd[64:128, 0:TC], AF.Identity, rd=[("psxd",)], wr=[("dtlr",)])
        for dt in range(ET):
            dv = delta[:, dt * TC:(dt + 1) * TC]
            p.mm(ps_d, wdt[:, dt * 128:(dt + 1) * 128], dtlr_bf[:, :],
                 rd=[("dtlr",), ("mwdt",)], wr=[("psxd",)])
            p.act(sp0[:, :], ps_d, AF.Identity, bias=bdtsb[:, dt:dt + 1],
                  rd=[("psxd",), ("mbdt",)], wr=[("sp0",)])
            p.act(cbuf1[:, 0:TC], sp0[:, :], AF.Abs, rd=[("sp0",), ("cb", 0)], wr=[("cb", 0)])
            p.act(cbuf0[:, 0:TC], cbuf1[:, 0:TC], AF.Exp, scale=-1.0, rd=[("cb", 0)], wr=[("cb", 0)])
            p.act(cbuf1[:, 0:TC], cbuf0[:, 0:TC], AF.Ln, bias=1.0, rd=[("cb", 0)], wr=[("cb", 0)])
            p.stt(dv, sp0[:, :], 0.0, cbuf1[:, 0:TC], ALU.max, ALU.add,
                  rd=[("sp0",), ("cb", 0)], wr=[("delta", dt)])
            p.tt(wdu[:, dt * TC:(dt + 1) * TC], dv, xi_v(dt)[:, 3:3 + TC], ALU.mult,
                 rd=[("delta", dt), ("xi", dt)], wr=[("w", dt)])
        hkeys = [("hall", d_) for d_ in range(ET)]
        for n in range(NST):
            np_ = n % 2
            p.mm(ps_B, sel[:, n * 128:(n + 1) * 128], Bt_bf[:, :],
                 rd=[("Bt",), ("msel",)], wr=[("pso",)])
            p.act(Bb[np_][:, :], ps_B, AF.Identity, rd=[("pso",)], wr=[("Bb", np_)])
            p.mm(ps_C, sel[:, n * 128:(n + 1) * 128], Ct_bf[:, :],
                 rd=[("Ct",), ("msel",)], wr=[("pso",)])
            p.act(Cb[np_][:, :], ps_C, AF.Identity, rd=[("pso",)], wr=[("Cb", np_)])
            p.tt(dBuall[:, :].rearrange("p (d t) -> p d t", t=TC),
                 wdu[:, :].rearrange("p (d t) -> p d t", t=TC),
                 Bb[np_][:, :].rearrange("p (o t) -> p o t", o=1).broadcast_to((128, ET, TC)),
                 ALU.mult,
                 rd=[("w", d_) for d_ in range(ET)] + [("Bb", np_)], wr=[("dbu",)])
            for dt in range(ET):
                a3 = dt % 2
                p.act(dAb[a3][:, :], delta[:, dt * TC:(dt + 1) * TC], AF.Exp,
                      scale=Asb[:, dt * NST + n: dt * NST + n + 1],
                      rd=[("delta", dt), ("mA",)], wr=[("dA", a3)])
                p.scan(hall[:, dt * TC:(dt + 1) * TC], dAb[a3][:, :],
                       dBuall[:, dt * TC:(dt + 1) * TC],
                       hstate[:, dt * NST + n: dt * NST + n + 1],
                       rd=[("dA", a3), ("dbu",), ("hst", n)], wr=[("hall", dt)])
            p.cp(hstate[:, :].rearrange("p (d n2) -> p d n2", n2=NST)[:, :, n:n + 1],
                 hall[:, :].rearrange("p (d t) -> p d t", t=TC)[:, :, TC - 1:TC],
                 rd=hkeys, wr=[("hst", n)], eng="gpsimd")
            p.tt(hall[:, :].rearrange("p (d t) -> p d t", t=TC),
                 hall[:, :].rearrange("p (d t) -> p d t", t=TC),
                 Cb[np_][:, :].rearrange("p (o t) -> p o t", o=1).broadcast_to((128, ET, TC)),
                 ALU.mult,
                 rd=hkeys + [("Cb", np_), ("hst", n)], wr=hkeys)
            p.tt(yac[:, :], yac[:, :], hall[:, :], ALU.add,
                 rd=hkeys + [("yac", d_) for d_ in range(ET)],
                 wr=[("yac", d_) for d_ in range(ET)], eng="gpsimd")
        for et in range(ET):
            yv = yac[:, et * TC:(et + 1) * TC]
            p.tt(ygbf[:, et * TC:(et + 1) * TC], yv, zs[:, et * TC:(et + 1) * TC],
                 ALU.mult, rd=[("yac", et), ("zs", et)],
                 wr=[("yg", et), ("hall", et // 2)])
        for et in range(ET):
            wp = et % 2
            p.dma(wo_et[wp][:, :], scratch["B_wout"][et * 128:(et + 1) * 128, :],
                  rd=[("B", "B_wout")], wr=[("winb", wp)], q="scalar")
            for tsub in range(TSUB):
                for oh in range(2):
                    p.mm(ps_o[:, (tsub * 2 + oh) * 512:(tsub * 2 + oh + 1) * 512],
                         ygbf[:, et * TC + tsub * 128: et * TC + tsub * 128 + 128],
                         wo_et[wp][:, oh * 512:(oh + 1) * 512],
                         rd=[("yg", et), ("hall", et // 2), ("winb", wp)], wr=[("pso",)],
                         start=(et == 0), stop=(et == ET - 1))
        for tsub in range(TSUB):
            tt = ch * TSUB + tsub
            for oh in range(2):
                xs = X[:, tt * D + oh * 512: tt * D + (oh + 1) * 512]
                p.tt(xs, xs, ps_o[:, (tsub * 2 + oh) * 512:(tsub * 2 + oh + 1) * 512],
                     ALU.add, rd=[("X", tt), ("pso",)], wr=[("X", tt)])



def emit_moe(p, st, c, X, dram, scratch):
    nc = p.nc
    TTI = T // 128
    HT = HMOE // 128
    xnT = st.enter_context(nc.sbuf_tensor("mo_xnT", [128, DT * T], BF16))
    w1bf = st.enter_context(nc.sbuf_tensor("mo_w1", [128, DT * HMOE], BF16))
    w2bf = st.enter_context(nc.sbuf_tensor("mo_w2", [128, HT * D], BF16))
    hbf = st.enter_context(nc.sbuf_tensor("mo_h", [128, HT * 256], BF16))
    gwts = st.enter_context(nc.sbuf_tensor("mo_gw", [128, TTI * NEXP], F32))
    sg_ = [st.enter_context(nc.sbuf_tensor(f"mo_sg{i}", [128, NEXP], F32)) for i in range(2)]
    m8_ = [st.enter_context(nc.sbuf_tensor(f"mo_m8{i}", [128, 8], F32)) for i in range(2)]
    mk_ = [st.enter_context(nc.sbuf_tensor(f"mo_mask{i}", [128, NEXP], F32)) for i in range(2)]
    stats = [st.enter_context(nc.sbuf_tensor(f"mo_stats{i}", [128, 8], F32)) for i in range(2)]
    scrA = st.enter_context(nc.sbuf_tensor("mo_scrA", [128, 2048], F32))
    scrB = st.enter_context(nc.sbuf_tensor("mo_scrB", [128, 2048], F32))
    rowst = st.enter_context(nc.sbuf_tensor("mo_rowst", [1, 512], F32))
    _xb = st.enter_context(nc.sbuf_tensor("mo_xnbf", [128, 1024], BF16))
    xnbf_t = [_xb, _xb]
    gwT_bf = st.enter_context(nc.sbuf_tensor("mo_gwT", [128, DT * NEXP], BF16))
    gb_bf = st.enter_context(nc.sbuf_tensor("mo_gb", [1, NEXP], BF16))
    b1sb = st.enter_context(nc.sbuf_tensor("mo_b1", [128, NEXP * HT], F32))
    b2bf = [st.enter_context(nc.sbuf_tensor("mo_b2", [1, D], BF16))] * 2
    ps_trB = st.enter_context(nc.psum_tensor("mo_ptr", [128, 1024], BF16))
    ps_a = st.enter_context(nc.psum_tensor("mo_pa", [128, 512], F32))
    ps_h = [st.enter_context(nc.psum_tensor(f"mo_ph{i}", [128, 512], F32)) for i in range(2)]
    ps_o = [st.enter_context(nc.psum_tensor(f"mo_po{i}", [128, 512], F32)) for i in range(2)]
    ps_g = ps_a[:, 0:NEXP]

    p.dma(gwT_bf[:, :].rearrange("p (dc e) -> p dc e", e=NEXP),
          scratch["B_gwT"].rearrange("(dc p) e -> p dc e", p=128),
          rd=[("B", "B_gwT")], wr=[("gwT",)], q="scalar")
    p.dma(rowst[0:1, 0:NEXP], dram["g_b"], wr=[("sAb",)])
    p.cp(gb_bf[:, :], rowst[0:1, 0:NEXP], rd=[("sAb",)], wr=[("gb",)])
    p.dma(b1sb[:, :].rearrange("p (e h) -> p e h", h=HT), dram["g_b1"], wr=[("b1",)])
    for tt in range(TTI):
        tp = tt % 2
        xt = X[:, tt * D:(tt + 1) * D]
        emit_ln(p, xt, xnbf_t[tp][:, :], stats[tp], scrA[:, 0:1024],
                LN_EPS, rd_x=[("X", tt)], wr_out=[("xnbf", tp)], tag=("gln", tp),
                sckey=("sA",))
        dst3d = xnT[:, :].rearrange("p (k t) -> p k t", t=T)[:, :, tt * 128:tt * 128 + 128]
        emit_tr8(p, dst3d, xnbf_t[tp][:, :], ps_trB, c["ident_bf"],
                 rd_src=[("xnbf", tp)],
                 wr_dst=[("xnT", tt, kd) for kd in range(DT)], pkey=("pstr",))
        for kd in range(DT):
            p.mm(ps_g, xnT[:, kd * T + tt * 128: kd * T + tt * 128 + 128],
                 gwT_bf[:, kd * NEXP:(kd + 1) * NEXP],
                 rd=[("xnT", tt, kd), ("gwT",)], wr=[("psg",)],
                 start=(kd == 0), stop=False)
        p.mm(ps_g, c["ones_bf"][0:1, 0:128], gb_bf[0:1, :], rd=[("gb",)], wr=[("psg",)],
             start=False, stop=True)
        sg = sg_[tp]
        m8 = m8_[tp]
        maskb = mk_[tp]
        p.act(sg[:, :], ps_g, AF.Identity, rd=[("psg",)], wr=[("sg", tp)])
        p.red(stats[tp][:, 6:7], sg[:, :], ALU.max, rd=[("sg", tp)], wr=[("gst", tp)])
        p.ts(sg[:, :], sg[:, :], stats[tp][:, 6:7], ALU.subtract, rd=[("sg", tp), ("gst", tp)], wr=[("sg", tp)])
        p.act(sg[:, :], sg[:, :], AF.Exp, rd=[("sg", tp)], wr=[("sg", tp)])
        p.red(stats[tp][:, 6:7], sg[:, :], ALU.add, rd=[("sg", tp)], wr=[("gst", tp)])
        p.recip(stats[tp][:, 7:8], stats[tp][:, 6:7], rd=[("gst", tp)], wr=[("gst", tp)])
        p.ts(sg[:, :], sg[:, :], stats[tp][:, 7:8], ALU.mult, rd=[("sg", tp), ("gst", tp)], wr=[("sg", tp)])
        p.max8(m8[:, :], sg[:, :], rd=[("sg", tp)], wr=[("m8", tp)])
        p.ts(maskb[:, :], sg[:, :], m8[:, 1:2], ALU.is_ge, rd=[("sg", tp), ("m8", tp)], wr=[("mk", tp)])
        p.tt(gwts[:, tt * NEXP:(tt + 1) * NEXP], sg[:, :], maskb[:, :], ALU.mult,
             rd=[("sg", tp), ("mk", tp)], wr=[("gwts", tt)])
    for e in range(NEXP):
        ep = e % 2
        for dc in range(DT):
            sc, key, qq = (scrA, ("sw1", 0), "sync") if dc % 2 == 0 else (scrB, ("sw1", 1), "gpsimd")
            p.dma(sc[:, :], dram["g_w1"][e, dc * 128:(dc + 1) * 128, :],
                  rd=[key, ("sA",)], wr=[key, ("sA",)] if dc % 2 == 0 else [key], q=qq)
            p.cp(w1bf[:, dc * HMOE:(dc + 1) * HMOE], sc[:, :],
                 rd=[key], wr=[("w1", dc)], eng="vector" if dc % 2 == 0 else "gpsimd")
        for ht in range(HT):
            sc, key = (scrA[:, 0:D], ("sw1", 0)) if ht % 2 == 0 else (scrB[:, 0:D], ("sw1", 1))
            p.dma(sc, dram["g_w2"][e, ht * 128:(ht + 1) * 128, :],
                  rd=[key], wr=[key], q="scalar")
            p.cp(w2bf[:, ht * D:(ht + 1) * D], sc,
                 rd=[key], wr=[("w2", ht)], eng="vector" if ht % 2 == 0 else "gpsimd")
        for bh in range(2):
            p.dma(rowst[0:1, :], dram["g_b2"][e:e + 1, bh * 512:(bh + 1) * 512],
                  rd=[("sAb",)], wr=[("sAb",)])
            p.cp(b2bf[ep][:, bh * 512:(bh + 1) * 512], rowst[0:1, :],
                 rd=[("sAb",)], wr=[("b2", ep)])
        for tc in range(T // 256):
            for ht in range(HT):
                hp = ht % 2
                for dc in range(DT):
                    p.mm(ps_h[hp][:, :].rearrange("p (a b) -> p a b", a=2)[:, 0, 0:256]
                         if False else ps_h[hp][:, 0:256],
                         w1bf[:, dc * HMOE + ht * 128: dc * HMOE + ht * 128 + 128],
                         xnT[:, dc * T + tc * 256: dc * T + tc * 256 + 256],
                         rd=[("w1", dc)] + [("xnT", tc * 2 + q_, dc) for q_ in range(2)],
                         wr=[("psh", hp)],
                         start=(dc == 0), stop=(dc == DT - 1))
                p.act(hbf[:, ht * 256:(ht + 1) * 256], ps_h[hp][:, 0:256], AF.Relu,
                      bias=b1sb[:, e * HT + ht: e * HT + ht + 1],
                      rd=[("psh", hp), ("b1",)], wr=[("hbf", ht)])
            for tsub in range(2):
                tt = tc * 2 + tsub
                gcol = gwts[:, tt * NEXP + e: tt * NEXP + e + 1]
                for oh in range(2):
                    for ht in range(HT):
                        p.mm(ps_o[oh][:, :], hbf[:, ht * 256 + tsub * 128: ht * 256 + tsub * 128 + 128],
                             w2bf[:, ht * D + oh * 512: ht * D + oh * 512 + 512],
                             rd=[("hbf", ht), ("w2", ht)], wr=[("pso", oh)],
                             start=(ht == 0), stop=False)
                    p.mm(ps_o[oh][:, :], c["ones_bf"][0:1, 0:128], b2bf[ep][0:1, oh * 512:(oh + 1) * 512],
                         rd=[("b2", ep)], wr=[("pso", oh)], start=False, stop=True)
                    xs = X[:, tt * D + oh * 512: tt * D + (oh + 1) * 512]
                    p.stt(xs, ps_o[oh][:, :], gcol, xs, ALU.mult, ALU.add,
                          rd=[("pso", oh), ("gwts", tt), ("X", tt)], wr=[("X", tt)])



ATTN_PHASES = "ABqkvwstzcuTd"


def emit_attn(p, st, c, X, dram, scratch):
    nc = p.nc
    TTI = T // 128
    qT = st.enter_context(nc.sbuf_tensor("at_qT", [128, DT * T], BF16))
    wqkv = st.enter_context(nc.sbuf_tensor("at_wqkv", [128, 3 * DT * D], BF16))
    wd_bf = wqkv[:, 0:DT * D]
    bias_bf = st.enter_context(nc.sbuf_tensor("at_bias", [1, 4 * D], BF16))
    gb_bc = st.enter_context(nc.sbuf_tensor("at_gb", [128, 2 * D], F32))
    scr3 = st.enter_context(nc.sbuf_tensor("at_scr3", [128, 3072], F32))
    stats = [st.enter_context(nc.sbuf_tensor(f"at_stats{i}", [128, 8], F32)) for i in range(2)]
    xnbf_p = [st.enter_context(nc.sbuf_tensor(f"at_xnbf{i}", [128, 1024], BF16)) for i in range(2)]
    xnT_p = [st.enter_context(nc.sbuf_tensor(f"at_xnT{i}", [128, DT * 128], BF16)) for i in range(2)]
    qkbf = st.enter_context(nc.sbuf_tensor("at_qkbf", [128, 1024], BF16))
    vbf = st.enter_context(nc.sbuf_tensor("at_vbf", [128, 1024], BF16))
    kvbf = st.enter_context(nc.sbuf_tensor("at_kvbf", [128, NH * HD], BF16))
    kvacc = st.enter_context(nc.sbuf_tensor("at_kvacc", [64, NH * HD], F32))
    ksacc = st.enter_context(nc.sbuf_tensor("at_ksacc", [128, DT], F32))
    ksz = st.enter_context(nc.sbuf_tensor("at_ksz", [128, NH], BF16))
    ksf = st.enter_context(nc.sbuf_tensor("at_ksf", [128, DT], F32))
    zrec = st.enter_context(nc.sbuf_tensor("at_z", [128, NH], F32))
    czbf = st.enter_context(nc.sbuf_tensor("at_cz", [128, 1024], BF16))
    ctxT_t = st.enter_context(nc.sbuf_tensor("at_ctxT", [128, DT * 128], BF16))
    ps_trB = st.enter_context(nc.psum_tensor("at_ptr", [128, 1024], BF16))
    ps_w = [st.enter_context(nc.psum_tensor(f"at_pw{i}", [128, 512], F32)) for i in range(2)]
    ps_kv = st.enter_context(nc.psum_tensor("at_pkv", [64, NH * HD], F32))
    ps_small = st.enter_context(nc.psum_tensor("at_psm", [128, 512], F32))
    ps_ksc = ps_small[:, 0:DT]
    ps_z = ps_small[:, 16:16 + NH]
    ps_ctx = st.enter_context(nc.psum_tensor("at_pctx", [128, 1024], F32))
    p.memset(kvacc[:, :], 0.0, wr=[("kvacc",)])
    p.memset(ksacc[:, :], 0.0, wr=[("ksacc",)])
    efp = scr3[:, 0:1024]
    et0 = scr3[:, 1024:1536]
    et1 = scr3[:, 1536:2048]
    scrB = scr3[:, 1024:2048]
    hsn = scr3[:, 2048:3072]

    for j, wname in enumerate(("B_wq", "B_wk", "B_wv")):
        p.dma(wqkv[:, j * DT * D:(j + 1) * DT * D].rearrange("p (dc dd) -> p dc dd", dd=D),
              scratch[wname].rearrange("(dc p) dd -> p dc dd", p=128),
              rd=[("B", wname)], wr=[("wqkv", j)], q="sync")
    for j, bname in enumerate(("a_bq", "a_bk", "a_bv", "a_bd")):
        p.dma(scrB[0:1, 0:D], dram[bname], rd=[("scd",)], wr=[("scd",)])
        p.cp(bias_bf[:, j * D:(j + 1) * D], scrB[0:1, 0:D], rd=[("scd",)], wr=[("abias", j)])
    for j, gname in enumerate(("a_gamma", "a_beta")):
        p.dma(scrB[0:1, 0:D], dram[gname], rd=[("scd",)], wr=[("scd",)])
        p.cp(qkbf[0:1, 0:D], scrB[0:1, 0:D], rd=[("scd",)], wr=[("qkbf",)])
        for oh in range(2):
            p.mm(ps_w[oh][:, :], c["ones_bf"][0:1, 0:128], qkbf[0:1, oh * 512:(oh + 1) * 512],
                 rd=[("qkbf",)], wr=[("psw", oh)])
            p.act(gb_bc[:, j * D + oh * 512: j * D + (oh + 1) * 512], ps_w[oh][:, :],
                  AF.Identity, rd=[("psw", oh)], wr=[("gbbc",)])

    def proj(j, oh, pw, xnT_t, xk):
        for kd in range(DT):
            p.mm(ps_w[pw][:, :], xnT_t[:, kd * 128:(kd + 1) * 128],
                 wqkv[:, (j * DT + kd) * D + oh * 512:(j * DT + kd) * D + (oh + 1) * 512],
                 rd=[("xnTt", xk), ("wqkv", j)], wr=[("psw", pw)],
                 start=(kd == 0), stop=False)
        p.mm(ps_w[pw][:, :], c["ones_bf"][0:1, 0:128],
             bias_bf[0:1, j * D + oh * 512: j * D + (oh + 1) * 512],
             rd=[("abias", j)], wr=[("psw", pw)], start=False, stop=True)

    def elu1(dst_half, pw, wrk):
        p.ts(et0, ps_w[pw][:, :], 0.0, ALU.min, rd=[("psw", pw), ("scd",)], wr=[("scd",)])
        p.act(et1, et0, AF.Exp, rd=[("scd",)], wr=[("scd",)])
        p.stt(dst_half, ps_w[pw][:, :], 0.0, et1, ALU.max, ALU.add,
              rd=[("psw", pw), ("scd",)], wr=[wrk])

    for tt in range(TTI):
        tp = tt % 2
        xnbf_t = xnbf_p[tp]
        xnT_t = xnT_p[tp]
        xt = X[:, tt * D:(tt + 1) * D]
        emit_ln(p, xt, xnbf_t[:, :], stats[tp], scrB,
                LN_EPS, rd_x=[("X", tt)], wr_out=[("xnbf", tp)], tag=("aln", tp),
                sckey=("scd",))
        emit_tr8(p, xnT_t[:, :].rearrange("p (k f) -> p k f", f=128),
                 xnbf_t[:, :], ps_trB, c["ident_bf"], rd_src=[("xnbf", tp)],
                 wr_dst=[("xnTt", tp)], pkey=("pstr",))
        for oh in range(2):
            proj(0, oh, oh, xnT_t, tp)
            elu1(efp[:, oh * 512:(oh + 1) * 512], oh, ("efp", oh))
        p.cp(qkbf[:, :], efp, rd=[("efp", 0), ("efp", 1)], wr=[("qkbf",)])
        qdst = qT[:, :].rearrange("p (k t) -> p k t", t=T)[:, :, tt * 128:tt * 128 + 128]
        emit_tr8(p, qdst, qkbf[:, :], ps_trB, c["ident_bf"],
                 rd_src=[("qkbf",)],
                 wr_dst=[("qT", tt, kd) for kd in range(DT)], pkey=("pstr",))
        for oh in range(2):
            proj(1, oh, oh, xnT_t, tp)
            elu1(efp[:, oh * 512:(oh + 1) * 512], oh, ("efp", oh))
        p.cp(qkbf[:, :], efp, rd=[("efp", 0), ("efp", 1)], wr=[("qkbf",)])
        for oh in range(2):
            proj(2, oh, oh, xnT_t, tp)
            p.act(vbf[:, oh * 512:(oh + 1) * 512], ps_w[oh][:, :], AF.Identity,
                  rd=[("psw", oh)], wr=[("vbf",)])
        for h in range(NH):
            p.mm(ps_kv[:, h * HD:(h + 1) * HD], qkbf[:, h * HD:(h + 1) * HD],
                 vbf[:, h * HD:(h + 1) * HD],
                 rd=[("qkbf",), ("vbf",)], wr=[("pskv",)])
        p.tt(kvacc[:, :], kvacc[:, :], ps_kv[:, :], ALU.add,
             rd=[("kvacc",), ("pskv",)], wr=[("kvacc",)])
        for ic in range(DT):
            p.mm(ps_small[:, ic:ic + 1], qkbf[:, ic * 128:(ic + 1) * 128],
                 c["onescol_bf"][:, :],
                 rd=[("qkbf",)], wr=[("psks",)])
        p.tt(ksacc[:, :], ksacc[:, :], ps_small[:, 0:DT], ALU.add,
             rd=[("ksacc",), ("psks",)], wr=[("ksacc",)])
    kvt = vbf[0:64, :]
    p.act(kvt, kvacc[:, :], AF.Identity, rd=[("kvacc",)], wr=[("vbf",), ("kvt",)])
    p.memset(kvbf[:, :], 0.0, wr=[("kv0",)])
    p.cp(kvbf[0:64, :].rearrange("p (h m) -> p h m", m=2 * HD)[:, :, 0:HD],
         kvt.rearrange("p (h m) -> p h m", m=2 * HD)[:, :, 0:HD],
         rd=[("kvt",), ("kv0",)], wr=[("kv0",)])
    p.dma(kvbf[64:128, :].rearrange("p (h m) -> p h m", m=2 * HD)[:, :, HD:2 * HD],
          kvt.rearrange("p (h m) -> p h m", m=2 * HD)[:, :, HD:2 * HD],
          rd=[("kvt",), ("kv0",)], wr=[("kv0",)])
    p.act(ksf[:, :], ksacc[:, :], AF.Identity, bias=p.ccol(EPS_ATTN),
          rd=[("ksacc",)], wr=[("ksf",)])
    p.memset(ksz[:, :], 0.0, wr=[("ksbf",)])
    p.cp(ksz[0:64, :].rearrange("p (a b) -> p a b", b=2)[:, :, 0:1],
         ksf[0:64, :].rearrange("p (a o) -> p a o", o=1),
         rd=[("ksf",), ("ksbf",)], wr=[("ksbf",)])
    p.cp(ksz[64:128, :].rearrange("p (a b) -> p a b", b=2)[:, :, 1:2],
         ksf[64:128, :].rearrange("p (a o) -> p a o", o=1),
         rd=[("ksf",), ("ksbf",)], wr=[("ksbf",)])
    p.dma(wd_bf.rearrange("p (dc dd) -> p dc dd", dd=D),
          scratch["B_wd"].rearrange("(dc p) dd -> p dc dd", p=128),
          rd=[("B", "B_wd"), ("wqkv", 0)], wr=[("wqkv", 0), ("wd",)], q="sync")
    for tt in (range(TTI) if "B" in ATTN_PHASES and "A" in ATTN_PHASES else []):
        tp = tt % 2
        if "z" in ATTN_PHASES:
            for h in range(NH):
                ic = h // 2
                p.mm(ps_small[:, 16 + h:17 + h],
                     qT[:, ic * T + tt * 128: ic * T + tt * 128 + 128],
                     ksz[:, h:h + 1],
                     rd=[("qT", tt, ic), ("ksbf",)], wr=[("psks",)])
            p.recip(zrec[:, :], ps_z, rd=[("psks",)], wr=[("zrec",)])
        if "c" in ATTN_PHASES:
            for h in range(NH):
                ic = h // 2
                p.mm(ps_ctx[:, h * HD:(h + 1) * HD],
                     qT[:, ic * T + tt * 128: ic * T + tt * 128 + 128],
                     kvbf[:, h * HD:(h + 1) * HD],
                     rd=[("qT", tt, ic), ("kv0",)], wr=[("psctx",)])
            for h in (range(NH) if "u" in ATTN_PHASES else []):
                p.ts(czbf[:, h * HD:(h + 1) * HD], ps_ctx[:, h * HD:(h + 1) * HD],
                     zrec[:, h:h + 1], ALU.mult,
                     rd=[("psctx",), ("zrec",)], wr=[("czbf",)])
        if "T" in ATTN_PHASES:
            emit_tr8(p, ctxT_t[:, :].rearrange("p (k f) -> p k f", f=128),
                     czbf[:, :], ps_trB, c["ident_bf"], rd_src=[("czbf",)],
                     wr_dst=[("ctxT", kd) for kd in range(DT)], pkey=("pstr",))
        xt = X[:, tt * D:(tt + 1) * D]
        if "d" in ATTN_PHASES:
            emit_ln(p, xt, efp, stats[tp], scrB,
                    LN_EPS, rd_x=[("X", tt)], wr_out=[("efp", 0), ("efp", 1)], tag=("aln2", tp),
                    sckey=("scd",))
            for oh in range(2):
                for kd in range(DT):
                    p.mm(ps_w[1][:, :], ctxT_t[:, kd * 128:(kd + 1) * 128],
                         wd_bf[:, kd * D + oh * 512: kd * D + (oh + 1) * 512],
                         rd=[("ctxT", kd), ("wd",)], wr=[("psw", 1)],
                         start=(kd == 0), stop=False)
                p.mm(ps_w[1][:, :], c["ones_bf"][0:1, 0:128],
                     bias_bf[0:1, 3 * D + oh * 512: 3 * D + (oh + 1) * 512],
                     rd=[("abias", 3)], wr=[("psw", 1)], start=False, stop=True)
                p.tt(hsn[:, oh * 512:(oh + 1) * 512], ps_w[1][:, :],
                     efp[:, oh * 512:(oh + 1) * 512], ALU.add,
                     rd=[("psw", 1), ("efp", oh)], wr=[("hsn",)])
            emit_ln(p, hsn, hsn, stats[tp], scrB,
                    EPS_LN_ATTN, rd_x=[("hsn",)], wr_out=[("hsn",)], tag=("aln3", tp),
                    sckey=("scd",))
            p.tt(hsn, hsn, gb_bc[:, 0:D], ALU.mult,
                 rd=[("hsn",), ("gbbc",)], wr=[("hsn",)])
            p.tt(hsn, hsn, gb_bc[:, D:2 * D], ALU.add,
                 rd=[("hsn",), ("gbbc",)], wr=[("hsn",)])
            p.tt(xt, xt, hsn, ALU.add,
                 rd=[("X", tt), ("hsn",)], wr=[("X", tt)])



def emit_ffn1(p, st, c, X, dram, scratch):
    nc = p.nc
    TTI = T // 128
    JT = NB // 128
    HT1 = H1 // 128
    xnT = st.enter_context(nc.sbuf_tensor("ff_xnT", [128, DT * T], BF16))
    UT_bf = st.enter_context(nc.sbuf_tensor("ff_UT", [128, DT * NB], BF16))
    rWT_bf = st.enter_context(nc.sbuf_tensor("ff_rWT", [128, DT * NB], BF16))
    rV_bf = st.enter_context(nc.sbuf_tensor("ff_rV", [128, JT * D], BF16))
    rb_bf = st.enter_context(nc.sbuf_tensor("ff_rb", [1, NB], BF16))
    lngb = st.enter_context(nc.sbuf_tensor("ff_lngb", [128, 2 * D], F32))
    gamb = st.enter_context(nc.sbuf_tensor("ff_gamb", [128, D], F32))
    scrA = st.enter_context(nc.sbuf_tensor("ff_scrA", [128, 2048], F32))
    scrB = st.enter_context(nc.sbuf_tensor("ff_scrB", [128, 2048], F32))
    stats = [st.enter_context(nc.sbuf_tensor(f"ff_stats{i}", [128, 8], F32)) for i in range(2)]
    invUV = st.enter_context(nc.sbuf_tensor("ff_iuv", [128, 16], F32))
    ral = st.enter_context(nc.sbuf_tensor("ff_ral", [128, NB], F32))
    alpha = st.enter_context(nc.sbuf_tensor("ff_alpha", [128, NB], F32))
    aw = st.enter_context(nc.sbuf_tensor("ff_aw", [128, NB], F32))
    m8 = st.enter_context(nc.sbuf_tensor("ff_m8", [128, 8], F32))
    xr = st.enter_context(nc.sbuf_tensor("ff_xr", [128, D], F32))
    xrbf = st.enter_context(nc.sbuf_tensor("ff_xrbf", [128, D], BF16))
    xrT_t = st.enter_context(nc.sbuf_tensor("ff_xrT", [128, DT * 128], BF16))
    czbf = st.enter_context(nc.sbuf_tensor("ff_cz", [128, NB], BF16))
    cT_t = st.enter_context(nc.sbuf_tensor("ff_cT", [128, JT * 128], BF16))
    h1t = [st.enter_context(nc.sbuf_tensor(f"ff_h1{i}", [128, 256], BF16)) for i in range(2)]
    w1t = [st.enter_context(nc.sbuf_tensor(f"ff_w1t{i}", [128, 2 * DT * 128], BF16)) for i in range(2)]
    w2t = [st.enter_context(nc.sbuf_tensor(f"ff_w2t{i}", [128, 2 * D], BF16)) for i in range(2)]
    ps_trB = st.enter_context(nc.psum_tensor("ff_ptr", [128, 1024], BF16))
    ps_s = st.enter_context(nc.psum_tensor("ff_ps", [128, 2048], F32))
    ps_h0 = st.enter_context(nc.psum_tensor("ff_ph0", [128, 256], F32))
    ps_h1 = st.enter_context(nc.psum_tensor("ff_ph1", [128, 256], F32))
    ps_h = [ps_h0[:, :], ps_h1[:, :]]
    ps_wk = st.enter_context(nc.psum_tensor("ff_pwk", [128, 512], F32))

    p.dma(UT_bf[:, :].rearrange("p (dc nb) -> p dc nb", nb=NB),
          scratch["B_UT"].rearrange("(dc p) nb -> p dc nb", p=128),
          rd=[("B", "B_UT")], wr=[("UT",)], q="sync")
    p.dma(rWT_bf[:, :].rearrange("p (dc nb) -> p dc nb", nb=NB),
          scratch["B_rWT"].rearrange("(dc p) nb -> p dc nb", p=128),
          rd=[("B", "B_rWT")], wr=[("rWT",)], q="sync")
    p.dma(rV_bf[:, :].rearrange("p (jt dd) -> p jt dd", dd=D),
          scratch["B_rV"].rearrange("(jt p) dd -> p jt dd", p=128),
          rd=[("B", "B_rV")], wr=[("rV",)], q="sync")
    p.dma(scrB[0:1, 0:NB], dram["f_rb"], wr=[("sB",)])
    p.cp(rb_bf[:, :], scrB[0:1, 0:NB], rd=[("sB",)], wr=[("rb",)])
    for j, gname in enumerate(("f_lng", "f_lnb", "f_gam")):
        p.dma(scrB[0:1, 0:D], dram[gname], wr=[("sB",)])
        p.cp(xrbf[0:1, 0:D], scrB[0:1, 0:D], rd=[("sB",)], wr=[("xrbf",)])
        for oh in range(2):
            p.mm(ps_wk[:, :], c["ones_bf"][0:1, 0:128], xrbf[0:1, oh * 512:(oh + 1) * 512],
                 rd=[("xrbf",)], wr=[("pswk",)])
            dst = lngb[:, j * D + oh * 512: j * D + (oh + 1) * 512] if j < 2 else \
                  gamb[:, oh * 512:(oh + 1) * 512]
            p.act(dst, ps_wk[:, :], AF.Identity, rd=[("pswk",)], wr=[("lngb",)])
    for jt in range(JT):
        p.dma(scrB[:, 0:D], dram["f_U"][jt * 128:(jt + 1) * 128, :], wr=[("sB",)])
        p.act(scrB[:, 1024:2048], scrB[:, 0:D], AF.Square, accum_out=invUV[:, jt:jt + 1],
              rd=[("sB",)], wr=[("sB2",), ("iuv",)])
        p.dma(scrB[:, 0:D], dram["f_V"][jt * 128:(jt + 1) * 128, :], wr=[("sB",)])
        p.act(scrB[:, 1024:2048], scrB[:, 0:D], AF.Square, accum_out=invUV[:, 4 + jt:5 + jt],
              rd=[("sB",)], wr=[("sB2",), ("iuv",)])
    p.act(invUV[:, 8:16], invUV[:, 0:8], AF.Ln, rd=[("iuv",)], wr=[("iuv",)])
    p.act(invUV[:, 0:8], invUV[:, 8:16], AF.Exp, scale=p.ccol(0.5), rd=[("iuv",)], wr=[("iuv",)])
    p.ts(invUV[:, 0:8], invUV[:, 0:8], FFN_EPS, ALU.max, rd=[("iuv",)], wr=[("iuv",)])
    p.recip(invUV[:, 0:8], invUV[:, 0:8], rd=[("iuv",)], wr=[("iuv",)])
    p.tt(invUV[:, 8:12], invUV[:, 0:4], invUV[:, 4:8], ALU.mult, rd=[("iuv",)], wr=[("iuv",)])
    for tt in range(TTI):
        tp = tt % 2
        xt = X[:, tt * D:(tt + 1) * D]
        emit_ln(p, xt, xrbf[:, :], stats[tp], scrA[:, 0:1024],
                LN_EPS, rd_x=[("X", tt)], wr_out=[("xrbf",)], tag=("fln", tp))
        dst3d = xnT[:, :].rearrange("p (k t) -> p k t", t=T)[:, :, tt * 128:tt * 128 + 128]
        emit_tr8(p, dst3d, xrbf[:, :], ps_trB, c["ident_bf"],
                 rd_src=[("xrbf",)],
                 wr_dst=[("xnT", tt, kd) for kd in range(DT)], pkey=("pstr",))
    for tt in range(TTI):
        tp = tt % 2
        xt = X[:, tt * D:(tt + 1) * D]
        emit_ln(p, xt, scrA[:, 0:1024], stats[tp], scrA[:, 1024:2048],
                LN_EPS, rd_x=[("X", tt)], wr_out=[("xn",)], tag=("fln2", tp))
        emit_ln(p, scrA[:, 0:1024], xr[:, :], stats[tp], scrA[:, 1024:2048],
                LN_EPS, rd_x=[("xn",)], wr_out=[("xr",)], tag=("fln3", tp))
        p.tt(xr[:, :], xr[:, :], lngb[:, 0:D], ALU.mult, rd=[("xr",), ("lngb",)], wr=[("xr",)])
        p.tt(xr[:, :], xr[:, :], lngb[:, D:2 * D], ALU.add, rd=[("xr",), ("lngb",)], wr=[("xr",)])
        p.cp(xrbf[:, :], xr[:, :], rd=[("xr",)], wr=[("xrbf",)])
        emit_tr8(p, xrT_t[:, :].rearrange("p (k f) -> p k f", f=128),
                 xrbf[:, :], ps_trB, c["ident_bf"], rd_src=[("xrbf",)],
                 wr_dst=[("xrT", kd) for kd in range(DT)], pkey=("pstr",))
        for kd in range(DT):
            p.mm(ps_wk[:, :], xrT_t[:, kd * 128:(kd + 1) * 128],
                 rWT_bf[:, kd * NB:(kd + 1) * NB],
                 rd=[("xrT", kd), ("rWT",)], wr=[("pswk",)],
                 start=(kd == 0), stop=False)
        p.mm(ps_wk[:, :], c["ones_bf"][0:1, 0:128], rb_bf[0:1, :],
             rd=[("rb",)], wr=[("pswk",)], start=False, stop=True)
        p.ts(ral[:, :], ps_wk[:, :], TAU, ALU.min, -TAU, ALU.max,
             rd=[("pswk",)], wr=[("ral",)])
        p.act(alpha[:, :], ral[:, :], AF.Abs, rd=[("ral",)], wr=[("alpha",)])
        p.act(aw[:, :], alpha[:, :], AF.Exp, scale=-1.0, rd=[("alpha",)], wr=[("aw",)])
        p.act(alpha[:, :], aw[:, :], AF.Ln, bias=1.0, rd=[("aw",)], wr=[("alpha",)])
        p.stt(alpha[:, :], ral[:, :], 0.0, alpha[:, :], ALU.max, ALU.add,
              rd=[("ral",), ("alpha",)], wr=[("alpha",)])
        p.max8(m8[:, :], alpha[:, :], rd=[("alpha",)], wr=[("m8",)])
        p.ts(aw[:, :], alpha[:, :], m8[:, 7:8], ALU.is_ge, rd=[("alpha",), ("m8",)], wr=[("aw",)])
        p.tt(aw[:, :], aw[:, :], alpha[:, :], ALU.mult, rd=[("aw",), ("alpha",)], wr=[("aw",)])
        p.red(stats[tp][:, 6:7], aw[:, :], ALU.add, rd=[("aw",)], wr=[("fst", tp)])
        p.act(stats[tp][:, 7:8], stats[tp][:, 6:7], AF.Tanh, rd=[("fst", tp)], wr=[("fst", tp)])
        p.ts(stats[tp][:, 6:7], stats[tp][:, 6:7], FFN_EPS, ALU.add, rd=[("fst", tp)], wr=[("fst", tp)])
        p.recip(stats[tp][:, 6:7], stats[tp][:, 6:7], rd=[("fst", tp)], wr=[("fst", tp)])
        p.tt(stats[tp][:, 6:7], stats[tp][:, 6:7], stats[tp][:, 7:8], ALU.mult,
             rd=[("fst", tp)], wr=[("fst", tp)])
        p.ts(aw[:, :], aw[:, :], stats[tp][:, 6:7], ALU.mult, rd=[("aw",), ("fst", tp)], wr=[("aw",)])
        for kd in range(DT):
            p.mm(ps_wk[:, :], xnT[:, kd * T + tt * 128: kd * T + tt * 128 + 128],
                 UT_bf[:, kd * NB:(kd + 1) * NB],
                 rd=[("xnT", tt, kd), ("UT",)], wr=[("pswk",)],
                 start=(kd == 0), stop=(kd == DT - 1))
        p.tt(ral[:, :], ps_wk[:, :], aw[:, :], ALU.mult, rd=[("pswk",), ("aw",)], wr=[("ral",)])
        p.cp(czbf[:, :], ral[:, :], rd=[("ral",)], wr=[("czbf",)])
        for jt in range(JT):
            p.trans(ps_trB[:, jt * 128:(jt + 1) * 128], czbf[:, jt * 128:(jt + 1) * 128],
                    c["ident_bf"][:, :], rd=[("czbf",)], wr=[("pstr",)])
        for jt in range(JT):
            p.ts(cT_t[:, jt * 128:(jt + 1) * 128], ps_trB[:, jt * 128:(jt + 1) * 128],
                 invUV[:, 8 + jt:9 + jt], ALU.mult,
                 rd=[("pstr",), ("iuv",)], wr=[("cT", jt)])
        for oh in range(2):
            for jt in range(JT):
                p.mm(ps_wk[:, :], cT_t[:, jt * 128:(jt + 1) * 128],
                     rV_bf[:, jt * D + oh * 512: jt * D + (oh + 1) * 512],
                     rd=[("cT", jt), ("rV",)], wr=[("pswk",)],
                     start=(jt == 0), stop=(jt == JT - 1))
            p.tt(scrA[:, 1024:1536], ps_wk[:, :], gamb[:, oh * 512:(oh + 1) * 512], ALU.mult,
                 rd=[("pswk",), ("lngb",)], wr=[("dyn",)])
            xs = X[:, tt * D + oh * 512: tt * D + (oh + 1) * 512]
            p.tt(xs, xs, scrA[:, 1024:1536], ALU.add,
                 rd=[("X", tt), ("dyn",)], wr=[("X", tt)])
    for tc in range(T // 256):
        for ht in range(HT1):
            hp = (ht // 2) % 2
            sub = ht % 2
            if sub == 0:
                p.dma(w1t[hp][:, :].rearrange("p (dc h) -> p dc h", h=256),
                      scratch["B_w1T"][:, ht * 128:(ht + 2) * 128]
                      .rearrange("(dc p) h -> p dc h", p=128),
                      rd=[("B", "B_w1T")], wr=[("w1t", hp)], q="sync" if hp == 0 else "gpsimd")
                p.dma(w2t[hp][:, :].rearrange("p (o d) -> p o d", d=D),
                      scratch["B_w2T"][ht * 128:(ht + 2) * 128, :]
                      .rearrange("(o p) d -> p o d", p=128),
                      rd=[("B", "B_w2T")], wr=[("w2t", hp)], q="scalar")
            hq = hp * 2 + sub if False else hp
            for dc in range(DT):
                p.mm(ps_h[ht % 2], w1t[hp][:, (dc * 2 + sub) * 128:(dc * 2 + sub + 1) * 128],
                     xnT[:, dc * T + tc * 256: dc * T + (tc + 1) * 256],
                     rd=[("w1t", hp), ("xnT", tc * 2, dc), ("xnT", tc * 2 + 1, dc)],
                     wr=[("psh", ht % 2)],
                     start=(dc == 0), stop=(dc == DT - 1))
            p.act(h1t[ht % 2][:, :], ps_h[ht % 2], AF.Gelu, rd=[("psh", ht % 2)], wr=[("h1t", ht % 2)])
            for tsub in range(2):
                for oh in range(2):
                    p.mm(ps_s[:, (tsub * 2 + oh) * 512:(tsub * 2 + oh + 1) * 512],
                         h1t[ht % 2][:, tsub * 128:(tsub + 1) * 128],
                         w2t[hp][:, sub * D + oh * 512: sub * D + (oh + 1) * 512],
                         rd=[("h1t", ht % 2), ("w2t", hp)], wr=[("pss",)],
                         start=(ht == 0), stop=(ht == HT1 - 1))
        for tsub in range(2):
            tt = tc * 2 + tsub
            for oh in range(2):
                xs = X[:, tt * D + oh * 512: tt * D + (oh + 1) * 512]
                p.tt(xs, xs, ps_s[:, (tsub * 2 + oh) * 512:(tsub * 2 + oh + 1) * 512],
                     ALU.add, rd=[("X", tt), ("pss",)], wr=[("X", tt)])



def _mamba_host(inp):
    f = np.ascontiguousarray
    cw = inp["L0_attn_conv_w"].reshape(E, 4)
    d = {
        "m_A": f(inp["L0_attn_A_log"].reshape(ET, 128, NST).transpose(1, 0, 2).reshape(128, ET * NST)),
        "m_cw": f(cw.reshape(ET, 128, 4).transpose(1, 0, 2).reshape(128, ET * 4)),
        "m_cb": f(inp["L0_attn_conv_b"].reshape(ET, 128).T),
        "m_bdt": f(inp["L0_attn_b_dt"].reshape(ET, 128).T),
        "m_dsk": f(inp["L0_attn_D_skip"].reshape(ET, 128).T),
        "m_wdt": f(inp["L0_attn_W_dt"]),
        "m_win": f(inp["L0_attn_W_in"]),
        "m_wout": f(inp["L0_attn_W_out"]),
    }
    wxp = np.zeros((E, 128), np.float32)
    wx = inp["L0_attn_W_xproj"]
    wxp[:, 0:16] = wx[:, DTR:DTR + NST]
    wxp[:, 32:48] = wx[:, DTR + NST:DTR + 2 * NST]
    wxp[:, 64:128] = wx[:, 0:DTR]
    d["m_wxp"] = f(wxp)
    return d


def _moe_host(inp):
    f = np.ascontiguousarray
    HT = HMOE // 128
    return {
        "g_wT": f(inp["L0_ffn_gate_w"].T),
        "g_b": f(inp["L0_ffn_gate_b"].reshape(1, NEXP)),
        "g_w1": f(inp["L0_ffn_w1"]),
        "g_b1": f(inp["L0_ffn_b1"].reshape(NEXP, HT, 128).transpose(2, 0, 1).reshape(128, NEXP * HT)),
        "g_w2": f(inp["L0_ffn_w2"]),
        "g_b2": f(inp["L0_ffn_b2"]),
    }


def _attn_host(inp):
    f = np.ascontiguousarray
    return {
        "a_wq": f(inp["L1_attn_Wq"].T),
        "a_wk": f(inp["L1_attn_Wk"].T),
        "a_wv": f(inp["L1_attn_Wv"].T),
        "a_wd": f(inp["L1_attn_Wd"].T),
        "a_bq": f(inp["L1_attn_bq"].reshape(1, D)),
        "a_bk": f(inp["L1_attn_bk"].reshape(1, D)),
        "a_bv": f(inp["L1_attn_bv"].reshape(1, D)),
        "a_bd": f(inp["L1_attn_bd"].reshape(1, D)),
        "a_gamma": f(inp["L1_attn_gamma"].reshape(1, D)),
        "a_beta": f(inp["L1_attn_beta"].reshape(1, D)),
    }


def _ffn1_host(inp):
    f = np.ascontiguousarray
    return {
        "f_UT": f(inp["L1_ffn_raw_U"].T),
        "f_rWT": f(inp["L1_ffn_router_W"].T),
        "f_rV": f(inp["L1_ffn_raw_V"]),
        "f_rb": f(inp["L1_ffn_router_b"].reshape(1, NB)),
        "f_lng": f(inp["L1_ffn_ln_g"].reshape(1, D)),
        "f_lnb": f(inp["L1_ffn_ln_b"].reshape(1, D)),
        "f_gam": f(inp["L1_ffn_gamma"].reshape(1, D)),
        "f_U": f(inp["L1_ffn_raw_U"]),
        "f_V": f(inp["L1_ffn_raw_V"]),
        "f_w1T": f(inp["L1_ffn_W1"].T),
        "f_w2T": f(inp["L1_ffn_W2"].T),
    }


_DRAM_SHAPES = {
    "m_A": (128, ET * NST), "m_cw": (128, ET * 4), "m_cb": (128, ET),
    "m_bdt": (128, ET), "m_dsk": (128, ET), "m_wdt": (DTR, E),
    "m_win": (D, 2 * E), "m_wout": (E, D), "m_wxp": (E, 128),
    "g_wT": (D, NEXP), "g_b": (1, NEXP), "g_w1": (NEXP, D, HMOE),
    "g_b1": (128, NEXP * (HMOE // 128)), "g_w2": (NEXP, HMOE, D), "g_b2": (NEXP, D),
    "a_wq": (D, D), "a_wk": (D, D), "a_wv": (D, D), "a_wd": (D, D),
    "a_bq": (1, D), "a_bk": (1, D), "a_bv": (1, D), "a_bd": (1, D),
    "a_gamma": (1, D), "a_beta": (1, D),
    "f_UT": (D, NB), "f_rWT": (D, NB), "f_rV": (NB, D), "f_rb": (1, NB),
    "f_lng": (1, D), "f_lnb": (1, D), "f_gam": (1, D),
    "f_U": (NB, D), "f_V": (NB, D), "f_w1T": (D, H1), "f_w2T": (H1, D),
}

_SCRATCH_SHAPES = {
    "B_win": (D, 2 * E), "B_wxp": (E, 128), "B_wdt": (DTR, E), "B_wout": (E, D),
    "B_gwT": (D, NEXP),
    "B_wq": (D, D), "B_wk": (D, D), "B_wv": (D, D), "B_wd": (D, D),
    "B_UT": (D, NB), "B_rWT": (D, NB), "B_rV": (NB, D),
    "B_w1T": (D, H1), "B_w2T": (H1, D),
}


def _build(emits, TC=256):
    nc = bass.Bass("TRN2", target_bir_lowering=False, debug=False)
    x_in = nc.dram_tensor("x_res_in", [T, D], F32, kind="ExternalInput").ap()
    x_out = nc.dram_tensor("x_res_out", [T, D], F32, kind="ExternalOutput").ap()
    dram = {k: nc.dram_tensor(k, list(v), F32, kind="ExternalInput").ap()
            for k, v in _DRAM_SHAPES.items()}
    scratch = {k: nc.dram_tensor(k, list(v), BF16, kind="Internal").ap()
               for k, v in _SCRATCH_SHAPES.items()}
    with ExitStack() as st:
        X = st.enter_context(nc.sbuf_tensor("X", [128, (T // 128) * D], F32))
        sems = {}
        for e in ENGS:
            sems[("c", e)] = st.enter_context(nc.semaphore(f"semc_{e}"))
            sems[("d", e)] = st.enter_context(nc.semaphore(f"semd_{e}"))
        blk = st.enter_context(nc.Block())
        p = P(nc)
        c = make_consts(p, st)
        p.barrier()
        pc_f = [st.enter_context(nc.sbuf_tensor(f"pc_f{i}", [128, 1024], F32)) for i in range(2)]
        pc_b = [st.enter_context(nc.sbuf_tensor(f"pc_b{i}", [128, 1024], BF16)) for i in range(2)]
        p.dma(X[:, :].rearrange("p (tt d) -> p tt d", d=D),
              x_in.rearrange("(tt p) d -> p tt d", p=128),
              wr=[("X", tt) for tt in range(T // 128)], q="gpsimd")
        emit_precast(p, st, dram, scratch, pc_f, pc_b)
        for em in emits:
            with ExitStack() as sti:
                if em is emit_mamba:
                    em(p, sti, c, X, dram, scratch, TC=TC)
                else:
                    em(p, sti, c, X, dram, scratch)
            p.barrier()
        p.dma(x_out.rearrange("(tt p) d -> p tt d", p=128),
              X[:, :].rearrange("p (tt d) -> p tt d", d=D),
              rd=[("X", tt) for tt in range(T // 128)], wr=[("OUT",)], q="gpsimd")
        p.finalize(sems)
    return nc


def build_mamba_program(TC=256):
    return _build([emit_mamba], TC=TC)


def build_moe_program():
    return _build([emit_moe])


def build_attn_program():
    return _build([emit_attn])


def build_ffn1_program():
    return _build([emit_ffn1])


def build_fused_program(TC=256):
    return _build([emit_mamba, emit_moe, emit_attn, emit_ffn1], TC=TC)


_PROGS = {}


def _all_host(inp):
    h = {}
    h.update(_mamba_host(inp))
    h.update(_moe_host(inp))
    h.update(_attn_host(inp))
    h.update(_ffn1_host(inp))
    return h


def kernel(**inputs):
    inp = {k: np.asarray(v) for k, v in inputs.items()}
    x = np.ascontiguousarray(inp["L0_attn_x"], dtype=np.float32)
    if "fused" not in _PROGS:
        _PROGS["fused"] = build_fused_program(TC=256)
    host = _all_host(inp)
    in_maps = [dict(host, x_res_in=np.ascontiguousarray(x[ci])) for ci in range(8)]
    res = run_bass_kernel_spmd(_PROGS["fused"], in_maps, core_ids=list(range(8)))
    return np.stack([np.asarray(r["x_res_out"]) for r in res.results]).astype(np.float32)


# version 10
# speedup vs baseline: 1.3571x; 1.0285x over previous
import numpy as np
import concourse.bass as bass
import concourse.mybir as mybir
from concourse.bass_utils import run_bass_kernel_spmd
from contextlib import ExitStack

F32 = mybir.dt.float32
BF16 = mybir.dt.bfloat16
AF = mybir.ActivationFunctionType
ALU = mybir.AluOpType
AX = mybir.AxisListType

T = 2048
D = 1024
E = 2048
NST = 16
DTR = 64
NEXP = 8
HMOE = 2048
NH = 16
HD = 64
NB = 512
H1 = 4096
DT = D // 128
ET = E // 128

LN_EPS = 1e-5
EPS_ATTN = 1e-6
EPS_LN_ATTN = 1e-12
FFN_EPS = 1e-6
TAU = 10.0

ENGS = ("tensor", "vector", "scalar", "gpsimd", "sync")


class _Node:
    __slots__ = ("eng", "fn", "inc", "deps", "need", "val", "dma", "sid")

    def __init__(self, eng, fn, inc, dma):
        self.eng = eng
        self.fn = fn
        self.inc = inc
        self.deps = []
        self.need = False
        self.val = 0
        self.dma = dma
        self.sid = ("d", eng) if dma else ("c", eng)


class P:

    def __init__(self, nc):
        self.nc = nc
        self.nodes = []
        self.res = {}
        self.cvals = None
        self.cmap = {}
        self.frozen_consts = False
        self.last_dma = {}

    def emit(self, eng_name, fn, rd=(), wr=(), inc=1, dma=False):
        node = _Node(eng_name, fn, inc, dma)
        dep = {}

        def add_dep(pn):
            if pn is node:
                return
            if pn.eng == eng_name == "tensor" and not (pn.dma or dma):
                return
            dep[id(pn)] = pn
        for k in rd:
            e = self.res.get(k)
            if e and e[0] is not None:
                add_dep(e[0])
        for k in wr:
            e = self.res.get(k)
            if e:
                if e[0] is not None:
                    add_dep(e[0])
                for rn in e[1].values():
                    add_dep(rn)
        if dma:
            prev = self.last_dma.get(eng_name)
            if prev is not None:
                add_dep(prev)
            self.last_dma[eng_name] = node
            node.need = True
        node.deps = list(dep.values())
        for pn in node.deps:
            pn.need = True
        for k in rd:
            e = self.res.setdefault(k, [None, {}])
            e[1][eng_name] = node
        for k in wr:
            self.res[k] = [node, {}]
        self.nodes.append(node)
        return node

    def barrier(self):
        self.nodes.append("BARRIER")

    def finalize(self, sems):
        SIDS = [("c", e) for e in ENGS] + [("d", e) for e in ENGS]
        counts = {s: 0 for s in SIDS}
        last = {s: None for s in SIDS}
        for nd in self.nodes:
            if nd == "BARRIER":
                for s in SIDS:
                    if last[s] is not None:
                        last[s].need = True
                continue
            last[nd.sid] = nd
        for s in SIDS:
            if last[s] is not None:
                last[s].need = True
        for nd in self.nodes:
            if nd == "BARRIER":
                continue
            if nd.need:
                counts[nd.sid] += nd.inc
                nd.val = counts[nd.sid]
        lastv = {s: 0 for s in SIDS}
        waited = {e: {s: 0 for s in SIDS} for e in ENGS}
        for nd in self.nodes:
            if nd == "BARRIER":
                snap = dict(lastv)
                for e in ENGS:
                    eng = getattr(self.nc, e)
                    for s in SIDS:
                        if s != ("c", e) and snap[s] > waited[e][s]:
                            eng.wait_ge(sems[s], snap[s])
                            waited[e][s] = snap[s]
                continue
            eng = getattr(self.nc, nd.eng)
            need_wait = {}
            for pn in nd.deps:
                if pn.val > waited[nd.eng].get(pn.sid, 0):
                    need_wait[pn.sid] = max(need_wait.get(pn.sid, 0), pn.val)
            for s, v in need_wait.items():
                eng.wait_ge(sems[s], v)
                waited[nd.eng][s] = v
            inst = nd.fn(eng)
            if nd.need:
                inst.then_inc(sems[nd.sid], nd.inc)
                lastv[nd.sid] = nd.val
        for s in SIDS:
            if s != ("c", "sync") and lastv[s] > waited["sync"][s]:
                self.nc.sync.wait_ge(sems[s], lastv[s])

    def dma(self, out, in_, rd=(), wr=(), q="sync"):
        self.emit(q, lambda e: e.dma_start(out=out, in_=in_), rd=rd, wr=wr,
                  inc=16, dma=True)

    def mm(self, out, lhsT, rhs, rd=(), wr=(), start=True, stop=True):
        self.emit("tensor", lambda e: e.matmul(out, lhsT, rhs, start=start, stop=stop),
                  rd=rd, wr=wr)

    def trans(self, out, in_, ident, rd=(), wr=()):
        self.emit("tensor", lambda e: e.transpose(out, in_, ident), rd=rd, wr=wr)

    def act(self, out, in_, func, rd=(), wr=(), bias=0.0, scale=1.0, accum_out=None):
        self.emit("scalar", lambda e: e.activation(out, in_, func, bias=bias,
                                                   scale=scale, accum_out=accum_out),
                  rd=rd, wr=wr)

    def cp(self, out, in_, rd=(), wr=(), eng="vector"):
        if eng == "scalar":
            self.emit(eng, lambda e: e.copy(out, in_), rd=rd, wr=wr)
        else:
            self.emit(eng, lambda e: e.tensor_copy(out, in_), rd=rd, wr=wr)

    def tt(self, out, a, b, op, rd=(), wr=(), eng="vector"):
        self.emit(eng, lambda e: e.tensor_tensor(out, a, b, op), rd=rd, wr=wr)

    def ts(self, out, in0, s1, op0, s2=None, op1=ALU.bypass, rd=(), wr=(), eng="vector"):
        if s2 is None:
            self.emit(eng, lambda e: e.tensor_single_scalar(out, in0, s1, op0),
                      rd=rd, wr=wr)
        else:
            self.emit(eng, lambda e: e.tensor_scalar(out, in0, s1, s2, op0=op0, op1=op1),
                      rd=rd, wr=wr)

    def stt(self, out, in0, scalar, in1, op0, op1, rd=(), wr=(), eng="vector"):
        self.emit(eng, lambda e: e.scalar_tensor_tensor(out, in0, scalar, in1,
                                                        op0=op0, op1=op1), rd=rd, wr=wr)

    def scan(self, out, d0, d1, init, rd=(), wr=(), eng="vector"):
        self.emit(eng, lambda e: e.tensor_tensor_scan(out, d0, d1, init,
                                                      op0=ALU.mult, op1=ALU.add),
                  rd=rd, wr=wr)

    def red(self, out, in_, op, rd=(), wr=(), eng="vector"):
        self.emit(eng, lambda e: e.tensor_reduce(out, in_, axis=AX.X, op=op),
                  rd=rd, wr=wr)

    def max8(self, out, in_, rd=(), wr=()):
        self.emit("vector", lambda e: e.max(out, in_), rd=rd, wr=wr)

    def recip(self, out, in_, rd=(), wr=()):
        self.emit("vector", lambda e: e.reciprocal(out, in_), rd=rd, wr=wr)

    def memset(self, ap, v, rd=(), wr=(), eng="vector"):
        self.emit(eng, lambda e: e.memset(ap, v), rd=rd, wr=wr)

    def iota(self, ap, pattern, rd=(), wr=(), base=0, mult=0):
        self.emit("gpsimd", lambda e: e.iota(ap, pattern=pattern, base=base,
                                             channel_multiplier=mult,
                                             allow_small_or_imprecise_dtypes=True),
                  rd=rd, wr=wr)

    def ccol(self, v):
        v = float(v)
        if v not in self.cmap:
            assert not self.frozen_consts, f"const {v} after freeze"
            i = len(self.cmap)
            self.cmap[v] = i
            self.memset(self.cvals[:, i:i + 1], v)
        return self.cvals[:, self.cmap[v]:self.cmap[v] + 1]


def make_consts(p, st):
    nc = p.nc
    c = {}
    c["ones_bf"] = st.enter_context(nc.sbuf_tensor("ones_bf", [1, 128], BF16))
    c["ident_f"] = st.enter_context(nc.sbuf_tensor("ident_f", [128, 128], F32))
    c["ident_bf"] = st.enter_context(nc.sbuf_tensor("ident_bf", [128, 128], BF16))
    c["onescol_bf"] = st.enter_context(nc.sbuf_tensor("onescol_bf", [128, 1], BF16))
    p.cvals = st.enter_context(nc.sbuf_tensor("cvals", [128, 16], F32))
    p.memset(c["ones_bf"][:, :], 1.0)
    p.memset(c["onescol_bf"][:, :], 1.0)
    p.iota(c["ident_f"][:, :], pattern=[[1, 128]], base=0, mult=-1, wr=[("iota",)])
    p.ts(c["ident_f"][:, :], c["ident_f"][:, :], 0.0, ALU.is_equal, rd=[("iota",)], wr=[("iota",)])
    p.cp(c["ident_bf"][:, :], c["ident_f"][:, :], rd=[("iota",)])
    for v in (LN_EPS, EPS_LN_ATTN, EPS_ATTN, -0.5, 0.5):
        p.ccol(v)
    p.frozen_consts = True
    return c



PRECAST = [
    ("m_win", D, 2 * E, "B_win"),
    ("m_wxp", E, 128, "B_wxp"),
    ("m_wdt", DTR, E, "B_wdt"),
    ("m_wout", E, D, "B_wout"),
    ("g_wT", D, NEXP, "B_gwT"),
    ("a_wq", D, D, "B_wq"),
    ("a_wk", D, D, "B_wk"),
    ("a_wv", D, D, "B_wv"),
    ("a_wd", D, D, "B_wd"),
    ("f_UT", D, NB, "B_UT"),
    ("f_rWT", D, NB, "B_rWT"),
    ("f_rV", NB, D, "B_rV"),
    ("f_w1T", D, H1, "B_w1T"),
    ("f_w2T", H1, D, "B_w2T"),
]

_PC_ENGS = ("gpsimd",)


def emit_precast(p, st, dram, scratch, sf, sb):
    i = 0
    for key, R, C, bname in PRECAST:
        src = dram[key]
        if len(src.shape) == 3:
            src = src.rearrange("a b c -> (a b) c")
        dst = scratch[bname]
        if len(dst.shape) == 3:
            dst = dst.rearrange("a b c -> (a b) c")
        for r0 in range(0, R, 128):
            rr = min(128, R - r0)
            for c0 in range(0, C, 1024):
                cc = min(1024, C - c0)
                g = i % 2
                p.dma(sf[g][0:rr, 0:cc], src[r0:r0 + rr, c0:c0 + cc],
                      wr=[("pcf", g)], q="sync")
                p.cp(sb[g][0:rr, 0:cc], sf[g][0:rr, 0:cc],
                     rd=[("pcf", g)], wr=[("pcb", g)], eng="scalar")
                p.dma(dst[r0:r0 + rr, c0:c0 + cc], sb[g][0:rr, 0:cc],
                      rd=[("pcb", g)], wr=[("B", bname)], q="gpsimd")
                i += 1


def emit_ln(p, x_ap, out_ap, stats, scr, eps, rd_x, wr_out, tag, sckey=None):
    Df = x_ap.shape[-1]
    ks = ("st",) + tag
    kc = sckey if sckey is not None else ("sc",) + tag
    mu = stats[:, 0:1]
    sq = stats[:, 1:2]
    var = stats[:, 2:3]
    rstd = stats[:, 3:4]
    p.red(mu, x_ap, ALU.add, rd=list(rd_x), wr=[ks])
    p.ts(mu, mu, 1.0 / Df, ALU.mult, rd=[ks], wr=[ks])
    p.act(scr, x_ap, AF.Square, accum_out=sq, rd=list(rd_x), wr=[kc, ks])
    p.ts(var, sq, 1.0 / Df, ALU.mult, rd=[ks], wr=[ks])
    p.tt(stats[:, 4:5], mu, mu, ALU.mult, rd=[ks], wr=[ks])
    p.tt(var, var, stats[:, 4:5], ALU.subtract, rd=[ks], wr=[ks])
    p.act(stats[:, 5:6], var, AF.Ln, bias=p.ccol(eps), rd=[ks], wr=[ks])
    p.act(rstd, stats[:, 5:6], AF.Exp, scale=p.ccol(-0.5), rd=[ks], wr=[ks])
    p.ts(out_ap, x_ap, mu, ALU.subtract, rstd, ALU.mult,
         rd=list(rd_x) + [ks], wr=list(wr_out))


def emit_tr(p, dst_ap, src_ap, ps_tr, ident_bf, rd_src, wr_dst, pkey):
    p.trans(ps_tr, src_ap, ident_bf[:, :], rd=list(rd_src), wr=[pkey])
    p.cp(dst_ap, ps_tr, rd=[pkey], wr=list(wr_dst))


def emit_tr8(p, dst3d, src_ap, ps_trB, ident_bf, rd_src, wr_dst, pkey, nk=8):
    for kd in range(nk):
        p.trans(ps_trB[:, kd * 128:(kd + 1) * 128], src_ap[:, kd * 128:(kd + 1) * 128],
                ident_bf[:, :], rd=list(rd_src), wr=[pkey])
    p.cp(dst3d, ps_trB[:, 0:nk * 128].rearrange("p (k f) -> p k f", f=128),
         rd=[pkey], wr=list(wr_dst), eng="scalar")



def emit_mamba(p, st, c, X, dram, scratch, TC=256):
    nc = p.nc
    NCH = T // TC
    TSUB = TC // 128

    _xnT = st.enter_context(nc.sbuf_tensor("mb_xnT", [128, DT * TC], BF16))
    xnT = [_xnT, _xnT]
    xi = st.enter_context(nc.sbuf_tensor("mb_xi", [128, ET * (TC + 3)], F32))
    zs = st.enter_context(nc.sbuf_tensor("mb_zs", [128, ET * TC], BF16))
    delta = st.enter_context(nc.sbuf_tensor("mb_delta", [128, ET * TC], BF16))
    wdu = st.enter_context(nc.sbuf_tensor("mb_w", [128, ET * TC], BF16))
    yac = st.enter_context(nc.sbuf_tensor("mb_y", [128, ET * TC], F32))
    hall = st.enter_context(nc.sbuf_tensor("mb_hall", [128, ET * TC], F32))
    dBuall = st.enter_context(nc.sbuf_tensor("mb_dbu", [128, ET * TC], BF16))
    ygbf = hall[:, 0:ET * TC // 2].bitcast(BF16)
    hstate = st.enter_context(nc.sbuf_tensor("mb_h", [128, ET * NST], F32))
    stash = st.enter_context(nc.sbuf_tensor("mb_stash", [128, ET * 3], F32))
    stats = [st.enter_context(nc.sbuf_tensor(f"mb_stats{i}", [128, 8], F32)) for i in range(2)]
    scrA = st.enter_context(nc.sbuf_tensor("mb_scrA", [128, 1024], F32))
    xnbf_t = [st.enter_context(nc.sbuf_tensor(f"mb_xnbf{i}", [128, 1024], BF16)) for i in range(2)]
    cbuf0 = st.enter_context(nc.sbuf_tensor("mb_cb0", [128, 2 * TC], F32))
    cbuf1 = st.enter_context(nc.sbuf_tensor("mb_cb1", [128, 2 * TC], F32))
    ubf = [st.enter_context(nc.sbuf_tensor(f"mb_ubf{i}", [128, TC], BF16)) for i in range(2)]
    dtlr_bf = st.enter_context(nc.sbuf_tensor("mb_dtlr", [64, TC], BF16))
    Bt_bf = st.enter_context(nc.sbuf_tensor("mb_Bt", [16, TC], BF16))
    Ct_bf = st.enter_context(nc.sbuf_tensor("mb_Ct", [16, TC], BF16))
    Bb = [st.enter_context(nc.sbuf_tensor(f"mb_Bb{i}", [128, TC], BF16)) for i in range(2)]
    Cb = [st.enter_context(nc.sbuf_tensor(f"mb_Cb{i}", [128, TC], BF16)) for i in range(2)]
    dAb = [st.enter_context(nc.sbuf_tensor(f"mb_dA{i}", [128, TC], F32)) for i in range(2)]
    sp0 = st.enter_context(nc.sbuf_tensor("mb_sp0", [128, TC], F32))
    Asb = st.enter_context(nc.sbuf_tensor("mb_A", [128, ET * NST], F32))
    cwsb = st.enter_context(nc.sbuf_tensor("mb_cw", [128, ET * 4], F32))
    cbsb = st.enter_context(nc.sbuf_tensor("mb_cbv", [128, ET], F32))
    bdtsb = st.enter_context(nc.sbuf_tensor("mb_bdt", [128, ET], F32))
    dsksb = st.enter_context(nc.sbuf_tensor("mb_dsk", [128, ET], F32))
    wxp = st.enter_context(nc.sbuf_tensor("mb_wxp", [128, ET * 128], BF16))
    wdt = st.enter_context(nc.sbuf_tensor("mb_wdt", [64, E], BF16))
    sel = st.enter_context(nc.sbuf_tensor("mb_sel", [16, NST * 128], BF16))
    win_bf = [st.enter_context(nc.sbuf_tensor(f"mb_winb{i}", [128, DT * 128], BF16)) for i in range(2)]
    wo_et = win_bf
    ps_trB = st.enter_context(nc.psum_tensor("mb_ptr", [128, 1024], BF16))
    ps_m0 = st.enter_context(nc.psum_tensor("mb_pm0", [128, 256], F32))
    ps_m1 = st.enter_context(nc.psum_tensor("mb_pm1", [128, 256], F32))
    ps_xd = st.enter_context(nc.psum_tensor("mb_pxd", [128, 512], F32))
    ps_o = st.enter_context(nc.psum_tensor("mb_po", [128, 2048], F32))
    assert TC == 256
    ps_mm = [ps_m0[:, :], ps_m1[:, :]]
    ps_d = ps_xd[:, TC:2 * TC]
    ps_B = ps_o[:, 0:TC]
    ps_C = ps_o[:, TC:2 * TC]

    p.dma(Asb[:, :], dram["m_A"], wr=[("mA",)])
    p.dma(cwsb[:, :], dram["m_cw"], wr=[("mcw",)])
    p.dma(cbsb[:, :], dram["m_cb"], wr=[("mcb",)])
    p.dma(bdtsb[:, :], dram["m_bdt"], wr=[("mbdt",)])
    p.dma(dsksb[:, :], dram["m_dsk"], wr=[("mdsk",)])
    p.dma(wxp[:, :].rearrange("p (et cc) -> p et cc", cc=128),
          scratch["B_wxp"].rearrange("(et p) cc -> p et cc", p=128),
          rd=[("B", "B_wxp")], wr=[("mwxp",)], q="scalar")
    p.dma(wdt[:, :], scratch["B_wdt"], rd=[("B", "B_wdt")], wr=[("mwdt",)], q="scalar")
    p.act(Asb[:, :], Asb[:, :], AF.Exp, rd=[("mA",)], wr=[("mA",)])
    p.ts(Asb[:, :], Asb[:, :], -1.0, ALU.mult, rd=[("mA",)], wr=[("mA",)])
    for half in range(2):
        p.iota(scrA[0:16, 0:1024], pattern=[[1, 8], [0, 128]], base=-half * 8, mult=-1,
               wr=[("scA",)])
        p.ts(scrA[0:16, 0:1024], scrA[0:16, 0:1024], 0.0, ALU.is_equal,
             rd=[("scA",)], wr=[("scA",)])
        p.cp(sel[:, half * 1024:(half + 1) * 1024], scrA[0:16, 0:1024],
             rd=[("scA",)], wr=[("msel",)])
    p.memset(hstate[:, :], 0.0, wr=[("hst", n) for n in range(NST)])

    xi_v = lambda et: xi[:, et * (TC + 3):(et + 1) * (TC + 3)]

    for ch in range(NCH):
        cp_ = ch % 2
        for tsub in range(TSUB):
            tt = ch * TSUB + tsub
            xt = X[:, tt * D:(tt + 1) * D]
            emit_ln(p, xt, xnbf_t[tsub][:, :], stats[tsub], scrA[:, 0:1024],
                    LN_EPS, rd_x=[("X", tt)], wr_out=[("xnbf", tsub)], tag=("mln", tsub),
                    sckey=("scA",))
            dst3d = xnT[cp_][:, :].rearrange("p (k t) -> p k t", t=TC)[:, :, tsub * 128:tsub * 128 + 128]
            emit_tr8(p, dst3d, xnbf_t[tsub][:, :], ps_trB, c["ident_bf"],
                     rd_src=[("xnbf", tsub)],
                     wr_dst=[("xnT", cp_, kd) for kd in range(DT)], pkey=("pstr",))
        for ec in range(32):
            wp = ec % 2
            p.dma(win_bf[wp][:, :].rearrange("p (dc e) -> p dc e", e=128),
                  scratch["B_win"][:, ec * 128:(ec + 1) * 128]
                  .rearrange("(dc p) e -> p dc e", p=128),
                  rd=[("B", "B_win")], wr=[("winb", wp)], q="sync")
            for dc in range(DT):
                p.mm(ps_mm[wp], win_bf[wp][:, dc * 128:(dc + 1) * 128],
                     xnT[cp_][:, dc * TC:(dc + 1) * TC],
                     rd=[("winb", wp), ("xnT", cp_, dc)], wr=[("psmm", wp)],
                     start=(dc == 0), stop=(dc == DT - 1))
            if ec < 16:
                p.act(xi_v(ec)[:, 3:3 + TC], ps_mm[wp], AF.Identity,
                      rd=[("psmm", wp)], wr=[("xi", ec)])
            else:
                p.act(zs[:, (ec - 16) * TC:(ec - 15) * TC], ps_mm[wp], AF.Silu,
                      rd=[("psmm", wp)], wr=[("zs", ec - 16)])
        for et in range(ET):
            xv = xi_v(et)
            up = et % 2
            if ch == 0:
                p.memset(xv[:, 0:3], 0.0, wr=[("xi", et)])
            else:
                p.cp(xv[:, 0:3], stash[:, et * 3:(et + 1) * 3],
                     rd=[("stash", et)], wr=[("xi", et)])
            cw = lambda k: cwsb[:, et * 4 + k: et * 4 + k + 1]
            ce = "vector"
            ck = ("cb", et % 2)
            cb0 = cbuf0[:, (et % 2) * TC:(et % 2 + 1) * TC]
            cb1 = cbuf1[:, (et % 2) * TC:(et % 2 + 1) * TC]
            p.ts(cb0, xv[:, 0:TC], cw(0), ALU.mult, rd=[("xi", et), ("mcw",)], wr=[ck], eng=ce)
            p.stt(cb1, xv[:, 1:1 + TC], cw(1), cb0, ALU.mult, ALU.add,
                  rd=[("xi", et), ck], wr=[ck], eng=ce)
            p.stt(cb0, xv[:, 2:2 + TC], cw(2), cb1, ALU.mult, ALU.add,
                  rd=[("xi", et), ck], wr=[ck], eng=ce)
            p.stt(cb1, xv[:, 3:3 + TC], cw(3), cb0, ALU.mult, ALU.add,
                  rd=[("xi", et), ck], wr=[ck], eng=ce)
            p.cp(stash[:, et * 3:(et + 1) * 3], xv[:, TC:TC + 3],
                 rd=[("xi", et)], wr=[("stash", et)], eng=ce)
            p.act(xv[:, 3:3 + TC], cb1, AF.Silu, bias=cbsb[:, et:et + 1],
                  rd=[ck, ("stash", et), ("mcb",)], wr=[("xi", et)])
            p.cp(ubf[up][:, :], xv[:, 3:3 + TC], rd=[("xi", et)], wr=[("ubf", up)], eng="gpsimd")
            p.act(yac[:, et * TC:(et + 1) * TC], xv[:, 3:3 + TC], AF.Copy,
                  scale=dsksb[:, et:et + 1], rd=[("xi", et), ("mdsk",)], wr=[("yac", et)])
            p.mm(ps_xd[:, 0:TC], wxp[:, et * 128:(et + 1) * 128], ubf[up][:, :],
                 rd=[("ubf", up), ("mwxp",)], wr=[("psxd",)],
                 start=(et == 0), stop=(et == ET - 1))
        p.act(Bt_bf[:, :], ps_xd[0:16, 0:TC], AF.Identity, rd=[("psxd",)], wr=[("Bt",)])
        p.act(Ct_bf[:, :], ps_xd[32:48, 0:TC], AF.Identity, rd=[("psxd",)], wr=[("Ct",)])
        p.act(dtlr_bf[:, :], ps_xd[64:128, 0:TC], AF.Identity, rd=[("psxd",)], wr=[("dtlr",)])
        for dt in range(ET):
            dv = delta[:, dt * TC:(dt + 1) * TC]
            p.mm(ps_d, wdt[:, dt * 128:(dt + 1) * 128], dtlr_bf[:, :],
                 rd=[("dtlr",), ("mwdt",)], wr=[("psxd",)])
            p.act(sp0[:, :], ps_d, AF.Identity, bias=bdtsb[:, dt:dt + 1],
                  rd=[("psxd",), ("mbdt",)], wr=[("sp0",)])
            p.act(cbuf1[:, 0:TC], sp0[:, :], AF.Abs, rd=[("sp0",), ("cb", 0)], wr=[("cb", 0)])
            p.act(cbuf0[:, 0:TC], cbuf1[:, 0:TC], AF.Exp, scale=-1.0, rd=[("cb", 0)], wr=[("cb", 0)])
            p.act(cbuf1[:, 0:TC], cbuf0[:, 0:TC], AF.Ln, bias=1.0, rd=[("cb", 0)], wr=[("cb", 0)])
            p.stt(dv, sp0[:, :], 0.0, cbuf1[:, 0:TC], ALU.max, ALU.add,
                  rd=[("sp0",), ("cb", 0)], wr=[("delta", dt)])
            p.tt(wdu[:, dt * TC:(dt + 1) * TC], dv, xi_v(dt)[:, 3:3 + TC], ALU.mult,
                 rd=[("delta", dt), ("xi", dt)], wr=[("w", dt)])
        hkeys = [("hall", d_) for d_ in range(ET)]
        for n in range(NST):
            np_ = n % 2
            p.mm(ps_B, sel[:, n * 128:(n + 1) * 128], Bt_bf[:, :],
                 rd=[("Bt",), ("msel",)], wr=[("pso",)])
            p.act(Bb[np_][:, :], ps_B, AF.Identity, rd=[("pso",)], wr=[("Bb", np_)])
            p.mm(ps_C, sel[:, n * 128:(n + 1) * 128], Ct_bf[:, :],
                 rd=[("Ct",), ("msel",)], wr=[("pso",)])
            p.act(Cb[np_][:, :], ps_C, AF.Identity, rd=[("pso",)], wr=[("Cb", np_)])
            p.tt(dBuall[:, :].rearrange("p (d t) -> p d t", t=TC),
                 wdu[:, :].rearrange("p (d t) -> p d t", t=TC),
                 Bb[np_][:, :].rearrange("p (o t) -> p o t", o=1).broadcast_to((128, ET, TC)),
                 ALU.mult,
                 rd=[("w", d_) for d_ in range(ET)] + [("Bb", np_)], wr=[("dbu",)])
            for dt in range(ET):
                a3 = dt % 2
                p.act(dAb[a3][:, :], delta[:, dt * TC:(dt + 1) * TC], AF.Exp,
                      scale=Asb[:, dt * NST + n: dt * NST + n + 1],
                      rd=[("delta", dt), ("mA",)], wr=[("dA", a3)])
                p.scan(hall[:, dt * TC:(dt + 1) * TC], dAb[a3][:, :],
                       dBuall[:, dt * TC:(dt + 1) * TC],
                       hstate[:, dt * NST + n: dt * NST + n + 1],
                       rd=[("dA", a3), ("dbu",), ("hst", n)], wr=[("hall", dt)])
            p.cp(hstate[:, :].rearrange("p (d n2) -> p d n2", n2=NST)[:, :, n:n + 1],
                 hall[:, :].rearrange("p (d t) -> p d t", t=TC)[:, :, TC - 1:TC],
                 rd=hkeys, wr=[("hst", n)], eng="gpsimd")
            p.tt(hall[:, :].rearrange("p (d t) -> p d t", t=TC),
                 hall[:, :].rearrange("p (d t) -> p d t", t=TC),
                 Cb[np_][:, :].rearrange("p (o t) -> p o t", o=1).broadcast_to((128, ET, TC)),
                 ALU.mult,
                 rd=hkeys + [("Cb", np_), ("hst", n)], wr=hkeys)
            p.tt(yac[:, :], yac[:, :], hall[:, :], ALU.add,
                 rd=hkeys + [("yac", d_) for d_ in range(ET)],
                 wr=[("yac", d_) for d_ in range(ET)], eng="gpsimd")
        for et in range(ET):
            yv = yac[:, et * TC:(et + 1) * TC]
            p.tt(ygbf[:, et * TC:(et + 1) * TC], yv, zs[:, et * TC:(et + 1) * TC],
                 ALU.mult, rd=[("yac", et), ("zs", et)],
                 wr=[("yg", et), ("hall", et // 2)])
        for et in range(ET):
            wp = et % 2
            p.dma(wo_et[wp][:, :], scratch["B_wout"][et * 128:(et + 1) * 128, :],
                  rd=[("B", "B_wout")], wr=[("winb", wp)], q="scalar")
            for tsub in range(TSUB):
                for oh in range(2):
                    p.mm(ps_o[:, (tsub * 2 + oh) * 512:(tsub * 2 + oh + 1) * 512],
                         ygbf[:, et * TC + tsub * 128: et * TC + tsub * 128 + 128],
                         wo_et[wp][:, oh * 512:(oh + 1) * 512],
                         rd=[("yg", et), ("hall", et // 2), ("winb", wp)], wr=[("pso",)],
                         start=(et == 0), stop=(et == ET - 1))
        for tsub in range(TSUB):
            tt = ch * TSUB + tsub
            for oh in range(2):
                xs = X[:, tt * D + oh * 512: tt * D + (oh + 1) * 512]
                p.tt(xs, xs, ps_o[:, (tsub * 2 + oh) * 512:(tsub * 2 + oh + 1) * 512],
                     ALU.add, rd=[("X", tt), ("pso",)], wr=[("X", tt)])



def emit_moe(p, st, c, X, dram, scratch):
    nc = p.nc
    TTI = T // 128
    HT = HMOE // 128
    xnT = st.enter_context(nc.sbuf_tensor("mo_xnT", [128, DT * T], BF16))
    w1bf = st.enter_context(nc.sbuf_tensor("mo_w1", [128, DT * HMOE], BF16))
    w2bf = st.enter_context(nc.sbuf_tensor("mo_w2", [128, HT * D], BF16))
    hbf = st.enter_context(nc.sbuf_tensor("mo_h", [128, HT * 256], BF16))
    gwts = st.enter_context(nc.sbuf_tensor("mo_gw", [128, TTI * NEXP], F32))
    sgA = st.enter_context(nc.sbuf_tensor("mo_sgA", [128, TTI * NEXP], F32))
    gm = st.enter_context(nc.sbuf_tensor("mo_gm", [128, TTI], F32))
    gmk = st.enter_context(nc.sbuf_tensor("mo_gmk", [128, TTI * NEXP], F32))
    stats = [st.enter_context(nc.sbuf_tensor(f"mo_stats{i}", [128, 8], F32)) for i in range(2)]
    scrA = st.enter_context(nc.sbuf_tensor("mo_scrA", [128, 2048], F32))
    scrB = st.enter_context(nc.sbuf_tensor("mo_scrB", [128, 2048], F32))
    rowst = st.enter_context(nc.sbuf_tensor("mo_rowst", [1, 512], F32))
    _xb = st.enter_context(nc.sbuf_tensor("mo_xnbf", [128, 1024], BF16))
    xnbf_t = [_xb, _xb]
    gwT_bf = st.enter_context(nc.sbuf_tensor("mo_gwT", [128, DT * NEXP], BF16))
    gb_bf = st.enter_context(nc.sbuf_tensor("mo_gb", [1, NEXP], BF16))
    b1sb = st.enter_context(nc.sbuf_tensor("mo_b1", [128, NEXP * HT], F32))
    b2bf = [st.enter_context(nc.sbuf_tensor("mo_b2", [1, D], BF16))] * 2
    ps_trB = st.enter_context(nc.psum_tensor("mo_ptr", [128, 1024], BF16))
    ps_a = st.enter_context(nc.psum_tensor("mo_pa", [128, 512], F32))
    ps_h = [st.enter_context(nc.psum_tensor(f"mo_ph{i}", [128, 512], F32)) for i in range(2)]
    ps_o = [st.enter_context(nc.psum_tensor(f"mo_po{i}", [128, 512], F32)) for i in range(2)]
    ps_g = ps_a[:, 0:NEXP]

    p.dma(gwT_bf[:, :].rearrange("p (dc e) -> p dc e", e=NEXP),
          scratch["B_gwT"].rearrange("(dc p) e -> p dc e", p=128),
          rd=[("B", "B_gwT")], wr=[("gwT",)], q="scalar")
    p.dma(rowst[0:1, 0:NEXP], dram["g_b"], wr=[("sAb",)])
    p.cp(gb_bf[:, :], rowst[0:1, 0:NEXP], rd=[("sAb",)], wr=[("gb",)])
    p.dma(b1sb[:, :].rearrange("p (e h) -> p e h", h=HT), dram["g_b1"], wr=[("b1",)])
    for tt in range(TTI):
        tp = tt % 2
        xt = X[:, tt * D:(tt + 1) * D]
        emit_ln(p, xt, xnbf_t[tp][:, :], stats[tp], scrA[:, 0:1024],
                LN_EPS, rd_x=[("X", tt)], wr_out=[("xnbf", tp)], tag=("gln", tp),
                sckey=("sA",))
        dst3d = xnT[:, :].rearrange("p (k t) -> p k t", t=T)[:, :, tt * 128:tt * 128 + 128]
        emit_tr8(p, dst3d, xnbf_t[tp][:, :], ps_trB, c["ident_bf"],
                 rd_src=[("xnbf", tp)],
                 wr_dst=[("xnT", tt, kd) for kd in range(DT)], pkey=("pstr",))
        for kd in range(DT):
            p.mm(ps_g, xnT[:, kd * T + tt * 128: kd * T + tt * 128 + 128],
                 gwT_bf[:, kd * NEXP:(kd + 1) * NEXP],
                 rd=[("xnT", tt, kd), ("gwT",)], wr=[("psg",)],
                 start=(kd == 0), stop=False)
        p.mm(ps_g, c["ones_bf"][0:1, 0:128], gb_bf[0:1, :], rd=[("gb",)], wr=[("psg",)],
             start=False, stop=True)
        p.act(sgA[:, tt * NEXP:(tt + 1) * NEXP], ps_g, AF.Identity,
              rd=[("psg",)], wr=[("sgA",)])
    sg3 = sgA[:, :].rearrange("p (t e) -> p t e", e=NEXP)
    gm3 = gm[:, :].rearrange("p (t o) -> p t o", o=1).broadcast_to((128, TTI, NEXP))
    gk3 = gmk[:, :].rearrange("p (t e) -> p t e", e=NEXP)
    gw3 = gwts[:, :].rearrange("p (t e) -> p t e", e=NEXP)
    p.red(gm[:, :], sg3, ALU.max, rd=[("sgA",)], wr=[("gm",)])
    p.tt(sg3, sg3, gm3, ALU.subtract, rd=[("sgA",), ("gm",)], wr=[("sgA",)])
    p.act(sgA[:, :], sgA[:, :], AF.Exp, rd=[("sgA",)], wr=[("sgA",)])
    p.red(gm[:, :], sg3, ALU.add, rd=[("sgA",)], wr=[("gm",)])
    p.recip(gm[:, :], gm[:, :], rd=[("gm",)], wr=[("gm",)])
    p.tt(sg3, sg3, gm3, ALU.mult, rd=[("sgA",), ("gm",)], wr=[("sgA",)])
    p.red(gm[:, :], sg3, ALU.max, rd=[("sgA",)], wr=[("gm",)])
    p.tt(gk3, sg3, gm3, ALU.is_equal, rd=[("sgA",), ("gm",)], wr=[("gmk",)])
    p.stt(gmk[:, :], gmk[:, :], -1e9, sgA[:, :], ALU.mult, ALU.add,
          rd=[("gmk",), ("sgA",)], wr=[("gmk",)])
    p.red(gm[:, :], gk3, ALU.max, rd=[("gmk",)], wr=[("gm",)])
    p.tt(gk3, sg3, gm3, ALU.is_ge, rd=[("sgA",), ("gm",)], wr=[("gmk",)])
    p.tt(gw3, sg3, gk3, ALU.mult,
         rd=[("sgA",), ("gmk",)], wr=[("gwts", t_) for t_ in range(TTI)])
    for e in range(NEXP):
        ep = e % 2
        for dc in range(DT):
            sc, key, qq = (scrA, ("sw1", 0), "sync") if dc % 2 == 0 else (scrB, ("sw1", 1), "gpsimd")
            p.dma(sc[:, :], dram["g_w1"][e, dc * 128:(dc + 1) * 128, :],
                  rd=[key, ("sA",)], wr=[key, ("sA",)] if dc % 2 == 0 else [key], q=qq)
            p.cp(w1bf[:, dc * HMOE:(dc + 1) * HMOE], sc[:, :],
                 rd=[key], wr=[("w1", dc)], eng="vector" if dc % 2 == 0 else "gpsimd")
        for ht in range(HT):
            sc, key = (scrA[:, 0:D], ("sw1", 0)) if ht % 2 == 0 else (scrB[:, 0:D], ("sw1", 1))
            p.dma(sc, dram["g_w2"][e, ht * 128:(ht + 1) * 128, :],
                  rd=[key], wr=[key], q="scalar")
            p.cp(w2bf[:, ht * D:(ht + 1) * D], sc,
                 rd=[key], wr=[("w2", ht)], eng="vector" if ht % 2 == 0 else "gpsimd")
        for bh in range(2):
            p.dma(rowst[0:1, :], dram["g_b2"][e:e + 1, bh * 512:(bh + 1) * 512],
                  rd=[("sAb",)], wr=[("sAb",)])
            p.cp(b2bf[ep][:, bh * 512:(bh + 1) * 512], rowst[0:1, :],
                 rd=[("sAb",)], wr=[("b2", ep)])
        for tc in range(T // 256):
            for ht in range(HT):
                hp = ht % 2
                for dc in range(DT):
                    p.mm(ps_h[hp][:, :].rearrange("p (a b) -> p a b", a=2)[:, 0, 0:256]
                         if False else ps_h[hp][:, 0:256],
                         w1bf[:, dc * HMOE + ht * 128: dc * HMOE + ht * 128 + 128],
                         xnT[:, dc * T + tc * 256: dc * T + tc * 256 + 256],
                         rd=[("w1", dc)] + [("xnT", tc * 2 + q_, dc) for q_ in range(2)],
                         wr=[("psh", hp)],
                         start=(dc == 0), stop=(dc == DT - 1))
                p.act(hbf[:, ht * 256:(ht + 1) * 256], ps_h[hp][:, 0:256], AF.Relu,
                      bias=b1sb[:, e * HT + ht: e * HT + ht + 1],
                      rd=[("psh", hp), ("b1",)], wr=[("hbf", ht)])
            for tsub in range(2):
                tt = tc * 2 + tsub
                gcol = gwts[:, tt * NEXP + e: tt * NEXP + e + 1]
                for oh in range(2):
                    for ht in range(HT):
                        p.mm(ps_o[oh][:, :], hbf[:, ht * 256 + tsub * 128: ht * 256 + tsub * 128 + 128],
                             w2bf[:, ht * D + oh * 512: ht * D + oh * 512 + 512],
                             rd=[("hbf", ht), ("w2", ht)], wr=[("pso", oh)],
                             start=(ht == 0), stop=False)
                    p.mm(ps_o[oh][:, :], c["ones_bf"][0:1, 0:128], b2bf[ep][0:1, oh * 512:(oh + 1) * 512],
                         rd=[("b2", ep)], wr=[("pso", oh)], start=False, stop=True)
                    xs = X[:, tt * D + oh * 512: tt * D + (oh + 1) * 512]
                    p.stt(xs, ps_o[oh][:, :], gcol, xs, ALU.mult, ALU.add,
                          rd=[("pso", oh), ("gwts", tt), ("X", tt)], wr=[("X", tt)])



ATTN_PHASES = "ABqkvwstzcuTd"


def emit_attn(p, st, c, X, dram, scratch):
    nc = p.nc
    TTI = T // 128
    qT = st.enter_context(nc.sbuf_tensor("at_qT", [128, DT * T], BF16))
    wqkv = st.enter_context(nc.sbuf_tensor("at_wqkv", [128, 3 * DT * D], BF16))
    wd_bf = wqkv[:, 0:DT * D]
    bias_bf = st.enter_context(nc.sbuf_tensor("at_bias", [1, 4 * D], BF16))
    gb_bc = st.enter_context(nc.sbuf_tensor("at_gb", [128, 2 * D], F32))
    scr3 = st.enter_context(nc.sbuf_tensor("at_scr3", [128, 3072], F32))
    stats = [st.enter_context(nc.sbuf_tensor(f"at_stats{i}", [128, 8], F32)) for i in range(2)]
    xnbf_p = [st.enter_context(nc.sbuf_tensor(f"at_xnbf{i}", [128, 1024], BF16)) for i in range(2)]
    xnT_p = [st.enter_context(nc.sbuf_tensor(f"at_xnT{i}", [128, DT * 128], BF16)) for i in range(2)]
    qkbf = st.enter_context(nc.sbuf_tensor("at_qkbf", [128, 1024], BF16))
    vbf = st.enter_context(nc.sbuf_tensor("at_vbf", [128, 1024], BF16))
    kvbf = st.enter_context(nc.sbuf_tensor("at_kvbf", [128, NH * HD], BF16))
    kvacc = st.enter_context(nc.sbuf_tensor("at_kvacc", [64, NH * HD], F32))
    ksacc = st.enter_context(nc.sbuf_tensor("at_ksacc", [128, DT], F32))
    ksz = st.enter_context(nc.sbuf_tensor("at_ksz", [128, NH], BF16))
    ksf = st.enter_context(nc.sbuf_tensor("at_ksf", [128, DT], F32))
    zrec = st.enter_context(nc.sbuf_tensor("at_z", [128, NH], F32))
    czbf = st.enter_context(nc.sbuf_tensor("at_cz", [128, 1024], BF16))
    ctxT_t = st.enter_context(nc.sbuf_tensor("at_ctxT", [128, DT * 128], BF16))
    ps_trB = st.enter_context(nc.psum_tensor("at_ptr", [128, 1024], BF16))
    ps_w = [st.enter_context(nc.psum_tensor(f"at_pw{i}", [128, 512], F32)) for i in range(2)]
    ps_kv = st.enter_context(nc.psum_tensor("at_pkv", [64, NH * HD], F32))
    ps_small = st.enter_context(nc.psum_tensor("at_psm", [128, 512], F32))
    ps_ksc = ps_small[:, 0:DT]
    ps_z = ps_small[:, 16:16 + NH]
    ps_ctx = st.enter_context(nc.psum_tensor("at_pctx", [128, 1024], F32))
    p.memset(kvacc[:, :], 0.0, wr=[("kvacc",)])
    p.memset(ksacc[:, :], 0.0, wr=[("ksacc",)])
    efp = scr3[:, 0:1024]
    et0 = scr3[:, 1024:1536]
    et1 = scr3[:, 1536:2048]
    scrB = scr3[:, 1024:2048]
    hsn = scr3[:, 2048:3072]

    for j, wname in enumerate(("B_wq", "B_wk", "B_wv")):
        p.dma(wqkv[:, j * DT * D:(j + 1) * DT * D].rearrange("p (dc dd) -> p dc dd", dd=D),
              scratch[wname].rearrange("(dc p) dd -> p dc dd", p=128),
              rd=[("B", wname)], wr=[("wqkv", j)], q="sync")
    for j, bname in enumerate(("a_bq", "a_bk", "a_bv", "a_bd")):
        p.dma(scrB[0:1, 0:D], dram[bname], rd=[("scd",)], wr=[("scd",)])
        p.cp(bias_bf[:, j * D:(j + 1) * D], scrB[0:1, 0:D], rd=[("scd",)], wr=[("abias", j)])
    for j, gname in enumerate(("a_gamma", "a_beta")):
        p.dma(scrB[0:1, 0:D], dram[gname], rd=[("scd",)], wr=[("scd",)])
        p.cp(qkbf[0:1, 0:D], scrB[0:1, 0:D], rd=[("scd",)], wr=[("qkbf",)])
        for oh in range(2):
            p.mm(ps_w[oh][:, :], c["ones_bf"][0:1, 0:128], qkbf[0:1, oh * 512:(oh + 1) * 512],
                 rd=[("qkbf",)], wr=[("psw", oh)])
            p.act(gb_bc[:, j * D + oh * 512: j * D + (oh + 1) * 512], ps_w[oh][:, :],
                  AF.Identity, rd=[("psw", oh)], wr=[("gbbc",)])

    def proj(j, oh, pw, xnT_t, xk):
        for kd in range(DT):
            p.mm(ps_w[pw][:, :], xnT_t[:, kd * 128:(kd + 1) * 128],
                 wqkv[:, (j * DT + kd) * D + oh * 512:(j * DT + kd) * D + (oh + 1) * 512],
                 rd=[("xnTt", xk), ("wqkv", j)], wr=[("psw", pw)],
                 start=(kd == 0), stop=False)
        p.mm(ps_w[pw][:, :], c["ones_bf"][0:1, 0:128],
             bias_bf[0:1, j * D + oh * 512: j * D + (oh + 1) * 512],
             rd=[("abias", j)], wr=[("psw", pw)], start=False, stop=True)

    def elu1(dst_half, pw, wrk):
        p.ts(et0, ps_w[pw][:, :], 0.0, ALU.min, rd=[("psw", pw), ("scd",)], wr=[("scd",)])
        p.act(et1, et0, AF.Exp, rd=[("scd",)], wr=[("scd",)])
        p.stt(dst_half, ps_w[pw][:, :], 0.0, et1, ALU.max, ALU.add,
              rd=[("psw", pw), ("scd",)], wr=[wrk])

    for tt in range(TTI):
        tp = tt % 2
        xnbf_t = xnbf_p[tp]
        xnT_t = xnT_p[tp]
        xt = X[:, tt * D:(tt + 1) * D]
        emit_ln(p, xt, xnbf_t[:, :], stats[tp], scrB,
                LN_EPS, rd_x=[("X", tt)], wr_out=[("xnbf", tp)], tag=("aln", tp),
                sckey=("scd",))
        emit_tr8(p, xnT_t[:, :].rearrange("p (k f) -> p k f", f=128),
                 xnbf_t[:, :], ps_trB, c["ident_bf"], rd_src=[("xnbf", tp)],
                 wr_dst=[("xnTt", tp)], pkey=("pstr",))
        for oh in range(2):
            proj(0, oh, oh, xnT_t, tp)
            elu1(efp[:, oh * 512:(oh + 1) * 512], oh, ("efp", oh))
        p.cp(qkbf[:, :], efp, rd=[("efp", 0), ("efp", 1)], wr=[("qkbf",)])
        qdst = qT[:, :].rearrange("p (k t) -> p k t", t=T)[:, :, tt * 128:tt * 128 + 128]
        emit_tr8(p, qdst, qkbf[:, :], ps_trB, c["ident_bf"],
                 rd_src=[("qkbf",)],
                 wr_dst=[("qT", tt, kd) for kd in range(DT)], pkey=("pstr",))
        for oh in range(2):
            proj(1, oh, oh, xnT_t, tp)
            elu1(efp[:, oh * 512:(oh + 1) * 512], oh, ("efp", oh))
        p.cp(qkbf[:, :], efp, rd=[("efp", 0), ("efp", 1)], wr=[("qkbf",)])
        for oh in range(2):
            proj(2, oh, oh, xnT_t, tp)
            p.act(vbf[:, oh * 512:(oh + 1) * 512], ps_w[oh][:, :], AF.Identity,
                  rd=[("psw", oh)], wr=[("vbf",)])
        for h in range(NH):
            p.mm(ps_kv[:, h * HD:(h + 1) * HD], qkbf[:, h * HD:(h + 1) * HD],
                 vbf[:, h * HD:(h + 1) * HD],
                 rd=[("qkbf",), ("vbf",)], wr=[("pskv",)])
        p.tt(kvacc[:, :], kvacc[:, :], ps_kv[:, :], ALU.add,
             rd=[("kvacc",), ("pskv",)], wr=[("kvacc",)])
        for ic in range(DT):
            p.mm(ps_small[:, ic:ic + 1], qkbf[:, ic * 128:(ic + 1) * 128],
                 c["onescol_bf"][:, :],
                 rd=[("qkbf",)], wr=[("psks",)])
        p.tt(ksacc[:, :], ksacc[:, :], ps_small[:, 0:DT], ALU.add,
             rd=[("ksacc",), ("psks",)], wr=[("ksacc",)])
    kvt = vbf[0:64, :]
    p.act(kvt, kvacc[:, :], AF.Identity, rd=[("kvacc",)], wr=[("vbf",), ("kvt",)])
    p.memset(kvbf[:, :], 0.0, wr=[("kv0",)])
    p.cp(kvbf[0:64, :].rearrange("p (h m) -> p h m", m=2 * HD)[:, :, 0:HD],
         kvt.rearrange("p (h m) -> p h m", m=2 * HD)[:, :, 0:HD],
         rd=[("kvt",), ("kv0",)], wr=[("kv0",)])
    p.dma(kvbf[64:128, :].rearrange("p (h m) -> p h m", m=2 * HD)[:, :, HD:2 * HD],
          kvt.rearrange("p (h m) -> p h m", m=2 * HD)[:, :, HD:2 * HD],
          rd=[("kvt",), ("kv0",)], wr=[("kv0",)])
    p.act(ksf[:, :], ksacc[:, :], AF.Identity, bias=p.ccol(EPS_ATTN),
          rd=[("ksacc",)], wr=[("ksf",)])
    p.memset(ksz[:, :], 0.0, wr=[("ksbf",)])
    p.cp(ksz[0:64, :].rearrange("p (a b) -> p a b", b=2)[:, :, 0:1],
         ksf[0:64, :].rearrange("p (a o) -> p a o", o=1),
         rd=[("ksf",), ("ksbf",)], wr=[("ksbf",)])
    p.cp(ksz[64:128, :].rearrange("p (a b) -> p a b", b=2)[:, :, 1:2],
         ksf[64:128, :].rearrange("p (a o) -> p a o", o=1),
         rd=[("ksf",), ("ksbf",)], wr=[("ksbf",)])
    p.dma(wd_bf.rearrange("p (dc dd) -> p dc dd", dd=D),
          scratch["B_wd"].rearrange("(dc p) dd -> p dc dd", p=128),
          rd=[("B", "B_wd"), ("wqkv", 0)], wr=[("wqkv", 0), ("wd",)], q="sync")
    for tt in (range(TTI) if "B" in ATTN_PHASES and "A" in ATTN_PHASES else []):
        tp = tt % 2
        if "z" in ATTN_PHASES:
            for h in range(NH):
                ic = h // 2
                p.mm(ps_small[:, 16 + h:17 + h],
                     qT[:, ic * T + tt * 128: ic * T + tt * 128 + 128],
                     ksz[:, h:h + 1],
                     rd=[("qT", tt, ic), ("ksbf",)], wr=[("psks",)])
            p.recip(zrec[:, :], ps_z, rd=[("psks",)], wr=[("zrec",)])
        if "c" in ATTN_PHASES:
            for h in range(NH):
                ic = h // 2
                p.mm(ps_ctx[:, h * HD:(h + 1) * HD],
                     qT[:, ic * T + tt * 128: ic * T + tt * 128 + 128],
                     kvbf[:, h * HD:(h + 1) * HD],
                     rd=[("qT", tt, ic), ("kv0",)], wr=[("psctx",)])
            for h in (range(NH) if "u" in ATTN_PHASES else []):
                p.ts(czbf[:, h * HD:(h + 1) * HD], ps_ctx[:, h * HD:(h + 1) * HD],
                     zrec[:, h:h + 1], ALU.mult,
                     rd=[("psctx",), ("zrec",)], wr=[("czbf",)])
        if "T" in ATTN_PHASES:
            emit_tr8(p, ctxT_t[:, :].rearrange("p (k f) -> p k f", f=128),
                     czbf[:, :], ps_trB, c["ident_bf"], rd_src=[("czbf",)],
                     wr_dst=[("ctxT", kd) for kd in range(DT)], pkey=("pstr",))
        xt = X[:, tt * D:(tt + 1) * D]
        if "d" in ATTN_PHASES:
            emit_ln(p, xt, efp, stats[tp], scrB,
                    LN_EPS, rd_x=[("X", tt)], wr_out=[("efp", 0), ("efp", 1)], tag=("aln2", tp),
                    sckey=("scd",))
            for oh in range(2):
                for kd in range(DT):
                    p.mm(ps_w[1][:, :], ctxT_t[:, kd * 128:(kd + 1) * 128],
                         wd_bf[:, kd * D + oh * 512: kd * D + (oh + 1) * 512],
                         rd=[("ctxT", kd), ("wd",)], wr=[("psw", 1)],
                         start=(kd == 0), stop=False)
                p.mm(ps_w[1][:, :], c["ones_bf"][0:1, 0:128],
                     bias_bf[0:1, 3 * D + oh * 512: 3 * D + (oh + 1) * 512],
                     rd=[("abias", 3)], wr=[("psw", 1)], start=False, stop=True)
                p.tt(hsn[:, oh * 512:(oh + 1) * 512], ps_w[1][:, :],
                     efp[:, oh * 512:(oh + 1) * 512], ALU.add,
                     rd=[("psw", 1), ("efp", oh)], wr=[("hsn",)])
            emit_ln(p, hsn, hsn, stats[tp], scrB,
                    EPS_LN_ATTN, rd_x=[("hsn",)], wr_out=[("hsn",)], tag=("aln3", tp),
                    sckey=("scd",))
            p.tt(hsn, hsn, gb_bc[:, 0:D], ALU.mult,
                 rd=[("hsn",), ("gbbc",)], wr=[("hsn",)])
            p.tt(hsn, hsn, gb_bc[:, D:2 * D], ALU.add,
                 rd=[("hsn",), ("gbbc",)], wr=[("hsn",)])
            p.tt(xt, xt, hsn, ALU.add,
                 rd=[("X", tt), ("hsn",)], wr=[("X", tt)])



def emit_ffn1(p, st, c, X, dram, scratch):
    nc = p.nc
    TTI = T // 128
    JT = NB // 128
    HT1 = H1 // 128
    xnT = st.enter_context(nc.sbuf_tensor("ff_xnT", [128, DT * T], BF16))
    UT_bf = st.enter_context(nc.sbuf_tensor("ff_UT", [128, DT * NB], BF16))
    rWT_bf = st.enter_context(nc.sbuf_tensor("ff_rWT", [128, DT * NB], BF16))
    rV_bf = st.enter_context(nc.sbuf_tensor("ff_rV", [128, JT * D], BF16))
    rb_bf = st.enter_context(nc.sbuf_tensor("ff_rb", [1, NB], BF16))
    lngb = st.enter_context(nc.sbuf_tensor("ff_lngb", [128, 2 * D], F32))
    gamb = st.enter_context(nc.sbuf_tensor("ff_gamb", [128, D], F32))
    scrA = st.enter_context(nc.sbuf_tensor("ff_scrA", [128, 2048], F32))
    scrB = st.enter_context(nc.sbuf_tensor("ff_scrB", [128, 2048], F32))
    stats = [st.enter_context(nc.sbuf_tensor(f"ff_stats{i}", [128, 8], F32)) for i in range(2)]
    invUV = st.enter_context(nc.sbuf_tensor("ff_iuv", [128, 16], F32))
    ral = st.enter_context(nc.sbuf_tensor("ff_ral", [128, NB], F32))
    alpha = st.enter_context(nc.sbuf_tensor("ff_alpha", [128, NB], F32))
    aw = st.enter_context(nc.sbuf_tensor("ff_aw", [128, NB], F32))
    m8 = st.enter_context(nc.sbuf_tensor("ff_m8", [128, 8], F32))
    xr = st.enter_context(nc.sbuf_tensor("ff_xr", [128, D], F32))
    xrbf = st.enter_context(nc.sbuf_tensor("ff_xrbf", [128, D], BF16))
    xrT_t = st.enter_context(nc.sbuf_tensor("ff_xrT", [128, DT * 128], BF16))
    czbf = st.enter_context(nc.sbuf_tensor("ff_cz", [128, NB], BF16))
    cT_t = st.enter_context(nc.sbuf_tensor("ff_cT", [128, JT * 128], BF16))
    h1t = [st.enter_context(nc.sbuf_tensor(f"ff_h1{i}", [128, 256], BF16)) for i in range(2)]
    w1t = [st.enter_context(nc.sbuf_tensor(f"ff_w1t{i}", [128, 2 * DT * 128], BF16)) for i in range(2)]
    w2t = [st.enter_context(nc.sbuf_tensor(f"ff_w2t{i}", [128, 2 * D], BF16)) for i in range(2)]
    ps_trB = st.enter_context(nc.psum_tensor("ff_ptr", [128, 1024], BF16))
    ps_s = st.enter_context(nc.psum_tensor("ff_ps", [128, 2048], F32))
    ps_h0 = st.enter_context(nc.psum_tensor("ff_ph0", [128, 256], F32))
    ps_h1 = st.enter_context(nc.psum_tensor("ff_ph1", [128, 256], F32))
    ps_h = [ps_h0[:, :], ps_h1[:, :]]
    ps_wk = st.enter_context(nc.psum_tensor("ff_pwk", [128, 512], F32))

    p.dma(UT_bf[:, :].rearrange("p (dc nb) -> p dc nb", nb=NB),
          scratch["B_UT"].rearrange("(dc p) nb -> p dc nb", p=128),
          rd=[("B", "B_UT")], wr=[("UT",)], q="sync")
    p.dma(rWT_bf[:, :].rearrange("p (dc nb) -> p dc nb", nb=NB),
          scratch["B_rWT"].rearrange("(dc p) nb -> p dc nb", p=128),
          rd=[("B", "B_rWT")], wr=[("rWT",)], q="sync")
    p.dma(rV_bf[:, :].rearrange("p (jt dd) -> p jt dd", dd=D),
          scratch["B_rV"].rearrange("(jt p) dd -> p jt dd", p=128),
          rd=[("B", "B_rV")], wr=[("rV",)], q="sync")
    p.dma(scrB[0:1, 0:NB], dram["f_rb"], wr=[("sB",)])
    p.cp(rb_bf[:, :], scrB[0:1, 0:NB], rd=[("sB",)], wr=[("rb",)])
    for j, gname in enumerate(("f_lng", "f_lnb", "f_gam")):
        p.dma(scrB[0:1, 0:D], dram[gname], wr=[("sB",)])
        p.cp(xrbf[0:1, 0:D], scrB[0:1, 0:D], rd=[("sB",)], wr=[("xrbf",)])
        for oh in range(2):
            p.mm(ps_wk[:, :], c["ones_bf"][0:1, 0:128], xrbf[0:1, oh * 512:(oh + 1) * 512],
                 rd=[("xrbf",)], wr=[("pswk",)])
            dst = lngb[:, j * D + oh * 512: j * D + (oh + 1) * 512] if j < 2 else \
                  gamb[:, oh * 512:(oh + 1) * 512]
            p.act(dst, ps_wk[:, :], AF.Identity, rd=[("pswk",)], wr=[("lngb",)])
    for jt in range(JT):
        p.dma(scrB[:, 0:D], dram["f_U"][jt * 128:(jt + 1) * 128, :], wr=[("sB",)])
        p.act(scrB[:, 1024:2048], scrB[:, 0:D], AF.Square, accum_out=invUV[:, jt:jt + 1],
              rd=[("sB",)], wr=[("sB2",), ("iuv",)])
        p.dma(scrB[:, 0:D], dram["f_V"][jt * 128:(jt + 1) * 128, :], wr=[("sB",)])
        p.act(scrB[:, 1024:2048], scrB[:, 0:D], AF.Square, accum_out=invUV[:, 4 + jt:5 + jt],
              rd=[("sB",)], wr=[("sB2",), ("iuv",)])
    p.act(invUV[:, 8:16], invUV[:, 0:8], AF.Ln, rd=[("iuv",)], wr=[("iuv",)])
    p.act(invUV[:, 0:8], invUV[:, 8:16], AF.Exp, scale=p.ccol(0.5), rd=[("iuv",)], wr=[("iuv",)])
    p.ts(invUV[:, 0:8], invUV[:, 0:8], FFN_EPS, ALU.max, rd=[("iuv",)], wr=[("iuv",)])
    p.recip(invUV[:, 0:8], invUV[:, 0:8], rd=[("iuv",)], wr=[("iuv",)])
    p.tt(invUV[:, 8:12], invUV[:, 0:4], invUV[:, 4:8], ALU.mult, rd=[("iuv",)], wr=[("iuv",)])
    for tt in range(TTI):
        tp = tt % 2
        xt = X[:, tt * D:(tt + 1) * D]
        emit_ln(p, xt, xrbf[:, :], stats[tp], scrA[:, 0:1024],
                LN_EPS, rd_x=[("X", tt)], wr_out=[("xrbf",)], tag=("fln", tp))
        dst3d = xnT[:, :].rearrange("p (k t) -> p k t", t=T)[:, :, tt * 128:tt * 128 + 128]
        emit_tr8(p, dst3d, xrbf[:, :], ps_trB, c["ident_bf"],
                 rd_src=[("xrbf",)],
                 wr_dst=[("xnT", tt, kd) for kd in range(DT)], pkey=("pstr",))
    for tt in range(TTI):
        tp = tt % 2
        xt = X[:, tt * D:(tt + 1) * D]
        emit_ln(p, xt, scrA[:, 0:1024], stats[tp], scrA[:, 1024:2048],
                LN_EPS, rd_x=[("X", tt)], wr_out=[("xn",)], tag=("fln2", tp))
        emit_ln(p, scrA[:, 0:1024], xr[:, :], stats[tp], scrA[:, 1024:2048],
                LN_EPS, rd_x=[("xn",)], wr_out=[("xr",)], tag=("fln3", tp))
        p.tt(xr[:, :], xr[:, :], lngb[:, 0:D], ALU.mult, rd=[("xr",), ("lngb",)], wr=[("xr",)])
        p.tt(xr[:, :], xr[:, :], lngb[:, D:2 * D], ALU.add, rd=[("xr",), ("lngb",)], wr=[("xr",)])
        p.cp(xrbf[:, :], xr[:, :], rd=[("xr",)], wr=[("xrbf",)])
        emit_tr8(p, xrT_t[:, :].rearrange("p (k f) -> p k f", f=128),
                 xrbf[:, :], ps_trB, c["ident_bf"], rd_src=[("xrbf",)],
                 wr_dst=[("xrT", kd) for kd in range(DT)], pkey=("pstr",))
        for kd in range(DT):
            p.mm(ps_wk[:, :], xrT_t[:, kd * 128:(kd + 1) * 128],
                 rWT_bf[:, kd * NB:(kd + 1) * NB],
                 rd=[("xrT", kd), ("rWT",)], wr=[("pswk",)],
                 start=(kd == 0), stop=False)
        p.mm(ps_wk[:, :], c["ones_bf"][0:1, 0:128], rb_bf[0:1, :],
             rd=[("rb",)], wr=[("pswk",)], start=False, stop=True)
        p.ts(ral[:, :], ps_wk[:, :], TAU, ALU.min, -TAU, ALU.max,
             rd=[("pswk",)], wr=[("ral",)])
        p.act(alpha[:, :], ral[:, :], AF.Abs, rd=[("ral",)], wr=[("alpha",)])
        p.act(aw[:, :], alpha[:, :], AF.Exp, scale=-1.0, rd=[("alpha",)], wr=[("aw",)])
        p.act(alpha[:, :], aw[:, :], AF.Ln, bias=1.0, rd=[("aw",)], wr=[("alpha",)])
        p.stt(alpha[:, :], ral[:, :], 0.0, alpha[:, :], ALU.max, ALU.add,
              rd=[("ral",), ("alpha",)], wr=[("alpha",)])
        p.max8(m8[:, :], alpha[:, :], rd=[("alpha",)], wr=[("m8",)])
        p.ts(aw[:, :], alpha[:, :], m8[:, 7:8], ALU.is_ge, rd=[("alpha",), ("m8",)], wr=[("aw",)])
        p.tt(aw[:, :], aw[:, :], alpha[:, :], ALU.mult, rd=[("aw",), ("alpha",)], wr=[("aw",)])
        p.red(stats[tp][:, 6:7], aw[:, :], ALU.add, rd=[("aw",)], wr=[("fst", tp)])
        p.act(stats[tp][:, 7:8], stats[tp][:, 6:7], AF.Tanh, rd=[("fst", tp)], wr=[("fst", tp)])
        p.ts(stats[tp][:, 6:7], stats[tp][:, 6:7], FFN_EPS, ALU.add, rd=[("fst", tp)], wr=[("fst", tp)])
        p.recip(stats[tp][:, 6:7], stats[tp][:, 6:7], rd=[("fst", tp)], wr=[("fst", tp)])
        p.tt(stats[tp][:, 6:7], stats[tp][:, 6:7], stats[tp][:, 7:8], ALU.mult,
             rd=[("fst", tp)], wr=[("fst", tp)])
        p.ts(aw[:, :], aw[:, :], stats[tp][:, 6:7], ALU.mult, rd=[("aw",), ("fst", tp)], wr=[("aw",)])
        for kd in range(DT):
            p.mm(ps_wk[:, :], xnT[:, kd * T + tt * 128: kd * T + tt * 128 + 128],
                 UT_bf[:, kd * NB:(kd + 1) * NB],
                 rd=[("xnT", tt, kd), ("UT",)], wr=[("pswk",)],
                 start=(kd == 0), stop=(kd == DT - 1))
        p.tt(ral[:, :], ps_wk[:, :], aw[:, :], ALU.mult, rd=[("pswk",), ("aw",)], wr=[("ral",)])
        p.cp(czbf[:, :], ral[:, :], rd=[("ral",)], wr=[("czbf",)])
        for jt in range(JT):
            p.trans(ps_trB[:, jt * 128:(jt + 1) * 128], czbf[:, jt * 128:(jt + 1) * 128],
                    c["ident_bf"][:, :], rd=[("czbf",)], wr=[("pstr",)])
        for jt in range(JT):
            p.ts(cT_t[:, jt * 128:(jt + 1) * 128], ps_trB[:, jt * 128:(jt + 1) * 128],
                 invUV[:, 8 + jt:9 + jt], ALU.mult,
                 rd=[("pstr",), ("iuv",)], wr=[("cT", jt)])
        for oh in range(2):
            for jt in range(JT):
                p.mm(ps_wk[:, :], cT_t[:, jt * 128:(jt + 1) * 128],
                     rV_bf[:, jt * D + oh * 512: jt * D + (oh + 1) * 512],
                     rd=[("cT", jt), ("rV",)], wr=[("pswk",)],
                     start=(jt == 0), stop=(jt == JT - 1))
            p.tt(scrA[:, 1024:1536], ps_wk[:, :], gamb[:, oh * 512:(oh + 1) * 512], ALU.mult,
                 rd=[("pswk",), ("lngb",)], wr=[("dyn",)])
            xs = X[:, tt * D + oh * 512: tt * D + (oh + 1) * 512]
            p.tt(xs, xs, scrA[:, 1024:1536], ALU.add,
                 rd=[("X", tt), ("dyn",)], wr=[("X", tt)])
    for tc in range(T // 256):
        for ht in range(HT1):
            hp = (ht // 2) % 2
            sub = ht % 2
            if sub == 0:
                p.dma(w1t[hp][:, :].rearrange("p (dc h) -> p dc h", h=256),
                      scratch["B_w1T"][:, ht * 128:(ht + 2) * 128]
                      .rearrange("(dc p) h -> p dc h", p=128),
                      rd=[("B", "B_w1T")], wr=[("w1t", hp)], q="sync" if hp == 0 else "gpsimd")
                p.dma(w2t[hp][:, :].rearrange("p (o d) -> p o d", d=D),
                      scratch["B_w2T"][ht * 128:(ht + 2) * 128, :]
                      .rearrange("(o p) d -> p o d", p=128),
                      rd=[("B", "B_w2T")], wr=[("w2t", hp)], q="scalar")
            hq = hp * 2 + sub if False else hp
            for dc in range(DT):
                p.mm(ps_h[ht % 2], w1t[hp][:, (dc * 2 + sub) * 128:(dc * 2 + sub + 1) * 128],
                     xnT[:, dc * T + tc * 256: dc * T + (tc + 1) * 256],
                     rd=[("w1t", hp), ("xnT", tc * 2, dc), ("xnT", tc * 2 + 1, dc)],
                     wr=[("psh", ht % 2)],
                     start=(dc == 0), stop=(dc == DT - 1))
            p.act(h1t[ht % 2][:, :], ps_h[ht % 2], AF.Gelu, rd=[("psh", ht % 2)], wr=[("h1t", ht % 2)])
            for tsub in range(2):
                for oh in range(2):
                    p.mm(ps_s[:, (tsub * 2 + oh) * 512:(tsub * 2 + oh + 1) * 512],
                         h1t[ht % 2][:, tsub * 128:(tsub + 1) * 128],
                         w2t[hp][:, sub * D + oh * 512: sub * D + (oh + 1) * 512],
                         rd=[("h1t", ht % 2), ("w2t", hp)], wr=[("pss",)],
                         start=(ht == 0), stop=(ht == HT1 - 1))
        for tsub in range(2):
            tt = tc * 2 + tsub
            for oh in range(2):
                xs = X[:, tt * D + oh * 512: tt * D + (oh + 1) * 512]
                p.tt(xs, xs, ps_s[:, (tsub * 2 + oh) * 512:(tsub * 2 + oh + 1) * 512],
                     ALU.add, rd=[("X", tt), ("pss",)], wr=[("X", tt)])



def _mamba_host(inp):
    f = np.ascontiguousarray
    cw = inp["L0_attn_conv_w"].reshape(E, 4)
    d = {
        "m_A": f(inp["L0_attn_A_log"].reshape(ET, 128, NST).transpose(1, 0, 2).reshape(128, ET * NST)),
        "m_cw": f(cw.reshape(ET, 128, 4).transpose(1, 0, 2).reshape(128, ET * 4)),
        "m_cb": f(inp["L0_attn_conv_b"].reshape(ET, 128).T),
        "m_bdt": f(inp["L0_attn_b_dt"].reshape(ET, 128).T),
        "m_dsk": f(inp["L0_attn_D_skip"].reshape(ET, 128).T),
        "m_wdt": f(inp["L0_attn_W_dt"]),
        "m_win": f(inp["L0_attn_W_in"]),
        "m_wout": f(inp["L0_attn_W_out"]),
    }
    wxp = np.zeros((E, 128), np.float32)
    wx = inp["L0_attn_W_xproj"]
    wxp[:, 0:16] = wx[:, DTR:DTR + NST]
    wxp[:, 32:48] = wx[:, DTR + NST:DTR + 2 * NST]
    wxp[:, 64:128] = wx[:, 0:DTR]
    d["m_wxp"] = f(wxp)
    return d


def _moe_host(inp):
    f = np.ascontiguousarray
    HT = HMOE // 128
    return {
        "g_wT": f(inp["L0_ffn_gate_w"].T),
        "g_b": f(inp["L0_ffn_gate_b"].reshape(1, NEXP)),
        "g_w1": f(inp["L0_ffn_w1"]),
        "g_b1": f(inp["L0_ffn_b1"].reshape(NEXP, HT, 128).transpose(2, 0, 1).reshape(128, NEXP * HT)),
        "g_w2": f(inp["L0_ffn_w2"]),
        "g_b2": f(inp["L0_ffn_b2"]),
    }


def _attn_host(inp):
    f = np.ascontiguousarray
    return {
        "a_wq": f(inp["L1_attn_Wq"].T),
        "a_wk": f(inp["L1_attn_Wk"].T),
        "a_wv": f(inp["L1_attn_Wv"].T),
        "a_wd": f(inp["L1_attn_Wd"].T),
        "a_bq": f(inp["L1_attn_bq"].reshape(1, D)),
        "a_bk": f(inp["L1_attn_bk"].reshape(1, D)),
        "a_bv": f(inp["L1_attn_bv"].reshape(1, D)),
        "a_bd": f(inp["L1_attn_bd"].reshape(1, D)),
        "a_gamma": f(inp["L1_attn_gamma"].reshape(1, D)),
        "a_beta": f(inp["L1_attn_beta"].reshape(1, D)),
    }


def _ffn1_host(inp):
    f = np.ascontiguousarray
    return {
        "f_UT": f(inp["L1_ffn_raw_U"].T),
        "f_rWT": f(inp["L1_ffn_router_W"].T),
        "f_rV": f(inp["L1_ffn_raw_V"]),
        "f_rb": f(inp["L1_ffn_router_b"].reshape(1, NB)),
        "f_lng": f(inp["L1_ffn_ln_g"].reshape(1, D)),
        "f_lnb": f(inp["L1_ffn_ln_b"].reshape(1, D)),
        "f_gam": f(inp["L1_ffn_gamma"].reshape(1, D)),
        "f_U": f(inp["L1_ffn_raw_U"]),
        "f_V": f(inp["L1_ffn_raw_V"]),
        "f_w1T": f(inp["L1_ffn_W1"].T),
        "f_w2T": f(inp["L1_ffn_W2"].T),
    }


_DRAM_SHAPES = {
    "m_A": (128, ET * NST), "m_cw": (128, ET * 4), "m_cb": (128, ET),
    "m_bdt": (128, ET), "m_dsk": (128, ET), "m_wdt": (DTR, E),
    "m_win": (D, 2 * E), "m_wout": (E, D), "m_wxp": (E, 128),
    "g_wT": (D, NEXP), "g_b": (1, NEXP), "g_w1": (NEXP, D, HMOE),
    "g_b1": (128, NEXP * (HMOE // 128)), "g_w2": (NEXP, HMOE, D), "g_b2": (NEXP, D),
    "a_wq": (D, D), "a_wk": (D, D), "a_wv": (D, D), "a_wd": (D, D),
    "a_bq": (1, D), "a_bk": (1, D), "a_bv": (1, D), "a_bd": (1, D),
    "a_gamma": (1, D), "a_beta": (1, D),
    "f_UT": (D, NB), "f_rWT": (D, NB), "f_rV": (NB, D), "f_rb": (1, NB),
    "f_lng": (1, D), "f_lnb": (1, D), "f_gam": (1, D),
    "f_U": (NB, D), "f_V": (NB, D), "f_w1T": (D, H1), "f_w2T": (H1, D),
}

_SCRATCH_SHAPES = {
    "B_win": (D, 2 * E), "B_wxp": (E, 128), "B_wdt": (DTR, E), "B_wout": (E, D),
    "B_gwT": (D, NEXP),
    "B_wq": (D, D), "B_wk": (D, D), "B_wv": (D, D), "B_wd": (D, D),
    "B_UT": (D, NB), "B_rWT": (D, NB), "B_rV": (NB, D),
    "B_w1T": (D, H1), "B_w2T": (H1, D),
}


def _build(emits, TC=256):
    nc = bass.Bass("TRN2", target_bir_lowering=False, debug=False)
    x_in = nc.dram_tensor("x_res_in", [T, D], F32, kind="ExternalInput").ap()
    x_out = nc.dram_tensor("x_res_out", [T, D], F32, kind="ExternalOutput").ap()
    dram = {k: nc.dram_tensor(k, list(v), F32, kind="ExternalInput").ap()
            for k, v in _DRAM_SHAPES.items()}
    scratch = {k: nc.dram_tensor(k, list(v), BF16, kind="Internal").ap()
               for k, v in _SCRATCH_SHAPES.items()}
    with ExitStack() as st:
        X = st.enter_context(nc.sbuf_tensor("X", [128, (T // 128) * D], F32))
        sems = {}
        for e in ENGS:
            sems[("c", e)] = st.enter_context(nc.semaphore(f"semc_{e}"))
            sems[("d", e)] = st.enter_context(nc.semaphore(f"semd_{e}"))
        blk = st.enter_context(nc.Block())
        p = P(nc)
        c = make_consts(p, st)
        p.barrier()
        pc_f = [st.enter_context(nc.sbuf_tensor(f"pc_f{i}", [128, 1024], F32)) for i in range(2)]
        pc_b = [st.enter_context(nc.sbuf_tensor(f"pc_b{i}", [128, 1024], BF16)) for i in range(2)]
        p.dma(X[:, :].rearrange("p (tt d) -> p tt d", d=D),
              x_in.rearrange("(tt p) d -> p tt d", p=128),
              wr=[("X", tt) for tt in range(T // 128)], q="gpsimd")
        emit_precast(p, st, dram, scratch, pc_f, pc_b)
        for em in emits:
            with ExitStack() as sti:
                if em is emit_mamba:
                    em(p, sti, c, X, dram, scratch, TC=TC)
                else:
                    em(p, sti, c, X, dram, scratch)
            p.barrier()
        p.dma(x_out.rearrange("(tt p) d -> p tt d", p=128),
              X[:, :].rearrange("p (tt d) -> p tt d", d=D),
              rd=[("X", tt) for tt in range(T // 128)], wr=[("OUT",)], q="gpsimd")
        p.finalize(sems)
    return nc


def build_mamba_program(TC=256):
    return _build([emit_mamba], TC=TC)


def build_moe_program():
    return _build([emit_moe])


def build_attn_program():
    return _build([emit_attn])


def build_ffn1_program():
    return _build([emit_ffn1])


def build_fused_program(TC=256):
    return _build([emit_mamba, emit_moe, emit_attn, emit_ffn1], TC=TC)


_PROGS = {}


def _all_host(inp):
    h = {}
    h.update(_mamba_host(inp))
    h.update(_moe_host(inp))
    h.update(_attn_host(inp))
    h.update(_ffn1_host(inp))
    return h


def kernel(**inputs):
    inp = {k: np.asarray(v) for k, v in inputs.items()}
    x = np.ascontiguousarray(inp["L0_attn_x"], dtype=np.float32)
    if "fused" not in _PROGS:
        _PROGS["fused"] = build_fused_program(TC=256)
    host = _all_host(inp)
    in_maps = [dict(host, x_res_in=np.ascontiguousarray(x[ci])) for ci in range(8)]
    res = run_bass_kernel_spmd(_PROGS["fused"], in_maps, core_ids=list(range(8)))
    return np.stack([np.asarray(r["x_res_out"]) for r in res.results]).astype(np.float32)


# version 11
# speedup vs baseline: 1.4299x; 1.0536x over previous
import numpy as np
import concourse.bass as bass
import concourse.mybir as mybir
from concourse.bass_utils import run_bass_kernel_spmd
from contextlib import ExitStack

F32 = mybir.dt.float32
BF16 = mybir.dt.bfloat16
AF = mybir.ActivationFunctionType
ALU = mybir.AluOpType
AX = mybir.AxisListType

T = 2048
D = 1024
E = 2048
NST = 16
DTR = 64
NEXP = 8
HMOE = 2048
NH = 16
HD = 64
NB = 512
H1 = 4096
DT = D // 128
ET = E // 128

LN_EPS = 1e-5
EPS_ATTN = 1e-6
EPS_LN_ATTN = 1e-12
FFN_EPS = 1e-6
TAU = 10.0

ENGS = ("tensor", "vector", "scalar", "gpsimd", "sync")


class _Node:
    __slots__ = ("eng", "fn", "inc", "deps", "need", "val", "dma", "sid")

    def __init__(self, eng, fn, inc, dma):
        self.eng = eng
        self.fn = fn
        self.inc = inc
        self.deps = []
        self.need = False
        self.val = 0
        self.dma = dma
        self.sid = ("d", eng) if dma else ("c", eng)


class P:

    def __init__(self, nc):
        self.nc = nc
        self.nodes = []
        self.res = {}
        self.cvals = None
        self.cmap = {}
        self.frozen_consts = False
        self.last_dma = {}

    def emit(self, eng_name, fn, rd=(), wr=(), inc=1, dma=False):
        node = _Node(eng_name, fn, inc, dma)
        dep = {}

        def add_dep(pn):
            if pn is node:
                return
            if pn.eng == eng_name == "tensor" and not (pn.dma or dma):
                return
            dep[id(pn)] = pn
        for k in rd:
            e = self.res.get(k)
            if e and e[0] is not None:
                add_dep(e[0])
        for k in wr:
            e = self.res.get(k)
            if e:
                if e[0] is not None:
                    add_dep(e[0])
                for rn in e[1].values():
                    add_dep(rn)
        if dma:
            prev = self.last_dma.get(eng_name)
            if prev is not None:
                add_dep(prev)
            self.last_dma[eng_name] = node
            node.need = True
        node.deps = list(dep.values())
        for pn in node.deps:
            pn.need = True
        for k in rd:
            e = self.res.setdefault(k, [None, {}])
            e[1][eng_name] = node
        for k in wr:
            self.res[k] = [node, {}]
        self.nodes.append(node)
        return node

    def barrier(self):
        self.nodes.append("BARRIER")

    def finalize(self, sems):
        SIDS = [("c", e) for e in ENGS] + [("d", e) for e in ENGS]
        counts = {s: 0 for s in SIDS}
        last = {s: None for s in SIDS}
        for nd in self.nodes:
            if nd == "BARRIER":
                for s in SIDS:
                    if last[s] is not None:
                        last[s].need = True
                continue
            last[nd.sid] = nd
        for s in SIDS:
            if last[s] is not None:
                last[s].need = True
        for nd in self.nodes:
            if nd == "BARRIER":
                continue
            if nd.need:
                counts[nd.sid] += nd.inc
                nd.val = counts[nd.sid]
        lastv = {s: 0 for s in SIDS}
        waited = {e: {s: 0 for s in SIDS} for e in ENGS}
        for nd in self.nodes:
            if nd == "BARRIER":
                snap = dict(lastv)
                for e in ENGS:
                    eng = getattr(self.nc, e)
                    for s in SIDS:
                        if s != ("c", e) and snap[s] > waited[e][s]:
                            eng.wait_ge(sems[s], snap[s])
                            waited[e][s] = snap[s]
                continue
            eng = getattr(self.nc, nd.eng)
            need_wait = {}
            for pn in nd.deps:
                if pn.val > waited[nd.eng].get(pn.sid, 0):
                    need_wait[pn.sid] = max(need_wait.get(pn.sid, 0), pn.val)
            for s, v in need_wait.items():
                eng.wait_ge(sems[s], v)
                waited[nd.eng][s] = v
            inst = nd.fn(eng)
            if nd.need:
                inst.then_inc(sems[nd.sid], nd.inc)
                lastv[nd.sid] = nd.val
        for s in SIDS:
            if s != ("c", "sync") and lastv[s] > waited["sync"][s]:
                self.nc.sync.wait_ge(sems[s], lastv[s])

    def dma(self, out, in_, rd=(), wr=(), q="sync"):
        self.emit(q, lambda e: e.dma_start(out=out, in_=in_), rd=rd, wr=wr,
                  inc=16, dma=True)

    def mm(self, out, lhsT, rhs, rd=(), wr=(), start=True, stop=True):
        self.emit("tensor", lambda e: e.matmul(out, lhsT, rhs, start=start, stop=stop),
                  rd=rd, wr=wr)

    def trans(self, out, in_, ident, rd=(), wr=()):
        self.emit("tensor", lambda e: e.transpose(out, in_, ident), rd=rd, wr=wr)

    def act(self, out, in_, func, rd=(), wr=(), bias=0.0, scale=1.0, accum_out=None):
        self.emit("scalar", lambda e: e.activation(out, in_, func, bias=bias,
                                                   scale=scale, accum_out=accum_out),
                  rd=rd, wr=wr)

    def cp(self, out, in_, rd=(), wr=(), eng="vector"):
        if eng == "scalar":
            self.emit(eng, lambda e: e.copy(out, in_), rd=rd, wr=wr)
        else:
            self.emit(eng, lambda e: e.tensor_copy(out, in_), rd=rd, wr=wr)

    def tt(self, out, a, b, op, rd=(), wr=(), eng="vector"):
        self.emit(eng, lambda e: e.tensor_tensor(out, a, b, op), rd=rd, wr=wr)

    def ts(self, out, in0, s1, op0, s2=None, op1=ALU.bypass, rd=(), wr=(), eng="vector"):
        if s2 is None:
            self.emit(eng, lambda e: e.tensor_single_scalar(out, in0, s1, op0),
                      rd=rd, wr=wr)
        else:
            self.emit(eng, lambda e: e.tensor_scalar(out, in0, s1, s2, op0=op0, op1=op1),
                      rd=rd, wr=wr)

    def stt(self, out, in0, scalar, in1, op0, op1, rd=(), wr=(), eng="vector"):
        self.emit(eng, lambda e: e.scalar_tensor_tensor(out, in0, scalar, in1,
                                                        op0=op0, op1=op1), rd=rd, wr=wr)

    def scan(self, out, d0, d1, init, rd=(), wr=(), eng="vector"):
        self.emit(eng, lambda e: e.tensor_tensor_scan(out, d0, d1, init,
                                                      op0=ALU.mult, op1=ALU.add),
                  rd=rd, wr=wr)

    def red(self, out, in_, op, rd=(), wr=(), eng="vector"):
        self.emit(eng, lambda e: e.tensor_reduce(out, in_, axis=AX.X, op=op),
                  rd=rd, wr=wr)

    def max8(self, out, in_, rd=(), wr=()):
        self.emit("vector", lambda e: e.max(out, in_), rd=rd, wr=wr)

    def recip(self, out, in_, rd=(), wr=()):
        self.emit("vector", lambda e: e.reciprocal(out, in_), rd=rd, wr=wr)

    def memset(self, ap, v, rd=(), wr=(), eng="vector"):
        self.emit(eng, lambda e: e.memset(ap, v), rd=rd, wr=wr)

    def iota(self, ap, pattern, rd=(), wr=(), base=0, mult=0):
        self.emit("gpsimd", lambda e: e.iota(ap, pattern=pattern, base=base,
                                             channel_multiplier=mult,
                                             allow_small_or_imprecise_dtypes=True),
                  rd=rd, wr=wr)

    def ccol(self, v):
        v = float(v)
        if v not in self.cmap:
            assert not self.frozen_consts, f"const {v} after freeze"
            i = len(self.cmap)
            self.cmap[v] = i
            self.memset(self.cvals[:, i:i + 1], v)
        return self.cvals[:, self.cmap[v]:self.cmap[v] + 1]


def make_consts(p, st):
    nc = p.nc
    c = {}
    c["ones_bf"] = st.enter_context(nc.sbuf_tensor("ones_bf", [1, 128], BF16))
    c["ident_f"] = st.enter_context(nc.sbuf_tensor("ident_f", [128, 128], F32))
    c["ident_bf"] = st.enter_context(nc.sbuf_tensor("ident_bf", [128, 128], BF16))
    c["onescol_bf"] = st.enter_context(nc.sbuf_tensor("onescol_bf", [128, 1], BF16))
    p.cvals = st.enter_context(nc.sbuf_tensor("cvals", [128, 16], F32))
    p.memset(c["ones_bf"][:, :], 1.0)
    p.memset(c["onescol_bf"][:, :], 1.0)
    p.iota(c["ident_f"][:, :], pattern=[[1, 128]], base=0, mult=-1, wr=[("iota",)])
    p.ts(c["ident_f"][:, :], c["ident_f"][:, :], 0.0, ALU.is_equal, rd=[("iota",)], wr=[("iota",)])
    p.cp(c["ident_bf"][:, :], c["ident_f"][:, :], rd=[("iota",)])
    for v in (LN_EPS, EPS_LN_ATTN, EPS_ATTN, -0.5, 0.5):
        p.ccol(v)
    p.frozen_consts = True
    return c



PRECAST = [
    ("m_win", D, 2 * E, "B_win"),
    ("m_wxp", E, 128, "B_wxp"),
    ("m_wdt", DTR, E, "B_wdt"),
    ("m_wout", E, D, "B_wout"),
    ("g_wT", D, NEXP, "B_gwT"),
    ("a_wq", D, D, "B_wq"),
    ("a_wk", D, D, "B_wk"),
    ("a_wv", D, D, "B_wv"),
    ("a_wd", D, D, "B_wd"),
    ("f_UT", D, NB, "B_UT"),
    ("f_rWT", D, NB, "B_rWT"),
    ("f_rV", NB, D, "B_rV"),
    ("f_w1T", D, H1, "B_w1T"),
    ("f_w2T", H1, D, "B_w2T"),
]

_PC_ENGS = ("gpsimd",)


def emit_precast(p, st, dram, scratch, sf, sb):
    i = 0
    for key, R, C, bname in PRECAST:
        src = dram[key]
        if len(src.shape) == 3:
            src = src.rearrange("a b c -> (a b) c")
        dst = scratch[bname]
        if len(dst.shape) == 3:
            dst = dst.rearrange("a b c -> (a b) c")
        for r0 in range(0, R, 128):
            rr = min(128, R - r0)
            for c0 in range(0, C, 1024):
                cc = min(1024, C - c0)
                g = i % 2
                p.dma(sf[g][0:rr, 0:cc], src[r0:r0 + rr, c0:c0 + cc],
                      wr=[("pcf", g)], q="sync")
                p.cp(sb[g][0:rr, 0:cc], sf[g][0:rr, 0:cc],
                     rd=[("pcf", g)], wr=[("pcb", g)], eng="scalar")
                p.dma(dst[r0:r0 + rr, c0:c0 + cc], sb[g][0:rr, 0:cc],
                      rd=[("pcb", g)], wr=[("B", bname)], q="gpsimd")
                i += 1


def emit_ln(p, x_ap, out_ap, stats, scr, eps, rd_x, wr_out, tag, sckey=None):
    Df = x_ap.shape[-1]
    ks = ("st",) + tag
    kc = sckey if sckey is not None else ("sc",) + tag
    mu = stats[:, 0:1]
    sq = stats[:, 1:2]
    var = stats[:, 2:3]
    rstd = stats[:, 3:4]
    p.red(mu, x_ap, ALU.add, rd=list(rd_x), wr=[ks])
    p.ts(mu, mu, 1.0 / Df, ALU.mult, rd=[ks], wr=[ks])
    p.act(scr, x_ap, AF.Square, accum_out=sq, rd=list(rd_x), wr=[kc, ks])
    p.ts(var, sq, 1.0 / Df, ALU.mult, rd=[ks], wr=[ks])
    p.tt(stats[:, 4:5], mu, mu, ALU.mult, rd=[ks], wr=[ks])
    p.tt(var, var, stats[:, 4:5], ALU.subtract, rd=[ks], wr=[ks])
    p.act(stats[:, 5:6], var, AF.Ln, bias=p.ccol(eps), rd=[ks], wr=[ks])
    p.act(rstd, stats[:, 5:6], AF.Exp, scale=p.ccol(-0.5), rd=[ks], wr=[ks])
    p.ts(out_ap, x_ap, mu, ALU.subtract, rstd, ALU.mult,
         rd=list(rd_x) + [ks], wr=list(wr_out))


def emit_tr(p, dst_ap, src_ap, ps_tr, ident_bf, rd_src, wr_dst, pkey):
    p.trans(ps_tr, src_ap, ident_bf[:, :], rd=list(rd_src), wr=[pkey])
    p.cp(dst_ap, ps_tr, rd=[pkey], wr=list(wr_dst))


def emit_tr8(p, dst3d, src_ap, ps_trB, ident_bf, rd_src, wr_dst, pkey, nk=8):
    for kd in range(nk):
        p.trans(ps_trB[:, kd * 128:(kd + 1) * 128], src_ap[:, kd * 128:(kd + 1) * 128],
                ident_bf[:, :], rd=list(rd_src), wr=[pkey])
    p.cp(dst3d, ps_trB[:, 0:nk * 128].rearrange("p (k f) -> p k f", f=128),
         rd=[pkey], wr=list(wr_dst), eng="scalar")



def emit_mamba(p, st, c, X, dram, scratch, TC=256):
    nc = p.nc
    NCH = T // TC
    TSUB = TC // 128

    _xnT = st.enter_context(nc.sbuf_tensor("mb_xnT", [128, DT * TC], BF16))
    xnT = [_xnT, _xnT]
    xi = st.enter_context(nc.sbuf_tensor("mb_xi", [128, ET * (TC + 3)], F32))
    zs = st.enter_context(nc.sbuf_tensor("mb_zs", [128, ET * TC], BF16))
    delta = st.enter_context(nc.sbuf_tensor("mb_delta", [128, ET * TC], BF16))
    wdu = st.enter_context(nc.sbuf_tensor("mb_w", [128, ET * TC], BF16))
    yac = st.enter_context(nc.sbuf_tensor("mb_y", [128, ET * TC], F32))
    hall = st.enter_context(nc.sbuf_tensor("mb_hall", [128, ET * TC], F32))
    dBuall = st.enter_context(nc.sbuf_tensor("mb_dbu", [128, ET * TC], BF16))
    ygbf = hall[:, 0:ET * TC // 2].bitcast(BF16)
    hstate = st.enter_context(nc.sbuf_tensor("mb_h", [128, ET * NST], F32))
    stash = st.enter_context(nc.sbuf_tensor("mb_stash", [128, ET * 3], F32))
    stats = [st.enter_context(nc.sbuf_tensor(f"mb_stats{i}", [128, 8], F32)) for i in range(2)]
    scrA = st.enter_context(nc.sbuf_tensor("mb_scrA", [128, 1024], F32))
    xnbf_t = [st.enter_context(nc.sbuf_tensor(f"mb_xnbf{i}", [128, 1024], BF16)) for i in range(2)]
    cbuf0 = st.enter_context(nc.sbuf_tensor("mb_cb0", [128, 2 * TC], F32))
    cbuf1 = st.enter_context(nc.sbuf_tensor("mb_cb1", [128, 2 * TC], F32))
    ubf = [st.enter_context(nc.sbuf_tensor(f"mb_ubf{i}", [128, TC], BF16)) for i in range(2)]
    dtlr_bf = st.enter_context(nc.sbuf_tensor("mb_dtlr", [64, TC], BF16))
    BtCt = st.enter_context(nc.sbuf_tensor("mb_BtCt", [16, 2 * TC], BF16))
    Bt_bf = BtCt[:, 0:TC]
    Ct_bf = BtCt[:, TC:2 * TC]
    BbCb = [st.enter_context(nc.sbuf_tensor(f"mb_BC{i}", [128, 2 * TC], BF16)) for i in range(2)]
    Bb = [BbCb[0][:, 0:TC], BbCb[1][:, 0:TC]]
    Cb = [BbCb[0][:, TC:2 * TC], BbCb[1][:, TC:2 * TC]]
    dAb = [st.enter_context(nc.sbuf_tensor(f"mb_dA{i}", [128, TC], F32)) for i in range(2)]
    sp0 = st.enter_context(nc.sbuf_tensor("mb_sp0", [128, TC], F32))
    Asb = st.enter_context(nc.sbuf_tensor("mb_A", [128, ET * NST], F32))
    cwsb = st.enter_context(nc.sbuf_tensor("mb_cw", [128, ET * 4], F32))
    cbsb = st.enter_context(nc.sbuf_tensor("mb_cbv", [128, ET], F32))
    bdtsb = st.enter_context(nc.sbuf_tensor("mb_bdt", [128, ET], F32))
    dsksb = st.enter_context(nc.sbuf_tensor("mb_dsk", [128, ET], F32))
    wxp = st.enter_context(nc.sbuf_tensor("mb_wxp", [128, ET * 128], BF16))
    wdt = st.enter_context(nc.sbuf_tensor("mb_wdt", [64, E], BF16))
    sel = st.enter_context(nc.sbuf_tensor("mb_sel", [16, NST * 128], BF16))
    win_bf = [st.enter_context(nc.sbuf_tensor(f"mb_winb{i}", [128, DT * 128], BF16)) for i in range(2)]
    wo_et = win_bf
    ps_trB = st.enter_context(nc.psum_tensor("mb_ptr", [128, 1024], BF16))
    ps_m0 = st.enter_context(nc.psum_tensor("mb_pm0", [128, 256], F32))
    ps_m1 = st.enter_context(nc.psum_tensor("mb_pm1", [128, 256], F32))
    ps_xd = st.enter_context(nc.psum_tensor("mb_pxd", [128, 512], F32))
    ps_o = st.enter_context(nc.psum_tensor("mb_po", [128, 2048], F32))
    assert TC == 256
    ps_mm = [ps_m0[:, :], ps_m1[:, :]]
    ps_d = ps_xd[:, TC:2 * TC]
    ps_B = ps_o[:, 0:TC]
    ps_C = ps_o[:, TC:2 * TC]

    p.dma(Asb[:, :], dram["m_A"], wr=[("mA",)])
    p.dma(cwsb[:, :], dram["m_cw"], wr=[("mcw",)])
    p.dma(cbsb[:, :], dram["m_cb"], wr=[("mcb",)])
    p.dma(bdtsb[:, :], dram["m_bdt"], wr=[("mbdt",)])
    p.dma(dsksb[:, :], dram["m_dsk"], wr=[("mdsk",)])
    p.dma(wxp[:, :].rearrange("p (et cc) -> p et cc", cc=128),
          scratch["B_wxp"].rearrange("(et p) cc -> p et cc", p=128),
          rd=[("B", "B_wxp")], wr=[("mwxp",)], q="scalar")
    p.dma(wdt[:, :], scratch["B_wdt"], rd=[("B", "B_wdt")], wr=[("mwdt",)], q="scalar")
    p.act(Asb[:, :], Asb[:, :], AF.Exp, rd=[("mA",)], wr=[("mA",)])
    p.ts(Asb[:, :], Asb[:, :], -1.0, ALU.mult, rd=[("mA",)], wr=[("mA",)])
    for half in range(2):
        p.iota(scrA[0:16, 0:1024], pattern=[[1, 8], [0, 128]], base=-half * 8, mult=-1,
               wr=[("scA",)])
        p.ts(scrA[0:16, 0:1024], scrA[0:16, 0:1024], 0.0, ALU.is_equal,
             rd=[("scA",)], wr=[("scA",)])
        p.cp(sel[:, half * 1024:(half + 1) * 1024], scrA[0:16, 0:1024],
             rd=[("scA",)], wr=[("msel",)])
    p.memset(hstate[:, :], 0.0, wr=[("hst", n) for n in range(NST)])

    xi_v = lambda et: xi[:, et * (TC + 3):(et + 1) * (TC + 3)]

    for ch in range(NCH):
        cp_ = ch % 2
        for tsub in range(TSUB):
            tt = ch * TSUB + tsub
            xt = X[:, tt * D:(tt + 1) * D]
            emit_ln(p, xt, xnbf_t[tsub][:, :], stats[tsub], scrA[:, 0:1024],
                    LN_EPS, rd_x=[("X", tt)], wr_out=[("xnbf", tsub)], tag=("mln", tsub),
                    sckey=("scA",))
            dst3d = xnT[cp_][:, :].rearrange("p (k t) -> p k t", t=TC)[:, :, tsub * 128:tsub * 128 + 128]
            emit_tr8(p, dst3d, xnbf_t[tsub][:, :], ps_trB, c["ident_bf"],
                     rd_src=[("xnbf", tsub)],
                     wr_dst=[("xnT", cp_, kd) for kd in range(DT)], pkey=("pstr",))
        for ec in range(32):
            wp = ec % 2
            p.dma(win_bf[wp][:, :].rearrange("p (dc e) -> p dc e", e=128),
                  scratch["B_win"][:, ec * 128:(ec + 1) * 128]
                  .rearrange("(dc p) e -> p dc e", p=128),
                  rd=[("B", "B_win")], wr=[("winb", wp)], q="sync")
            for dc in range(DT):
                p.mm(ps_mm[wp], win_bf[wp][:, dc * 128:(dc + 1) * 128],
                     xnT[cp_][:, dc * TC:(dc + 1) * TC],
                     rd=[("winb", wp), ("xnT", cp_, dc)], wr=[("psmm", wp)],
                     start=(dc == 0), stop=(dc == DT - 1))
            if ec < 16:
                p.act(xi_v(ec)[:, 3:3 + TC], ps_mm[wp], AF.Identity,
                      rd=[("psmm", wp)], wr=[("xi", ec)])
            else:
                p.act(zs[:, (ec - 16) * TC:(ec - 15) * TC], ps_mm[wp], AF.Silu,
                      rd=[("psmm", wp)], wr=[("zs", ec - 16)])
        for et in range(ET):
            xv = xi_v(et)
            up = et % 2
            if ch == 0:
                p.memset(xv[:, 0:3], 0.0, wr=[("xi", et)])
            else:
                p.cp(xv[:, 0:3], stash[:, et * 3:(et + 1) * 3],
                     rd=[("stash", et)], wr=[("xi", et)])
            cw = lambda k: cwsb[:, et * 4 + k: et * 4 + k + 1]
            ce = "vector"
            ck = ("cb", et % 2)
            cb0 = cbuf0[:, (et % 2) * TC:(et % 2 + 1) * TC]
            cb1 = cbuf1[:, (et % 2) * TC:(et % 2 + 1) * TC]
            p.ts(cb0, xv[:, 0:TC], cw(0), ALU.mult, rd=[("xi", et), ("mcw",)], wr=[ck], eng=ce)
            p.stt(cb1, xv[:, 1:1 + TC], cw(1), cb0, ALU.mult, ALU.add,
                  rd=[("xi", et), ck], wr=[ck], eng=ce)
            p.stt(cb0, xv[:, 2:2 + TC], cw(2), cb1, ALU.mult, ALU.add,
                  rd=[("xi", et), ck], wr=[ck], eng=ce)
            p.stt(cb1, xv[:, 3:3 + TC], cw(3), cb0, ALU.mult, ALU.add,
                  rd=[("xi", et), ck], wr=[ck], eng=ce)
            p.cp(stash[:, et * 3:(et + 1) * 3], xv[:, TC:TC + 3],
                 rd=[("xi", et)], wr=[("stash", et)], eng=ce)
            p.act(xv[:, 3:3 + TC], cb1, AF.Silu, bias=cbsb[:, et:et + 1],
                  rd=[ck, ("stash", et), ("mcb",)], wr=[("xi", et)])
            p.cp(ubf[up][:, :], xv[:, 3:3 + TC], rd=[("xi", et)], wr=[("ubf", up)], eng="gpsimd")
            p.act(yac[:, et * TC:(et + 1) * TC], xv[:, 3:3 + TC], AF.Copy,
                  scale=dsksb[:, et:et + 1], rd=[("xi", et), ("mdsk",)], wr=[("yac", et)])
            p.mm(ps_xd[:, 0:TC], wxp[:, et * 128:(et + 1) * 128], ubf[up][:, :],
                 rd=[("ubf", up), ("mwxp",)], wr=[("psxd",)],
                 start=(et == 0), stop=(et == ET - 1))
        p.act(Bt_bf, ps_xd[0:16, 0:TC], AF.Identity, rd=[("psxd",)], wr=[("Bt",)])
        p.act(Ct_bf, ps_xd[32:48, 0:TC], AF.Identity, rd=[("psxd",)], wr=[("Ct",)])
        p.act(dtlr_bf[:, :], ps_xd[64:128, 0:TC], AF.Identity, rd=[("psxd",)], wr=[("dtlr",)])
        for dt in range(ET):
            dv = delta[:, dt * TC:(dt + 1) * TC]
            p.mm(ps_d, wdt[:, dt * 128:(dt + 1) * 128], dtlr_bf[:, :],
                 rd=[("dtlr",), ("mwdt",)], wr=[("psxd",)])
            p.act(sp0[:, :], ps_d, AF.Identity, bias=bdtsb[:, dt:dt + 1],
                  rd=[("psxd",), ("mbdt",)], wr=[("sp0",)])
            p.act(cbuf1[:, 0:TC], sp0[:, :], AF.Abs, rd=[("sp0",), ("cb", 0)], wr=[("cb", 0)])
            p.act(cbuf0[:, 0:TC], cbuf1[:, 0:TC], AF.Exp, scale=-1.0, rd=[("cb", 0)], wr=[("cb", 0)])
            p.act(cbuf1[:, 0:TC], cbuf0[:, 0:TC], AF.Ln, bias=1.0, rd=[("cb", 0)], wr=[("cb", 0)])
            p.stt(dv, sp0[:, :], 0.0, cbuf1[:, 0:TC], ALU.max, ALU.add,
                  rd=[("sp0",), ("cb", 0)], wr=[("delta", dt)])
            p.tt(wdu[:, dt * TC:(dt + 1) * TC], dv, xi_v(dt)[:, 3:3 + TC], ALU.mult,
                 rd=[("delta", dt), ("xi", dt)], wr=[("w", dt)])
        hkeys = [("hall", d_) for d_ in range(ET)]
        for n in range(NST):
            np_ = n % 2
            p.mm(ps_o[:, 0:2 * TC], sel[:, n * 128:(n + 1) * 128], BtCt[:, :],
                 rd=[("Bt",), ("Ct",), ("msel",)], wr=[("pso",)])
            p.act(BbCb[np_][:, :], ps_o[:, 0:2 * TC], AF.Identity,
                  rd=[("pso",)], wr=[("Bb", np_), ("Cb", np_)])
            p.tt(dBuall[:, :].rearrange("p (d t) -> p d t", t=TC),
                 wdu[:, :].rearrange("p (d t) -> p d t", t=TC),
                 Bb[np_].rearrange("p (o t) -> p o t", o=1).broadcast_to((128, ET, TC)),
                 ALU.mult,
                 rd=[("w", d_) for d_ in range(ET)] + [("Bb", np_)], wr=[("dbu",)])
            for dt in range(ET):
                a3 = dt % 2
                p.act(dAb[a3][:, :], delta[:, dt * TC:(dt + 1) * TC], AF.Exp,
                      scale=Asb[:, dt * NST + n: dt * NST + n + 1],
                      rd=[("delta", dt), ("mA",)], wr=[("dA", a3)])
                p.scan(hall[:, dt * TC:(dt + 1) * TC], dAb[a3][:, :],
                       dBuall[:, dt * TC:(dt + 1) * TC],
                       hstate[:, dt * NST + n: dt * NST + n + 1],
                       rd=[("dA", a3), ("dbu",), ("hst", n)], wr=[("hall", dt)])
            p.cp(hstate[:, :].rearrange("p (d n2) -> p d n2", n2=NST)[:, :, n:n + 1],
                 hall[:, :].rearrange("p (d t) -> p d t", t=TC)[:, :, TC - 1:TC],
                 rd=hkeys, wr=[("hst", n)], eng="gpsimd")
            p.tt(hall[:, :].rearrange("p (d t) -> p d t", t=TC),
                 hall[:, :].rearrange("p (d t) -> p d t", t=TC),
                 Cb[np_].rearrange("p (o t) -> p o t", o=1).broadcast_to((128, ET, TC)),
                 ALU.mult,
                 rd=hkeys + [("Cb", np_), ("hst", n)], wr=hkeys)
            p.tt(yac[:, :], yac[:, :], hall[:, :], ALU.add,
                 rd=hkeys + [("yac", d_) for d_ in range(ET)],
                 wr=[("yac", d_) for d_ in range(ET)])
        for et in range(ET):
            yv = yac[:, et * TC:(et + 1) * TC]
            p.tt(ygbf[:, et * TC:(et + 1) * TC], yv, zs[:, et * TC:(et + 1) * TC],
                 ALU.mult, rd=[("yac", et), ("zs", et)],
                 wr=[("yg", et), ("hall", et // 2)])
        for et in range(ET):
            wp = et % 2
            p.dma(wo_et[wp][:, :], scratch["B_wout"][et * 128:(et + 1) * 128, :],
                  rd=[("B", "B_wout")], wr=[("winb", wp)], q="scalar")
            for tsub in range(TSUB):
                for oh in range(2):
                    p.mm(ps_o[:, (tsub * 2 + oh) * 512:(tsub * 2 + oh + 1) * 512],
                         ygbf[:, et * TC + tsub * 128: et * TC + tsub * 128 + 128],
                         wo_et[wp][:, oh * 512:(oh + 1) * 512],
                         rd=[("yg", et), ("hall", et // 2), ("winb", wp)], wr=[("pso",)],
                         start=(et == 0), stop=(et == ET - 1))
        for tsub in range(TSUB):
            tt = ch * TSUB + tsub
            for oh in range(2):
                xs = X[:, tt * D + oh * 512: tt * D + (oh + 1) * 512]
                p.tt(xs, xs, ps_o[:, (tsub * 2 + oh) * 512:(tsub * 2 + oh + 1) * 512],
                     ALU.add, rd=[("X", tt), ("pso",)], wr=[("X", tt)])



def emit_moe(p, st, c, X, dram, scratch):
    nc = p.nc
    TTI = T // 128
    HT = HMOE // 128
    xnT = st.enter_context(nc.sbuf_tensor("mo_xnT", [128, DT * T], BF16))
    w1bf = st.enter_context(nc.sbuf_tensor("mo_w1", [128, DT * HMOE], BF16))
    w2bf = st.enter_context(nc.sbuf_tensor("mo_w2", [128, HT * D], BF16))
    hbf = st.enter_context(nc.sbuf_tensor("mo_h", [128, HT * 256], BF16))
    gwts = st.enter_context(nc.sbuf_tensor("mo_gw", [128, TTI * NEXP], F32))
    sgA = st.enter_context(nc.sbuf_tensor("mo_sgA", [128, TTI * NEXP], F32))
    gm = st.enter_context(nc.sbuf_tensor("mo_gm", [128, TTI], F32))
    gmk = st.enter_context(nc.sbuf_tensor("mo_gmk", [128, TTI * NEXP], F32))
    stats = [st.enter_context(nc.sbuf_tensor(f"mo_stats{i}", [128, 8], F32)) for i in range(2)]
    scrA = st.enter_context(nc.sbuf_tensor("mo_scrA", [128, 2048], F32))
    scrB = st.enter_context(nc.sbuf_tensor("mo_scrB", [128, 2048], F32))
    rowst = st.enter_context(nc.sbuf_tensor("mo_rowst", [1, 512], F32))
    _xb = st.enter_context(nc.sbuf_tensor("mo_xnbf", [128, 1024], BF16))
    xnbf_t = [_xb, _xb]
    gwT_bf = st.enter_context(nc.sbuf_tensor("mo_gwT", [128, DT * NEXP], BF16))
    gb_bf = st.enter_context(nc.sbuf_tensor("mo_gb", [1, NEXP], BF16))
    b1sb = st.enter_context(nc.sbuf_tensor("mo_b1", [128, NEXP * HT], F32))
    b2bf = [st.enter_context(nc.sbuf_tensor("mo_b2", [1, D], BF16))] * 2
    ps_trB = st.enter_context(nc.psum_tensor("mo_ptr", [128, 1024], BF16))
    ps_a = st.enter_context(nc.psum_tensor("mo_pa", [128, 512], F32))
    ps_h = [st.enter_context(nc.psum_tensor(f"mo_ph{i}", [128, 512], F32)) for i in range(2)]
    ps_o = [st.enter_context(nc.psum_tensor(f"mo_po{i}", [128, 512], F32)) for i in range(2)]
    ps_g = ps_a[:, 0:NEXP]

    p.dma(gwT_bf[:, :].rearrange("p (dc e) -> p dc e", e=NEXP),
          scratch["B_gwT"].rearrange("(dc p) e -> p dc e", p=128),
          rd=[("B", "B_gwT")], wr=[("gwT",)], q="scalar")
    p.dma(rowst[0:1, 0:NEXP], dram["g_b"], wr=[("sAb",)])
    p.cp(gb_bf[:, :], rowst[0:1, 0:NEXP], rd=[("sAb",)], wr=[("gb",)])
    p.dma(b1sb[:, :].rearrange("p (e h) -> p e h", h=HT), dram["g_b1"], wr=[("b1",)])
    for tt in range(TTI):
        tp = tt % 2
        xt = X[:, tt * D:(tt + 1) * D]
        emit_ln(p, xt, xnbf_t[tp][:, :], stats[tp], scrA[:, 0:1024],
                LN_EPS, rd_x=[("X", tt)], wr_out=[("xnbf", tp)], tag=("gln", tp),
                sckey=("sA",))
        dst3d = xnT[:, :].rearrange("p (k t) -> p k t", t=T)[:, :, tt * 128:tt * 128 + 128]
        emit_tr8(p, dst3d, xnbf_t[tp][:, :], ps_trB, c["ident_bf"],
                 rd_src=[("xnbf", tp)],
                 wr_dst=[("xnT", tt, kd) for kd in range(DT)], pkey=("pstr",))
        for kd in range(DT):
            p.mm(ps_g, xnT[:, kd * T + tt * 128: kd * T + tt * 128 + 128],
                 gwT_bf[:, kd * NEXP:(kd + 1) * NEXP],
                 rd=[("xnT", tt, kd), ("gwT",)], wr=[("psg",)],
                 start=(kd == 0), stop=False)
        p.mm(ps_g, c["ones_bf"][0:1, 0:128], gb_bf[0:1, :], rd=[("gb",)], wr=[("psg",)],
             start=False, stop=True)
        p.act(sgA[:, tt * NEXP:(tt + 1) * NEXP], ps_g, AF.Identity,
              rd=[("psg",)], wr=[("sgA",)])
    sg3 = sgA[:, :].rearrange("p (t e) -> p t e", e=NEXP)
    gm3 = gm[:, :].rearrange("p (t o) -> p t o", o=1).broadcast_to((128, TTI, NEXP))
    gk3 = gmk[:, :].rearrange("p (t e) -> p t e", e=NEXP)
    gw3 = gwts[:, :].rearrange("p (t e) -> p t e", e=NEXP)
    p.red(gm[:, :], sg3, ALU.max, rd=[("sgA",)], wr=[("gm",)])
    p.tt(sg3, sg3, gm3, ALU.subtract, rd=[("sgA",), ("gm",)], wr=[("sgA",)])
    p.act(sgA[:, :], sgA[:, :], AF.Exp, rd=[("sgA",)], wr=[("sgA",)])
    p.red(gm[:, :], sg3, ALU.add, rd=[("sgA",)], wr=[("gm",)])
    p.recip(gm[:, :], gm[:, :], rd=[("gm",)], wr=[("gm",)])
    p.tt(sg3, sg3, gm3, ALU.mult, rd=[("sgA",), ("gm",)], wr=[("sgA",)])
    p.red(gm[:, :], sg3, ALU.max, rd=[("sgA",)], wr=[("gm",)])
    p.tt(gk3, sg3, gm3, ALU.is_equal, rd=[("sgA",), ("gm",)], wr=[("gmk",)])
    p.stt(gmk[:, :], gmk[:, :], -1e9, sgA[:, :], ALU.mult, ALU.add,
          rd=[("gmk",), ("sgA",)], wr=[("gmk",)])
    p.red(gm[:, :], gk3, ALU.max, rd=[("gmk",)], wr=[("gm",)])
    p.tt(gk3, sg3, gm3, ALU.is_ge, rd=[("sgA",), ("gm",)], wr=[("gmk",)])
    p.tt(gw3, sg3, gk3, ALU.mult,
         rd=[("sgA",), ("gmk",)], wr=[("gwts", t_) for t_ in range(TTI)])
    for e in range(NEXP):
        ep = e % 2
        for dc in range(DT):
            sc, key, qq = (scrA, ("sw1", 0), "sync") if dc % 2 == 0 else (scrB, ("sw1", 1), "gpsimd")
            p.dma(sc[:, :], dram["g_w1"][e, dc * 128:(dc + 1) * 128, :],
                  rd=[key, ("sA",)], wr=[key, ("sA",)] if dc % 2 == 0 else [key], q=qq)
            p.cp(w1bf[:, dc * HMOE:(dc + 1) * HMOE], sc[:, :],
                 rd=[key], wr=[("w1", dc)], eng="vector" if dc % 2 == 0 else "gpsimd")
        for ht in range(HT):
            sc, key = (scrA[:, 0:D], ("sw1", 0)) if ht % 2 == 0 else (scrB[:, 0:D], ("sw1", 1))
            p.dma(sc, dram["g_w2"][e, ht * 128:(ht + 1) * 128, :],
                  rd=[key], wr=[key], q="scalar")
            p.cp(w2bf[:, ht * D:(ht + 1) * D], sc,
                 rd=[key], wr=[("w2", ht)], eng="vector" if ht % 2 == 0 else "gpsimd")
        for bh in range(2):
            p.dma(rowst[0:1, :], dram["g_b2"][e:e + 1, bh * 512:(bh + 1) * 512],
                  rd=[("sAb",)], wr=[("sAb",)])
            p.cp(b2bf[ep][:, bh * 512:(bh + 1) * 512], rowst[0:1, :],
                 rd=[("sAb",)], wr=[("b2", ep)])
        for tc in range(T // 256):
            for ht in range(HT):
                hp = ht % 2
                for dc in range(DT):
                    p.mm(ps_h[hp][:, :].rearrange("p (a b) -> p a b", a=2)[:, 0, 0:256]
                         if False else ps_h[hp][:, 0:256],
                         w1bf[:, dc * HMOE + ht * 128: dc * HMOE + ht * 128 + 128],
                         xnT[:, dc * T + tc * 256: dc * T + tc * 256 + 256],
                         rd=[("w1", dc)] + [("xnT", tc * 2 + q_, dc) for q_ in range(2)],
                         wr=[("psh", hp)],
                         start=(dc == 0), stop=(dc == DT - 1))
                p.act(hbf[:, ht * 256:(ht + 1) * 256], ps_h[hp][:, 0:256], AF.Relu,
                      bias=b1sb[:, e * HT + ht: e * HT + ht + 1],
                      rd=[("psh", hp), ("b1",)], wr=[("hbf", ht)])
            for tsub in range(2):
                tt = tc * 2 + tsub
                gcol = gwts[:, tt * NEXP + e: tt * NEXP + e + 1]
                for oh in range(2):
                    for ht in range(HT):
                        p.mm(ps_o[oh][:, :], hbf[:, ht * 256 + tsub * 128: ht * 256 + tsub * 128 + 128],
                             w2bf[:, ht * D + oh * 512: ht * D + oh * 512 + 512],
                             rd=[("hbf", ht), ("w2", ht)], wr=[("pso", oh)],
                             start=(ht == 0), stop=False)
                    p.mm(ps_o[oh][:, :], c["ones_bf"][0:1, 0:128], b2bf[ep][0:1, oh * 512:(oh + 1) * 512],
                         rd=[("b2", ep)], wr=[("pso", oh)], start=False, stop=True)
                    xs = X[:, tt * D + oh * 512: tt * D + (oh + 1) * 512]
                    p.stt(xs, ps_o[oh][:, :], gcol, xs, ALU.mult, ALU.add,
                          rd=[("pso", oh), ("gwts", tt), ("X", tt)], wr=[("X", tt)])



ATTN_PHASES = "ABqkvwstzcuTd"


def emit_attn(p, st, c, X, dram, scratch):
    nc = p.nc
    TTI = T // 128
    qT = st.enter_context(nc.sbuf_tensor("at_qT", [128, DT * T], BF16))
    wqkv = st.enter_context(nc.sbuf_tensor("at_wqkv", [128, 3 * DT * D], BF16))
    wd_bf = wqkv[:, 0:DT * D]
    bias_bf = st.enter_context(nc.sbuf_tensor("at_bias", [1, 4 * D], BF16))
    gb_bc = st.enter_context(nc.sbuf_tensor("at_gb", [128, 2 * D], F32))
    scr3 = st.enter_context(nc.sbuf_tensor("at_scr3", [128, 3072], F32))
    stats = [st.enter_context(nc.sbuf_tensor(f"at_stats{i}", [128, 8], F32)) for i in range(2)]
    xnbf_p = [st.enter_context(nc.sbuf_tensor(f"at_xnbf{i}", [128, 1024], BF16)) for i in range(2)]
    xnT_p = [st.enter_context(nc.sbuf_tensor(f"at_xnT{i}", [128, DT * 128], BF16)) for i in range(2)]
    qkbf = st.enter_context(nc.sbuf_tensor("at_qkbf", [128, 1024], BF16))
    vbf = st.enter_context(nc.sbuf_tensor("at_vbf", [128, 1024], BF16))
    kvbf = st.enter_context(nc.sbuf_tensor("at_kvbf", [128, NH * HD], BF16))
    kvacc = st.enter_context(nc.sbuf_tensor("at_kvacc", [64, NH * HD], F32))
    ksacc = st.enter_context(nc.sbuf_tensor("at_ksacc", [128, DT], F32))
    ksz = st.enter_context(nc.sbuf_tensor("at_ksz", [128, NH], BF16))
    ksf = st.enter_context(nc.sbuf_tensor("at_ksf", [128, DT], F32))
    zrec = st.enter_context(nc.sbuf_tensor("at_z", [128, NH], F32))
    czbf = st.enter_context(nc.sbuf_tensor("at_cz", [128, 1024], BF16))
    ctxT_t = st.enter_context(nc.sbuf_tensor("at_ctxT", [128, DT * 128], BF16))
    ps_trB = st.enter_context(nc.psum_tensor("at_ptr", [128, 1024], BF16))
    ps_w = [st.enter_context(nc.psum_tensor(f"at_pw{i}", [128, 512], F32)) for i in range(2)]
    ps_kv = st.enter_context(nc.psum_tensor("at_pkv", [64, NH * HD], F32))
    ps_small = st.enter_context(nc.psum_tensor("at_psm", [128, 512], F32))
    ps_ksc = ps_small[:, 0:DT]
    ps_z = ps_small[:, 16:16 + NH]
    ps_ctx = st.enter_context(nc.psum_tensor("at_pctx", [128, 1024], F32))
    p.memset(kvacc[:, :], 0.0, wr=[("kvacc",)])
    p.memset(ksacc[:, :], 0.0, wr=[("ksacc",)])
    efp = scr3[:, 0:1024]
    et0 = scr3[:, 1024:1536]
    et1 = scr3[:, 1536:2048]
    scrB = scr3[:, 1024:2048]
    hsn = scr3[:, 2048:3072]

    for j, wname in enumerate(("B_wq", "B_wk", "B_wv")):
        p.dma(wqkv[:, j * DT * D:(j + 1) * DT * D].rearrange("p (dc dd) -> p dc dd", dd=D),
              scratch[wname].rearrange("(dc p) dd -> p dc dd", p=128),
              rd=[("B", wname)], wr=[("wqkv", j)], q="sync")
    for j, bname in enumerate(("a_bq", "a_bk", "a_bv", "a_bd")):
        p.dma(scrB[0:1, 0:D], dram[bname], rd=[("scd",)], wr=[("scd",)])
        p.cp(bias_bf[:, j * D:(j + 1) * D], scrB[0:1, 0:D], rd=[("scd",)], wr=[("abias", j)])
    for j, gname in enumerate(("a_gamma", "a_beta")):
        p.dma(scrB[0:1, 0:D], dram[gname], rd=[("scd",)], wr=[("scd",)])
        p.cp(qkbf[0:1, 0:D], scrB[0:1, 0:D], rd=[("scd",)], wr=[("qkbf",)])
        for oh in range(2):
            p.mm(ps_w[oh][:, :], c["ones_bf"][0:1, 0:128], qkbf[0:1, oh * 512:(oh + 1) * 512],
                 rd=[("qkbf",)], wr=[("psw", oh)])
            p.act(gb_bc[:, j * D + oh * 512: j * D + (oh + 1) * 512], ps_w[oh][:, :],
                  AF.Identity, rd=[("psw", oh)], wr=[("gbbc",)])

    def proj(j, oh, pw, xnT_t, xk):
        for kd in range(DT):
            p.mm(ps_w[pw][:, :], xnT_t[:, kd * 128:(kd + 1) * 128],
                 wqkv[:, (j * DT + kd) * D + oh * 512:(j * DT + kd) * D + (oh + 1) * 512],
                 rd=[("xnTt", xk), ("wqkv", j)], wr=[("psw", pw)],
                 start=(kd == 0), stop=False)
        p.mm(ps_w[pw][:, :], c["ones_bf"][0:1, 0:128],
             bias_bf[0:1, j * D + oh * 512: j * D + (oh + 1) * 512],
             rd=[("abias", j)], wr=[("psw", pw)], start=False, stop=True)

    def elu1(dst_half, pw, wrk):
        p.ts(et0, ps_w[pw][:, :], 0.0, ALU.min, rd=[("psw", pw), ("scd",)], wr=[("scd",)])
        p.act(et1, et0, AF.Exp, rd=[("scd",)], wr=[("scd",)])
        p.stt(dst_half, ps_w[pw][:, :], 0.0, et1, ALU.max, ALU.add,
              rd=[("psw", pw), ("scd",)], wr=[wrk])

    for tt in range(TTI):
        tp = tt % 2
        xnbf_t = xnbf_p[tp]
        xnT_t = xnT_p[tp]
        xt = X[:, tt * D:(tt + 1) * D]
        emit_ln(p, xt, xnbf_t[:, :], stats[tp], scrB,
                LN_EPS, rd_x=[("X", tt)], wr_out=[("xnbf", tp)], tag=("aln", tp),
                sckey=("scd",))
        emit_tr8(p, xnT_t[:, :].rearrange("p (k f) -> p k f", f=128),
                 xnbf_t[:, :], ps_trB, c["ident_bf"], rd_src=[("xnbf", tp)],
                 wr_dst=[("xnTt", tp)], pkey=("pstr",))
        for oh in range(2):
            proj(0, oh, oh, xnT_t, tp)
            elu1(efp[:, oh * 512:(oh + 1) * 512], oh, ("efp", oh))
        p.cp(qkbf[:, :], efp, rd=[("efp", 0), ("efp", 1)], wr=[("qkbf",)])
        qdst = qT[:, :].rearrange("p (k t) -> p k t", t=T)[:, :, tt * 128:tt * 128 + 128]
        emit_tr8(p, qdst, qkbf[:, :], ps_trB, c["ident_bf"],
                 rd_src=[("qkbf",)],
                 wr_dst=[("qT", tt, kd) for kd in range(DT)], pkey=("pstr",))
        for oh in range(2):
            proj(1, oh, oh, xnT_t, tp)
            elu1(efp[:, oh * 512:(oh + 1) * 512], oh, ("efp", oh))
        p.cp(qkbf[:, :], efp, rd=[("efp", 0), ("efp", 1)], wr=[("qkbf",)])
        for oh in range(2):
            proj(2, oh, oh, xnT_t, tp)
            p.act(vbf[:, oh * 512:(oh + 1) * 512], ps_w[oh][:, :], AF.Identity,
                  rd=[("psw", oh)], wr=[("vbf",)])
        for h in range(NH):
            p.mm(ps_kv[:, h * HD:(h + 1) * HD], qkbf[:, h * HD:(h + 1) * HD],
                 vbf[:, h * HD:(h + 1) * HD],
                 rd=[("qkbf",), ("vbf",)], wr=[("pskv",)])
        p.tt(kvacc[:, :], kvacc[:, :], ps_kv[:, :], ALU.add,
             rd=[("kvacc",), ("pskv",)], wr=[("kvacc",)])
        for ic in range(DT):
            p.mm(ps_small[:, ic:ic + 1], qkbf[:, ic * 128:(ic + 1) * 128],
                 c["onescol_bf"][:, :],
                 rd=[("qkbf",)], wr=[("psks",)])
        p.tt(ksacc[:, :], ksacc[:, :], ps_small[:, 0:DT], ALU.add,
             rd=[("ksacc",), ("psks",)], wr=[("ksacc",)])
    kvt = vbf[0:64, :]
    p.act(kvt, kvacc[:, :], AF.Identity, rd=[("kvacc",)], wr=[("vbf",), ("kvt",)])
    p.memset(kvbf[:, :], 0.0, wr=[("kv0",)])
    p.cp(kvbf[0:64, :].rearrange("p (h m) -> p h m", m=2 * HD)[:, :, 0:HD],
         kvt.rearrange("p (h m) -> p h m", m=2 * HD)[:, :, 0:HD],
         rd=[("kvt",), ("kv0",)], wr=[("kv0",)])
    p.dma(kvbf[64:128, :].rearrange("p (h m) -> p h m", m=2 * HD)[:, :, HD:2 * HD],
          kvt.rearrange("p (h m) -> p h m", m=2 * HD)[:, :, HD:2 * HD],
          rd=[("kvt",), ("kv0",)], wr=[("kv0",)])
    p.act(ksf[:, :], ksacc[:, :], AF.Identity, bias=p.ccol(EPS_ATTN),
          rd=[("ksacc",)], wr=[("ksf",)])
    p.memset(ksz[:, :], 0.0, wr=[("ksbf",)])
    p.cp(ksz[0:64, :].rearrange("p (a b) -> p a b", b=2)[:, :, 0:1],
         ksf[0:64, :].rearrange("p (a o) -> p a o", o=1),
         rd=[("ksf",), ("ksbf",)], wr=[("ksbf",)])
    p.cp(ksz[64:128, :].rearrange("p (a b) -> p a b", b=2)[:, :, 1:2],
         ksf[64:128, :].rearrange("p (a o) -> p a o", o=1),
         rd=[("ksf",), ("ksbf",)], wr=[("ksbf",)])
    p.dma(wd_bf.rearrange("p (dc dd) -> p dc dd", dd=D),
          scratch["B_wd"].rearrange("(dc p) dd -> p dc dd", p=128),
          rd=[("B", "B_wd"), ("wqkv", 0)], wr=[("wqkv", 0), ("wd",)], q="sync")
    for tt in (range(TTI) if "B" in ATTN_PHASES and "A" in ATTN_PHASES else []):
        tp = tt % 2
        if "z" in ATTN_PHASES:
            for h in range(NH):
                ic = h // 2
                p.mm(ps_small[:, 16 + h:17 + h],
                     qT[:, ic * T + tt * 128: ic * T + tt * 128 + 128],
                     ksz[:, h:h + 1],
                     rd=[("qT", tt, ic), ("ksbf",)], wr=[("psks",)])
            p.recip(zrec[:, :], ps_z, rd=[("psks",)], wr=[("zrec",)])
        if "c" in ATTN_PHASES:
            for h in range(NH):
                ic = h // 2
                p.mm(ps_ctx[:, h * HD:(h + 1) * HD],
                     qT[:, ic * T + tt * 128: ic * T + tt * 128 + 128],
                     kvbf[:, h * HD:(h + 1) * HD],
                     rd=[("qT", tt, ic), ("kv0",)], wr=[("psctx",)])
            for h in (range(NH) if "u" in ATTN_PHASES else []):
                p.ts(czbf[:, h * HD:(h + 1) * HD], ps_ctx[:, h * HD:(h + 1) * HD],
                     zrec[:, h:h + 1], ALU.mult,
                     rd=[("psctx",), ("zrec",)], wr=[("czbf",)])
        if "T" in ATTN_PHASES:
            emit_tr8(p, ctxT_t[:, :].rearrange("p (k f) -> p k f", f=128),
                     czbf[:, :], ps_trB, c["ident_bf"], rd_src=[("czbf",)],
                     wr_dst=[("ctxT", kd) for kd in range(DT)], pkey=("pstr",))
        xt = X[:, tt * D:(tt + 1) * D]
        if "d" in ATTN_PHASES:
            emit_ln(p, xt, efp, stats[tp], scrB,
                    LN_EPS, rd_x=[("X", tt)], wr_out=[("efp", 0), ("efp", 1)], tag=("aln2", tp),
                    sckey=("scd",))
            for oh in range(2):
                for kd in range(DT):
                    p.mm(ps_w[1][:, :], ctxT_t[:, kd * 128:(kd + 1) * 128],
                         wd_bf[:, kd * D + oh * 512: kd * D + (oh + 1) * 512],
                         rd=[("ctxT", kd), ("wd",)], wr=[("psw", 1)],
                         start=(kd == 0), stop=False)
                p.mm(ps_w[1][:, :], c["ones_bf"][0:1, 0:128],
                     bias_bf[0:1, 3 * D + oh * 512: 3 * D + (oh + 1) * 512],
                     rd=[("abias", 3)], wr=[("psw", 1)], start=False, stop=True)
                p.tt(hsn[:, oh * 512:(oh + 1) * 512], ps_w[1][:, :],
                     efp[:, oh * 512:(oh + 1) * 512], ALU.add,
                     rd=[("psw", 1), ("efp", oh)], wr=[("hsn",)])
            emit_ln(p, hsn, hsn, stats[tp], scrB,
                    EPS_LN_ATTN, rd_x=[("hsn",)], wr_out=[("hsn",)], tag=("aln3", tp),
                    sckey=("scd",))
            p.tt(hsn, hsn, gb_bc[:, 0:D], ALU.mult,
                 rd=[("hsn",), ("gbbc",)], wr=[("hsn",)])
            p.tt(hsn, hsn, gb_bc[:, D:2 * D], ALU.add,
                 rd=[("hsn",), ("gbbc",)], wr=[("hsn",)])
            p.tt(xt, xt, hsn, ALU.add,
                 rd=[("X", tt), ("hsn",)], wr=[("X", tt)])



def emit_ffn1(p, st, c, X, dram, scratch):
    nc = p.nc
    TTI = T // 128
    JT = NB // 128
    HT1 = H1 // 128
    xnT = st.enter_context(nc.sbuf_tensor("ff_xnT", [128, DT * T], BF16))
    UT_bf = st.enter_context(nc.sbuf_tensor("ff_UT", [128, DT * NB], BF16))
    rWT_bf = st.enter_context(nc.sbuf_tensor("ff_rWT", [128, DT * NB], BF16))
    rV_bf = st.enter_context(nc.sbuf_tensor("ff_rV", [128, JT * D], BF16))
    rb_bf = st.enter_context(nc.sbuf_tensor("ff_rb", [1, NB], BF16))
    lngb = st.enter_context(nc.sbuf_tensor("ff_lngb", [128, 2 * D], F32))
    gamb = st.enter_context(nc.sbuf_tensor("ff_gamb", [128, D], F32))
    scrA = st.enter_context(nc.sbuf_tensor("ff_scrA", [128, 2048], F32))
    scrB = st.enter_context(nc.sbuf_tensor("ff_scrB", [128, 2048], F32))
    stats = [st.enter_context(nc.sbuf_tensor(f"ff_stats{i}", [128, 8], F32)) for i in range(2)]
    invUV = st.enter_context(nc.sbuf_tensor("ff_iuv", [128, 16], F32))
    ral = st.enter_context(nc.sbuf_tensor("ff_ral", [128, NB], F32))
    alpha = st.enter_context(nc.sbuf_tensor("ff_alpha", [128, NB], F32))
    aw = st.enter_context(nc.sbuf_tensor("ff_aw", [128, NB], F32))
    m8 = st.enter_context(nc.sbuf_tensor("ff_m8", [128, 8], F32))
    xr = st.enter_context(nc.sbuf_tensor("ff_xr", [128, D], F32))
    xrbf = st.enter_context(nc.sbuf_tensor("ff_xrbf", [128, D], BF16))
    xrT_t = st.enter_context(nc.sbuf_tensor("ff_xrT", [128, DT * 128], BF16))
    czbf = st.enter_context(nc.sbuf_tensor("ff_cz", [128, NB], BF16))
    cT_t = st.enter_context(nc.sbuf_tensor("ff_cT", [128, JT * 128], BF16))
    h1t = [st.enter_context(nc.sbuf_tensor(f"ff_h1{i}", [128, 256], BF16)) for i in range(2)]
    w1t = [st.enter_context(nc.sbuf_tensor(f"ff_w1t{i}", [128, 2 * DT * 128], BF16)) for i in range(2)]
    w2t = [st.enter_context(nc.sbuf_tensor(f"ff_w2t{i}", [128, 2 * D], BF16)) for i in range(2)]
    ps_trB = st.enter_context(nc.psum_tensor("ff_ptr", [128, 1024], BF16))
    ps_s = st.enter_context(nc.psum_tensor("ff_ps", [128, 2048], F32))
    ps_h0 = st.enter_context(nc.psum_tensor("ff_ph0", [128, 256], F32))
    ps_h1 = st.enter_context(nc.psum_tensor("ff_ph1", [128, 256], F32))
    ps_h = [ps_h0[:, :], ps_h1[:, :]]
    ps_wk = st.enter_context(nc.psum_tensor("ff_pwk", [128, 512], F32))

    p.dma(UT_bf[:, :].rearrange("p (dc nb) -> p dc nb", nb=NB),
          scratch["B_UT"].rearrange("(dc p) nb -> p dc nb", p=128),
          rd=[("B", "B_UT")], wr=[("UT",)], q="sync")
    p.dma(rWT_bf[:, :].rearrange("p (dc nb) -> p dc nb", nb=NB),
          scratch["B_rWT"].rearrange("(dc p) nb -> p dc nb", p=128),
          rd=[("B", "B_rWT")], wr=[("rWT",)], q="sync")
    p.dma(rV_bf[:, :].rearrange("p (jt dd) -> p jt dd", dd=D),
          scratch["B_rV"].rearrange("(jt p) dd -> p jt dd", p=128),
          rd=[("B", "B_rV")], wr=[("rV",)], q="sync")
    p.dma(scrB[0:1, 0:NB], dram["f_rb"], wr=[("sB",)])
    p.cp(rb_bf[:, :], scrB[0:1, 0:NB], rd=[("sB",)], wr=[("rb",)])
    for j, gname in enumerate(("f_lng", "f_lnb", "f_gam")):
        p.dma(scrB[0:1, 0:D], dram[gname], wr=[("sB",)])
        p.cp(xrbf[0:1, 0:D], scrB[0:1, 0:D], rd=[("sB",)], wr=[("xrbf",)])
        for oh in range(2):
            p.mm(ps_wk[:, :], c["ones_bf"][0:1, 0:128], xrbf[0:1, oh * 512:(oh + 1) * 512],
                 rd=[("xrbf",)], wr=[("pswk",)])
            dst = lngb[:, j * D + oh * 512: j * D + (oh + 1) * 512] if j < 2 else \
                  gamb[:, oh * 512:(oh + 1) * 512]
            p.act(dst, ps_wk[:, :], AF.Identity, rd=[("pswk",)], wr=[("lngb",)])
    for jt in range(JT):
        p.dma(scrB[:, 0:D], dram["f_U"][jt * 128:(jt + 1) * 128, :], wr=[("sB",)])
        p.act(scrB[:, 1024:2048], scrB[:, 0:D], AF.Square, accum_out=invUV[:, jt:jt + 1],
              rd=[("sB",)], wr=[("sB2",), ("iuv",)])
        p.dma(scrB[:, 0:D], dram["f_V"][jt * 128:(jt + 1) * 128, :], wr=[("sB",)])
        p.act(scrB[:, 1024:2048], scrB[:, 0:D], AF.Square, accum_out=invUV[:, 4 + jt:5 + jt],
              rd=[("sB",)], wr=[("sB2",), ("iuv",)])
    p.act(invUV[:, 8:16], invUV[:, 0:8], AF.Ln, rd=[("iuv",)], wr=[("iuv",)])
    p.act(invUV[:, 0:8], invUV[:, 8:16], AF.Exp, scale=p.ccol(0.5), rd=[("iuv",)], wr=[("iuv",)])
    p.ts(invUV[:, 0:8], invUV[:, 0:8], FFN_EPS, ALU.max, rd=[("iuv",)], wr=[("iuv",)])
    p.recip(invUV[:, 0:8], invUV[:, 0:8], rd=[("iuv",)], wr=[("iuv",)])
    p.tt(invUV[:, 8:12], invUV[:, 0:4], invUV[:, 4:8], ALU.mult, rd=[("iuv",)], wr=[("iuv",)])
    for tt in range(TTI):
        tp = tt % 2
        xt = X[:, tt * D:(tt + 1) * D]
        emit_ln(p, xt, xrbf[:, :], stats[tp], scrA[:, 0:1024],
                LN_EPS, rd_x=[("X", tt)], wr_out=[("xrbf",)], tag=("fln", tp))
        dst3d = xnT[:, :].rearrange("p (k t) -> p k t", t=T)[:, :, tt * 128:tt * 128 + 128]
        emit_tr8(p, dst3d, xrbf[:, :], ps_trB, c["ident_bf"],
                 rd_src=[("xrbf",)],
                 wr_dst=[("xnT", tt, kd) for kd in range(DT)], pkey=("pstr",))
    for tt in range(TTI):
        tp = tt % 2
        xt = X[:, tt * D:(tt + 1) * D]
        emit_ln(p, xt, scrA[:, 0:1024], stats[tp], scrA[:, 1024:2048],
                LN_EPS, rd_x=[("X", tt)], wr_out=[("xn",)], tag=("fln2", tp))
        emit_ln(p, scrA[:, 0:1024], xr[:, :], stats[tp], scrA[:, 1024:2048],
                LN_EPS, rd_x=[("xn",)], wr_out=[("xr",)], tag=("fln3", tp))
        p.tt(xr[:, :], xr[:, :], lngb[:, 0:D], ALU.mult, rd=[("xr",), ("lngb",)], wr=[("xr",)])
        p.tt(xr[:, :], xr[:, :], lngb[:, D:2 * D], ALU.add, rd=[("xr",), ("lngb",)], wr=[("xr",)])
        p.cp(xrbf[:, :], xr[:, :], rd=[("xr",)], wr=[("xrbf",)])
        emit_tr8(p, xrT_t[:, :].rearrange("p (k f) -> p k f", f=128),
                 xrbf[:, :], ps_trB, c["ident_bf"], rd_src=[("xrbf",)],
                 wr_dst=[("xrT", kd) for kd in range(DT)], pkey=("pstr",))
        for kd in range(DT):
            p.mm(ps_wk[:, :], xrT_t[:, kd * 128:(kd + 1) * 128],
                 rWT_bf[:, kd * NB:(kd + 1) * NB],
                 rd=[("xrT", kd), ("rWT",)], wr=[("pswk",)],
                 start=(kd == 0), stop=False)
        p.mm(ps_wk[:, :], c["ones_bf"][0:1, 0:128], rb_bf[0:1, :],
             rd=[("rb",)], wr=[("pswk",)], start=False, stop=True)
        p.ts(ral[:, :], ps_wk[:, :], TAU, ALU.min, -TAU, ALU.max,
             rd=[("pswk",)], wr=[("ral",)])
        p.act(alpha[:, :], ral[:, :], AF.Abs, rd=[("ral",)], wr=[("alpha",)])
        p.act(aw[:, :], alpha[:, :], AF.Exp, scale=-1.0, rd=[("alpha",)], wr=[("aw",)])
        p.act(alpha[:, :], aw[:, :], AF.Ln, bias=1.0, rd=[("aw",)], wr=[("alpha",)])
        p.stt(alpha[:, :], ral[:, :], 0.0, alpha[:, :], ALU.max, ALU.add,
              rd=[("ral",), ("alpha",)], wr=[("alpha",)])
        p.max8(m8[:, :], alpha[:, :], rd=[("alpha",)], wr=[("m8",)])
        p.ts(aw[:, :], alpha[:, :], m8[:, 7:8], ALU.is_ge, rd=[("alpha",), ("m8",)], wr=[("aw",)])
        p.tt(aw[:, :], aw[:, :], alpha[:, :], ALU.mult, rd=[("aw",), ("alpha",)], wr=[("aw",)])
        p.red(stats[tp][:, 6:7], aw[:, :], ALU.add, rd=[("aw",)], wr=[("fst", tp)])
        p.act(stats[tp][:, 7:8], stats[tp][:, 6:7], AF.Tanh, rd=[("fst", tp)], wr=[("fst", tp)])
        p.ts(stats[tp][:, 6:7], stats[tp][:, 6:7], FFN_EPS, ALU.add, rd=[("fst", tp)], wr=[("fst", tp)])
        p.recip(stats[tp][:, 6:7], stats[tp][:, 6:7], rd=[("fst", tp)], wr=[("fst", tp)])
        p.tt(stats[tp][:, 6:7], stats[tp][:, 6:7], stats[tp][:, 7:8], ALU.mult,
             rd=[("fst", tp)], wr=[("fst", tp)])
        p.ts(aw[:, :], aw[:, :], stats[tp][:, 6:7], ALU.mult, rd=[("aw",), ("fst", tp)], wr=[("aw",)])
        for kd in range(DT):
            p.mm(ps_wk[:, :], xnT[:, kd * T + tt * 128: kd * T + tt * 128 + 128],
                 UT_bf[:, kd * NB:(kd + 1) * NB],
                 rd=[("xnT", tt, kd), ("UT",)], wr=[("pswk",)],
                 start=(kd == 0), stop=(kd == DT - 1))
        p.tt(ral[:, :], ps_wk[:, :], aw[:, :], ALU.mult, rd=[("pswk",), ("aw",)], wr=[("ral",)])
        p.cp(czbf[:, :], ral[:, :], rd=[("ral",)], wr=[("czbf",)])
        for jt in range(JT):
            p.trans(ps_trB[:, jt * 128:(jt + 1) * 128], czbf[:, jt * 128:(jt + 1) * 128],
                    c["ident_bf"][:, :], rd=[("czbf",)], wr=[("pstr",)])
        for jt in range(JT):
            p.ts(cT_t[:, jt * 128:(jt + 1) * 128], ps_trB[:, jt * 128:(jt + 1) * 128],
                 invUV[:, 8 + jt:9 + jt], ALU.mult,
                 rd=[("pstr",), ("iuv",)], wr=[("cT", jt)])
        for oh in range(2):
            for jt in range(JT):
                p.mm(ps_wk[:, :], cT_t[:, jt * 128:(jt + 1) * 128],
                     rV_bf[:, jt * D + oh * 512: jt * D + (oh + 1) * 512],
                     rd=[("cT", jt), ("rV",)], wr=[("pswk",)],
                     start=(jt == 0), stop=(jt == JT - 1))
            p.tt(scrA[:, 1024:1536], ps_wk[:, :], gamb[:, oh * 512:(oh + 1) * 512], ALU.mult,
                 rd=[("pswk",), ("lngb",)], wr=[("dyn",)])
            xs = X[:, tt * D + oh * 512: tt * D + (oh + 1) * 512]
            p.tt(xs, xs, scrA[:, 1024:1536], ALU.add,
                 rd=[("X", tt), ("dyn",)], wr=[("X", tt)])
    for tc in range(T // 256):
        for ht in range(HT1):
            hp = (ht // 2) % 2
            sub = ht % 2
            if sub == 0:
                p.dma(w1t[hp][:, :].rearrange("p (dc h) -> p dc h", h=256),
                      scratch["B_w1T"][:, ht * 128:(ht + 2) * 128]
                      .rearrange("(dc p) h -> p dc h", p=128),
                      rd=[("B", "B_w1T")], wr=[("w1t", hp)], q="sync" if hp == 0 else "gpsimd")
                p.dma(w2t[hp][:, :].rearrange("p (o d) -> p o d", d=D),
                      scratch["B_w2T"][ht * 128:(ht + 2) * 128, :]
                      .rearrange("(o p) d -> p o d", p=128),
                      rd=[("B", "B_w2T")], wr=[("w2t", hp)], q="scalar")
            hq = hp * 2 + sub if False else hp
            for dc in range(DT):
                p.mm(ps_h[ht % 2], w1t[hp][:, (dc * 2 + sub) * 128:(dc * 2 + sub + 1) * 128],
                     xnT[:, dc * T + tc * 256: dc * T + (tc + 1) * 256],
                     rd=[("w1t", hp), ("xnT", tc * 2, dc), ("xnT", tc * 2 + 1, dc)],
                     wr=[("psh", ht % 2)],
                     start=(dc == 0), stop=(dc == DT - 1))
            p.act(h1t[ht % 2][:, :], ps_h[ht % 2], AF.Gelu, rd=[("psh", ht % 2)], wr=[("h1t", ht % 2)])
            for tsub in range(2):
                for oh in range(2):
                    p.mm(ps_s[:, (tsub * 2 + oh) * 512:(tsub * 2 + oh + 1) * 512],
                         h1t[ht % 2][:, tsub * 128:(tsub + 1) * 128],
                         w2t[hp][:, sub * D + oh * 512: sub * D + (oh + 1) * 512],
                         rd=[("h1t", ht % 2), ("w2t", hp)], wr=[("pss",)],
                         start=(ht == 0), stop=(ht == HT1 - 1))
        for tsub in range(2):
            tt = tc * 2 + tsub
            for oh in range(2):
                xs = X[:, tt * D + oh * 512: tt * D + (oh + 1) * 512]
                p.tt(xs, xs, ps_s[:, (tsub * 2 + oh) * 512:(tsub * 2 + oh + 1) * 512],
                     ALU.add, rd=[("X", tt), ("pss",)], wr=[("X", tt)])



def _mamba_host(inp):
    f = np.ascontiguousarray
    cw = inp["L0_attn_conv_w"].reshape(E, 4)
    d = {
        "m_A": f(inp["L0_attn_A_log"].reshape(ET, 128, NST).transpose(1, 0, 2).reshape(128, ET * NST)),
        "m_cw": f(cw.reshape(ET, 128, 4).transpose(1, 0, 2).reshape(128, ET * 4)),
        "m_cb": f(inp["L0_attn_conv_b"].reshape(ET, 128).T),
        "m_bdt": f(inp["L0_attn_b_dt"].reshape(ET, 128).T),
        "m_dsk": f(inp["L0_attn_D_skip"].reshape(ET, 128).T),
        "m_wdt": f(inp["L0_attn_W_dt"]),
        "m_win": f(inp["L0_attn_W_in"]),
        "m_wout": f(inp["L0_attn_W_out"]),
    }
    wxp = np.zeros((E, 128), np.float32)
    wx = inp["L0_attn_W_xproj"]
    wxp[:, 0:16] = wx[:, DTR:DTR + NST]
    wxp[:, 32:48] = wx[:, DTR + NST:DTR + 2 * NST]
    wxp[:, 64:128] = wx[:, 0:DTR]
    d["m_wxp"] = f(wxp)
    return d


def _moe_host(inp):
    f = np.ascontiguousarray
    HT = HMOE // 128
    return {
        "g_wT": f(inp["L0_ffn_gate_w"].T),
        "g_b": f(inp["L0_ffn_gate_b"].reshape(1, NEXP)),
        "g_w1": f(inp["L0_ffn_w1"]),
        "g_b1": f(inp["L0_ffn_b1"].reshape(NEXP, HT, 128).transpose(2, 0, 1).reshape(128, NEXP * HT)),
        "g_w2": f(inp["L0_ffn_w2"]),
        "g_b2": f(inp["L0_ffn_b2"]),
    }


def _attn_host(inp):
    f = np.ascontiguousarray
    return {
        "a_wq": f(inp["L1_attn_Wq"].T),
        "a_wk": f(inp["L1_attn_Wk"].T),
        "a_wv": f(inp["L1_attn_Wv"].T),
        "a_wd": f(inp["L1_attn_Wd"].T),
        "a_bq": f(inp["L1_attn_bq"].reshape(1, D)),
        "a_bk": f(inp["L1_attn_bk"].reshape(1, D)),
        "a_bv": f(inp["L1_attn_bv"].reshape(1, D)),
        "a_bd": f(inp["L1_attn_bd"].reshape(1, D)),
        "a_gamma": f(inp["L1_attn_gamma"].reshape(1, D)),
        "a_beta": f(inp["L1_attn_beta"].reshape(1, D)),
    }


def _ffn1_host(inp):
    f = np.ascontiguousarray
    return {
        "f_UT": f(inp["L1_ffn_raw_U"].T),
        "f_rWT": f(inp["L1_ffn_router_W"].T),
        "f_rV": f(inp["L1_ffn_raw_V"]),
        "f_rb": f(inp["L1_ffn_router_b"].reshape(1, NB)),
        "f_lng": f(inp["L1_ffn_ln_g"].reshape(1, D)),
        "f_lnb": f(inp["L1_ffn_ln_b"].reshape(1, D)),
        "f_gam": f(inp["L1_ffn_gamma"].reshape(1, D)),
        "f_U": f(inp["L1_ffn_raw_U"]),
        "f_V": f(inp["L1_ffn_raw_V"]),
        "f_w1T": f(inp["L1_ffn_W1"].T),
        "f_w2T": f(inp["L1_ffn_W2"].T),
    }


_DRAM_SHAPES = {
    "m_A": (128, ET * NST), "m_cw": (128, ET * 4), "m_cb": (128, ET),
    "m_bdt": (128, ET), "m_dsk": (128, ET), "m_wdt": (DTR, E),
    "m_win": (D, 2 * E), "m_wout": (E, D), "m_wxp": (E, 128),
    "g_wT": (D, NEXP), "g_b": (1, NEXP), "g_w1": (NEXP, D, HMOE),
    "g_b1": (128, NEXP * (HMOE // 128)), "g_w2": (NEXP, HMOE, D), "g_b2": (NEXP, D),
    "a_wq": (D, D), "a_wk": (D, D), "a_wv": (D, D), "a_wd": (D, D),
    "a_bq": (1, D), "a_bk": (1, D), "a_bv": (1, D), "a_bd": (1, D),
    "a_gamma": (1, D), "a_beta": (1, D),
    "f_UT": (D, NB), "f_rWT": (D, NB), "f_rV": (NB, D), "f_rb": (1, NB),
    "f_lng": (1, D), "f_lnb": (1, D), "f_gam": (1, D),
    "f_U": (NB, D), "f_V": (NB, D), "f_w1T": (D, H1), "f_w2T": (H1, D),
}

_SCRATCH_SHAPES = {
    "B_win": (D, 2 * E), "B_wxp": (E, 128), "B_wdt": (DTR, E), "B_wout": (E, D),
    "B_gwT": (D, NEXP),
    "B_wq": (D, D), "B_wk": (D, D), "B_wv": (D, D), "B_wd": (D, D),
    "B_UT": (D, NB), "B_rWT": (D, NB), "B_rV": (NB, D),
    "B_w1T": (D, H1), "B_w2T": (H1, D),
}


def _build(emits, TC=256):
    nc = bass.Bass("TRN2", target_bir_lowering=False, debug=False)
    x_in = nc.dram_tensor("x_res_in", [T, D], F32, kind="ExternalInput").ap()
    x_out = nc.dram_tensor("x_res_out", [T, D], F32, kind="ExternalOutput").ap()
    dram = {k: nc.dram_tensor(k, list(v), F32, kind="ExternalInput").ap()
            for k, v in _DRAM_SHAPES.items()}
    scratch = {k: nc.dram_tensor(k, list(v), BF16, kind="Internal").ap()
               for k, v in _SCRATCH_SHAPES.items()}
    with ExitStack() as st:
        X = st.enter_context(nc.sbuf_tensor("X", [128, (T // 128) * D], F32))
        sems = {}
        for e in ENGS:
            sems[("c", e)] = st.enter_context(nc.semaphore(f"semc_{e}"))
            sems[("d", e)] = st.enter_context(nc.semaphore(f"semd_{e}"))
        blk = st.enter_context(nc.Block())
        p = P(nc)
        c = make_consts(p, st)
        p.barrier()
        pc_f = [st.enter_context(nc.sbuf_tensor(f"pc_f{i}", [128, 1024], F32)) for i in range(2)]
        pc_b = [st.enter_context(nc.sbuf_tensor(f"pc_b{i}", [128, 1024], BF16)) for i in range(2)]
        p.dma(X[:, :].rearrange("p (tt d) -> p tt d", d=D),
              x_in.rearrange("(tt p) d -> p tt d", p=128),
              wr=[("X", tt) for tt in range(T // 128)], q="gpsimd")
        emit_precast(p, st, dram, scratch, pc_f, pc_b)
        for em in emits:
            with ExitStack() as sti:
                if em is emit_mamba:
                    em(p, sti, c, X, dram, scratch, TC=TC)
                else:
                    em(p, sti, c, X, dram, scratch)
            p.barrier()
        p.dma(x_out.rearrange("(tt p) d -> p tt d", p=128),
              X[:, :].rearrange("p (tt d) -> p tt d", d=D),
              rd=[("X", tt) for tt in range(T // 128)], wr=[("OUT",)], q="gpsimd")
        p.finalize(sems)
    return nc


def build_mamba_program(TC=256):
    return _build([emit_mamba], TC=TC)


def build_moe_program():
    return _build([emit_moe])


def build_attn_program():
    return _build([emit_attn])


def build_ffn1_program():
    return _build([emit_ffn1])


def build_fused_program(TC=256):
    return _build([emit_mamba, emit_moe, emit_attn, emit_ffn1], TC=TC)


_PROGS = {}


def _all_host(inp):
    h = {}
    h.update(_mamba_host(inp))
    h.update(_moe_host(inp))
    h.update(_attn_host(inp))
    h.update(_ffn1_host(inp))
    return h


def kernel(**inputs):
    inp = {k: np.asarray(v) for k, v in inputs.items()}
    x = np.ascontiguousarray(inp["L0_attn_x"], dtype=np.float32)
    if "fused" not in _PROGS:
        _PROGS["fused"] = build_fused_program(TC=256)
    host = _all_host(inp)
    in_maps = [dict(host, x_res_in=np.ascontiguousarray(x[ci])) for ci in range(8)]
    res = run_bass_kernel_spmd(_PROGS["fused"], in_maps, core_ids=list(range(8)))
    return np.stack([np.asarray(r["x_res_out"]) for r in res.results]).astype(np.float32)
